# version 17
# baseline (speedup 1.0000x reference)
import numpy as np
import ml_dtypes
import jax
import concourse.bass as bass
import concourse.mybir as mybir
from concourse.bass_utils import run_bass_kernel_spmd

F32 = mybir.dt.float32
BF16 = mybir.dt.bfloat16
AF = mybir.ActivationFunctionType
ALU = mybir.AluOpType
AX = mybir.AxisListType

B_, S_, E_, H_ = 2, 8192, 1024, 16
HD_ = 64
NCORE = 8
T_ = 2048
NCH = 8
MAGIC = 12582912.0


class Prog:

    def __init__(self, nc):
        self.nc = nc
        self.items = []
        self.phase = 0

    def new_phase(self):
        self.phase += 1

    def add(self, eng, fn, inc=1, pre=()):
        self.items.append((eng, fn, inc, self.phase, tuple(pre)))

    def dma(self, eng, out, in_, pre=()):
        self.add(eng, (lambda e, o=out, i=in_: e.dma_start(out=o, in_=i)), 16, pre)

    def emit(self, block, sems):
        nph = self.phase + 1
        assert len(sems) >= nph
        cum = [0] * nph
        plan = []
        for eng, fn, inc, ph, pre in self.items:
            w = [(sems[ph], cum[ph])] if cum[ph] > 0 else []
            if cum[ph] == 0 and ph > 0:
                w.append((sems[ph - 1], cum[ph - 1]))
            w.extend(pre)
            plan.append((eng, fn, inc, sems[ph], w))
            cum[ph] += inc
        used = sorted({en for en, *_ in plan})
        final = (sems[self.phase], cum[self.phase])

        def mk(engname):
            def emitter(eng):
                for en, fn, inc, sem, waits in plan:
                    if en != engname:
                        continue
                    for ws, wv in waits:
                        eng.wait_ge(ws, wv)
                    r = fn(eng)
                    if inc:
                        r.then_inc(sem, inc)
                if engname == used[0]:
                    eng.wait_ge(final[0], final[1])
            return emitter
        for name in used:
            getattr(block, name)(mk(name))


def _ap(t, off, dims):
    return bass.AP(t, off, dims)


class P2:
    ENGS = ("sync", "scalar", "vector", "tensor", "gpsimd")

    def __init__(self, nc):
        self.nc = nc
        self.items = []
        self.phase = 0
        self.cnt = {}
        self.serial = False
        self._last = None

    def tok(self, eng):
        return (self.phase, eng, self.cnt.get((self.phase, eng), 0))

    def new_phase(self):
        self.phase += 1

    def add(self, eng, fn, inc=1, deps=()):
        deps = list(deps)
        if self.serial and self._last is not None:
            deps.append(self._last)
        ph = self.phase
        c = self.cnt.get((ph, eng), 0) + inc
        self.cnt[(ph, eng)] = c
        self.items.append((eng, fn, inc, ph, tuple(deps)))
        t = (ph, eng, c, inc)
        self._last = t
        return t

    def dma(self, eng, out, in_, deps=()):
        return self.add(eng, (lambda e, o=out, i=in_: e.dma_start(out=o, in_=i)),
                        16, deps)

    SERIALIZE_ALL = False
    SERIALIZE_PHASES = set()
    SERIALIZE_RANGES = []

    def emit(self, block):
        nc = self.nc
        nph = self.phase + 1
        sems = {}
        guards = []
        for ph in range(nph):
            for eng in self.ENGS:
                if self.cnt.get((ph, eng), 0) > 0:
                    g = nc.semaphore(f"p{ph}_{eng}")
                    sems[(ph, eng)] = g.__enter__()
                    guards.append(g)
        finals = {}
        for ph in range(nph):
            finals[ph] = [(ph, eng, self.cnt[(ph, eng)], 1)
                          for eng in self.ENGS if self.cnt.get((ph, eng), 0) > 0]
        seen = set()
        plan = []
        cum = {}
        prev_cmp = {}
        for eng, fn, inc, ph, deps in self.items:
            waits = []
            if (ph, eng) not in seen:
                seen.add((ph, eng))
                if ph > 0:
                    waits.extend((p_, e_, c_) for p_, e_, c_, *_ in finals[ph - 1])
            if inc != 16 and eng in ("vector", "scalar", "gpsimd"):
                pc = prev_cmp.get(eng)
                if pc is not None:
                    waits.append(pc)
                if inc == 1:
                    prev_cmp[eng] = (ph, eng, cum.get((ph, eng), 0) + inc)
            for d in deps:
                if isinstance(d[0], int):
                    waits.append((d[0], d[1], d[2]))
                else:
                    waits.append(d)
            idx = len(plan)
            in_rng = any(lo <= idx < hi for lo, hi in self.SERIALIZE_RANGES)
            if (self.SERIALIZE_ALL or ph in self.SERIALIZE_PHASES or in_rng) and plan:
                pe, _, pinc, pph, _ = plan[-1]
                waits.append((pph, pe, cum.get((pph, pe), 0)))
            plan.append((eng, fn, inc, ph, waits))
            cum[(ph, eng)] = cum.get((ph, eng), 0) + inc
        used = sorted({e for e, *_ in plan})
        lastph = self.phase

        def mk(engname):
            def emitter(eng):
                for en, fn, inc, ph, waits in plan:
                    if en != engname:
                        continue
                    dedup = {}
                    for w in waits:
                        if isinstance(w[0], int):
                            key = ("s", w[0], w[1])
                            dedup[key] = max(dedup.get(key, 0), w[2])
                        else:
                            key = ("r", id(w[0]))
                            dedup[key] = (w[0], max(
                                dedup[key][1], w[1]) if key in dedup else w[1])
                    for key, v in dedup.items():
                        if key[0] == "s":
                            if v > 0 and (key[1], key[2]) in sems:
                                eng.wait_ge(sems[(key[1], key[2])], v)
                        else:
                            eng.wait_ge(v[0], v[1])
                    r = fn(eng)
                    if inc:
                        r.then_inc(sems[(ph, en)], inc)
                if engname == used[0]:
                    for (p_, e_, c_, *_x) in finals[lastph]:
                        eng.wait_ge(sems[(p_, e_)], c_)
            return emitter
        for name in used:
            getattr(block, name)(mk(name))
        return guards


DEBUG_A = False
DEBUG_FULL = False
MARKS = {}
A4_NOLAG = False


def build_A():
    nc = bass.Bass("TRN2", target_bir_lowering=False)
    xT = nc.dram_tensor("xT", [E_, T_], F32, kind="ExternalInput")
    wqT = nc.dram_tensor("wqT", [E_, E_], F32, kind="ExternalInput")
    wkT = nc.dram_tensor("wkT", [E_, E_], F32, kind="ExternalInput")
    wvT = nc.dram_tensor("wvT", [E_, E_], F32, kind="ExternalInput")
    selT = nc.dram_tensor("selT", [E_, 64], F32, kind="ExternalInput")
    keys = nc.dram_tensor("keys", [64, E_, 128], F32, kind="ExternalInput")
    values = nc.dram_tensor("values", [64, 128, E_], F32, kind="ExternalInput")
    cosL = nc.dram_tensor("cosL", [T_, 32], F32, kind="ExternalInput")
    sinL = nc.dram_tensor("sinL", [T_, 32], F32, kind="ExternalInput")
    idf = nc.dram_tensor("idf", [128, 128], F32, kind="ExternalInput")
    onesr = nc.dram_tensor("onesr", [1, 128], F32, kind="ExternalInput")
    onesc = nc.dram_tensor("onesc", [128, 1], F32, kind="ExternalInput")

    qT_o = nc.dram_tensor("qT_o", [E_, T_], BF16, kind="ExternalOutput")
    kT_o = nc.dram_tensor("kT_o", [E_, T_], BF16, kind="ExternalOutput")
    v_o = nc.dram_tensor("v_o", [T_, E_], BF16, kind="ExternalOutput")
    g_o = nc.dram_tensor("g_o", [E_, T_], F32, kind="ExternalOutput")
    hg_d = nc.dram_tensor("hg_d", [T_, 64 * 128], BF16,
                          kind="ExternalOutput" if DEBUG_A else "Internal")

    from contextlib import ExitStack
    es = ExitStack()
    with es:
        xn_bf = es.enter_context(nc.sbuf_tensor("xn_bf", [128, NCH * T_], BF16))
        gates = es.enter_context(nc.sbuf_tensor("gates", [128, 16 * 64], F32))
        id_f = es.enter_context(nc.sbuf_tensor("id_f", [128, 128], F32))
        id_bf = es.enter_context(nc.sbuf_tensor("id_bf", [128, 128], BF16))
        ones_r = es.enter_context(nc.sbuf_tensor("ones_r", [1, 128], F32))
        ones_c = es.enter_context(nc.sbuf_tensor("ones_c", [128, 1], F32))
        PSA = es.enter_context(nc.psum_tensor("PSA", [128, 2048], F32))
        PSB = es.enter_context(nc.psum_tensor("PSB", [128, 1024], F32))
        PSC = es.enter_context(nc.psum_tensor("PSC", [128, 512], F32))
        PST = es.enter_context(nc.psum_tensor("PST", [128, 1024], BF16))

        sem = es.enter_context(nc.semaphore("step"))
        block = es.enter_context(nc.Block())
        p = Prog(nc)

        es1 = ExitStack()
        es1.__enter__()
        selT_s = es1.enter_context(nc.sbuf_tensor("selT_s", [128, NCH * 64], F32))
        mu_rep = es1.enter_context(nc.sbuf_tensor("mu_rep", [128, T_], F32))
        rs_rep = es1.enter_context(nc.sbuf_tensor("rs_rep", [128, T_], F32))
        xch = es1.enter_context(nc.sbuf_tensor("xch", [128, T_], F32))
        sq = es1.enter_context(nc.sbuf_tensor("sq", [128, T_], F32))
        row_a = es1.enter_context(nc.sbuf_tensor("row_a", [1, T_], F32))
        row_b = es1.enter_context(nc.sbuf_tensor("row_b", [1, T_], F32))
        row_c = es1.enter_context(nc.sbuf_tensor("row_c", [1, 512], F32))
        row_d = es1.enter_context(nc.sbuf_tensor("row_d", [1, 512], F32))
        sels = es1.enter_context(nc.sbuf_tensor("sels", [64, T_], F32))
        seltm = es1.enter_context(nc.sbuf_tensor("seltm", [128, 64], F32))
        wk1 = es1.enter_context(nc.sbuf_tensor("wk1", [128, 64], F32))
        wk2 = es1.enter_context(nc.sbuf_tensor("wk2", [128, 64], F32))
        mcol = es1.enter_context(nc.sbuf_tensor("mcol", [128, 1], F32))

        p.dma("sync", id_f[:, :], idf[:, :])
        p.dma("gpsimd", id_bf[:, :], idf[:, :])
        p.dma("sync", ones_r[:, :], onesr[:, :])
        p.dma("sync", ones_c[:, :], onesc[:, :])
        for c in range(NCH):
            p.dma("sync", selT_s[:, c * 64:(c + 1) * 64],
                  selT[c * 128:(c + 1) * 128, :])

        for ts in range(4):
            for c in range(NCH):
                p.dma("sync", xch[:, 0:512],
                      xT[c * 128:(c + 1) * 128, ts * 512:(ts + 1) * 512])
                p.add("scalar", lambda e, c=c: e.activation(
                    sq[:, 0:512], xch[:, 0:512], AF.Square))
                p.add("tensor", lambda e, c=c: e.matmul(
                    PSC[0:1, :], ones_c[:, :], xch[:, 0:512],
                    start=(c == 0), stop=(c == NCH - 1)))
                p.add("tensor", lambda e, c=c: e.matmul(
                    PSB[0:1, 0:512], ones_c[:, :], sq[:, 0:512],
                    start=(c == 0), stop=(c == NCH - 1)))
            sl = slice(ts * 512, (ts + 1) * 512)
            p.add("vector", lambda e, sl=sl: e.tensor_scalar_mul(
                row_a[:, sl], PSC[0:1, :], 1.0 / 1024.0))
            p.add("vector", lambda e: e.tensor_scalar_mul(
                row_c[:, :], PSB[0:1, 0:512], 1.0 / 1024.0))
            p.add("vector", lambda e, sl=sl: e.tensor_tensor(
                row_d[:, :], row_a[:, sl], row_a[:, sl], ALU.mult))
            p.add("vector", lambda e: e.tensor_tensor(
                row_c[:, :], row_c[:, :], row_d[:, :], ALU.subtract))
            p.add("vector", lambda e: e.tensor_scalar_add(
                row_c[:, :], row_c[:, :], 1e-5))
            p.add("scalar", lambda e: e.activation(
                row_d[:, :], row_c[:, :], AF.Sqrt))
            p.add("vector", lambda e, sl=sl: e.reciprocal(row_b[:, sl], row_d[:, :]))
            p.add("tensor", lambda e, sl=sl: e.matmul(
                PSB[0:128, 0:512], ones_r[:, :], row_a[:, sl], start=True, stop=True))
            p.add("vector", lambda e, sl=sl: e.tensor_copy(
                mu_rep[:, sl], PSB[0:128, 0:512]))
            p.add("tensor", lambda e, sl=sl: e.matmul(
                PSC[0:128, :], ones_r[:, :], row_b[:, sl], start=True, stop=True))
            p.add("vector", lambda e, sl=sl: e.tensor_copy(
                rs_rep[:, sl], PSC[0:128, :]))

        for c in range(NCH):
            p.dma("sync", xch[:, :], xT[c * 128:(c + 1) * 128, :])
            p.add("vector", lambda e: e.tensor_tensor(
                sq[:, :], xch[:, :], mu_rep[:, :], ALU.subtract))
            p.add("vector", lambda e: e.tensor_tensor(
                xch[:, :], sq[:, :], rs_rep[:, :], ALU.mult))
            p.add("vector", lambda e, c=c: e.tensor_copy(
                xn_bf[:, c * T_:(c + 1) * T_], xch[:, :]))
            for ts in range(4):
                p.add("tensor", lambda e, c=c, ts=ts: e.matmul(
                    PSA[0:64, ts * 512:(ts + 1) * 512],
                    selT_s[:, c * 64:(c + 1) * 64],
                    xch[:, ts * 512:(ts + 1) * 512],
                    start=(c == 0), stop=(c == NCH - 1)))

        p.add("scalar", lambda e: e.activation(sels[:, :], PSA[0:64, :], AF.Sigmoid))
        for tt in range(16):
            p.add("tensor", lambda e, tt=tt: e.transpose(
                PSB[0:128, 0:64], sels[0:64, tt * 128:(tt + 1) * 128], id_f[0:64, 0:64]))
            p.add("vector", lambda e: e.tensor_copy(seltm[:, :], PSB[0:128, 0:64]))
            p.add("vector", lambda e: e.tensor_copy(wk1[:, :], seltm[:, :]))
            for r in range(4):
                src = wk1 if r % 2 == 0 else wk2
                dst = wk2 if r % 2 == 0 else wk1
                p.add("vector", lambda e, src=src: e.reduce_max(
                    mcol[:, :], src[:, :], AX.X))
                if r < 3:
                    mb = _ap(mcol, 0, [[1, 128], [0, 64]])
                    p.add("vector", lambda e, src=src, dst=dst, mb=mb: e.tensor_tensor(
                        dst[:, :], src[:, :], mb, ALU.is_equal))
                    p.add("vector", lambda e, src=src, dst=dst: e.scalar_tensor_tensor(
                        dst[:, :], dst[:, :], -1e30, src[:, :], ALU.mult, ALU.add))
            mb = _ap(mcol, 0, [[1, 128], [0, 64]])
            p.add("vector", lambda e, mb=mb: e.tensor_tensor(
                wk1[:, :], seltm[:, :], mb, ALU.is_ge))
            p.add("vector", lambda e, tt=tt: e.tensor_tensor(
                gates[:, tt * 64:(tt + 1) * 64], seltm[:, :], wk1[:, :], ALU.mult))
        if DEBUG_A:
            sels_o = nc.dram_tensor("sels_o", [64, T_], F32, kind="ExternalOutput")
            gates_o = nc.dram_tensor("gates_o", [128, 1024], F32, kind="ExternalOutput")
            p.dma("sync", sels_o[:, :], sels[:, :])
            p.dma("sync", gates_o[:, :], gates[:, :])
        es1.__exit__(None, None, None)

        es2 = ExitStack()
        with es2:
            W_sb = es2.enter_context(nc.sbuf_tensor("W_sb", [128, 3 * NCH * 1024], BF16))
            qT_sb = es2.enter_context(nc.sbuf_tensor("qT_sb", [128, NCH * T_], BF16))
            kT_sb = es2.enter_context(nc.sbuf_tensor("kT_sb", [128, NCH * T_], BF16))
            qf = es2.enter_context(nc.sbuf_tensor("qf", [128, 1024], F32))
            qr = es2.enter_context(nc.sbuf_tensor("qr", [128, 1024], F32))
            qr_bf = es2.enter_context(nc.sbuf_tensor("qr_bf", [128, 1024], BF16))
            t1s = es2.enter_context(nc.sbuf_tensor("t1s", [128, 512], F32))
            t2s = es2.enter_context(nc.sbuf_tensor("t2s", [128, 512], F32))
            ct = es2.enter_context(nc.sbuf_tensor("ct", [128, 32], F32))
            st = es2.enter_context(nc.sbuf_tensor("st", [128, 32], F32))
            vstage = es2.enter_context(nc.sbuf_tensor("vstage", [128, 1024], BF16))

            for w_i, wsrc in enumerate((wqT, wkT, wvT)):
                for c in range(NCH):
                    p.dma("gpsimd", W_sb[:, (w_i * NCH + c) * 1024:(w_i * NCH + c + 1) * 1024],
                          wsrc[c * 128:(c + 1) * 128, :])

            cosap = _ap(ct, 0, [[32, 128], [0, 16], [1, 32]])
            sinap = _ap(st, 0, [[32, 128], [0, 16], [1, 32]])

            def rope(p, src, dst):
                ev = lambda t, o=0: _ap(t, o, [[1024, 128], [64, 16], [2, 32]])
                p.add("vector", lambda e: e.tensor_tensor(t1s[:, :], ev(src), cosap, ALU.mult))
                p.add("vector", lambda e: e.tensor_tensor(t2s[:, :], ev(src, 1), sinap, ALU.mult))
                p.add("vector", lambda e: e.tensor_tensor(ev(dst), t1s[:, :], t2s[:, :], ALU.subtract))
                p.add("vector", lambda e: e.tensor_tensor(t1s[:, :], ev(src), sinap, ALU.mult))
                p.add("vector", lambda e: e.tensor_tensor(t2s[:, :], ev(src, 1), cosap, ALU.mult))
                p.add("vector", lambda e: e.tensor_tensor(ev(dst, 1), t1s[:, :], t2s[:, :], ALU.add))

            for tt in range(16):
                tsl = slice(tt * 128, (tt + 1) * 128)
                p.dma("sync", ct[:, :], cosL[tsl, :])
                p.dma("sync", st[:, :], sinL[tsl, :])
                for w_i, dstT in ((0, qT_sb), (1, kT_sb)):
                    for os in range(2):
                        for c in range(NCH):
                            p.add("tensor", lambda e, w_i=w_i, os=os, c=c, tt=tt: e.matmul(
                                PSA[0:128, w_i * 1024 + os * 512: w_i * 1024 + (os + 1) * 512],
                                xn_bf[:, c * T_ + tt * 128: c * T_ + (tt + 1) * 128],
                                W_sb[:, (w_i * NCH + c) * 1024 + os * 512:
                                        (w_i * NCH + c) * 1024 + (os + 1) * 512],
                                start=(c == 0), stop=(c == NCH - 1)))
                    p.add("scalar", lambda e, w_i=w_i: e.activation(
                        qf[:, :], PSA[0:128, w_i * 1024:(w_i + 1) * 1024], AF.Copy))
                    rope(p, qf, qr)
                    p.add("vector", lambda e: e.tensor_copy(qr_bf[:, :], qr[:, :]))
                    for c8 in range(NCH):
                        p.add("tensor", lambda e, c8=c8: e.transpose(
                            PST[:, c8 * 128:(c8 + 1) * 128],
                            qr_bf[:, c8 * 128:(c8 + 1) * 128], id_bf[:, :]))
                    for c8 in range(NCH):
                        p.add("vector", lambda e, c8=c8, tt=tt, dstT=dstT: e.tensor_copy(
                            dstT[:, c8 * T_ + tt * 128: c8 * T_ + (tt + 1) * 128],
                            PST[:, c8 * 128:(c8 + 1) * 128]))
                for os in range(2):
                    for c in range(NCH):
                        p.add("tensor", lambda e, os=os, c=c, tt=tt: e.matmul(
                            PSB[0:128, os * 512:(os + 1) * 512],
                            xn_bf[:, c * T_ + tt * 128: c * T_ + (tt + 1) * 128],
                            W_sb[:, (2 * NCH + c) * 1024 + os * 512:
                                    (2 * NCH + c) * 1024 + (os + 1) * 512],
                            start=(c == 0), stop=(c == NCH - 1)))
                p.add("vector", lambda e: e.tensor_copy(vstage[:, :], PSB[0:128, :]))
                p.dma("sync", v_o[tsl, :], vstage[:, :])
            p.dma("sync",
                  _ap(qT_o, 0, [[T_, 128], [128 * T_, NCH], [1, T_]]),
                  _ap(qT_sb, 0, [[NCH * T_, 128], [T_, NCH], [1, T_]]))
            p.dma("sync",
                  _ap(kT_o, 0, [[T_, 128], [128 * T_, NCH], [1, T_]]),
                  _ap(kT_sb, 0, [[NCH * T_, 128], [T_, NCH], [1, T_]]))

        es3 = ExitStack()
        with es3:
            keys4 = es3.enter_context(nc.sbuf_tensor("keys4", [128, NCH * 512], BF16))
            hg_bf = es3.enter_context(nc.sbuf_tensor("hg_bf", [128, 512], BF16))
            for kg in range(16):
                for c in range(NCH):
                    p.dma("gpsimd", keys4[:, c * 512:(c + 1) * 512],
                          _ap(keys, (4 * kg) * E_ * 128 + c * 128 * 128,
                              [[128, 128], [E_ * 128, 4], [1, 128]]))
                for tt in range(16):
                    for c in range(NCH):
                        p.add("tensor", lambda e, c=c, tt=tt: e.matmul(
                            PSC[:, :],
                            xn_bf[:, c * T_ + tt * 128: c * T_ + (tt + 1) * 128],
                            keys4[:, c * 512:(c + 1) * 512],
                            start=(c == 0), stop=(c == NCH - 1)))
                    gap = _ap(gates, tt * 64 + kg * 4, [[16 * 64, 128], [1, 4], [0, 128]])
                    p.add("vector", lambda e, gap=gap: e.scalar_tensor_tensor(
                        hg_bf[:, :], PSC[:, :], 0.0, gap, ALU.max, ALU.mult))
                    p.dma("sync", hg_d[tt * 128:(tt + 1) * 128, kg * 512:(kg + 1) * 512],
                          hg_bf[:, :])

        es4 = ExitStack()
        with es4:
            vals = es4.enter_context(nc.sbuf_tensor("vals", [128, 64 * 1024], BF16))
            hgrow = es4.enter_context(nc.sbuf_tensor("hgrow", [128, 64 * 128], BF16))
            gsT = es4.enter_context(nc.sbuf_tensor("gsT", [128, 512], BF16))
            gstage = es4.enter_context(nc.sbuf_tensor("gstage", [128, 1024], F32))
            for k in range(64):
                p.dma("gpsimd", vals[:, k * 1024:(k + 1) * 1024],
                      _ap(values, k * 128 * E_, [[E_, 128], [1, E_]]))
            for tt in range(16):
                p.dma("sync", hgrow[:, :], hg_d[tt * 128:(tt + 1) * 128, :])
                for kg in range(16):
                    for k4 in range(4):
                        p.add("tensor", lambda e, kg=kg, k4=k4: e.transpose(
                            PST[:, k4 * 128:(k4 + 1) * 128],
                            hgrow[:, kg * 512 + k4 * 128: kg * 512 + (k4 + 1) * 128],
                            id_bf[:, :]))
                    p.add("vector", lambda e: e.tensor_copy(gsT[:, :], PST[:, 0:512]))
                    for k4 in range(4):
                        k = kg * 4 + k4
                        for vc in range(NCH):
                            first = (kg == 0 and k4 == 0 and vc in (0, 4))
                            last = (kg == 15 and k4 == 3 and vc in (3, 7))
                            p.add("tensor", lambda e, k=k, k4=k4, vc=vc, first=first, last=last: e.matmul(
                                PSA[0:128, vc * 128:(vc + 1) * 128],
                                vals[:, k * 1024 + vc * 128: k * 1024 + (vc + 1) * 128],
                                gsT[:, k4 * 128:(k4 + 1) * 128],
                                start=first, stop=last, skip_group_check=True))
                p.add("scalar", lambda e: e.activation(
                    gstage[:, 0:512], PSA[0:128, 0:512], AF.Silu))
                p.add("scalar", lambda e: e.activation(
                    gstage[:, 512:1024], PSA[0:128, 512:1024], AF.Silu))
                p.dma("sync",
                      _ap(g_o, tt * 128, [[T_, 128], [128 * T_, NCH], [1, 128]]),
                      _ap(gstage, 0, [[1024, 128], [128, NCH], [1, 128]]))

        p.emit(block, [sem])
    return nc


def build_B():
    nc = bass.Bass("TRN2", target_bir_lowering=False)
    qT = nc.dram_tensor("qT", [E_, T_], BF16, kind="ExternalInput")
    kTf = nc.dram_tensor("kTf", [E_, S_], BF16, kind="ExternalInput")
    vgf = nc.dram_tensor("vgf", [4 * S_, 260], BF16, kind="ExternalInput")
    xT = nc.dram_tensor("xT", [E_, T_], F32, kind="ExternalInput")
    g0T = nc.dram_tensor("g0T", [E_, T_], F32, kind="ExternalInput")
    woT = nc.dram_tensor("woT", [E_, E_], F32, kind="ExternalInput")
    wq1T = nc.dram_tensor("wq1T", [E_, E_], F32, kind="ExternalInput")
    wk1T = nc.dram_tensor("wk1T", [E_, E_], F32, kind="ExternalInput")
    wv1T = nc.dram_tensor("wv1T", [E_, E_], F32, kind="ExternalInput")
    kern1 = nc.dram_tensor("kern1", [E_, E_], F32, kind="ExternalInput")
    bias1 = nc.dram_tensor("bias1", [E_, 1], F32, kind="ExternalInput")
    a1 = nc.dram_tensor("a1", [E_, 16], F32, kind="ExternalInput")
    b1 = nc.dram_tensor("b1", [16, E_], F32, kind="ExternalInput")
    gam = nc.dram_tensor("gam", [16, 1], F32, kind="ExternalInput")
    selq = nc.dram_tensor("selq", [16, NCH * 128], F32, kind="ExternalInput")
    onesr = nc.dram_tensor("onesr", [1, 128], F32, kind="ExternalInput")
    onesc = nc.dram_tensor("onesc", [128, 1], F32, kind="ExternalInput")

    x1T_o = nc.dram_tensor("x1T_o", [E_, T_], F32, kind="ExternalOutput")
    qt1_o = nc.dram_tensor("qt1_o", [E_, T_], BF16, kind="ExternalOutput")
    kt1_o = nc.dram_tensor("kt1_o", [E_, T_], BF16, kind="ExternalOutput")
    v1_o = nc.dram_tensor("v1_o", [T_, E_], BF16, kind="ExternalOutput")
    g1_o = nc.dram_tensor("g1_o", [E_, T_], F32, kind="ExternalOutput")
    am_d = nc.dram_tensor("am_d", [128, 1], F32)

    from contextlib import ExitStack
    es = ExitStack()
    with es:
        ones_r = es.enter_context(nc.sbuf_tensor("ones_r", [1, 128], F32))
        ones_c = es.enter_context(nc.sbuf_tensor("ones_c", [128, 1], F32))
        esY = ExitStack()
        esY.__enter__()
        Yn = esY.enter_context(nc.sbuf_tensor("Yn", [128, NCH * T_], BF16))
        PSA = es.enter_context(nc.psum_tensor("PSA", [128, 2048], F32))
        PSB = es.enter_context(nc.psum_tensor("PSB", [128, 1024], F32))
        PSC = es.enter_context(nc.psum_tensor("PSC", [128, 512], F32))
        sem = es.enter_context(nc.semaphore("step"))
        block = es.enter_context(nc.Block())
        p = Prog(nc)
        p.dma("sync", ones_r[:, :], onesr[:, :])
        p.dma("sync", ones_c[:, :], onesc[:, :])

        es1 = ExitStack()
        es1.__enter__()
        Qs = es1.enter_context(nc.sbuf_tensor("Qs", [128, NCH * T_], BF16, side="right"))
        KA = es1.enter_context(nc.sbuf_tensor("KA", [128, S_], BF16, side="right"))
        KB = es1.enter_context(nc.sbuf_tensor("KB", [128, S_], BF16, side="right"))
        Vg = es1.enter_context(nc.sbuf_tensor("Vg", [128, 64 * 260], BF16, side="right"))
        Ebf = es1.enter_context(nc.sbuf_tensor("Ebf", [128, 512], BF16, side="right"))
        rrow = es1.enter_context(nc.sbuf_tensor("rrow", [1, 512], F32, side="right"))
        Rs = es1.enter_context(nc.sbuf_tensor("Rs", [64, 512], F32, side="right"))
        for c in range(NCH):
            p.dma("sync", Qs[:, c * T_:(c + 1) * T_], qT[c * 128:(c + 1) * 128, :])
        for g in range(4):
            p.dma("sync", KA[:, :], kTf[g * 256:g * 256 + 128, :])
            p.dma("sync", KB[:, :], kTf[g * 256 + 128:g * 256 + 256, :])
            p.dma("sync", Vg[:, :],
                  _ap(vgf, g * S_ * 260, [[260, 128], [128 * 260, 64], [1, 260]]))
            for hl in range(4):
                h = 4 * g + hl
                K_t = KA if hl < 2 else KB
                kr = 64 * (hl % 2)
                qr0 = 64 * (h % 2)
                qc0 = (h // 2) * T_
                for ts in range(4):
                    for sb in range(64):
                        p.add("tensor", lambda e, K_t=K_t, kr=kr, sb=sb, qr0=qr0, qc0=qc0, ts=ts: e.matmul(
                            PSA[0:128, 0:512],
                            K_t[kr:kr + 64, sb * 128:(sb + 1) * 128],
                            Qs[qr0:qr0 + 64, qc0 + ts * 512: qc0 + (ts + 1) * 512],
                            start=True, stop=True))
                        p.add("scalar", lambda e: e.activation(
                            Ebf[:, :], PSA[0:128, 0:512], AF.Exp, scale=0.125))
                        p.add("tensor", lambda e, sb=sb, hl=hl: e.matmul(
                            PSB[0:65, 0:512],
                            Vg[:, sb * 260 + hl * 65: sb * 260 + (hl + 1) * 65],
                            Ebf[:, :], start=(sb == 0), stop=(sb == 63)))
                    p.add("vector", lambda e: e.reciprocal(rrow[:, :], PSB[64:65, 0:512]))
                    p.add("tensor", lambda e: e.matmul(
                        PSC[0:64, :], ones_r[0:1, 0:64], rrow[:, :], start=True, stop=True))
                    p.add("scalar", lambda e: e.activation(Rs[:, :], PSC[0:64, :], AF.Copy))
                    p.add("vector", lambda e, qr0=qr0, qc0=qc0, ts=ts: e.tensor_tensor(
                        Yn[qr0:qr0 + 64, qc0 + ts * 512: qc0 + (ts + 1) * 512],
                        PSB[0:64, 0:512], Rs[:, :], ALU.mult))
        es1.__exit__(None, None, None)

        es2 = ExitStack()
        es2.__enter__()
        X1 = es2.enter_context(nc.sbuf_tensor("X1", [128, NCH * T_], F32, side="right"))
        wo_bf = es2.enter_context(nc.sbuf_tensor("wo_bf", [128, NCH * 1024], BF16, side="right"))
        xsl = es2.enter_context(nc.sbuf_tensor("xsl", [128, 512], F32, side="right"))
        gsl = es2.enter_context(nc.sbuf_tensor("gsl", [128, 512], F32, side="right"))
        tsl = es2.enter_context(nc.sbuf_tensor("tsl", [128, 512], F32, side="right"))
        for c in range(NCH):
            p.dma("gpsimd", wo_bf[:, c * 1024:(c + 1) * 1024],
                  woT[c * 128:(c + 1) * 128, :])
        for oc in range(NCH):
            for ts in range(4):
                for c in range(NCH):
                    p.add("tensor", lambda e, oc=oc, ts=ts, c=c: e.matmul(
                        PSA[0:128, 0:512],
                        wo_bf[:, c * 1024 + oc * 128: c * 1024 + oc * 128 + 128],
                        Yn[:, c * T_ + ts * 512: c * T_ + (ts + 1) * 512],
                        start=(c == 0), stop=(c == NCH - 1)))
                p.dma("sync", xsl[:, :], xT[oc * 128:(oc + 1) * 128, ts * 512:(ts + 1) * 512])
                p.dma("sync", gsl[:, :], g0T[oc * 128:(oc + 1) * 128, ts * 512:(ts + 1) * 512])
                p.add("vector", lambda e: e.tensor_tensor(
                    tsl[:, :], PSA[0:128, 0:512], gsl[:, :], ALU.mult))
                sl = lambda t, oc=oc, ts=ts: t[:, oc * T_ + ts * 512: oc * T_ + (ts + 1) * 512]
                p.add("vector", lambda e, oc=oc, ts=ts: e.tensor_tensor(
                    X1[:, oc * T_ + ts * 512: oc * T_ + (ts + 1) * 512],
                    tsl[:, :], xsl[:, :], ALU.add))
                p.dma("sync", x1T_o[oc * 128:(oc + 1) * 128, ts * 512:(ts + 1) * 512],
                      X1[:, oc * T_ + ts * 512: oc * T_ + (ts + 1) * 512])
        esY.__exit__(None, None, None)

        xn1 = es.enter_context(nc.sbuf_tensor("xn1", [128, NCH * T_], BF16))
        es3 = ExitStack()
        es3.__enter__()
        mu1 = es3.enter_context(nc.sbuf_tensor("mu1", [128, T_], F32, side="right"))
        rs1 = es3.enter_context(nc.sbuf_tensor("rs1", [128, T_], F32, side="right"))
        sqs = es3.enter_context(nc.sbuf_tensor("sqs", [128, 512], F32, side="right"))
        row_a = es3.enter_context(nc.sbuf_tensor("row_a", [1, T_], F32, side="right"))
        row_b = es3.enter_context(nc.sbuf_tensor("row_b", [1, T_], F32, side="right"))
        row_c = es3.enter_context(nc.sbuf_tensor("row_c", [1, 512], F32, side="right"))
        row_d = es3.enter_context(nc.sbuf_tensor("row_d", [1, 512], F32, side="right"))
        scr_a = es3.enter_context(nc.sbuf_tensor("scr_a", [128, T_], F32, side="right"))
        scr_b = es3.enter_context(nc.sbuf_tensor("scr_b", [128, T_], F32, side="right"))
        for ts in range(4):
            for c in range(NCH):
                csl = lambda t, c=c, ts=ts: t[:, c * T_ + ts * 512: c * T_ + (ts + 1) * 512]
                p.add("scalar", lambda e, c=c, ts=ts: e.activation(
                    sqs[:, :], X1[:, c * T_ + ts * 512: c * T_ + (ts + 1) * 512], AF.Square))
                p.add("tensor", lambda e, c=c, ts=ts: e.matmul(
                    PSC[0:1, :], ones_c[:, :],
                    X1[:, c * T_ + ts * 512: c * T_ + (ts + 1) * 512],
                    start=(c == 0), stop=(c == NCH - 1)))
                p.add("tensor", lambda e, c=c: e.matmul(
                    PSB[0:1, 0:512], ones_c[:, :], sqs[:, :],
                    start=(c == 0), stop=(c == NCH - 1)))
            sl = slice(ts * 512, (ts + 1) * 512)
            p.add("vector", lambda e, sl=sl: e.tensor_scalar_mul(
                row_a[:, sl], PSC[0:1, :], 1.0 / 1024.0))
            p.add("vector", lambda e: e.tensor_scalar_mul(
                row_c[:, :], PSB[0:1, 0:512], 1.0 / 1024.0))
            p.add("vector", lambda e, sl=sl: e.tensor_tensor(
                row_d[:, :], row_a[:, sl], row_a[:, sl], ALU.mult))
            p.add("vector", lambda e: e.tensor_tensor(
                row_c[:, :], row_c[:, :], row_d[:, :], ALU.subtract))
            p.add("vector", lambda e: e.tensor_scalar_add(row_c[:, :], row_c[:, :], 1e-5))
            p.add("scalar", lambda e: e.activation(row_d[:, :], row_c[:, :], AF.Sqrt))
            p.add("vector", lambda e, sl=sl: e.reciprocal(row_b[:, sl], row_d[:, :]))
            p.add("tensor", lambda e, sl=sl: e.matmul(
                PSB[0:128, 0:512], ones_r[:, :], row_a[:, sl], start=True, stop=True))
            p.add("vector", lambda e, sl=sl: e.tensor_copy(mu1[:, sl], PSB[0:128, 0:512]))
            p.add("tensor", lambda e, sl=sl: e.matmul(
                PSC[0:128, :], ones_r[:, :], row_b[:, sl], start=True, stop=True))
            p.add("vector", lambda e, sl=sl: e.tensor_copy(rs1[:, sl], PSC[0:128, :]))
        for c in range(NCH):
            p.add("vector", lambda e, c=c: e.tensor_tensor(
                scr_a[:, :], X1[:, c * T_:(c + 1) * T_], mu1[:, :], ALU.subtract))
            p.add("vector", lambda e: e.tensor_tensor(
                scr_b[:, :], scr_a[:, :], rs1[:, :], ALU.mult))
            p.add("vector", lambda e, c=c: e.tensor_copy(
                xn1[:, c * T_:(c + 1) * T_], scr_b[:, :]))
        es3.__exit__(None, None, None)
        es2.__exit__(None, None, None)

        es4 = ExitStack()
        es4.__enter__()
        W1 = es4.enter_context(nc.sbuf_tensor("W1", [128, 3 * NCH * 1024], BF16, side="right"))
        U512 = es4.enter_context(nc.sbuf_tensor("U512", [128, T_], F32, side="right"))
        U128 = es4.enter_context(nc.sbuf_tensor("U128", [128, T_], F32, side="right"))
        LG = es4.enter_context(nc.sbuf_tensor("LG", [128, NCH], F32, side="right"))
        NLG = es4.enter_context(nc.sbuf_tensor("NLG", [128, NCH], F32, side="right"))
        lnGr = es4.enter_context(nc.sbuf_tensor("lnGr", [16, 1], F32, side="right"))
        lnG = es4.enter_context(nc.sbuf_tensor("lnG", [16, 1], F32, side="right"))
        selq_s = es4.enter_context(nc.sbuf_tensor("selq_s", [16, NCH * 128], F32, side="right"))
        Pq = es4.enter_context(nc.sbuf_tensor("Pq", [128, T_], F32, side="right"))
        Pk = es4.enter_context(nc.sbuf_tensor("Pk", [128, T_], F32, side="right"))
        scp = es4.enter_context(nc.sbuf_tensor("scp", [128, T_], F32, side="right"))
        qst = es4.enter_context(nc.sbuf_tensor("qst", [128, T_], BF16, side="right"))
        v1st = es4.enter_context(nc.sbuf_tensor("v1st", [128, 1024], BF16, side="right"))
        for w_i, wsrc in enumerate((wq1T, wk1T, wv1T)):
            for c in range(NCH):
                p.dma("gpsimd", W1[:, (w_i * NCH + c) * 1024:(w_i * NCH + c + 1) * 1024],
                      wsrc[c * 128:(c + 1) * 128, :])
        p.dma("sync", selq_s[:, :], selq[:, :])
        p.dma("sync", lnGr[:, :], gam[:, :])
        p.add("scalar", lambda e: e.activation(lnG[:, :], lnGr[:, :], AF.Ln))
        for c in range(NCH):
            p.add("tensor", lambda e, c=c: e.matmul(
                PSC[0:128, 0:1], selq_s[:, c * 128:(c + 1) * 128], lnG[:, :],
                start=True, stop=True))
            p.add("vector", lambda e, c=c: e.tensor_copy(LG[:, c:c + 1], PSC[0:128, 0:1]))
        p.add("vector", lambda e: e.tensor_scalar_mul(NLG[:, :], LG[:, :], -1.0))
        p.add("gpsimd", lambda e: e.iota(U512[:, :], [[0, 4], [1, 512]], base=0,
                                         channel_multiplier=0,
                                         allow_small_or_imprecise_dtypes=True))
        p.add("gpsimd", lambda e: e.iota(U128[:, :], [[0, 16], [1, 128]], base=0,
                                         channel_multiplier=0,
                                         allow_small_or_imprecise_dtypes=True))
        LN8 = float(np.log(0.125))
        for oc in range(NCH):
            lgb = _ap(LG, oc, [[NCH, 128], [0, T_]])
            nlgb = _ap(NLG, oc, [[NCH, 128], [0, T_]])
            p.add("vector", lambda e, lgb=lgb: e.tensor_tensor(scp[:, :], U512[:, :], lgb, ALU.mult))
            p.add("scalar", lambda e: e.activation(Pq[:, :], scp[:, :], AF.Exp, bias=np.float32(LN8)))
            p.add("vector", lambda e, nlgb=nlgb: e.tensor_tensor(scp[:, :], U128[:, :], nlgb, ALU.mult))
            p.add("scalar", lambda e: e.activation(Pk[:, :], scp[:, :], AF.Exp))
            for w_i, dst, P in ((0, qt1_o, Pq), (1, kt1_o, Pk)):
                for ts in range(4):
                    for c in range(NCH):
                        p.add("tensor", lambda e, w_i=w_i, oc=oc, c=c, ts=ts: e.matmul(
                            PSA[0:128, 0:512],
                            W1[:, (w_i * NCH + c) * 1024 + oc * 128:
                                  (w_i * NCH + c) * 1024 + oc * 128 + 128],
                            xn1[:, c * T_ + ts * 512: c * T_ + (ts + 1) * 512],
                            start=(c == 0), stop=(c == NCH - 1)))
                    p.add("vector", lambda e, ts=ts, P=P: e.tensor_tensor(
                        qst[:, ts * 512:(ts + 1) * 512], PSA[0:128, 0:512],
                        P[:, ts * 512:(ts + 1) * 512], ALU.mult))
                p.dma("sync", dst[oc * 128:(oc + 1) * 128, :], qst[:, :])
        for tt in range(16):
            for os in range(2):
                for c in range(NCH):
                    p.add("tensor", lambda e, os=os, c=c, tt=tt: e.matmul(
                        PSB[0:128, os * 512:(os + 1) * 512],
                        xn1[:, c * T_ + tt * 128: c * T_ + (tt + 1) * 128],
                        W1[:, (2 * NCH + c) * 1024 + os * 512:
                              (2 * NCH + c) * 1024 + (os + 1) * 512],
                        start=(c == 0), stop=(c == NCH - 1)))
            p.add("vector", lambda e: e.tensor_copy(v1st[:, :], PSB[0:128, :]))
            p.dma("sync", v1_o[tt * 128:(tt + 1) * 128, :], v1st[:, :])
        es4.__exit__(None, None, None)

        es5 = ExitStack()
        es5.__enter__()
        wq1 = es5.enter_context(nc.sbuf_tensor("wq1", [128, NCH * 1024], BF16, side="right"))
        kch = es5.enter_context(nc.sbuf_tensor("kch", [128, 1024], F32, side="right"))
        tq = es5.enter_context(nc.sbuf_tensor("tq", [128, 1024], F32, side="right"))
        am8 = es5.enter_context(nc.sbuf_tensor("am8", [128, NCH], F32, side="right"))
        amc = es5.enter_context(nc.sbuf_tensor("amc", [128, 1], F32, side="right"))
        amr = es5.enter_context(nc.sbuf_tensor("amr", [1, 128], F32, side="right"))
        am = es5.enter_context(nc.sbuf_tensor("am", [1, 1], F32, side="right"))
        srow = es5.enter_context(nc.sbuf_tensor("srow", [1, 1], F32, side="right"))
        rsrow = es5.enter_context(nc.sbuf_tensor("rsrow", [1, 1], F32, side="right"))
        scol = es5.enter_context(nc.sbuf_tensor("scol", [128, 1], F32, side="right"))
        rscol = es5.enter_context(nc.sbuf_tensor("rscol", [128, 1], F32, side="right"))
        a_bf = es5.enter_context(nc.sbuf_tensor("a_bf", [128, NCH * 16], BF16, side="right"))
        b_bf = es5.enter_context(nc.sbuf_tensor("b_bf", [16, 1024], BF16, side="right"))
        bias_s = es5.enter_context(nc.sbuf_tensor("bias_s", [128, NCH], F32, side="right"))
        u_bf = es5.enter_context(nc.sbuf_tensor("u_bf", [16, T_], BF16, side="right"))
        g1st = es5.enter_context(nc.sbuf_tensor("g1st", [128, 512], F32, side="right"))
        for c in range(NCH):
            p.dma("sync", kch[:, :], kern1[c * 128:(c + 1) * 128, :])
            p.add("vector", lambda e, c=c: e.reduce_max(
                am8[:, c:c + 1], kch[:, :], AX.X, apply_absolute_value=True))
        p.add("vector", lambda e: e.reduce_max(amc[:, :], am8[:, :], AX.X))
        p.dma("sync", am_d[:, :], amc[:, :])
        p.dma("sync", amr[:, :], _ap(am_d, 0, [[128, 1], [1, 128]]))
        p.add("vector", lambda e: e.reduce_max(am[:, :], amr[:, :], AX.X))
        p.add("vector", lambda e: e.tensor_scalar_mul(srow[:, :], am[:, :], 1.0 / 7.0))
        p.add("vector", lambda e: e.reciprocal(rsrow[:, :], srow[:, :]))
        p.add("tensor", lambda e: e.matmul(PSC[0:128, 0:1], ones_r[:, :], srow[:, :],
                                           start=True, stop=True))
        p.add("vector", lambda e: e.tensor_copy(scol[:, :], PSC[0:128, 0:1]))
        p.add("tensor", lambda e: e.matmul(PSC[0:128, 0:1], ones_r[:, :], rsrow[:, :],
                                           start=True, stop=True))
        p.add("vector", lambda e: e.tensor_copy(rscol[:, :], PSC[0:128, 0:1]))
        for c in range(NCH):
            p.dma("sync", kch[:, :], kern1[c * 128:(c + 1) * 128, :])
            p.add("vector", lambda e: e.tensor_scalar(
                tq[:, :], kch[:, :], rscol[:, 0:1], MAGIC, ALU.mult, ALU.add))
            p.add("vector", lambda e: e.tensor_scalar(
                kch[:, :], tq[:, :], MAGIC, -8.0, ALU.subtract, ALU.max))
            p.add("vector", lambda e, c=c: e.tensor_scalar(
                wq1[:, c * 1024:(c + 1) * 1024], kch[:, :], 7.0, scol[:, 0:1],
                ALU.min, ALU.mult))
        for c in range(NCH):
            p.dma("gpsimd", a_bf[:, c * 16:(c + 1) * 16], a1[c * 128:(c + 1) * 128, :])
            p.dma("sync", bias_s[:, c:c + 1], bias1[c * 128:(c + 1) * 128, :])
        p.dma("gpsimd", b_bf[:, :], b1[:, :])
        for ts in range(4):
            for c in range(NCH):
                p.add("tensor", lambda e, c=c, ts=ts: e.matmul(
                    PSC[0:16, :], a_bf[:, c * 16:(c + 1) * 16],
                    xn1[:, c * T_ + ts * 512: c * T_ + (ts + 1) * 512],
                    start=(c == 0), stop=(c == NCH - 1)))
            p.add("vector", lambda e, ts=ts: e.tensor_copy(
                u_bf[:, ts * 512:(ts + 1) * 512], PSC[0:16, :]))
        for fc in range(NCH):
            for ts in range(4):
                for c in range(NCH):
                    p.add("tensor", lambda e, fc=fc, ts=ts, c=c: e.matmul(
                        PSA[0:128, 0:512],
                        wq1[:, c * 1024 + fc * 128: c * 1024 + fc * 128 + 128],
                        xn1[:, c * T_ + ts * 512: c * T_ + (ts + 1) * 512],
                        start=(c == 0), stop=False))
                p.add("tensor", lambda e, fc=fc, ts=ts: e.matmul(
                    PSA[0:128, 0:512], b_bf[:, fc * 128:(fc + 1) * 128],
                    u_bf[:, ts * 512:(ts + 1) * 512], start=False, stop=True))
                p.add("scalar", lambda e, fc=fc: e.activation(
                    g1st[:, :], PSA[0:128, 0:512], AF.Silu, bias=bias_s[:, fc:fc + 1]))
                p.dma("sync", g1_o[fc * 128:(fc + 1) * 128, ts * 512:(ts + 1) * 512],
                      g1st[:, :])
        es5.__exit__(None, None, None)
        p.emit(block, [sem])
    return nc


def build_D():
    nc = bass.Bass("TRN2", target_bir_lowering=False)
    qtg = nc.dram_tensor("qtg", [256, S_], BF16, kind="ExternalInput")
    ktg = nc.dram_tensor("ktg", [256, S_], BF16, kind="ExternalInput")
    vg1 = nc.dram_tensor("vg1", [S_, 256], BF16, kind="ExternalInput")
    gamh = nc.dram_tensor("gamh", [4, 1], F32, kind="ExternalInput")
    mgrid = nc.dram_tensor("mgrid", [1, 4 * 67], F32, kind="ExternalInput")
    onesr = nc.dram_tensor("onesr", [1, 128], F32, kind="ExternalInput")
    y1T_o = nc.dram_tensor("y1T_o", [256, S_], BF16, kind="ExternalOutput")

    from contextlib import ExitStack
    es = ExitStack()
    with es:
        qtA = es.enter_context(nc.sbuf_tensor("qtA", [128, S_], BF16))
        qtB = es.enter_context(nc.sbuf_tensor("qtB", [128, S_], BF16))
        ktA = es.enter_context(nc.sbuf_tensor("ktA", [128, S_], BF16))
        ktB = es.enter_context(nc.sbuf_tensor("ktB", [128, S_], BF16))
        Vs = es.enter_context(nc.sbuf_tensor("Vs", [128, 64 * 256], BF16))
        y1s = es.enter_context(nc.sbuf_tensor("y1s", [128, 2 * S_], BF16))
        ones_r = es.enter_context(nc.sbuf_tensor("ones_r", [1, 128], F32))
        mg = es.enter_context(nc.sbuf_tensor("mg", [1, 4 * 67], F32))
        gr = es.enter_context(nc.sbuf_tensor("gr", [1, 4], F32))
        lnr = es.enter_context(nc.sbuf_tensor("lnr", [1, 4], F32))
        erow = es.enter_context(nc.sbuf_tensor("erow", [1, 4 * 67], F32))
        crow = es.enter_context(nc.sbuf_tensor("crow", [1, 4 * 67], F32))
        ctab = es.enter_context(nc.sbuf_tensor("ctab", [128, 4 * 67], F32))
        UD = es.enter_context(nc.sbuf_tensor("UD", [128, 512], F32))
        M01 = es.enter_context(nc.sbuf_tensor("M01", [128, 512], F32))
        Wbf = es.enter_context(nc.sbuf_tensor("Wbf", [128, 512], BF16))
        PSA = es.enter_context(nc.psum_tensor("PSA", [128, 512], F32))
        PSB = es.enter_context(nc.psum_tensor("PSB", [128, 512], F32))
        sem = es.enter_context(nc.semaphore("step"))
        block = es.enter_context(nc.Block())
        p = Prog(nc)

        p.dma("sync", ones_r[:, :], onesr[:, :])
        p.dma("sync", mg[:, :], mgrid[:, :])
        p.dma("sync", gr[:, :], _ap(gamh, 0, [[4, 1], [1, 4]]))
        p.dma("sync", qtA[:, :], qtg[0:128, :])
        p.dma("sync", qtB[:, :], qtg[128:256, :])
        p.dma("sync", ktA[:, :], ktg[0:128, :])
        p.dma("sync", ktB[:, :], ktg[128:256, :])
        p.dma("sync", Vs[:, :], _ap(vg1, 0, [[256, 128], [128 * 256, 64], [1, 256]]))
        p.add("scalar", lambda e: e.activation(lnr[:, :], gr[:, :], AF.Ln))
        lnb = _ap(lnr, 0, [[4, 1], [1, 4], [0, 67]])
        p.add("vector", lambda e, lnb=lnb: e.tensor_tensor(erow[:, :], mg[:, :], lnb, ALU.mult))
        p.add("scalar", lambda e: e.activation(crow[:, :], erow[:, :], AF.Exp))
        p.add("tensor", lambda e: e.matmul(PSB[0:128, 0:268], ones_r[:, :], crow[:, :],
                                           start=True, stop=True))
        p.add("vector", lambda e: e.tensor_copy(ctab[:, :], PSB[0:128, 0:268]))
        p.add("gpsimd", lambda e: e.iota(UD[:, :], [[1, 512]], base=0,
                                         channel_multiplier=-1,
                                         allow_small_or_imprecise_dtypes=True))
        p.add("vector", lambda e: e.tensor_single_scalar(M01[:, :], UD[:, :], 0.0, ALU.is_ge))

        for hl in range(4):
            kt = ktA if hl < 2 else ktB
            qt = qtA if hl < 2 else qtB
            r0 = 64 * (hl % 2)
            for ts in range(16):
                nsb = 4 * ts + 4
                for sb in range(nsb):
                    p.add("tensor", lambda e, kt=kt, qt=qt, r0=r0, sb=sb, ts=ts: e.matmul(
                        PSA[:, :], kt[r0:r0 + 64, sb * 128:(sb + 1) * 128],
                        qt[r0:r0 + 64, ts * 512:(ts + 1) * 512], start=True, stop=True))
                    if sb < 4 * ts:
                        idx = hl * 67 + (4 * ts - sb) + 3
                        p.add("vector", lambda e, idx=idx: e.tensor_scalar_mul(
                            Wbf[:, :], PSA[:, :], ctab[:, idx:idx + 1]))
                    else:
                        j = sb - 4 * ts
                        idx = hl * 67 + 3 - j
                        if j > 0:
                            p.add("vector", lambda e, j=j: e.memset(Wbf[:, 0:j * 128], 0.0))
                        p.add("vector", lambda e, j=j, idx=idx: e.scalar_tensor_tensor(
                            Wbf[:, j * 128:512], PSA[:, j * 128:512], ctab[:, idx:idx + 1],
                            M01[:, 0:512 - j * 128], ALU.mult, ALU.mult))
                    p.add("tensor", lambda e, sb=sb, hl=hl, nsb=nsb: e.matmul(
                        PSB[0:64, :], Vs[:, sb * 256 + hl * 64: sb * 256 + (hl + 1) * 64],
                        Wbf[:, :], start=(sb == 0), stop=(sb == nsb - 1)))
                p.add("vector", lambda e, r0=r0, hl=hl, ts=ts: e.tensor_copy(
                    y1s[r0:r0 + 64, (hl // 2) * S_ + ts * 512: (hl // 2) * S_ + (ts + 1) * 512],
                    PSB[0:64, :]))
        p.dma("sync", y1T_o[0:128, :], y1s[:, 0:S_])
        p.dma("sync", y1T_o[128:256, :], y1s[:, S_:2 * S_])
        p.emit(block, [sem])
    return nc


def build_E():
    nc = bass.Bass("TRN2", target_bir_lowering=False)
    y1T = nc.dram_tensor("y1T", [E_, T_], BF16, kind="ExternalInput")
    wo1T = nc.dram_tensor("wo1T", [E_, E_], F32, kind="ExternalInput")
    x1T = nc.dram_tensor("x1T", [E_, T_], F32, kind="ExternalInput")
    g1T = nc.dram_tensor("g1T", [E_, T_], F32, kind="ExternalInput")
    x2T_o = nc.dram_tensor("x2T_o", [E_, T_], F32, kind="ExternalOutput")

    from contextlib import ExitStack
    es = ExitStack()
    with es:
        y1_sb = es.enter_context(nc.sbuf_tensor("y1_sb", [128, NCH * T_], BF16))
        wo1_bf = es.enter_context(nc.sbuf_tensor("wo1_bf", [128, NCH * 1024], BF16))
        xsl = es.enter_context(nc.sbuf_tensor("xsl", [128, 512], F32))
        gsl = es.enter_context(nc.sbuf_tensor("gsl", [128, 512], F32))
        tsl = es.enter_context(nc.sbuf_tensor("tsl", [128, 512], F32))
        osl = es.enter_context(nc.sbuf_tensor("osl", [128, 512], F32))
        PSA = es.enter_context(nc.psum_tensor("PSA", [128, 512], F32))
        sem = es.enter_context(nc.semaphore("step"))
        block = es.enter_context(nc.Block())
        p = Prog(nc)
        for c in range(NCH):
            p.dma("sync", y1_sb[:, c * T_:(c + 1) * T_], y1T[c * 128:(c + 1) * 128, :])
            p.dma("gpsimd", wo1_bf[:, c * 1024:(c + 1) * 1024],
                  wo1T[c * 128:(c + 1) * 128, :])
        for oc in range(NCH):
            for ts in range(4):
                for c in range(NCH):
                    p.add("tensor", lambda e, oc=oc, ts=ts, c=c: e.matmul(
                        PSA[:, :],
                        wo1_bf[:, c * 1024 + oc * 128: c * 1024 + oc * 128 + 128],
                        y1_sb[:, c * T_ + ts * 512: c * T_ + (ts + 1) * 512],
                        start=(c == 0), stop=(c == NCH - 1)))
                p.dma("sync", xsl[:, :], x1T[oc * 128:(oc + 1) * 128, ts * 512:(ts + 1) * 512])
                p.dma("sync", gsl[:, :], g1T[oc * 128:(oc + 1) * 128, ts * 512:(ts + 1) * 512])
                p.add("vector", lambda e: e.tensor_tensor(
                    tsl[:, :], PSA[:, :], gsl[:, :], ALU.mult))
                p.add("vector", lambda e: e.tensor_tensor(
                    osl[:, :], tsl[:, :], xsl[:, :], ALU.add))
                p.dma("sync", x2T_o[oc * 128:(oc + 1) * 128, ts * 512:(ts + 1) * 512],
                      osl[:, :])
        p.emit(block, [sem])
    return nc


def build_full():
    from concourse.aot_env import allow_ici_all_to_all
    from contextlib import ExitStack
    nc = bass.Bass("TRN2", target_bir_lowering=False)
    GRP = [[0, 1, 2, 3], [4, 5, 6, 7]]

    xT = nc.dram_tensor("xT", [E_, T_], F32, kind="ExternalInput")
    wqT = nc.dram_tensor("wqT", [E_, E_], F32, kind="ExternalInput")
    wkT = nc.dram_tensor("wkT", [E_, E_], F32, kind="ExternalInput")
    wvT = nc.dram_tensor("wvT", [E_, E_], F32, kind="ExternalInput")
    selT = nc.dram_tensor("selT", [E_, 64], F32, kind="ExternalInput")
    keys = nc.dram_tensor("keys", [64, E_, 128], F32, kind="ExternalInput")
    values = nc.dram_tensor("values", [64, 128, E_], F32, kind="ExternalInput")
    cosL = nc.dram_tensor("cosL", [T_, 32], F32, kind="ExternalInput")
    sinL = nc.dram_tensor("sinL", [T_, 32], F32, kind="ExternalInput")
    idf = nc.dram_tensor("idf", [128, 128], F32, kind="ExternalInput")
    onesr = nc.dram_tensor("onesr", [1, 128], F32, kind="ExternalInput")
    onesc = nc.dram_tensor("onesc", [128, 1], F32, kind="ExternalInput")
    woT = nc.dram_tensor("woT", [E_, E_], F32, kind="ExternalInput")
    wq1T = nc.dram_tensor("wq1T", [E_, E_], F32, kind="ExternalInput")
    wk1T = nc.dram_tensor("wk1T", [E_, E_], F32, kind="ExternalInput")
    wv1T = nc.dram_tensor("wv1T", [E_, E_], F32, kind="ExternalInput")
    wo1T = nc.dram_tensor("wo1T", [E_, E_], F32, kind="ExternalInput")
    kern1 = nc.dram_tensor("kern1", [E_, E_], F32, kind="ExternalInput")
    bias1 = nc.dram_tensor("bias1", [E_, 1], F32, kind="ExternalInput")
    a1 = nc.dram_tensor("a1", [E_, 16], F32, kind="ExternalInput")
    b1 = nc.dram_tensor("b1", [16, E_], F32, kind="ExternalInput")
    gam = nc.dram_tensor("gam", [16, 1], F32, kind="ExternalInput")
    selq = nc.dram_tensor("selq", [16, NCH * 128], F32, kind="ExternalInput")
    gamh = nc.dram_tensor("gamh", [4, 1], F32, kind="ExternalInput")
    mgrid = nc.dram_tensor("mgrid", [1, 4 * 67], F32, kind="ExternalInput")
    hoff = nc.dram_tensor("hoff", [1, 1], mybir.dt.int32, kind="ExternalInput")
    toff = nc.dram_tensor("toff", [1, 1], mybir.dt.int32, kind="ExternalInput")

    x2T_o = nc.dram_tensor("x2T_o", [E_, T_], F32, kind="ExternalOutput")

    qT_d = nc.dram_tensor("qT_d", [E_, T_], BF16)
    kT_d = nc.dram_tensor("kT_d", [E_, T_], BF16)
    vg_d = nc.dram_tensor("vg_d", [T_, 16 * 65], BF16)
    g0_d = nc.dram_tensor("g0_d", [E_, T_], F32)
    hg_d = nc.dram_tensor("hg_d", [T_, 64 * 128], BF16)
    x1_d = nc.dram_tensor("x1_d", [E_, T_], F32)
    qt1_d = nc.dram_tensor("qt1_d", [E_, T_], BF16)
    kt1_d = nc.dram_tensor("kt1_d", [E_, T_], BF16)
    v1_d = nc.dram_tensor("v1_d", [T_, E_], BF16)
    g1_d = nc.dram_tensor("g1_d", [E_, T_], F32)
    y1_d = nc.dram_tensor("y1_d", [256, S_], BF16)
    am_d = nc.dram_tensor("am_d", [128, 1], F32)
    kT_g = nc.dram_tensor("kT_g", [4 * E_, T_], BF16)
    vg_g = nc.dram_tensor("vg_g", [4 * T_, 16 * 65], BF16)
    qt1_g = nc.dram_tensor("qt1_g", [4 * E_, T_], BF16)
    kt1_g = nc.dram_tensor("kt1_g", [4 * E_, T_], BF16)
    v1_g = nc.dram_tensor("v1_g", [4 * T_, E_], BF16)
    y1_g = nc.dram_tensor("y1_g", [E_, S_], BF16)

    es = ExitStack()
    with es:
        id_f = es.enter_context(nc.sbuf_tensor("id_f", [128, 128], F32))
        id_bf = es.enter_context(nc.sbuf_tensor("id_bf", [128, 128], BF16))
        ones_r = es.enter_context(nc.sbuf_tensor("ones_r", [1, 128], F32))
        ones_c = es.enter_context(nc.sbuf_tensor("ones_c", [128, 1], F32))
        ones16 = es.enter_context(nc.sbuf_tensor("ones16", [128, 16], BF16))
        offs = es.enter_context(nc.sbuf_tensor("offs", [1, 2], mybir.dt.int32))
        PSA = es.enter_context(nc.psum_tensor("PSA", [128, 2048], F32))
        PSB = es.enter_context(nc.psum_tensor("PSB", [128, 1024], F32))
        PSC = es.enter_context(nc.psum_tensor("PSC", [128, 512], F32))
        PST = es.enter_context(nc.psum_tensor("PST", [128, 1024], BF16))
        ccs = [es.enter_context(nc.semaphore(f"cc{i}")) for i in range(6)]
        block = es.enter_context(nc.Block())
        p = P2(nc)

        def BAR():
            return [p.tok(e) for e in P2.ENGS]

        MARKS["A"] = len(p.items)
        esA = ExitStack()
        esA.__enter__()
        xn_bf = esA.enter_context(nc.sbuf_tensor("xn_bf", [128, NCH * T_], BF16))
        gates = esA.enter_context(nc.sbuf_tensor("gates", [128, 16 * 64], F32))
        es1 = ExitStack()
        es1.__enter__()
        selT_s = es1.enter_context(nc.sbuf_tensor("selT_s", [128, NCH * 64], F32, side="right"))
        mu_rep = es1.enter_context(nc.sbuf_tensor("mu_rep", [128, T_], F32, side="right"))
        rs_rep = es1.enter_context(nc.sbuf_tensor("rs_rep", [128, T_], F32, side="right"))
        xch = es1.enter_context(nc.sbuf_tensor("xch", [128, T_], F32, side="right"))
        sq = es1.enter_context(nc.sbuf_tensor("sq", [128, T_], F32, side="right"))
        row_a = es1.enter_context(nc.sbuf_tensor("row_a", [1, T_], F32, side="right"))
        row_b = es1.enter_context(nc.sbuf_tensor("row_b", [1, T_], F32, side="right"))
        row_c = es1.enter_context(nc.sbuf_tensor("row_c", [1, 512], F32, side="right"))
        row_d = es1.enter_context(nc.sbuf_tensor("row_d", [1, 512], F32, side="right"))
        sels = es1.enter_context(nc.sbuf_tensor("sels", [64, T_], F32, side="right"))
        seltm = es1.enter_context(nc.sbuf_tensor("seltm", [128, 64], F32, side="right"))
        wk1_ = es1.enter_context(nc.sbuf_tensor("wk1_", [128, 64], F32, side="right"))
        wk2_ = es1.enter_context(nc.sbuf_tensor("wk2_", [128, 64], F32, side="right"))
        mcol = es1.enter_context(nc.sbuf_tensor("mcol", [128, 1], F32, side="right"))

        tid = p.dma("sync", id_f[:, :], idf[:, :])
        tidb = p.dma("gpsimd", id_bf[:, :], idf[:, :])
        tor = p.dma("sync", ones_r[:, :], onesr[:, :])
        toc = p.dma("sync", ones_c[:, :], onesc[:, :])
        tof = p.dma("sync", offs[0:1, 0:1], hoff[:, :])
        tof2 = p.dma("sync", offs[0:1, 1:2], toff[:, :])
        t16 = p.add("gpsimd", lambda e: e.memset(ones16[:, :], 1.0))
        tsel = None
        for c in range(NCH):
            tsel = p.dma("sync", selT_s[:, c * 64:(c + 1) * 64],
                         selT[c * 128:(c + 1) * 128, :])

        tdx = [None] * 32
        tsq = [None] * 32
        ts1 = [None] * 32
        ts2 = [None] * 32
        t_muev = t_rsev = None
        i = 0
        for ts in range(4):
            for c in range(NCH):
                r4 = i % 4
                deps = [tsq[i - 4], ts1[i - 4]] if i >= 4 else []
                tdx[i] = p.dma("sync", xch[:, r4 * 512:(r4 + 1) * 512],
                               xT[c * 128:(c + 1) * 128, ts * 512:(ts + 1) * 512], deps)
                tsq[i] = p.add("scalar", lambda e, r4=r4: e.activation(
                    sq[:, r4 * 512:(r4 + 1) * 512], xch[:, r4 * 512:(r4 + 1) * 512],
                    AF.Square), deps=[tdx[i]] + ([ts2[i - 4]] if i >= 4 else []))
                d1 = [tdx[i]] + ([toc] if i == 0 else []) + ([t_rsev] if (c == 0 and ts > 0) else [])
                ts1[i] = p.add("tensor", lambda e, c=c, r4=r4: e.matmul(
                    PSC[0:1, :], ones_c[:, :], xch[:, r4 * 512:(r4 + 1) * 512],
                    start=(c == 0), stop=(c == NCH - 1)), deps=d1)
                d2 = [tsq[i]] + ([t_muev] if (c == 0 and ts > 0) else [])
                ts2[i] = p.add("tensor", lambda e, c=c, r4=r4: e.matmul(
                    PSB[0:1, 0:512], ones_c[:, :], sq[:, r4 * 512:(r4 + 1) * 512],
                    start=(c == 0), stop=(c == NCH - 1)), deps=d2)
                i += 1
            sl = slice(ts * 512, (ts + 1) * 512)
            t_mu = p.add("vector", lambda e, sl=sl: e.tensor_scalar_mul(
                row_a[:, sl], PSC[0:1, :], 1.0 / 1024.0), deps=[ts1[i - 1]])
            t_ex2 = p.add("vector", lambda e: e.tensor_scalar_mul(
                row_c[:, :], PSB[0:1, 0:512], 1.0 / 1024.0), deps=[ts2[i - 1]])
            p.add("vector", lambda e, sl=sl: e.tensor_tensor(
                row_d[:, :], row_a[:, sl], row_a[:, sl], ALU.mult))
            p.add("vector", lambda e: e.tensor_tensor(
                row_c[:, :], row_c[:, :], row_d[:, :], ALU.subtract))
            t_vp = p.add("vector", lambda e: e.tensor_scalar_add(row_c[:, :], row_c[:, :], 1e-5))
            t_sd = p.add("scalar", lambda e: e.activation(row_d[:, :], row_c[:, :], AF.Sqrt),
                         deps=[t_vp])
            t_rstd = p.add("vector", lambda e, sl=sl: e.reciprocal(row_b[:, sl], row_d[:, :]),
                           deps=[t_sd])
            t_repmu = p.add("tensor", lambda e, sl=sl: e.matmul(
                PSB[0:128, 0:512], ones_r[:, :], row_a[:, sl], start=True, stop=True),
                deps=[t_mu, t_ex2, tor])
            t_muev = p.add("vector", lambda e, sl=sl: e.tensor_copy(
                mu_rep[:, sl], PSB[0:128, 0:512]), deps=[t_repmu])
            t_reprs = p.add("tensor", lambda e, sl=sl: e.matmul(
                PSC[0:128, :], ones_r[:, :], row_b[:, sl], start=True, stop=True),
                deps=[t_rstd])
            t_rsev = p.add("vector", lambda e, sl=sl: e.tensor_copy(
                rs_rep[:, sl], PSC[0:128, :]), deps=[t_reprs])

        t_mult = [None] * NCH
        t_cast2 = [None] * NCH
        t_lg = [None] * NCH
        t_lgl = None
        for c in range(NCH):
            deps = ([t_cast2[c - 1], t_lg[c - 1]] if c > 0
                    else [tsq[31], ts1[31], ts2[31], t_rsev, t_muev])
            tdx2 = p.dma("sync", xch[:, :], xT[c * 128:(c + 1) * 128, :], deps)
            p.add("vector", lambda e: e.tensor_tensor(
                sq[:, :], xch[:, :], mu_rep[:, :], ALU.subtract), deps=[tdx2])
            t_mult[c] = p.add("vector", lambda e: e.tensor_tensor(
                xch[:, :], sq[:, :], rs_rep[:, :], ALU.mult))
            t_cast2[c] = p.add("vector", lambda e, c=c: e.tensor_copy(
                xn_bf[:, c * T_:(c + 1) * T_], xch[:, :]))
            for ts in range(4):
                t_lgl = p.add("tensor", lambda e, c=c, ts=ts: e.matmul(
                    PSA[0:64, ts * 512:(ts + 1) * 512], selT_s[:, c * 64:(c + 1) * 64],
                    xch[:, ts * 512:(ts + 1) * 512], start=(c == 0), stop=(c == NCH - 1)),
                    deps=[t_mult[c]] + ([tsel] if c == 0 else []))
            t_lg[c] = t_lgl

        t_sig = p.add("scalar", lambda e: e.activation(sels[:, :], PSA[0:64, :], AF.Sigmoid),
                      deps=[t_lgl])
        t_cp = None
        t_gate_last = None
        for tt in range(16):
            t_tr = p.add("tensor", lambda e, tt=tt: e.transpose(
                PSB[0:128, 0:64], sels[0:64, tt * 128:(tt + 1) * 128], id_f[0:64, 0:64]),
                deps=[t_sig, tid] + ([t_cp] if t_cp else []))
            t_cp = p.add("vector", lambda e: e.tensor_copy(seltm[:, :], PSB[0:128, 0:64]),
                         deps=[t_tr])
            p.add("vector", lambda e: e.tensor_copy(wk1_[:, :], seltm[:, :]))
            for r in range(4):
                src = wk1_ if r % 2 == 0 else wk2_
                dst = wk2_ if r % 2 == 0 else wk1_
                p.add("vector", lambda e, src=src: e.reduce_max(mcol[:, :], src[:, :], AX.X))
                if r < 3:
                    mb = _ap(mcol, 0, [[1, 128], [0, 64]])
                    p.add("vector", lambda e, src=src, dst=dst, mb=mb: e.tensor_tensor(
                        dst[:, :], src[:, :], mb, ALU.is_equal))
                    p.add("vector", lambda e, src=src, dst=dst: e.scalar_tensor_tensor(
                        dst[:, :], dst[:, :], -1e30, src[:, :], ALU.mult, ALU.add))
            mb = _ap(mcol, 0, [[1, 128], [0, 64]])
            p.add("vector", lambda e, mb=mb: e.tensor_tensor(wk1_[:, :], seltm[:, :], mb, ALU.is_ge))
            t_gate_last = p.add("vector", lambda e, tt=tt: e.tensor_tensor(
                gates[:, tt * 64:(tt + 1) * 64], seltm[:, :], wk1_[:, :], ALU.mult))
        es1.__exit__(None, None, None)

        MARKS["A4"] = len(p.items)
        es2 = ExitStack()
        es2.__enter__()
        W_sb = es2.enter_context(nc.sbuf_tensor("W_sb", [128, 3 * NCH * 1024], BF16, side="right"))
        qT_sb = es2.enter_context(nc.sbuf_tensor("qT_sb", [128, NCH * T_], BF16, side="right"))
        kT_sb = es2.enter_context(nc.sbuf_tensor("kT_sb", [128, NCH * T_], BF16, side="right"))
        qf = es2.enter_context(nc.sbuf_tensor("qf", [128, 1024], F32, side="right"))
        qr = es2.enter_context(nc.sbuf_tensor("qr", [128, 1024], F32, side="right"))
        qr_bf = es2.enter_context(nc.sbuf_tensor("qr_bf", [128, 2 * 1024], BF16, side="right"))
        t1s = es2.enter_context(nc.sbuf_tensor("t1s", [128, 512], F32, side="right"))
        t2s = es2.enter_context(nc.sbuf_tensor("t2s", [128, 512], F32, side="right"))
        ct = es2.enter_context(nc.sbuf_tensor("ct", [128, 32], F32, side="right"))
        st = es2.enter_context(nc.sbuf_tensor("st", [128, 32], F32, side="right"))
        vstage = es2.enter_context(nc.sbuf_tensor("vstage", [128, 1040], BF16, side="right"))
        a3bar = BAR()
        tW = None
        for w_i, wsrc in enumerate((wqT, wkT, wvT)):
            for c in range(NCH):
                tW = p.dma("gpsimd", W_sb[:, (w_i * NCH + c) * 1024:(w_i * NCH + c + 1) * 1024],
                           wsrc[c * 128:(c + 1) * 128, :],
                           deps=a3bar if (w_i == 0 and c == 0) else [])
        t_vst0 = p.add("vector", lambda e: e.memset(vstage[:, :], 1.0))
        cosap = _ap(ct, 0, [[32, 128], [0, 16], [1, 32]])
        sinap = _ap(st, 0, [[32, 128], [0, 16], [1, 32]])

        pend = []
        t_evq = {0: None, 1: None}
        t_rope6 = None
        t_ctst = None
        t_qtcpy = None
        t_vcpy = None
        t_vdma = None
        gpar = 0
        t_pend_trs = {0: None, 1: None}

        def flush_pend():
            nonlocal t_qtcpy
            for (par, dstT, ptt, t_cast) in pend:
                t_trs = None
                for c8 in range(NCH):
                    t_trs = p.add("tensor", lambda e, c8=c8, par=par: e.transpose(
                        PST[:, c8 * 128:(c8 + 1) * 128],
                        qr_bf[:, par * 1024 + c8 * 128: par * 1024 + (c8 + 1) * 128],
                        id_bf[:, :]),
                        deps=[t_cast, tidb] + ([t_qtcpy] if (t_qtcpy and c8 == 0) else []))
                t_pend_trs[par] = t_trs
                for c8 in range(NCH):
                    t_qtcpy = p.add("vector", lambda e, c8=c8, ptt=ptt, dstT=dstT: e.tensor_copy(
                        dstT[:, c8 * T_ + ptt * 128: c8 * T_ + (ptt + 1) * 128],
                        PST[:, c8 * 128:(c8 + 1) * 128]),
                        deps=[t_trs] if c8 == 0 else [])
            pend.clear()

        for tt in range(16):
            tsl = slice(tt * 128, (tt + 1) * 128)
            t_ct = p.dma("sync", ct[:, :], cosL[tsl, :], deps=[t_ctst] if t_ctst else [])
            t_st = p.dma("sync", st[:, :], sinL[tsl, :])
            for w_i, dstT in ((0, qT_sb), (1, kT_sb)):
                t_mm = None
                for os in range(2):
                    for c in range(NCH):
                        d = []
                        if os == 0 and c == 0:
                            d = [tW, t_gate_last] + ([t_evq[w_i]] if t_evq[w_i] else [])
                        t_mm = p.add("tensor", lambda e, w_i=w_i, os=os, c=c, tt=tt: e.matmul(
                            PSA[0:128, w_i * 1024 + os * 512: w_i * 1024 + (os + 1) * 512],
                            xn_bf[:, c * T_ + tt * 128: c * T_ + (tt + 1) * 128],
                            W_sb[:, (w_i * NCH + c) * 1024 + os * 512:
                                    (w_i * NCH + c) * 1024 + (os + 1) * 512],
                            start=(c == 0), stop=(c == NCH - 1)), deps=d)
                flush_pend()
                t_evq[w_i] = p.add("scalar", lambda e, w_i=w_i: e.activation(
                    qf[:, :], PSA[0:128, w_i * 1024:(w_i + 1) * 1024], AF.Copy),
                    deps=[t_mm] + ([t_rope6] if t_rope6 else []))
                ev = lambda t, o=0: _ap(t, o, [[1024, 128], [64, 16], [2, 32]])
                p.add("vector", lambda e: e.tensor_tensor(t1s[:, :], ev(qf), cosap, ALU.mult),
                      deps=[t_evq[w_i], t_ct, t_st])
                p.add("vector", lambda e: e.tensor_tensor(t2s[:, :], ev(qf, 1), sinap, ALU.mult))
                p.add("vector", lambda e: e.tensor_tensor(ev(qr), t1s[:, :], t2s[:, :], ALU.subtract))
                p.add("vector", lambda e: e.tensor_tensor(t1s[:, :], ev(qf), sinap, ALU.mult))
                t_ctst = p.add("vector", lambda e: e.tensor_tensor(t2s[:, :], ev(qf, 1), cosap, ALU.mult))
                t_rope6 = p.add("vector", lambda e: e.tensor_tensor(ev(qr, 1), t1s[:, :], t2s[:, :], ALU.add))
                t_cast = p.add("vector", lambda e, gpar=gpar: e.tensor_copy(
                    qr_bf[:, gpar * 1024:(gpar + 1) * 1024], qr[:, :]),
                    deps=[t_pend_trs[gpar]] if t_pend_trs[gpar] else [])
                pend.append((gpar, dstT, tt, t_cast))
                gpar ^= 1
                if A4_NOLAG:
                    flush_pend()
            t_vmm = None
            for os in range(2):
                for c in range(NCH):
                    d = []
                    if os == 0 and c == 0:
                        d = [tW] + ([t_vcpy] if t_vcpy else [t_gate_last])
                    t_vmm = p.add("tensor", lambda e, os=os, c=c, tt=tt: e.matmul(
                        PSB[0:128, os * 512:(os + 1) * 512],
                        xn_bf[:, c * T_ + tt * 128: c * T_ + (tt + 1) * 128],
                        W_sb[:, (2 * NCH + c) * 1024 + os * 512:
                                (2 * NCH + c) * 1024 + (os + 1) * 512],
                        start=(c == 0), stop=(c == NCH - 1)), deps=d)
            t_vcpy = p.add("vector", lambda e: e.tensor_copy(
                _ap(vstage, 0, [[1040, 128], [65, 16], [1, 64]]), PSB[0:128, :]),
                deps=[t_vmm, t_vst0] + ([t_vdma] if t_vdma else []))
            t_vdma = p.dma("sync", vg_d[tt * 128:(tt + 1) * 128, :], vstage[:, :],
                           deps=[t_vcpy, t16])
        flush_pend()
        tqd = p.dma("sync", _ap(qT_d, 0, [[T_, 128], [128 * T_, NCH], [1, T_]]),
                    _ap(qT_sb, 0, [[NCH * T_, 128], [T_, NCH], [1, T_]]), deps=[t_qtcpy])
        tkd = p.dma("sync", _ap(kT_d, 0, [[T_, 128], [128 * T_, NCH], [1, T_]]),
                    _ap(kT_sb, 0, [[NCH * T_, 128], [T_, NCH], [1, T_]]), deps=[t_qtcpy])
        if DEBUG_FULL:
            xn_dbg = nc.dram_tensor("dbg_xn", [128, NCH * T_], BF16, kind="ExternalOutput")
            gates_dbg = nc.dram_tensor("dbg_gates", [128, 16 * 64], F32, kind="ExternalOutput")
            p.dma("sync", xn_dbg.ap(), xn_bf[:, :], deps=[p.tok(e) for e in P2.ENGS])
            p.dma("sync", gates_dbg.ap(), gates[:, :], deps=[p.tok(e) for e in P2.ENGS])
        es2.__exit__(None, None, None)

        MARKS["A5a"] = len(p.items)
        es3 = ExitStack()
        es3.__enter__()
        keys4 = es3.enter_context(nc.sbuf_tensor("keys4", [128, 2 * NCH * 512], BF16, side="right"))
        hg_bf = es3.enter_context(nc.sbuf_tensor("hg_bf", [128, 2 * 512], BF16, side="right"))
        tkey = {}
        tmm_last = {}
        tstt = {}
        thg = {}
        gi = 0
        for kg in range(16):
            kpar = kg % 2
            dk = [tmm_last[kg - 2]] if kg >= 2 else [tqd, tkd]
            tk = None
            for c in range(NCH):
                tk = p.dma("gpsimd", keys4[:, kpar * 4096 + c * 512: kpar * 4096 + (c + 1) * 512],
                           _ap(keys, (4 * kg) * E_ * 128 + c * 128 * 128,
                               [[128, 128], [E_ * 128, 4], [1, 128]]),
                           deps=dk if c == 0 else [])
            tkey[kg] = tk
            for tt in range(16):
                bank = gi % 2
                hpar = gi % 2
                t_mm = None
                for c in range(NCH):
                    d = []
                    if c == 0:
                        d = [tkey[kg]] + ([tstt[gi - 2]] if gi >= 2 else [])
                    t_mm = p.add("tensor", lambda e, c=c, tt=tt, kpar=kpar, bank=bank: e.matmul(
                        PSA[0:128, bank * 512:(bank + 1) * 512],
                        xn_bf[:, c * T_ + tt * 128: c * T_ + (tt + 1) * 128],
                        keys4[:, kpar * 4096 + c * 512: kpar * 4096 + (c + 1) * 512],
                        start=(c == 0), stop=(c == NCH - 1)), deps=d)
                tmm_last[kg] = t_mm
                gap = _ap(gates, tt * 64 + kg * 4, [[16 * 64, 128], [1, 4], [0, 128]])
                tstt[gi] = p.add("vector", lambda e, gap=gap, bank=bank, hpar=hpar: e.scalar_tensor_tensor(
                    hg_bf[:, hpar * 512:(hpar + 1) * 512], PSA[0:128, bank * 512:(bank + 1) * 512],
                    0.0, gap, ALU.max, ALU.mult),
                    deps=[t_mm] + ([thg[gi - 2]] if gi >= 2 else []))
                thg[gi] = p.dma("sync", hg_d[tt * 128:(tt + 1) * 128, kg * 512:(kg + 1) * 512],
                                hg_bf[:, hpar * 512:(hpar + 1) * 512], deps=[tstt[gi]])
                gi += 1
        es3.__exit__(None, None, None)

        MARKS["A5b"] = len(p.items)
        es4 = ExitStack()
        es4.__enter__()
        vals = es4.enter_context(nc.sbuf_tensor("vals", [128, 64 * 1024], BF16, side="right"))
        hgrow = es4.enter_context(nc.sbuf_tensor("hgrow", [128, 64 * 128], BF16, side="right"))
        gsT = es4.enter_context(nc.sbuf_tensor("gsT", [128, 2 * 512], BF16, side="right"))
        gstage = es4.enter_context(nc.sbuf_tensor("gstage", [128, 1024], F32, side="right"))
        tvals_k = {}
        a5bar = [tmm_last[15], tstt[gi - 1], thg[gi - 1]]
        for k in range(64):
            tvals_k[k] = p.dma("gpsimd", vals[:, k * 1024:(k + 1) * 1024],
                               _ap(values, k * 128 * E_, [[E_, 128], [1, E_]]),
                               deps=a5bar if k == 0 else [])
        t_trs_first = {}
        tev = {}
        tvmm = {}
        tsilu = None
        tgdma = None
        for tt in range(16):
            dh = [thg[15 * 16 + tt]]
            if tt >= 1:
                dh.append(t_trs_first[tt - 1])
            thgr = p.dma("sync", hgrow[:, :], hg_d[tt * 128:(tt + 1) * 128, :], deps=dh)
            for kg in range(16):
                gp = kg % 2
                t_trs = None
                for k4 in range(4):
                    d = []
                    if k4 == 0:
                        d = [thgr] + ([tev[(tt, kg - 1)]] if kg >= 1 else
                                      ([tev[(tt - 1, 15)]] if tt >= 1 else []))
                    t_trs = p.add("tensor", lambda e, kg=kg, k4=k4: e.transpose(
                        PST[:, k4 * 128:(k4 + 1) * 128],
                        hgrow[:, kg * 512 + k4 * 128: kg * 512 + (k4 + 1) * 128],
                        id_bf[:, :]), deps=d)
                    if k4 == 3:
                        t_trs_first[tt] = t_trs
                dv = [t_trs]
                if (tt, kg) >= (0, 2):
                    prev = (tt, kg - 2) if kg >= 2 else ((tt - 1, 14 + kg) if tt >= 1 else None)
                    if prev is not None and prev in tvmm:
                        dv.append(tvmm[prev])
                tev[(tt, kg)] = p.add("vector", lambda e, gp=gp: e.tensor_copy(
                    gsT[:, gp * 512:(gp + 1) * 512], PST[:, 0:512]), deps=dv)
                t_mm = None
                for k4 in range(4):
                    k = kg * 4 + k4
                    for vc in range(NCH):
                        first = (kg == 0 and k4 == 0 and vc in (0, 4))
                        last = (kg == 15 and k4 == 3 and vc in (3, 7))
                        d = []
                        if k4 == 0 and vc == 0:
                            d = [tev[(tt, kg)], tvals_k[4 * kg + 3]]
                            if kg == 0 and tsilu is not None:
                                d.append(tsilu)
                        t_mm = p.add("tensor", lambda e, k=k, k4=k4, vc=vc, first=first, last=last, gp=gp: e.matmul(
                            PSA[0:128, vc * 128:(vc + 1) * 128],
                            vals[:, k * 1024 + vc * 128: k * 1024 + (vc + 1) * 128],
                            gsT[:, gp * 512 + k4 * 128: gp * 512 + (k4 + 1) * 128],
                            start=first, stop=last, skip_group_check=True), deps=d)
                tvmm[(tt, kg)] = t_mm
            p.add("scalar", lambda e: e.activation(gstage[:, 0:512], PSA[0:128, 0:512], AF.Silu),
                  deps=[tvmm[(tt, 15)]] + ([tgdma] if tgdma else []))
            tsilu = p.add("scalar", lambda e: e.activation(
                gstage[:, 512:1024], PSA[0:128, 512:1024], AF.Silu))
            tgdma = p.dma("sync", _ap(g0_d, tt * 128, [[T_, 128], [128 * T_, NCH], [1, 128]]),
                          _ap(gstage, 0, [[1024, 128], [128, NCH], [1, 128]]), deps=[tsilu])
        es4.__exit__(None, None, None)
        esA.__exit__(None, None, None)

        MARKS["B"] = len(p.items)
        p.new_phase()
        p.add("gpsimd", lambda e: e.collective_compute(
            "AllGather", ALU.bypass, replica_groups=GRP,
            ins=[kT_d.ap()], outs=[kT_g.ap()]).then_inc(ccs[0], 1), inc=0)
        p.add("gpsimd", lambda e: e.collective_compute(
            "AllGather", ALU.bypass, replica_groups=GRP,
            ins=[vg_d.ap()], outs=[vg_g.ap()]).then_inc(ccs[1], 1), inc=0,
            deps=[(ccs[0], 1)])

        esB = ExitStack()
        esB.__enter__()
        esY = ExitStack()
        esY.__enter__()
        Yn = esY.enter_context(nc.sbuf_tensor("Yn", [128, NCH * T_], BF16))
        esb1 = ExitStack()
        esb1.__enter__()
        Qs = esb1.enter_context(nc.sbuf_tensor("Qs", [128, NCH * T_], BF16, side="right"))
        KA = esb1.enter_context(nc.sbuf_tensor("KA", [128, S_], BF16, side="right"))
        KB = esb1.enter_context(nc.sbuf_tensor("KB", [128, S_], BF16, side="right"))
        Vg = esb1.enter_context(nc.sbuf_tensor("Vg", [128, 64 * 260], BF16, side="right"))
        Ebf = esb1.enter_context(nc.sbuf_tensor("Ebf", [128, 4 * 1024], BF16, side="right"))
        rrow = esb1.enter_context(nc.sbuf_tensor("rrow", [1, 512], F32, side="right"))
        Rs = esb1.enter_context(nc.sbuf_tensor("Rs", [64, 512], F32, side="right"))
        tq0 = None
        for c in range(NCH):
            tq0 = p.dma("sync", Qs[:, c * T_:(c + 1) * T_], qT_d[c * 128:(c + 1) * 128, :])
        texp = {}
        tavt = {}
        t_ynorm = None
        t_recip = None
        t_repl = None
        t_Rs = None
        bp = 0
        t_klast = {}
        t_avlastg = {}
        for g in range(4):
            dK = [(ccs[0], 1)] + ([t_klast[g - 1]] if g >= 1 else [])
            tka = p.dma("sync", KA[:, :],
                        _ap(kT_g, g * 256 * T_, [[T_, 128], [E_ * T_, 4], [1, T_]]), deps=dK)
            tkb = p.dma("sync", KB[:, :],
                        _ap(kT_g, (g * 256 + 128) * T_, [[T_, 128], [E_ * T_, 4], [1, T_]]))
            dV = [(ccs[1], 1)] + ([t_avlastg[g - 1]] if g >= 1 else [])
            tvg = p.dma("sync", Vg[:, :],
                        _ap(vg_g, g * 260, [[1040, 128], [T_ * 1040, 4], [128 * 1040, 16], [1, 260]]),
                        deps=dV)
            for hlp in range(2):
                h0 = 4 * g + 2 * hlp
                K_t = KA if hlp == 0 else KB
                qc0 = (h0 // 2) * T_
                hl0 = 2 * hlp
                hl1 = 2 * hlp + 1
                for ts in range(4):
                    t_av = None

                    def em_av(u, hl0=hl0, hl1=hl1):
                        nonlocal t_av
                        sb = u - bp
                        ep = u % 4
                        d = [texp[u]]
                        if sb == 0:
                            d = d + [tvg] + ([t_ynorm] if t_ynorm else [])
                        p.add("tensor", lambda e, sb=sb, hl0=hl0, ep=ep: e.matmul(
                            PSB[0:65, 0:512],
                            Vg[:, sb * 260 + hl0 * 65: sb * 260 + (hl0 + 1) * 65],
                            Ebf[:, ep * 1024: ep * 1024 + 512],
                            start=(sb == 0), stop=(sb == 63)), deps=d)
                        t_av = p.add("tensor", lambda e, sb=sb, hl1=hl1, ep=ep: e.matmul(
                            PSB[0:65, 512:1024],
                            Vg[:, sb * 260 + hl1 * 65: sb * 260 + (hl1 + 1) * 65],
                            Ebf[:, ep * 1024 + 512: ep * 1024 + 1024],
                            start=(sb == 0), stop=(sb == 63)),
                            deps=[texp[u]] + ([tvg, t_ynorm] if (sb == 0 and t_ynorm) else
                                              ([tvg] if sb == 0 else [])))
                        tavt[u] = t_av

                    for sb in range(64):
                        u = bp + sb
                        rg = u % 2
                        d = [texp[u - 2]] if u >= 2 else []
                        if sb == 0:
                            d = d + [tka, tkb, tq0]
                        p.add("tensor",
                            lambda e, K_t=K_t, sb=sb, qc0=qc0, ts=ts, rg=rg: e.matmul(
                                PSA[0:128, rg * 1024: rg * 1024 + 512],
                                K_t[0:64, sb * 128:(sb + 1) * 128],
                                Qs[0:64, qc0 + ts * 512: qc0 + (ts + 1) * 512],
                                start=True, stop=True), deps=d)
                        tm = p.add("tensor",
                            lambda e, K_t=K_t, sb=sb, qc0=qc0, ts=ts, rg=rg: e.matmul(
                                PSA[0:128, rg * 1024 + 512: rg * 1024 + 1024],
                                K_t[64:128, sb * 128:(sb + 1) * 128],
                                Qs[64:128, qc0 + ts * 512: qc0 + (ts + 1) * 512],
                                start=True, stop=True))
                        ed = [tm] + ([tavt[u - 4]] if u >= 4 else [])
                        texp[u] = p.add("scalar",
                            lambda e, rg=rg, ep=u % 4: e.activation(
                                Ebf[:, ep * 1024:(ep + 1) * 1024],
                                PSA[0:128, rg * 1024:(rg + 1) * 1024],
                                AF.Exp, scale=0.125), deps=ed)
                        if sb >= 1:
                            em_av(u - 1)
                    em_av(bp + 63)
                    bp += 64
                    t_r0 = p.add("vector", lambda e: e.reciprocal(rrow[:, :], PSB[64:65, 0:512]),
                                 deps=[t_av] + ([t_repl] if t_repl else []))
                    t_p0 = p.add("tensor", lambda e: e.matmul(
                        PSC[0:64, :], ones_r[0:1, 0:64], rrow[:, :], start=True, stop=True),
                        deps=[t_r0] + ([t_Rs] if t_Rs else []))
                    t_R0 = p.add("scalar", lambda e: e.activation(Rs[:, :], PSC[0:64, :], AF.Copy),
                                 deps=[t_p0] + ([t_ynorm] if t_ynorm else []))
                    t_y0 = p.add("vector", lambda e, qc0=qc0, ts=ts: e.tensor_tensor(
                        Yn[0:64, qc0 + ts * 512: qc0 + (ts + 1) * 512],
                        PSB[0:64, 0:512], Rs[:, :], ALU.mult), deps=[t_R0])
                    t_r1 = p.add("vector", lambda e: e.reciprocal(rrow[:, :], PSB[64:65, 512:1024]),
                                 deps=[t_av, t_p0])
                    t_p1 = p.add("tensor", lambda e: e.matmul(
                        PSC[0:64, :], ones_r[0:1, 0:64], rrow[:, :], start=True, stop=True),
                        deps=[t_r1, t_R0])
                    t_R1 = p.add("scalar", lambda e: e.activation(Rs[:, :], PSC[0:64, :], AF.Copy),
                                 deps=[t_p1, t_y0])
                    t_ynorm = p.add("vector", lambda e, qc0=qc0, ts=ts: e.tensor_tensor(
                        Yn[64:128, qc0 + ts * 512: qc0 + (ts + 1) * 512],
                        PSB[0:64, 512:1024], Rs[:, :], ALU.mult), deps=[t_R1])
                    t_repl = t_p1
                    t_Rs = t_R1
                t_klast[g] = texp[bp - 1]
            t_avlastg[g] = tavt[bp - 1]
        esb1.__exit__(None, None, None)

        MARKS["B2"] = len(p.items)
        esb2 = ExitStack()
        esb2.__enter__()
        X1 = esb2.enter_context(nc.sbuf_tensor("X1", [128, NCH * T_], F32, side="right"))
        wo_bf = esb2.enter_context(nc.sbuf_tensor("wo_bf", [128, NCH * 1024], BF16, side="right"))
        xsl = esb2.enter_context(nc.sbuf_tensor("xsl", [128, 2 * 512], F32, side="right"))
        gsl = esb2.enter_context(nc.sbuf_tensor("gsl", [128, 2 * 512], F32, side="right"))
        tsl_ = esb2.enter_context(nc.sbuf_tensor("tsl_", [128, 512], F32, side="right"))
        b1bar = BAR()
        tWo = None
        for c in range(NCH):
            tWo = p.dma("gpsimd", wo_bf[:, c * 1024:(c + 1) * 1024], woT[c * 128:(c + 1) * 128, :],
                        deps=b1bar if c == 0 else [])
        t_addi = {}
        t_muli = {}
        txd = {}
        tgd = {}
        i = 0
        for oc in range(NCH):
            for ts in range(4):
                par = i % 2
                d = [tWo, t_ynorm] if i == 0 else ([t_muli[i - 1]] if i >= 1 else [])
                t_mm = None
                for c in range(NCH):
                    t_mm = p.add("tensor", lambda e, oc=oc, ts=ts, c=c: e.matmul(
                        PSA[0:128, 0:512],
                        wo_bf[:, c * 1024 + oc * 128: c * 1024 + oc * 128 + 128],
                        Yn[:, c * T_ + ts * 512: c * T_ + (ts + 1) * 512],
                        start=(c == 0), stop=(c == NCH - 1)), deps=d if c == 0 else [])
                txd[i] = p.dma("sync", xsl[:, par * 512:(par + 1) * 512],
                               xT[oc * 128:(oc + 1) * 128, ts * 512:(ts + 1) * 512],
                               deps=[t_addi[i - 2]] if i >= 2 else b1bar)
                tgd[i] = p.dma("sync", gsl[:, par * 512:(par + 1) * 512],
                               g0_d[oc * 128:(oc + 1) * 128, ts * 512:(ts + 1) * 512],
                               deps=[t_muli[i - 2]] if i >= 2 else b1bar)
                t_muli[i] = p.add("vector", lambda e, par=par: e.tensor_tensor(
                    tsl_[:, :], PSA[0:128, 0:512], gsl[:, par * 512:(par + 1) * 512], ALU.mult),
                    deps=[t_mm, tgd[i]])
                t_addi[i] = p.add("vector", lambda e, oc=oc, ts=ts, par=par: e.tensor_tensor(
                    X1[:, oc * T_ + ts * 512: oc * T_ + (ts + 1) * 512],
                    tsl_[:, :], xsl[:, par * 512:(par + 1) * 512], ALU.add), deps=[txd[i]])
                p.dma("sync", x1_d[oc * 128:(oc + 1) * 128, ts * 512:(ts + 1) * 512],
                      X1[:, oc * T_ + ts * 512: oc * T_ + (ts + 1) * 512], deps=[t_addi[i]])
                i += 1
        esY.__exit__(None, None, None)

        MARKS["B3"] = len(p.items)
        xn1 = esB.enter_context(nc.sbuf_tensor("xn1", [128, NCH * T_], BF16))
        esb3 = ExitStack()
        esb3.__enter__()
        mu1 = esb3.enter_context(nc.sbuf_tensor("mu1", [128, T_], F32, side="right"))
        rs1 = esb3.enter_context(nc.sbuf_tensor("rs1", [128, T_], F32, side="right"))
        sqs = esb3.enter_context(nc.sbuf_tensor("sqs", [128, 512], F32, side="right"))
        row_a2 = esb3.enter_context(nc.sbuf_tensor("row_a2", [1, T_], F32, side="right"))
        row_b2 = esb3.enter_context(nc.sbuf_tensor("row_b2", [1, T_], F32, side="right"))
        row_c2 = esb3.enter_context(nc.sbuf_tensor("row_c2", [1, 512], F32, side="right"))
        row_d2 = esb3.enter_context(nc.sbuf_tensor("row_d2", [1, 512], F32, side="right"))
        scr_a = esb3.enter_context(nc.sbuf_tensor("scr_a", [128, T_], F32, side="right"))
        scr_b = esb3.enter_context(nc.sbuf_tensor("scr_b", [128, T_], F32, side="right"))
        p.serial = True
        p._last = t_addi[31]
        for ts in range(4):
            for c in range(NCH):
                p.add("scalar", lambda e, c=c, ts=ts: e.activation(
                    sqs[:, :], X1[:, c * T_ + ts * 512: c * T_ + (ts + 1) * 512], AF.Square))
                p.add("tensor", lambda e, c=c, ts=ts: e.matmul(
                    PSC[0:1, :], ones_c[:, :],
                    X1[:, c * T_ + ts * 512: c * T_ + (ts + 1) * 512],
                    start=(c == 0), stop=(c == NCH - 1)))
                p.add("tensor", lambda e, c=c: e.matmul(
                    PSB[0:1, 0:512], ones_c[:, :], sqs[:, :],
                    start=(c == 0), stop=(c == NCH - 1)))
            sl = slice(ts * 512, (ts + 1) * 512)
            p.add("vector", lambda e, sl=sl: e.tensor_scalar_mul(row_a2[:, sl], PSC[0:1, :], 1.0 / 1024.0))
            p.add("vector", lambda e: e.tensor_scalar_mul(row_c2[:, :], PSB[0:1, 0:512], 1.0 / 1024.0))
            p.add("vector", lambda e, sl=sl: e.tensor_tensor(row_d2[:, :], row_a2[:, sl], row_a2[:, sl], ALU.mult))
            p.add("vector", lambda e: e.tensor_tensor(row_c2[:, :], row_c2[:, :], row_d2[:, :], ALU.subtract))
            p.add("vector", lambda e: e.tensor_scalar_add(row_c2[:, :], row_c2[:, :], 1e-5))
            p.add("scalar", lambda e: e.activation(row_d2[:, :], row_c2[:, :], AF.Sqrt))
            p.add("vector", lambda e, sl=sl: e.reciprocal(row_b2[:, sl], row_d2[:, :]))
            p.add("tensor", lambda e, sl=sl: e.matmul(
                PSB[0:128, 0:512], ones_r[:, :], row_a2[:, sl], start=True, stop=True))
            p.add("vector", lambda e, sl=sl: e.tensor_copy(mu1[:, sl], PSB[0:128, 0:512]))
            p.add("tensor", lambda e, sl=sl: e.matmul(
                PSC[0:128, :], ones_r[:, :], row_b2[:, sl], start=True, stop=True))
            p.add("vector", lambda e, sl=sl: e.tensor_copy(rs1[:, sl], PSC[0:128, :]))
        t_xn1 = None
        for c in range(NCH):
            p.add("vector", lambda e, c=c: e.tensor_tensor(
                scr_a[:, :], X1[:, c * T_:(c + 1) * T_], mu1[:, :], ALU.subtract))
            p.add("vector", lambda e: e.tensor_tensor(scr_b[:, :], scr_a[:, :], rs1[:, :], ALU.mult))
            t_xn1 = p.add("vector", lambda e, c=c: e.tensor_copy(xn1[:, c * T_:(c + 1) * T_], scr_b[:, :]))
        p.serial = False
        esb3.__exit__(None, None, None)
        esb2.__exit__(None, None, None)

        MARKS["B4"] = len(p.items)
        esb4 = ExitStack()
        esb4.__enter__()
        W1 = esb4.enter_context(nc.sbuf_tensor("W1", [128, 3 * NCH * 1024], BF16, side="right"))
        U512 = esb4.enter_context(nc.sbuf_tensor("U512", [128, T_], F32, side="right"))
        U128 = esb4.enter_context(nc.sbuf_tensor("U128", [128, T_], F32, side="right"))
        LG = esb4.enter_context(nc.sbuf_tensor("LG", [128, NCH], F32, side="right"))
        NLG = esb4.enter_context(nc.sbuf_tensor("NLG", [128, NCH], F32, side="right"))
        lnGr = esb4.enter_context(nc.sbuf_tensor("lnGr", [16, 1], F32, side="right"))
        lnG = esb4.enter_context(nc.sbuf_tensor("lnG", [16, 1], F32, side="right"))
        selq_s = esb4.enter_context(nc.sbuf_tensor("selq_s", [16, NCH * 128], F32, side="right"))
        Pq = esb4.enter_context(nc.sbuf_tensor("Pq", [128, T_], F32, side="right"))
        Pk = esb4.enter_context(nc.sbuf_tensor("Pk", [128, T_], F32, side="right"))
        scp = esb4.enter_context(nc.sbuf_tensor("scp", [128, T_], F32, side="right"))
        qst = esb4.enter_context(nc.sbuf_tensor("qst", [128, 2 * T_], BF16, side="right"))
        v1st = esb4.enter_context(nc.sbuf_tensor("v1st", [128, 2 * 1024], BF16, side="right"))
        b3bar = BAR()
        tW1 = None
        for w_i, wsrc in enumerate((wq1T, wk1T, wv1T)):
            for c in range(NCH):
                tW1 = p.dma("gpsimd", W1[:, (w_i * NCH + c) * 1024:(w_i * NCH + c + 1) * 1024],
                            wsrc[c * 128:(c + 1) * 128, :],
                            deps=b3bar if (w_i == 0 and c == 0) else [])
        p.serial = True
        p._last = t_xn1
        tselq = p.dma("sync", selq_s[:, :], selq[:, :])
        p.dma("sync", lnGr[:, :], gam[:, :])
        p.add("scalar", lambda e: e.activation(lnG[:, :], lnGr[:, :], AF.Ln))
        for c in range(NCH):
            p.add("tensor", lambda e, c=c: e.matmul(
                PSC[0:128, 0:1], selq_s[:, c * 128:(c + 1) * 128], lnG[:, :],
                start=True, stop=True))
            p.add("vector", lambda e, c=c: e.tensor_copy(LG[:, c:c + 1], PSC[0:128, 0:1]))
        p.add("vector", lambda e: e.tensor_scalar_mul(NLG[:, :], LG[:, :], -1.0))
        p.add("gpsimd", lambda e: e.iota(U512[:, :], [[0, 4], [1, 512]], base=0,
                                         channel_multiplier=0,
                                         allow_small_or_imprecise_dtypes=True))
        t_pre = p.add("gpsimd", lambda e: e.iota(U128[:, :], [[0, 16], [1, 128]], base=0,
                                                 channel_multiplier=0,
                                                 allow_small_or_imprecise_dtypes=True))
        p.serial = False
        LN8 = float(np.log(0.125))
        t_qev = {}
        t_qdma = {}
        gidx = 0
        t_Pq = t_Pk = None
        for oc in range(NCH):
            lgb = _ap(LG, oc, [[NCH, 128], [0, T_]])
            nlgb = _ap(NLG, oc, [[NCH, 128], [0, T_]])
            t_s1 = p.add("vector", lambda e, lgb=lgb: e.tensor_tensor(scp[:, :], U512[:, :], lgb, ALU.mult),
                         deps=[t_pre] + ([t_Pq] if t_Pq else []))
            t_Pq = p.add("scalar", lambda e: e.activation(Pq[:, :], scp[:, :], AF.Exp,
                                                          bias=np.float32(LN8)), deps=[t_s1])
            t_s2 = p.add("vector", lambda e, nlgb=nlgb: e.tensor_tensor(scp[:, :], U128[:, :], nlgb, ALU.mult),
                         deps=[t_Pq])
            t_Pk = p.add("scalar", lambda e: e.activation(Pk[:, :], scp[:, :], AF.Exp), deps=[t_s2])
            for w_i, dst, Pt, t_P in ((0, qt1_d, Pq, t_Pq), (1, kt1_d, Pk, t_Pk)):
                par = gidx % 2
                t_mm = None
                ev_in = {}
                for ts in range(4):
                    for c in range(NCH):
                        d = []
                        if ts == 0 and c == 0:
                            d = [tW1, t_xn1] + ([t_qev[gidx - 2]] if gidx >= 2 else [])
                        if c == 0 and ts >= 2:
                            d = d + [ev_in[ts - 2]]
                        t_mm = p.add("tensor", lambda e, w_i=w_i, oc=oc, c=c, ts=ts, par=par: e.matmul(
                            PSA[0:128, par * 1024 + (ts % 2) * 512: par * 1024 + (ts % 2 + 1) * 512],
                            W1[:, (w_i * NCH + c) * 1024 + oc * 128:
                                  (w_i * NCH + c) * 1024 + oc * 128 + 128],
                            xn1[:, c * T_ + ts * 512: c * T_ + (ts + 1) * 512],
                            start=(c == 0), stop=(c == NCH - 1)), deps=d)
                    t_qev[gidx] = p.add("vector", lambda e, ts=ts, Pt=Pt, par=par: e.tensor_tensor(
                        qst[:, par * T_ + ts * 512: par * T_ + (ts + 1) * 512],
                        PSA[0:128, par * 1024 + (ts % 2) * 512: par * 1024 + (ts % 2 + 1) * 512],
                        Pt[:, ts * 512:(ts + 1) * 512], ALU.mult),
                        deps=[t_mm, t_P] + ([t_qdma[gidx - 2]] if gidx >= 2 else []))
                    ev_in[ts] = t_qev[gidx]
                t_qdma[gidx] = p.dma("sync", dst[oc * 128:(oc + 1) * 128, :],
                                     qst[:, par * T_:(par + 1) * T_], deps=[t_qev[gidx]])
                gidx += 1
        t_v1c = {}
        t_v1d = {}
        for tt in range(16):
            par = tt % 2
            t_mm = None
            for os in range(2):
                for c in range(NCH):
                    d = []
                    if os == 0 and c == 0:
                        d = [tW1, t_xn1] + ([t_v1c[tt - 1]] if tt >= 1 else [])
                    t_mm = p.add("tensor", lambda e, os=os, c=c, tt=tt: e.matmul(
                        PSB[0:128, os * 512:(os + 1) * 512],
                        xn1[:, c * T_ + tt * 128: c * T_ + (tt + 1) * 128],
                        W1[:, (2 * NCH + c) * 1024 + os * 512:
                              (2 * NCH + c) * 1024 + (os + 1) * 512],
                        start=(c == 0), stop=(c == NCH - 1)), deps=d)
            t_v1c[tt] = p.add("vector", lambda e, par=par: e.tensor_copy(
                v1st[:, par * 1024:(par + 1) * 1024], PSB[0:128, 0:1024]),
                deps=[t_mm] + ([t_v1d[tt - 2]] if tt >= 2 else []))
            t_v1d[tt] = p.dma("sync", v1_d[tt * 128:(tt + 1) * 128, :],
                              v1st[:, par * 1024:(par + 1) * 1024], deps=[t_v1c[tt]])
        esb4.__exit__(None, None, None)

        MARKS["B5"] = len(p.items)
        esb5 = ExitStack()
        esb5.__enter__()
        wq1 = esb5.enter_context(nc.sbuf_tensor("wq1", [128, NCH * 1024], BF16, side="right"))
        kch = esb5.enter_context(nc.sbuf_tensor("kch", [128, 1024], F32, side="right"))
        tqx = esb5.enter_context(nc.sbuf_tensor("tqx", [128, 1024], F32, side="right"))
        am8 = esb5.enter_context(nc.sbuf_tensor("am8", [128, NCH], F32, side="right"))
        amc = esb5.enter_context(nc.sbuf_tensor("amc", [128, 1], F32, side="right"))
        amr = esb5.enter_context(nc.sbuf_tensor("amr", [1, 128], F32, side="right"))
        am = esb5.enter_context(nc.sbuf_tensor("am", [1, 1], F32, side="right"))
        srow = esb5.enter_context(nc.sbuf_tensor("srow", [1, 1], F32, side="right"))
        rsrow = esb5.enter_context(nc.sbuf_tensor("rsrow", [1, 1], F32, side="right"))
        scol = esb5.enter_context(nc.sbuf_tensor("scol", [128, 1], F32, side="right"))
        rscol = esb5.enter_context(nc.sbuf_tensor("rscol", [128, 1], F32, side="right"))
        a_bf = esb5.enter_context(nc.sbuf_tensor("a_bf", [128, NCH * 16], BF16, side="right"))
        b_bf = esb5.enter_context(nc.sbuf_tensor("b_bf", [16, 1024], BF16, side="right"))
        bias_s = esb5.enter_context(nc.sbuf_tensor("bias_s", [128, NCH], F32, side="right"))
        u_bf = esb5.enter_context(nc.sbuf_tensor("u_bf", [16, T_], BF16, side="right"))
        g1st = esb5.enter_context(nc.sbuf_tensor("g1st", [128, 2 * 512], F32, side="right"))
        p.serial = True
        for c in range(NCH):
            p.dma("sync", kch[:, :], kern1[c * 128:(c + 1) * 128, :])
            p.add("vector", lambda e, c=c: e.reduce_max(
                am8[:, c:c + 1], kch[:, :], AX.X, apply_absolute_value=True))
        p.add("vector", lambda e: e.reduce_max(amc[:, :], am8[:, :], AX.X))
        p.dma("sync", am_d[:, :], amc[:, :])
        p.dma("sync", amr[:, :], _ap(am_d, 0, [[128, 1], [1, 128]]))
        p.add("vector", lambda e: e.reduce_max(am[:, :], amr[:, :], AX.X))
        p.add("vector", lambda e: e.tensor_scalar_mul(srow[:, :], am[:, :], 1.0 / 7.0))
        p.add("vector", lambda e: e.reciprocal(rsrow[:, :], srow[:, :]))
        p.add("tensor", lambda e: e.matmul(PSC[0:128, 0:1], ones_r[:, :], srow[:, :],
                                           start=True, stop=True))
        p.add("vector", lambda e: e.tensor_copy(scol[:, :], PSC[0:128, 0:1]))
        p.add("tensor", lambda e: e.matmul(PSC[0:128, 0:1], ones_r[:, :], rsrow[:, :],
                                           start=True, stop=True))
        p.add("vector", lambda e: e.tensor_copy(rscol[:, :], PSC[0:128, 0:1]))
        for c in range(NCH):
            p.dma("sync", kch[:, :], kern1[c * 128:(c + 1) * 128, :])
            p.add("vector", lambda e: e.tensor_scalar(
                tqx[:, :], kch[:, :], rscol[:, 0:1], MAGIC, ALU.mult, ALU.add))
            p.add("vector", lambda e: e.tensor_scalar(
                kch[:, :], tqx[:, :], MAGIC, -8.0, ALU.subtract, ALU.max))
            p.add("vector", lambda e, c=c: e.tensor_scalar(
                wq1[:, c * 1024:(c + 1) * 1024], kch[:, :], 7.0, scol[:, 0:1],
                ALU.min, ALU.mult))
        for c in range(NCH):
            p.dma("gpsimd", a_bf[:, c * 16:(c + 1) * 16], a1[c * 128:(c + 1) * 128, :])
            p.dma("sync", bias_s[:, c:c + 1], bias1[c * 128:(c + 1) * 128, :])
        p.dma("gpsimd", b_bf[:, :], b1[:, :])
        for ts in range(4):
            for c in range(NCH):
                p.add("tensor", lambda e, c=c, ts=ts: e.matmul(
                    PSC[0:16, :], a_bf[:, c * 16:(c + 1) * 16],
                    xn1[:, c * T_ + ts * 512: c * T_ + (ts + 1) * 512],
                    start=(c == 0), stop=(c == NCH - 1)))
            p.add("vector", lambda e, ts=ts: e.tensor_copy(u_bf[:, ts * 512:(ts + 1) * 512], PSC[0:16, :]))
        t_quant = p._last
        p.serial = False
        t_sil = {}
        t_gd = {}
        i = 0
        for fc in range(NCH):
            for ts in range(4):
                par = i % 2
                t_mm = None
                for c in range(NCH):
                    d = [t_quant] + ([t_sil[i - 2]] if i >= 2 else []) if c == 0 else []
                    t_mm = p.add("tensor", lambda e, fc=fc, ts=ts, c=c, par=par: e.matmul(
                        PSA[0:128, par * 512:(par + 1) * 512],
                        wq1[:, c * 1024 + fc * 128: c * 1024 + fc * 128 + 128],
                        xn1[:, c * T_ + ts * 512: c * T_ + (ts + 1) * 512],
                        start=(c == 0), stop=False), deps=d)
                t_mm = p.add("tensor", lambda e, fc=fc, ts=ts, par=par: e.matmul(
                    PSA[0:128, par * 512:(par + 1) * 512], b_bf[:, fc * 128:(fc + 1) * 128],
                    u_bf[:, ts * 512:(ts + 1) * 512], start=False, stop=True))
                t_sil[i] = p.add("scalar", lambda e, fc=fc, par=par: e.activation(
                    g1st[:, par * 512:(par + 1) * 512], PSA[0:128, par * 512:(par + 1) * 512],
                    AF.Silu, bias=bias_s[:, fc:fc + 1]),
                    deps=[t_mm] + ([t_gd[i - 2]] if i >= 2 else []))
                t_gd[i] = p.dma("sync", g1_d[fc * 128:(fc + 1) * 128, ts * 512:(ts + 1) * 512],
                                g1st[:, par * 512:(par + 1) * 512], deps=[t_sil[i]])
                i += 1
        esb5.__exit__(None, None, None)
        esB.__exit__(None, None, None)

        MARKS["D"] = len(p.items)
        p.new_phase()
        p.add("gpsimd", lambda e: e.collective_compute(
            "AllGather", ALU.bypass, replica_groups=GRP,
            ins=[qt1_d.ap()], outs=[qt1_g.ap()]).then_inc(ccs[2], 1), inc=0)
        p.add("gpsimd", lambda e: e.collective_compute(
            "AllGather", ALU.bypass, replica_groups=GRP,
            ins=[kt1_d.ap()], outs=[kt1_g.ap()]).then_inc(ccs[3], 1), inc=0,
            deps=[(ccs[2], 1)])
        p.add("gpsimd", lambda e: e.collective_compute(
            "AllGather", ALU.bypass, replica_groups=GRP,
            ins=[v1_d.ap()], outs=[v1_g.ap()]).then_inc(ccs[4], 1), inc=0,
            deps=[(ccs[3], 1)])

        esD = ExitStack()
        esD.__enter__()
        qtA = esD.enter_context(nc.sbuf_tensor("qtA", [128, S_], BF16, side="right"))
        qtB = esD.enter_context(nc.sbuf_tensor("qtB", [128, S_], BF16, side="right"))
        ktA = esD.enter_context(nc.sbuf_tensor("ktA", [128, S_], BF16, side="right"))
        ktB = esD.enter_context(nc.sbuf_tensor("ktB", [128, S_], BF16, side="right"))
        Vs = esD.enter_context(nc.sbuf_tensor("Vs", [128, 64 * 256], BF16, side="right"))
        y1s = esD.enter_context(nc.sbuf_tensor("y1s", [128, 2 * S_], BF16, side="right"))
        mg = esD.enter_context(nc.sbuf_tensor("mg", [1, 4 * 67], F32, side="right"))
        gr2 = esD.enter_context(nc.sbuf_tensor("gr2", [1, 4], F32, side="right"))
        lnr2 = esD.enter_context(nc.sbuf_tensor("lnr2", [1, 4], F32, side="right"))
        erow = esD.enter_context(nc.sbuf_tensor("erow", [1, 4 * 67], F32, side="right"))
        crow = esD.enter_context(nc.sbuf_tensor("crow", [1, 4 * 67], F32, side="right"))
        ctab = esD.enter_context(nc.sbuf_tensor("ctab", [128, 4 * 67], F32, side="right"))
        UD = esD.enter_context(nc.sbuf_tensor("UD", [128, 512], F32, side="right"))
        M01 = esD.enter_context(nc.sbuf_tensor("M01", [128, 512], F32, side="right"))
        Wbf = esD.enter_context(nc.sbuf_tensor("Wbf", [128, 4 * 1024], BF16, side="right"))
        rh = nc.gpsimd.alloc_register("rh")
        rh2 = nc.gpsimd.alloc_register("rh2")
        p.serial = True
        p._last = None
        p.add("gpsimd", lambda e: e.reg_load(rh, offs[0:1, 0:1]),
              deps=[(ccs[2], 1), (ccs[3], 1), (ccs[4], 1), tof], inc=0)
        p.add("gpsimd", lambda e: e.reg_mul(rh2, rh, T_), inc=0)
        tqa = p.dma("gpsimd", qtA[:, :], _ap(qt1_g, rh2, [[T_, 128], [E_ * T_, 4], [1, T_]]))
        tkaD = p.dma("gpsimd", ktA[:, :], _ap(kt1_g, rh2, [[T_, 128], [E_ * T_, 4], [1, T_]]))
        p.add("gpsimd", lambda e: e.reg_add(rh2, rh2, 128 * T_), inc=0)
        tqb = p.dma("gpsimd", qtB[:, :], _ap(qt1_g, rh2, [[T_, 128], [E_ * T_, 4], [1, T_]]))
        tkbD = p.dma("gpsimd", ktB[:, :], _ap(kt1_g, rh2, [[T_, 128], [E_ * T_, 4], [1, T_]]))
        tvsD = p.dma("gpsimd", Vs[:, :], _ap(v1_g, rh, [[E_, 128], [T_ * E_, 4], [128 * E_, 16], [1, 256]]))
        p.dma("sync", mg[:, :], mgrid[:, :])
        p.dma("sync", gr2[:, :], _ap(gamh, 0, [[4, 1], [1, 4]]))
        p.add("scalar", lambda e: e.activation(lnr2[:, :], gr2[:, :], AF.Ln))
        lnb = _ap(lnr2, 0, [[4, 1], [1, 4], [0, 67]])
        p.add("vector", lambda e, lnb=lnb: e.tensor_tensor(erow[:, :], mg[:, :], lnb, ALU.mult))
        p.add("scalar", lambda e: e.activation(crow[:, :], erow[:, :], AF.Exp))
        p.add("tensor", lambda e: e.matmul(PSB[0:128, 0:268], ones_r[:, :], crow[:, :],
                                           start=True, stop=True))
        p.add("vector", lambda e: e.tensor_copy(ctab[:, :], PSB[0:128, 0:268]))
        p.add("gpsimd", lambda e: e.iota(UD[:, :], [[1, 512]], base=0,
                                         channel_multiplier=-1,
                                         allow_small_or_imprecise_dtypes=True))
        t_setupD = p.add("vector", lambda e: e.tensor_single_scalar(M01[:, :], UD[:, :], 0.0, ALU.is_ge))
        p.serial = False
        bpD = 0
        tevD = {}
        tav2 = {}
        t_yev = None
        for hlp in range(2):
            kt = ktA if hlp == 0 else ktB
            qt = qtA if hlp == 0 else qtB
            for ts in range(16):
                nsb = 4 * ts + 4
                t_av = None

                def em_avD(u, hlp=hlp, nsb=nsb):
                    nonlocal t_av
                    sb = u - bpD
                    wq = u % 4
                    for hh in (0, 1):
                        hl = 2 * hlp + hh
                        d = [tevD[(u, hh)]]
                        if sb == 0:
                            d = d + [tvsD] + ([t_yev] if t_yev else [])
                        t_av = p.add("tensor",
                            lambda e, sb=sb, hl=hl, nsb=nsb, wq=wq, hh=hh: e.matmul(
                                PSB[0:64, hh * 512:(hh + 1) * 512],
                                Vs[:, sb * 256 + hl * 64: sb * 256 + (hl + 1) * 64],
                                Wbf[:, wq * 1024 + hh * 512: wq * 1024 + (hh + 1) * 512],
                                start=(sb == 0), stop=(sb == nsb - 1)), deps=d)
                        tav2[(u, hh)] = t_av

                for sb in range(nsb):
                    u = bpD + sb
                    rg = u % 2
                    d = ([tevD[(u - 2, 0)], tevD[(u - 2, 1)]] if u >= 2
                         else [t_setupD, tqa, tkaD, tqb, tkbD])
                    p.add("tensor",
                        lambda e, kt=kt, qt=qt, sb=sb, ts=ts, rg=rg: e.matmul(
                            PSA[0:128, rg * 1024: rg * 1024 + 512],
                            kt[0:64, sb * 128:(sb + 1) * 128],
                            qt[0:64, ts * 512:(ts + 1) * 512],
                            start=True, stop=True), deps=d if sb == 0 or u >= 2 else [])
                    tm = p.add("tensor",
                        lambda e, kt=kt, qt=qt, sb=sb, ts=ts, rg=rg: e.matmul(
                            PSA[0:128, rg * 1024 + 512: rg * 1024 + 1024],
                            kt[64:128, sb * 128:(sb + 1) * 128],
                            qt[64:128, ts * 512:(ts + 1) * 512],
                            start=True, stop=True))
                    wq = u % 4
                    for hh in (0, 1):
                        hl = 2 * hlp + hh
                        ed = [tm] + ([tav2[(u - 4, hh)]] if u >= 4 else [])
                        if sb < 4 * ts:
                            idx = hl * 67 + (4 * ts - sb) + 3
                            if hh == 1:
                                tevD[(u, hh)] = p.add("scalar",
                                    lambda e, idx=idx, rg=rg, hh=hh, wq=wq: e.activation(
                                        Wbf[:, wq * 1024 + hh * 512: wq * 1024 + (hh + 1) * 512],
                                        PSA[0:128, rg * 1024 + hh * 512: rg * 1024 + (hh + 1) * 512],
                                        AF.Copy, scale=ctab[:, idx:idx + 1]), deps=ed)
                            else:
                                tevD[(u, hh)] = p.add("vector",
                                    lambda e, idx=idx, rg=rg, hh=hh, wq=wq: e.tensor_scalar_mul(
                                        Wbf[:, wq * 1024 + hh * 512: wq * 1024 + (hh + 1) * 512],
                                        PSA[0:128, rg * 1024 + hh * 512: rg * 1024 + (hh + 1) * 512],
                                        ctab[:, idx:idx + 1]), deps=ed)
                        else:
                            j = sb - 4 * ts
                            idx = hl * 67 + 3 - j
                            if j > 0:
                                p.add("vector", lambda e, j=j, wq=wq, hh=hh: e.memset(
                                    Wbf[:, wq * 1024 + hh * 512: wq * 1024 + hh * 512 + j * 128],
                                    0.0), deps=ed)
                            tevD[(u, hh)] = p.add("vector",
                                lambda e, j=j, idx=idx, rg=rg, hh=hh, wq=wq: e.scalar_tensor_tensor(
                                    Wbf[:, wq * 1024 + hh * 512 + j * 128: wq * 1024 + (hh + 1) * 512],
                                    PSA[0:128, rg * 1024 + hh * 512 + j * 128: rg * 1024 + (hh + 1) * 512],
                                    ctab[:, idx:idx + 1], M01[:, 0:512 - j * 128],
                                    ALU.mult, ALU.mult), deps=ed)
                    if sb >= 1:
                        em_avD(u - 1)
                em_avD(bpD + nsb - 1)
                bpD += nsb
                t_y0 = p.add("vector", lambda e, hlp=hlp, ts=ts: e.tensor_copy(
                    y1s[0:64, hlp * S_ + ts * 512: hlp * S_ + (ts + 1) * 512],
                    PSB[0:64, 0:512]), deps=[tav2[(bpD - 1, 0)], tav2[(bpD - 1, 1)]])
                t_yev = p.add("vector", lambda e, hlp=hlp, ts=ts: e.tensor_copy(
                    y1s[64:128, hlp * S_ + ts * 512: hlp * S_ + (ts + 1) * 512],
                    PSB[0:64, 512:1024]), deps=[tav2[(bpD - 1, 1)]])
        ty1a = p.dma("sync", y1_d[0:128, :], y1s[:, 0:S_], deps=[t_yev])
        ty1b = p.dma("sync", y1_d[128:256, :], y1s[:, S_:2 * S_], deps=[t_yev])
        esD.__exit__(None, None, None)

        MARKS["E"] = len(p.items)
        p.new_phase()
        p.add("gpsimd", lambda e: e.collective_compute(
            "AllGather", ALU.bypass, replica_groups=GRP,
            ins=[y1_d.ap()], outs=[y1_g.ap()]).then_inc(ccs[5], 1), inc=0)

        esE = ExitStack()
        esE.__enter__()
        y1_sb = esE.enter_context(nc.sbuf_tensor("y1_sb", [128, NCH * T_], BF16, side="right"))
        wo1_bf = esE.enter_context(nc.sbuf_tensor("wo1_bf", [128, NCH * 1024], BF16, side="right"))
        xsl2 = esE.enter_context(nc.sbuf_tensor("xsl2", [128, 2 * 512], F32, side="right"))
        gsl2 = esE.enter_context(nc.sbuf_tensor("gsl2", [128, 2 * 512], F32, side="right"))
        tsl2 = esE.enter_context(nc.sbuf_tensor("tsl2", [128, 512], F32, side="right"))
        osl2 = esE.enter_context(nc.sbuf_tensor("osl2", [128, 2 * 512], F32, side="right"))
        rt = nc.gpsimd.alloc_register("rt")
        rt2 = nc.gpsimd.alloc_register("rt2")
        p.add("gpsimd", lambda e: e.reg_load(rt, offs[0:1, 1:2]),
              deps=[(ccs[5], 1), tof2], inc=0)
        ty1l = None
        for c in range(NCH):
            p.add("gpsimd", lambda e, c=c: e.reg_add(rt2, rt, c * 128 * S_), inc=0)
            ty1l = p.dma("gpsimd", y1_sb[:, c * T_:(c + 1) * T_],
                         _ap(y1_g, rt2, [[S_, 128], [1, T_]]))
            p.dma("gpsimd", wo1_bf[:, c * 1024:(c + 1) * 1024], wo1T[c * 128:(c + 1) * 128, :])
        tWo1 = p.tok("gpsimd")
        t_addE = {}
        t_mulE = {}
        i = 0
        for oc in range(NCH):
            for ts in range(4):
                par = i % 2
                t_mm = None
                for c in range(NCH):
                    d = [tWo1] if (c == 0) else []
                    if c == 0 and i >= 1:
                        d.append(t_mulE[i - 1])
                    t_mm = p.add("tensor", lambda e, oc=oc, ts=ts, c=c: e.matmul(
                        PSA[0:128, 0:512],
                        wo1_bf[:, c * 1024 + oc * 128: c * 1024 + oc * 128 + 128],
                        y1_sb[:, c * T_ + ts * 512: c * T_ + (ts + 1) * 512],
                        start=(c == 0), stop=(c == NCH - 1)), deps=d)
                txd2 = p.dma("sync", xsl2[:, par * 512:(par + 1) * 512],
                             x1_d[oc * 128:(oc + 1) * 128, ts * 512:(ts + 1) * 512],
                             deps=[t_addE[i - 2]] if i >= 2 else [])
                tgd2 = p.dma("sync", gsl2[:, par * 512:(par + 1) * 512],
                             g1_d[oc * 128:(oc + 1) * 128, ts * 512:(ts + 1) * 512],
                             deps=[t_mulE[i - 2]] if i >= 2 else [])
                t_mulE[i] = p.add("vector", lambda e, par=par: e.tensor_tensor(
                    tsl2[:, :], PSA[0:128, 0:512], gsl2[:, par * 512:(par + 1) * 512], ALU.mult),
                    deps=[t_mm, tgd2])
                t_addE[i] = p.add("vector", lambda e, par=par: e.tensor_tensor(
                    osl2[:, par * 512:(par + 1) * 512], tsl2[:, :],
                    xsl2[:, par * 512:(par + 1) * 512], ALU.add), deps=[txd2])
                p.dma("sync", x2T_o[oc * 128:(oc + 1) * 128, ts * 512:(ts + 1) * 512],
                      osl2[:, par * 512:(par + 1) * 512], deps=[t_addE[i]])
                i += 1
        esE.__exit__(None, None, None)

        if DEBUG_FULL:
            dbg_srcs = dict(qT_d=qT_d, kT_d=kT_d, g0_d=g0_d, x1_d=x1_d,
                            qt1_d=qt1_d, kt1_d=kt1_d, v1_d=v1_d, g1_d=g1_d,
                            y1_d=y1_d)
            b0 = [p.tok(e) for e in P2.ENGS]
            for nm, t in dbg_srcs.items():
                dt = {"g0_d": F32, "x1_d": F32, "g1_d": F32}.get(nm, BF16)
                sh = [t.shape[0], t.shape[1]] if hasattr(t, "shape") else None
                od = nc.dram_tensor("dbg_" + nm, list(t.shape), dt, kind="ExternalOutput")
                p.dma("sync", od.ap(), t.ap(), deps=b0)

        MARKS["END"] = len(p.items)
        with allow_ici_all_to_all():
            p.emit(block)
    return nc


def _rope_tables_host():
    import jax.numpy as jnp
    with jax.default_device(jax.devices("cpu")[0]):
        idx = jnp.arange(HD_ // 2, dtype=jnp.float32)
        ang = jnp.power(jnp.asarray(10000.0, jnp.float32), 2.0 * idx / HD_)
        th = jnp.arange(S_, dtype=jnp.float32)[:, None] * ang[None, :]
        return np.asarray(jnp.cos(th)), np.asarray(jnp.sin(th))


def kernel(**inputs):
    asc = np.ascontiguousarray
    cores = list(range(NCORE))

    cos, sin = _rope_tables_host()
    idf = np.eye(128, dtype=np.float32)
    onesr = np.ones((1, 128), np.float32)
    onesc = np.ones((128, 1), np.float32)
    selq = np.zeros((16, 1024), np.float32)
    for c in range(NCH):
        selq[2 * c, c * 128:c * 128 + 64] = 1.0
        selq[2 * c + 1, c * 128 + 64:(c + 1) * 128] = 1.0
    mgrid = np.concatenate(
        [128.0 * np.arange(-3, 64, dtype=np.float32)[None, :]] * 4, axis=1)

    x = np.asarray(inputs["L0_attn_x"], np.float32)
    wqT = asc(inputs["L0_attn_w_q"].T)
    wkT = asc(inputs["L0_attn_w_k"].T)
    wvT = asc(inputs["L0_attn_w_v"].T)
    woT = asc(inputs["L0_attn_w_o"].T)
    selT = asc(inputs["L0_ffn_expert_sel"].T)
    keys = np.asarray(inputs["L0_ffn_keys"], np.float32)
    values = np.asarray(inputs["L0_ffn_values"], np.float32)
    wq1T = asc(inputs["L1_attn_wq"].T)
    wk1T = asc(inputs["L1_attn_wk"].T)
    wv1T = asc(inputs["L1_attn_wv"].T)
    wo1T = asc(inputs["L1_attn_wo"].T)
    gammas = np.asarray(inputs["L1_attn_gammas"], np.float32)
    kern1 = np.asarray(inputs["L1_ffn_kernel"], np.float32)
    bias1 = asc(inputs["L1_ffn_bias"].reshape(E_, 1))
    a1 = np.asarray(inputs["L1_ffn_a"], np.float32)
    b1 = np.asarray(inputs["L1_ffn_b"], np.float32)

    def core_bt(c):
        return c // 4, 2048 * (c % 4)

    in_maps = []
    for c in cores:
        b, t0 = core_bt(c)
        sl = slice(t0, t0 + T_)
        hg0 = 4 * (c % 4)
        in_maps.append(dict(
            xT=asc(x[b, sl].T), wqT=wqT, wkT=wkT, wvT=wvT,
            selT=selT, keys=keys, values=values,
            cosL=asc(cos[sl]), sinL=asc(sin[sl]),
            idf=idf, onesr=onesr, onesc=onesc,
            woT=woT, wq1T=wq1T, wk1T=wk1T, wv1T=wv1T, wo1T=wo1T,
            kern1=kern1, bias1=bias1, a1=a1, b1=b1,
            gam=asc(gammas.reshape(16, 1)), selq=selq,
            gamh=asc(gammas[hg0:hg0 + 4].reshape(4, 1)), mgrid=mgrid,
            hoff=np.array([[hg0 * 64]], np.int32),
            toff=np.array([[t0]], np.int32)))
    res = run_bass_kernel_spmd(build_full(), in_maps, core_ids=cores).results

    out = np.empty((B_, S_, E_), np.float32)
    for c in cores:
        b, t0 = core_bt(c)
        out[b, t0:t0 + T_, :] = np.asarray(res[c]["x2T_o"]).T
    return out


# version 18
# speedup vs baseline: 1.0132x; 1.0132x over previous
import numpy as np
import ml_dtypes
import jax
import concourse.bass as bass
import concourse.mybir as mybir
from concourse.bass_utils import run_bass_kernel_spmd

F32 = mybir.dt.float32
BF16 = mybir.dt.bfloat16
AF = mybir.ActivationFunctionType
ALU = mybir.AluOpType
AX = mybir.AxisListType

B_, S_, E_, H_ = 2, 8192, 1024, 16
HD_ = 64
NCORE = 8
T_ = 2048
NCH = 8
MAGIC = 12582912.0


class Prog:

    def __init__(self, nc):
        self.nc = nc
        self.items = []
        self.phase = 0

    def new_phase(self):
        self.phase += 1

    def add(self, eng, fn, inc=1, pre=()):
        self.items.append((eng, fn, inc, self.phase, tuple(pre)))

    def dma(self, eng, out, in_, pre=()):
        self.add(eng, (lambda e, o=out, i=in_: e.dma_start(out=o, in_=i)), 16, pre)

    def emit(self, block, sems):
        nph = self.phase + 1
        assert len(sems) >= nph
        cum = [0] * nph
        plan = []
        for eng, fn, inc, ph, pre in self.items:
            w = [(sems[ph], cum[ph])] if cum[ph] > 0 else []
            if cum[ph] == 0 and ph > 0:
                w.append((sems[ph - 1], cum[ph - 1]))
            w.extend(pre)
            plan.append((eng, fn, inc, sems[ph], w))
            cum[ph] += inc
        used = sorted({en for en, *_ in plan})
        final = (sems[self.phase], cum[self.phase])

        def mk(engname):
            def emitter(eng):
                for en, fn, inc, sem, waits in plan:
                    if en != engname:
                        continue
                    for ws, wv in waits:
                        eng.wait_ge(ws, wv)
                    r = fn(eng)
                    if inc:
                        r.then_inc(sem, inc)
                if engname == used[0]:
                    eng.wait_ge(final[0], final[1])
            return emitter
        for name in used:
            getattr(block, name)(mk(name))


def _ap(t, off, dims):
    return bass.AP(t, off, dims)


class P2:
    ENGS = ("sync", "scalar", "vector", "tensor", "gpsimd")

    def __init__(self, nc):
        self.nc = nc
        self.items = []
        self.phase = 0
        self.cnt = {}
        self.serial = False
        self._last = None

    def tok(self, eng):
        return (self.phase, eng, self.cnt.get((self.phase, eng), 0))

    def new_phase(self):
        self.phase += 1

    def add(self, eng, fn, inc=1, deps=()):
        deps = list(deps)
        if self.serial and self._last is not None:
            deps.append(self._last)
        ph = self.phase
        c = self.cnt.get((ph, eng), 0) + inc
        self.cnt[(ph, eng)] = c
        self.items.append((eng, fn, inc, ph, tuple(deps)))
        t = (ph, eng, c, inc)
        self._last = t
        return t

    def dma(self, eng, out, in_, deps=()):
        return self.add(eng, (lambda e, o=out, i=in_: e.dma_start(out=o, in_=i)),
                        16, deps)

    SERIALIZE_ALL = False
    SERIALIZE_PHASES = set()
    SERIALIZE_RANGES = []

    def emit(self, block):
        nc = self.nc
        nph = self.phase + 1
        sems = {}
        guards = []
        for ph in range(nph):
            for eng in self.ENGS:
                if self.cnt.get((ph, eng), 0) > 0:
                    g = nc.semaphore(f"p{ph}_{eng}")
                    sems[(ph, eng)] = g.__enter__()
                    guards.append(g)
        finals = {}
        for ph in range(nph):
            finals[ph] = [(ph, eng, self.cnt[(ph, eng)], 1)
                          for eng in self.ENGS if self.cnt.get((ph, eng), 0) > 0]
        seen = set()
        plan = []
        cum = {}
        prev_cmp = {}
        for eng, fn, inc, ph, deps in self.items:
            waits = []
            if (ph, eng) not in seen:
                seen.add((ph, eng))
                if ph > 0:
                    waits.extend((p_, e_, c_) for p_, e_, c_, *_ in finals[ph - 1])
            if inc != 16 and eng in ("vector", "scalar", "gpsimd"):
                pc = prev_cmp.get(eng)
                if pc is not None:
                    waits.append(pc)
                if inc == 1:
                    prev_cmp[eng] = (ph, eng, cum.get((ph, eng), 0) + inc)
            for d in deps:
                if isinstance(d[0], int):
                    waits.append((d[0], d[1], d[2]))
                else:
                    waits.append(d)
            idx = len(plan)
            in_rng = any(lo <= idx < hi for lo, hi in self.SERIALIZE_RANGES)
            if (self.SERIALIZE_ALL or ph in self.SERIALIZE_PHASES or in_rng) and plan:
                pe, _, pinc, pph, _ = plan[-1]
                waits.append((pph, pe, cum.get((pph, pe), 0)))
            plan.append((eng, fn, inc, ph, waits))
            cum[(ph, eng)] = cum.get((ph, eng), 0) + inc
        used = sorted({e for e, *_ in plan})
        lastph = self.phase

        def mk(engname):
            def emitter(eng):
                for en, fn, inc, ph, waits in plan:
                    if en != engname:
                        continue
                    dedup = {}
                    for w in waits:
                        if isinstance(w[0], int):
                            key = ("s", w[0], w[1])
                            dedup[key] = max(dedup.get(key, 0), w[2])
                        else:
                            key = ("r", id(w[0]))
                            dedup[key] = (w[0], max(
                                dedup[key][1], w[1]) if key in dedup else w[1])
                    for key, v in dedup.items():
                        if key[0] == "s":
                            if v > 0 and (key[1], key[2]) in sems:
                                eng.wait_ge(sems[(key[1], key[2])], v)
                        else:
                            eng.wait_ge(v[0], v[1])
                    r = fn(eng)
                    if inc:
                        r.then_inc(sems[(ph, en)], inc)
                if engname == used[0]:
                    for (p_, e_, c_, *_x) in finals[lastph]:
                        eng.wait_ge(sems[(p_, e_)], c_)
            return emitter
        for name in used:
            getattr(block, name)(mk(name))
        return guards


DEBUG_A = False
DEBUG_FULL = False
MARKS = {}
A4_NOLAG = False


def build_A():
    nc = bass.Bass("TRN2", target_bir_lowering=False)
    xT = nc.dram_tensor("xT", [E_, T_], F32, kind="ExternalInput")
    wqT = nc.dram_tensor("wqT", [E_, E_], F32, kind="ExternalInput")
    wkT = nc.dram_tensor("wkT", [E_, E_], F32, kind="ExternalInput")
    wvT = nc.dram_tensor("wvT", [E_, E_], F32, kind="ExternalInput")
    selT = nc.dram_tensor("selT", [E_, 64], F32, kind="ExternalInput")
    keys = nc.dram_tensor("keys", [64, E_, 128], F32, kind="ExternalInput")
    values = nc.dram_tensor("values", [64, 128, E_], F32, kind="ExternalInput")
    cosL = nc.dram_tensor("cosL", [T_, 32], F32, kind="ExternalInput")
    sinL = nc.dram_tensor("sinL", [T_, 32], F32, kind="ExternalInput")
    idf = nc.dram_tensor("idf", [128, 128], F32, kind="ExternalInput")
    onesr = nc.dram_tensor("onesr", [1, 128], F32, kind="ExternalInput")
    onesc = nc.dram_tensor("onesc", [128, 1], F32, kind="ExternalInput")

    qT_o = nc.dram_tensor("qT_o", [E_, T_], BF16, kind="ExternalOutput")
    kT_o = nc.dram_tensor("kT_o", [E_, T_], BF16, kind="ExternalOutput")
    v_o = nc.dram_tensor("v_o", [T_, E_], BF16, kind="ExternalOutput")
    g_o = nc.dram_tensor("g_o", [E_, T_], F32, kind="ExternalOutput")
    hg_d = nc.dram_tensor("hg_d", [T_, 64 * 128], BF16,
                          kind="ExternalOutput" if DEBUG_A else "Internal")

    from contextlib import ExitStack
    es = ExitStack()
    with es:
        xn_bf = es.enter_context(nc.sbuf_tensor("xn_bf", [128, NCH * T_], BF16))
        gates = es.enter_context(nc.sbuf_tensor("gates", [128, 16 * 64], F32))
        id_f = es.enter_context(nc.sbuf_tensor("id_f", [128, 128], F32))
        id_bf = es.enter_context(nc.sbuf_tensor("id_bf", [128, 128], BF16))
        ones_r = es.enter_context(nc.sbuf_tensor("ones_r", [1, 128], F32))
        ones_c = es.enter_context(nc.sbuf_tensor("ones_c", [128, 1], F32))
        PSA = es.enter_context(nc.psum_tensor("PSA", [128, 2048], F32))
        PSB = es.enter_context(nc.psum_tensor("PSB", [128, 1024], F32))
        PSC = es.enter_context(nc.psum_tensor("PSC", [128, 512], F32))
        PST = es.enter_context(nc.psum_tensor("PST", [128, 1024], BF16))

        sem = es.enter_context(nc.semaphore("step"))
        block = es.enter_context(nc.Block())
        p = Prog(nc)

        es1 = ExitStack()
        es1.__enter__()
        selT_s = es1.enter_context(nc.sbuf_tensor("selT_s", [128, NCH * 64], F32))
        mu_rep = es1.enter_context(nc.sbuf_tensor("mu_rep", [128, T_], F32))
        rs_rep = es1.enter_context(nc.sbuf_tensor("rs_rep", [128, T_], F32))
        xch = es1.enter_context(nc.sbuf_tensor("xch", [128, T_], F32))
        sq = es1.enter_context(nc.sbuf_tensor("sq", [128, T_], F32))
        row_a = es1.enter_context(nc.sbuf_tensor("row_a", [1, T_], F32))
        row_b = es1.enter_context(nc.sbuf_tensor("row_b", [1, T_], F32))
        row_c = es1.enter_context(nc.sbuf_tensor("row_c", [1, 512], F32))
        row_d = es1.enter_context(nc.sbuf_tensor("row_d", [1, 512], F32))
        sels = es1.enter_context(nc.sbuf_tensor("sels", [64, T_], F32))
        seltm = es1.enter_context(nc.sbuf_tensor("seltm", [128, 64], F32))
        wk1 = es1.enter_context(nc.sbuf_tensor("wk1", [128, 64], F32))
        wk2 = es1.enter_context(nc.sbuf_tensor("wk2", [128, 64], F32))
        mcol = es1.enter_context(nc.sbuf_tensor("mcol", [128, 1], F32))

        p.dma("sync", id_f[:, :], idf[:, :])
        p.dma("gpsimd", id_bf[:, :], idf[:, :])
        p.dma("sync", ones_r[:, :], onesr[:, :])
        p.dma("sync", ones_c[:, :], onesc[:, :])
        for c in range(NCH):
            p.dma("sync", selT_s[:, c * 64:(c + 1) * 64],
                  selT[c * 128:(c + 1) * 128, :])

        for ts in range(4):
            for c in range(NCH):
                p.dma("sync", xch[:, 0:512],
                      xT[c * 128:(c + 1) * 128, ts * 512:(ts + 1) * 512])
                p.add("scalar", lambda e, c=c: e.activation(
                    sq[:, 0:512], xch[:, 0:512], AF.Square))
                p.add("tensor", lambda e, c=c: e.matmul(
                    PSC[0:1, :], ones_c[:, :], xch[:, 0:512],
                    start=(c == 0), stop=(c == NCH - 1)))
                p.add("tensor", lambda e, c=c: e.matmul(
                    PSB[0:1, 0:512], ones_c[:, :], sq[:, 0:512],
                    start=(c == 0), stop=(c == NCH - 1)))
            sl = slice(ts * 512, (ts + 1) * 512)
            p.add("vector", lambda e, sl=sl: e.tensor_scalar_mul(
                row_a[:, sl], PSC[0:1, :], 1.0 / 1024.0))
            p.add("vector", lambda e: e.tensor_scalar_mul(
                row_c[:, :], PSB[0:1, 0:512], 1.0 / 1024.0))
            p.add("vector", lambda e, sl=sl: e.tensor_tensor(
                row_d[:, :], row_a[:, sl], row_a[:, sl], ALU.mult))
            p.add("vector", lambda e: e.tensor_tensor(
                row_c[:, :], row_c[:, :], row_d[:, :], ALU.subtract))
            p.add("vector", lambda e: e.tensor_scalar_add(
                row_c[:, :], row_c[:, :], 1e-5))
            p.add("scalar", lambda e: e.activation(
                row_d[:, :], row_c[:, :], AF.Sqrt))
            p.add("vector", lambda e, sl=sl: e.reciprocal(row_b[:, sl], row_d[:, :]))
            p.add("tensor", lambda e, sl=sl: e.matmul(
                PSB[0:128, 0:512], ones_r[:, :], row_a[:, sl], start=True, stop=True))
            p.add("vector", lambda e, sl=sl: e.tensor_copy(
                mu_rep[:, sl], PSB[0:128, 0:512]))
            p.add("tensor", lambda e, sl=sl: e.matmul(
                PSC[0:128, :], ones_r[:, :], row_b[:, sl], start=True, stop=True))
            p.add("vector", lambda e, sl=sl: e.tensor_copy(
                rs_rep[:, sl], PSC[0:128, :]))

        for c in range(NCH):
            p.dma("sync", xch[:, :], xT[c * 128:(c + 1) * 128, :])
            p.add("vector", lambda e: e.tensor_tensor(
                sq[:, :], xch[:, :], mu_rep[:, :], ALU.subtract))
            p.add("vector", lambda e: e.tensor_tensor(
                xch[:, :], sq[:, :], rs_rep[:, :], ALU.mult))
            p.add("vector", lambda e, c=c: e.tensor_copy(
                xn_bf[:, c * T_:(c + 1) * T_], xch[:, :]))
            for ts in range(4):
                p.add("tensor", lambda e, c=c, ts=ts: e.matmul(
                    PSA[0:64, ts * 512:(ts + 1) * 512],
                    selT_s[:, c * 64:(c + 1) * 64],
                    xch[:, ts * 512:(ts + 1) * 512],
                    start=(c == 0), stop=(c == NCH - 1)))

        p.add("scalar", lambda e: e.activation(sels[:, :], PSA[0:64, :], AF.Sigmoid))
        for tt in range(16):
            p.add("tensor", lambda e, tt=tt: e.transpose(
                PSB[0:128, 0:64], sels[0:64, tt * 128:(tt + 1) * 128], id_f[0:64, 0:64]))
            p.add("vector", lambda e: e.tensor_copy(seltm[:, :], PSB[0:128, 0:64]))
            p.add("vector", lambda e: e.tensor_copy(wk1[:, :], seltm[:, :]))
            for r in range(4):
                src = wk1 if r % 2 == 0 else wk2
                dst = wk2 if r % 2 == 0 else wk1
                p.add("vector", lambda e, src=src: e.reduce_max(
                    mcol[:, :], src[:, :], AX.X))
                if r < 3:
                    mb = _ap(mcol, 0, [[1, 128], [0, 64]])
                    p.add("vector", lambda e, src=src, dst=dst, mb=mb: e.tensor_tensor(
                        dst[:, :], src[:, :], mb, ALU.is_equal))
                    p.add("vector", lambda e, src=src, dst=dst: e.scalar_tensor_tensor(
                        dst[:, :], dst[:, :], -1e30, src[:, :], ALU.mult, ALU.add))
            mb = _ap(mcol, 0, [[1, 128], [0, 64]])
            p.add("vector", lambda e, mb=mb: e.tensor_tensor(
                wk1[:, :], seltm[:, :], mb, ALU.is_ge))
            p.add("vector", lambda e, tt=tt: e.tensor_tensor(
                gates[:, tt * 64:(tt + 1) * 64], seltm[:, :], wk1[:, :], ALU.mult))
        if DEBUG_A:
            sels_o = nc.dram_tensor("sels_o", [64, T_], F32, kind="ExternalOutput")
            gates_o = nc.dram_tensor("gates_o", [128, 1024], F32, kind="ExternalOutput")
            p.dma("sync", sels_o[:, :], sels[:, :])
            p.dma("sync", gates_o[:, :], gates[:, :])
        es1.__exit__(None, None, None)

        es2 = ExitStack()
        with es2:
            W_sb = es2.enter_context(nc.sbuf_tensor("W_sb", [128, 3 * NCH * 1024], BF16))
            qT_sb = es2.enter_context(nc.sbuf_tensor("qT_sb", [128, NCH * T_], BF16))
            kT_sb = es2.enter_context(nc.sbuf_tensor("kT_sb", [128, NCH * T_], BF16))
            qf = es2.enter_context(nc.sbuf_tensor("qf", [128, 1024], F32))
            qr = es2.enter_context(nc.sbuf_tensor("qr", [128, 1024], F32))
            qr_bf = es2.enter_context(nc.sbuf_tensor("qr_bf", [128, 1024], BF16))
            t1s = es2.enter_context(nc.sbuf_tensor("t1s", [128, 512], F32))
            t2s = es2.enter_context(nc.sbuf_tensor("t2s", [128, 512], F32))
            ct = es2.enter_context(nc.sbuf_tensor("ct", [128, 32], F32))
            st = es2.enter_context(nc.sbuf_tensor("st", [128, 32], F32))
            vstage = es2.enter_context(nc.sbuf_tensor("vstage", [128, 1024], BF16))

            for w_i, wsrc in enumerate((wqT, wkT, wvT)):
                for c in range(NCH):
                    p.dma("gpsimd", W_sb[:, (w_i * NCH + c) * 1024:(w_i * NCH + c + 1) * 1024],
                          wsrc[c * 128:(c + 1) * 128, :])

            cosap = _ap(ct, 0, [[32, 128], [0, 16], [1, 32]])
            sinap = _ap(st, 0, [[32, 128], [0, 16], [1, 32]])

            def rope(p, src, dst):
                ev = lambda t, o=0: _ap(t, o, [[1024, 128], [64, 16], [2, 32]])
                p.add("vector", lambda e: e.tensor_tensor(t1s[:, :], ev(src), cosap, ALU.mult))
                p.add("vector", lambda e: e.tensor_tensor(t2s[:, :], ev(src, 1), sinap, ALU.mult))
                p.add("vector", lambda e: e.tensor_tensor(ev(dst), t1s[:, :], t2s[:, :], ALU.subtract))
                p.add("vector", lambda e: e.tensor_tensor(t1s[:, :], ev(src), sinap, ALU.mult))
                p.add("vector", lambda e: e.tensor_tensor(t2s[:, :], ev(src, 1), cosap, ALU.mult))
                p.add("vector", lambda e: e.tensor_tensor(ev(dst, 1), t1s[:, :], t2s[:, :], ALU.add))

            for tt in range(16):
                tsl = slice(tt * 128, (tt + 1) * 128)
                p.dma("sync", ct[:, :], cosL[tsl, :])
                p.dma("sync", st[:, :], sinL[tsl, :])
                for w_i, dstT in ((0, qT_sb), (1, kT_sb)):
                    for os in range(2):
                        for c in range(NCH):
                            p.add("tensor", lambda e, w_i=w_i, os=os, c=c, tt=tt: e.matmul(
                                PSA[0:128, w_i * 1024 + os * 512: w_i * 1024 + (os + 1) * 512],
                                xn_bf[:, c * T_ + tt * 128: c * T_ + (tt + 1) * 128],
                                W_sb[:, (w_i * NCH + c) * 1024 + os * 512:
                                        (w_i * NCH + c) * 1024 + (os + 1) * 512],
                                start=(c == 0), stop=(c == NCH - 1)))
                    p.add("scalar", lambda e, w_i=w_i: e.activation(
                        qf[:, :], PSA[0:128, w_i * 1024:(w_i + 1) * 1024], AF.Copy))
                    rope(p, qf, qr)
                    p.add("vector", lambda e: e.tensor_copy(qr_bf[:, :], qr[:, :]))
                    for c8 in range(NCH):
                        p.add("tensor", lambda e, c8=c8: e.transpose(
                            PST[:, c8 * 128:(c8 + 1) * 128],
                            qr_bf[:, c8 * 128:(c8 + 1) * 128], id_bf[:, :]))
                    for c8 in range(NCH):
                        p.add("vector", lambda e, c8=c8, tt=tt, dstT=dstT: e.tensor_copy(
                            dstT[:, c8 * T_ + tt * 128: c8 * T_ + (tt + 1) * 128],
                            PST[:, c8 * 128:(c8 + 1) * 128]))
                for os in range(2):
                    for c in range(NCH):
                        p.add("tensor", lambda e, os=os, c=c, tt=tt: e.matmul(
                            PSB[0:128, os * 512:(os + 1) * 512],
                            xn_bf[:, c * T_ + tt * 128: c * T_ + (tt + 1) * 128],
                            W_sb[:, (2 * NCH + c) * 1024 + os * 512:
                                    (2 * NCH + c) * 1024 + (os + 1) * 512],
                            start=(c == 0), stop=(c == NCH - 1)))
                p.add("vector", lambda e: e.tensor_copy(vstage[:, :], PSB[0:128, :]))
                p.dma("sync", v_o[tsl, :], vstage[:, :])
            p.dma("sync",
                  _ap(qT_o, 0, [[T_, 128], [128 * T_, NCH], [1, T_]]),
                  _ap(qT_sb, 0, [[NCH * T_, 128], [T_, NCH], [1, T_]]))
            p.dma("sync",
                  _ap(kT_o, 0, [[T_, 128], [128 * T_, NCH], [1, T_]]),
                  _ap(kT_sb, 0, [[NCH * T_, 128], [T_, NCH], [1, T_]]))

        es3 = ExitStack()
        with es3:
            keys4 = es3.enter_context(nc.sbuf_tensor("keys4", [128, NCH * 512], BF16))
            hg_bf = es3.enter_context(nc.sbuf_tensor("hg_bf", [128, 512], BF16))
            for kg in range(16):
                for c in range(NCH):
                    p.dma("gpsimd", keys4[:, c * 512:(c + 1) * 512],
                          _ap(keys, (4 * kg) * E_ * 128 + c * 128 * 128,
                              [[128, 128], [E_ * 128, 4], [1, 128]]))
                for tt in range(16):
                    for c in range(NCH):
                        p.add("tensor", lambda e, c=c, tt=tt: e.matmul(
                            PSC[:, :],
                            xn_bf[:, c * T_ + tt * 128: c * T_ + (tt + 1) * 128],
                            keys4[:, c * 512:(c + 1) * 512],
                            start=(c == 0), stop=(c == NCH - 1)))
                    gap = _ap(gates, tt * 64 + kg * 4, [[16 * 64, 128], [1, 4], [0, 128]])
                    p.add("vector", lambda e, gap=gap: e.scalar_tensor_tensor(
                        hg_bf[:, :], PSC[:, :], 0.0, gap, ALU.max, ALU.mult))
                    p.dma("sync", hg_d[tt * 128:(tt + 1) * 128, kg * 512:(kg + 1) * 512],
                          hg_bf[:, :])

        es4 = ExitStack()
        with es4:
            vals = es4.enter_context(nc.sbuf_tensor("vals", [128, 64 * 1024], BF16))
            hgrow = es4.enter_context(nc.sbuf_tensor("hgrow", [128, 64 * 128], BF16))
            gsT = es4.enter_context(nc.sbuf_tensor("gsT", [128, 512], BF16))
            gstage = es4.enter_context(nc.sbuf_tensor("gstage", [128, 1024], F32))
            for k in range(64):
                p.dma("gpsimd", vals[:, k * 1024:(k + 1) * 1024],
                      _ap(values, k * 128 * E_, [[E_, 128], [1, E_]]))
            for tt in range(16):
                p.dma("sync", hgrow[:, :], hg_d[tt * 128:(tt + 1) * 128, :])
                for kg in range(16):
                    for k4 in range(4):
                        p.add("tensor", lambda e, kg=kg, k4=k4: e.transpose(
                            PST[:, k4 * 128:(k4 + 1) * 128],
                            hgrow[:, kg * 512 + k4 * 128: kg * 512 + (k4 + 1) * 128],
                            id_bf[:, :]))
                    p.add("vector", lambda e: e.tensor_copy(gsT[:, :], PST[:, 0:512]))
                    for k4 in range(4):
                        k = kg * 4 + k4
                        for vc in range(NCH):
                            first = (kg == 0 and k4 == 0 and vc in (0, 4))
                            last = (kg == 15 and k4 == 3 and vc in (3, 7))
                            p.add("tensor", lambda e, k=k, k4=k4, vc=vc, first=first, last=last: e.matmul(
                                PSA[0:128, vc * 128:(vc + 1) * 128],
                                vals[:, k * 1024 + vc * 128: k * 1024 + (vc + 1) * 128],
                                gsT[:, k4 * 128:(k4 + 1) * 128],
                                start=first, stop=last, skip_group_check=True))
                p.add("scalar", lambda e: e.activation(
                    gstage[:, 0:512], PSA[0:128, 0:512], AF.Silu))
                p.add("scalar", lambda e: e.activation(
                    gstage[:, 512:1024], PSA[0:128, 512:1024], AF.Silu))
                p.dma("sync",
                      _ap(g_o, tt * 128, [[T_, 128], [128 * T_, NCH], [1, 128]]),
                      _ap(gstage, 0, [[1024, 128], [128, NCH], [1, 128]]))

        p.emit(block, [sem])
    return nc


def build_B():
    nc = bass.Bass("TRN2", target_bir_lowering=False)
    qT = nc.dram_tensor("qT", [E_, T_], BF16, kind="ExternalInput")
    kTf = nc.dram_tensor("kTf", [E_, S_], BF16, kind="ExternalInput")
    vgf = nc.dram_tensor("vgf", [4 * S_, 260], BF16, kind="ExternalInput")
    xT = nc.dram_tensor("xT", [E_, T_], F32, kind="ExternalInput")
    g0T = nc.dram_tensor("g0T", [E_, T_], F32, kind="ExternalInput")
    woT = nc.dram_tensor("woT", [E_, E_], F32, kind="ExternalInput")
    wq1T = nc.dram_tensor("wq1T", [E_, E_], F32, kind="ExternalInput")
    wk1T = nc.dram_tensor("wk1T", [E_, E_], F32, kind="ExternalInput")
    wv1T = nc.dram_tensor("wv1T", [E_, E_], F32, kind="ExternalInput")
    kern1 = nc.dram_tensor("kern1", [E_, E_], F32, kind="ExternalInput")
    bias1 = nc.dram_tensor("bias1", [E_, 1], F32, kind="ExternalInput")
    a1 = nc.dram_tensor("a1", [E_, 16], F32, kind="ExternalInput")
    b1 = nc.dram_tensor("b1", [16, E_], F32, kind="ExternalInput")
    gam = nc.dram_tensor("gam", [16, 1], F32, kind="ExternalInput")
    selq = nc.dram_tensor("selq", [16, NCH * 128], F32, kind="ExternalInput")
    onesr = nc.dram_tensor("onesr", [1, 128], F32, kind="ExternalInput")
    onesc = nc.dram_tensor("onesc", [128, 1], F32, kind="ExternalInput")

    x1T_o = nc.dram_tensor("x1T_o", [E_, T_], F32, kind="ExternalOutput")
    qt1_o = nc.dram_tensor("qt1_o", [E_, T_], BF16, kind="ExternalOutput")
    kt1_o = nc.dram_tensor("kt1_o", [E_, T_], BF16, kind="ExternalOutput")
    v1_o = nc.dram_tensor("v1_o", [T_, E_], BF16, kind="ExternalOutput")
    g1_o = nc.dram_tensor("g1_o", [E_, T_], F32, kind="ExternalOutput")
    am_d = nc.dram_tensor("am_d", [128, 1], F32)

    from contextlib import ExitStack
    es = ExitStack()
    with es:
        ones_r = es.enter_context(nc.sbuf_tensor("ones_r", [1, 128], F32))
        ones_c = es.enter_context(nc.sbuf_tensor("ones_c", [128, 1], F32))
        esY = ExitStack()
        esY.__enter__()
        Yn = esY.enter_context(nc.sbuf_tensor("Yn", [128, NCH * T_], BF16))
        PSA = es.enter_context(nc.psum_tensor("PSA", [128, 2048], F32))
        PSB = es.enter_context(nc.psum_tensor("PSB", [128, 1024], F32))
        PSC = es.enter_context(nc.psum_tensor("PSC", [128, 512], F32))
        sem = es.enter_context(nc.semaphore("step"))
        block = es.enter_context(nc.Block())
        p = Prog(nc)
        p.dma("sync", ones_r[:, :], onesr[:, :])
        p.dma("sync", ones_c[:, :], onesc[:, :])

        es1 = ExitStack()
        es1.__enter__()
        Qs = es1.enter_context(nc.sbuf_tensor("Qs", [128, NCH * T_], BF16, side="right"))
        KA = es1.enter_context(nc.sbuf_tensor("KA", [128, S_], BF16, side="right"))
        KB = es1.enter_context(nc.sbuf_tensor("KB", [128, S_], BF16, side="right"))
        Vg = es1.enter_context(nc.sbuf_tensor("Vg", [128, 64 * 260], BF16, side="right"))
        Ebf = es1.enter_context(nc.sbuf_tensor("Ebf", [128, 512], BF16, side="right"))
        rrow = es1.enter_context(nc.sbuf_tensor("rrow", [1, 512], F32, side="right"))
        Rs = es1.enter_context(nc.sbuf_tensor("Rs", [64, 512], F32, side="right"))
        for c in range(NCH):
            p.dma("sync", Qs[:, c * T_:(c + 1) * T_], qT[c * 128:(c + 1) * 128, :])
        for g in range(4):
            p.dma("sync", KA[:, :], kTf[g * 256:g * 256 + 128, :])
            p.dma("sync", KB[:, :], kTf[g * 256 + 128:g * 256 + 256, :])
            p.dma("sync", Vg[:, :],
                  _ap(vgf, g * S_ * 260, [[260, 128], [128 * 260, 64], [1, 260]]))
            for hl in range(4):
                h = 4 * g + hl
                K_t = KA if hl < 2 else KB
                kr = 64 * (hl % 2)
                qr0 = 64 * (h % 2)
                qc0 = (h // 2) * T_
                for ts in range(4):
                    for sb in range(64):
                        p.add("tensor", lambda e, K_t=K_t, kr=kr, sb=sb, qr0=qr0, qc0=qc0, ts=ts: e.matmul(
                            PSA[0:128, 0:512],
                            K_t[kr:kr + 64, sb * 128:(sb + 1) * 128],
                            Qs[qr0:qr0 + 64, qc0 + ts * 512: qc0 + (ts + 1) * 512],
                            start=True, stop=True))
                        p.add("scalar", lambda e: e.activation(
                            Ebf[:, :], PSA[0:128, 0:512], AF.Exp, scale=0.125))
                        p.add("tensor", lambda e, sb=sb, hl=hl: e.matmul(
                            PSB[0:65, 0:512],
                            Vg[:, sb * 260 + hl * 65: sb * 260 + (hl + 1) * 65],
                            Ebf[:, :], start=(sb == 0), stop=(sb == 63)))
                    p.add("vector", lambda e: e.reciprocal(rrow[:, :], PSB[64:65, 0:512]))
                    p.add("tensor", lambda e: e.matmul(
                        PSC[0:64, :], ones_r[0:1, 0:64], rrow[:, :], start=True, stop=True))
                    p.add("scalar", lambda e: e.activation(Rs[:, :], PSC[0:64, :], AF.Copy))
                    p.add("vector", lambda e, qr0=qr0, qc0=qc0, ts=ts: e.tensor_tensor(
                        Yn[qr0:qr0 + 64, qc0 + ts * 512: qc0 + (ts + 1) * 512],
                        PSB[0:64, 0:512], Rs[:, :], ALU.mult))
        es1.__exit__(None, None, None)

        es2 = ExitStack()
        es2.__enter__()
        X1 = es2.enter_context(nc.sbuf_tensor("X1", [128, NCH * T_], F32, side="right"))
        wo_bf = es2.enter_context(nc.sbuf_tensor("wo_bf", [128, NCH * 1024], BF16, side="right"))
        xsl = es2.enter_context(nc.sbuf_tensor("xsl", [128, 512], F32, side="right"))
        gsl = es2.enter_context(nc.sbuf_tensor("gsl", [128, 512], F32, side="right"))
        tsl = es2.enter_context(nc.sbuf_tensor("tsl", [128, 512], F32, side="right"))
        for c in range(NCH):
            p.dma("gpsimd", wo_bf[:, c * 1024:(c + 1) * 1024],
                  woT[c * 128:(c + 1) * 128, :])
        for oc in range(NCH):
            for ts in range(4):
                for c in range(NCH):
                    p.add("tensor", lambda e, oc=oc, ts=ts, c=c: e.matmul(
                        PSA[0:128, 0:512],
                        wo_bf[:, c * 1024 + oc * 128: c * 1024 + oc * 128 + 128],
                        Yn[:, c * T_ + ts * 512: c * T_ + (ts + 1) * 512],
                        start=(c == 0), stop=(c == NCH - 1)))
                p.dma("sync", xsl[:, :], xT[oc * 128:(oc + 1) * 128, ts * 512:(ts + 1) * 512])
                p.dma("sync", gsl[:, :], g0T[oc * 128:(oc + 1) * 128, ts * 512:(ts + 1) * 512])
                p.add("vector", lambda e: e.tensor_tensor(
                    tsl[:, :], PSA[0:128, 0:512], gsl[:, :], ALU.mult))
                sl = lambda t, oc=oc, ts=ts: t[:, oc * T_ + ts * 512: oc * T_ + (ts + 1) * 512]
                p.add("vector", lambda e, oc=oc, ts=ts: e.tensor_tensor(
                    X1[:, oc * T_ + ts * 512: oc * T_ + (ts + 1) * 512],
                    tsl[:, :], xsl[:, :], ALU.add))
                p.dma("sync", x1T_o[oc * 128:(oc + 1) * 128, ts * 512:(ts + 1) * 512],
                      X1[:, oc * T_ + ts * 512: oc * T_ + (ts + 1) * 512])
        esY.__exit__(None, None, None)

        xn1 = es.enter_context(nc.sbuf_tensor("xn1", [128, NCH * T_], BF16))
        es3 = ExitStack()
        es3.__enter__()
        mu1 = es3.enter_context(nc.sbuf_tensor("mu1", [128, T_], F32, side="right"))
        rs1 = es3.enter_context(nc.sbuf_tensor("rs1", [128, T_], F32, side="right"))
        sqs = es3.enter_context(nc.sbuf_tensor("sqs", [128, 512], F32, side="right"))
        row_a = es3.enter_context(nc.sbuf_tensor("row_a", [1, T_], F32, side="right"))
        row_b = es3.enter_context(nc.sbuf_tensor("row_b", [1, T_], F32, side="right"))
        row_c = es3.enter_context(nc.sbuf_tensor("row_c", [1, 512], F32, side="right"))
        row_d = es3.enter_context(nc.sbuf_tensor("row_d", [1, 512], F32, side="right"))
        scr_a = es3.enter_context(nc.sbuf_tensor("scr_a", [128, T_], F32, side="right"))
        scr_b = es3.enter_context(nc.sbuf_tensor("scr_b", [128, T_], F32, side="right"))
        for ts in range(4):
            for c in range(NCH):
                csl = lambda t, c=c, ts=ts: t[:, c * T_ + ts * 512: c * T_ + (ts + 1) * 512]
                p.add("scalar", lambda e, c=c, ts=ts: e.activation(
                    sqs[:, :], X1[:, c * T_ + ts * 512: c * T_ + (ts + 1) * 512], AF.Square))
                p.add("tensor", lambda e, c=c, ts=ts: e.matmul(
                    PSC[0:1, :], ones_c[:, :],
                    X1[:, c * T_ + ts * 512: c * T_ + (ts + 1) * 512],
                    start=(c == 0), stop=(c == NCH - 1)))
                p.add("tensor", lambda e, c=c: e.matmul(
                    PSB[0:1, 0:512], ones_c[:, :], sqs[:, :],
                    start=(c == 0), stop=(c == NCH - 1)))
            sl = slice(ts * 512, (ts + 1) * 512)
            p.add("vector", lambda e, sl=sl: e.tensor_scalar_mul(
                row_a[:, sl], PSC[0:1, :], 1.0 / 1024.0))
            p.add("vector", lambda e: e.tensor_scalar_mul(
                row_c[:, :], PSB[0:1, 0:512], 1.0 / 1024.0))
            p.add("vector", lambda e, sl=sl: e.tensor_tensor(
                row_d[:, :], row_a[:, sl], row_a[:, sl], ALU.mult))
            p.add("vector", lambda e: e.tensor_tensor(
                row_c[:, :], row_c[:, :], row_d[:, :], ALU.subtract))
            p.add("vector", lambda e: e.tensor_scalar_add(row_c[:, :], row_c[:, :], 1e-5))
            p.add("scalar", lambda e: e.activation(row_d[:, :], row_c[:, :], AF.Sqrt))
            p.add("vector", lambda e, sl=sl: e.reciprocal(row_b[:, sl], row_d[:, :]))
            p.add("tensor", lambda e, sl=sl: e.matmul(
                PSB[0:128, 0:512], ones_r[:, :], row_a[:, sl], start=True, stop=True))
            p.add("vector", lambda e, sl=sl: e.tensor_copy(mu1[:, sl], PSB[0:128, 0:512]))
            p.add("tensor", lambda e, sl=sl: e.matmul(
                PSC[0:128, :], ones_r[:, :], row_b[:, sl], start=True, stop=True))
            p.add("vector", lambda e, sl=sl: e.tensor_copy(rs1[:, sl], PSC[0:128, :]))
        for c in range(NCH):
            p.add("vector", lambda e, c=c: e.tensor_tensor(
                scr_a[:, :], X1[:, c * T_:(c + 1) * T_], mu1[:, :], ALU.subtract))
            p.add("vector", lambda e: e.tensor_tensor(
                scr_b[:, :], scr_a[:, :], rs1[:, :], ALU.mult))
            p.add("vector", lambda e, c=c: e.tensor_copy(
                xn1[:, c * T_:(c + 1) * T_], scr_b[:, :]))
        es3.__exit__(None, None, None)
        es2.__exit__(None, None, None)

        es4 = ExitStack()
        es4.__enter__()
        W1 = es4.enter_context(nc.sbuf_tensor("W1", [128, 3 * NCH * 1024], BF16, side="right"))
        U512 = es4.enter_context(nc.sbuf_tensor("U512", [128, T_], F32, side="right"))
        U128 = es4.enter_context(nc.sbuf_tensor("U128", [128, T_], F32, side="right"))
        LG = es4.enter_context(nc.sbuf_tensor("LG", [128, NCH], F32, side="right"))
        NLG = es4.enter_context(nc.sbuf_tensor("NLG", [128, NCH], F32, side="right"))
        lnGr = es4.enter_context(nc.sbuf_tensor("lnGr", [16, 1], F32, side="right"))
        lnG = es4.enter_context(nc.sbuf_tensor("lnG", [16, 1], F32, side="right"))
        selq_s = es4.enter_context(nc.sbuf_tensor("selq_s", [16, NCH * 128], F32, side="right"))
        Pq = es4.enter_context(nc.sbuf_tensor("Pq", [128, T_], F32, side="right"))
        Pk = es4.enter_context(nc.sbuf_tensor("Pk", [128, T_], F32, side="right"))
        scp = es4.enter_context(nc.sbuf_tensor("scp", [128, T_], F32, side="right"))
        qst = es4.enter_context(nc.sbuf_tensor("qst", [128, T_], BF16, side="right"))
        v1st = es4.enter_context(nc.sbuf_tensor("v1st", [128, 1024], BF16, side="right"))
        for w_i, wsrc in enumerate((wq1T, wk1T, wv1T)):
            for c in range(NCH):
                p.dma("gpsimd", W1[:, (w_i * NCH + c) * 1024:(w_i * NCH + c + 1) * 1024],
                      wsrc[c * 128:(c + 1) * 128, :])
        p.dma("sync", selq_s[:, :], selq[:, :])
        p.dma("sync", lnGr[:, :], gam[:, :])
        p.add("scalar", lambda e: e.activation(lnG[:, :], lnGr[:, :], AF.Ln))
        for c in range(NCH):
            p.add("tensor", lambda e, c=c: e.matmul(
                PSC[0:128, 0:1], selq_s[:, c * 128:(c + 1) * 128], lnG[:, :],
                start=True, stop=True))
            p.add("vector", lambda e, c=c: e.tensor_copy(LG[:, c:c + 1], PSC[0:128, 0:1]))
        p.add("vector", lambda e: e.tensor_scalar_mul(NLG[:, :], LG[:, :], -1.0))
        p.add("gpsimd", lambda e: e.iota(U512[:, :], [[0, 4], [1, 512]], base=0,
                                         channel_multiplier=0,
                                         allow_small_or_imprecise_dtypes=True))
        p.add("gpsimd", lambda e: e.iota(U128[:, :], [[0, 16], [1, 128]], base=0,
                                         channel_multiplier=0,
                                         allow_small_or_imprecise_dtypes=True))
        LN8 = float(np.log(0.125))
        for oc in range(NCH):
            lgb = _ap(LG, oc, [[NCH, 128], [0, T_]])
            nlgb = _ap(NLG, oc, [[NCH, 128], [0, T_]])
            p.add("vector", lambda e, lgb=lgb: e.tensor_tensor(scp[:, :], U512[:, :], lgb, ALU.mult))
            p.add("scalar", lambda e: e.activation(Pq[:, :], scp[:, :], AF.Exp, bias=np.float32(LN8)))
            p.add("vector", lambda e, nlgb=nlgb: e.tensor_tensor(scp[:, :], U128[:, :], nlgb, ALU.mult))
            p.add("scalar", lambda e: e.activation(Pk[:, :], scp[:, :], AF.Exp))
            for w_i, dst, P in ((0, qt1_o, Pq), (1, kt1_o, Pk)):
                for ts in range(4):
                    for c in range(NCH):
                        p.add("tensor", lambda e, w_i=w_i, oc=oc, c=c, ts=ts: e.matmul(
                            PSA[0:128, 0:512],
                            W1[:, (w_i * NCH + c) * 1024 + oc * 128:
                                  (w_i * NCH + c) * 1024 + oc * 128 + 128],
                            xn1[:, c * T_ + ts * 512: c * T_ + (ts + 1) * 512],
                            start=(c == 0), stop=(c == NCH - 1)))
                    p.add("vector", lambda e, ts=ts, P=P: e.tensor_tensor(
                        qst[:, ts * 512:(ts + 1) * 512], PSA[0:128, 0:512],
                        P[:, ts * 512:(ts + 1) * 512], ALU.mult))
                p.dma("sync", dst[oc * 128:(oc + 1) * 128, :], qst[:, :])
        for tt in range(16):
            for os in range(2):
                for c in range(NCH):
                    p.add("tensor", lambda e, os=os, c=c, tt=tt: e.matmul(
                        PSB[0:128, os * 512:(os + 1) * 512],
                        xn1[:, c * T_ + tt * 128: c * T_ + (tt + 1) * 128],
                        W1[:, (2 * NCH + c) * 1024 + os * 512:
                              (2 * NCH + c) * 1024 + (os + 1) * 512],
                        start=(c == 0), stop=(c == NCH - 1)))
            p.add("vector", lambda e: e.tensor_copy(v1st[:, :], PSB[0:128, :]))
            p.dma("sync", v1_o[tt * 128:(tt + 1) * 128, :], v1st[:, :])
        es4.__exit__(None, None, None)

        es5 = ExitStack()
        es5.__enter__()
        wq1 = es5.enter_context(nc.sbuf_tensor("wq1", [128, NCH * 1024], BF16, side="right"))
        kch = es5.enter_context(nc.sbuf_tensor("kch", [128, 1024], F32, side="right"))
        tq = es5.enter_context(nc.sbuf_tensor("tq", [128, 1024], F32, side="right"))
        am8 = es5.enter_context(nc.sbuf_tensor("am8", [128, NCH], F32, side="right"))
        amc = es5.enter_context(nc.sbuf_tensor("amc", [128, 1], F32, side="right"))
        amr = es5.enter_context(nc.sbuf_tensor("amr", [1, 128], F32, side="right"))
        am = es5.enter_context(nc.sbuf_tensor("am", [1, 1], F32, side="right"))
        srow = es5.enter_context(nc.sbuf_tensor("srow", [1, 1], F32, side="right"))
        rsrow = es5.enter_context(nc.sbuf_tensor("rsrow", [1, 1], F32, side="right"))
        scol = es5.enter_context(nc.sbuf_tensor("scol", [128, 1], F32, side="right"))
        rscol = es5.enter_context(nc.sbuf_tensor("rscol", [128, 1], F32, side="right"))
        a_bf = es5.enter_context(nc.sbuf_tensor("a_bf", [128, NCH * 16], BF16, side="right"))
        b_bf = es5.enter_context(nc.sbuf_tensor("b_bf", [16, 1024], BF16, side="right"))
        bias_s = es5.enter_context(nc.sbuf_tensor("bias_s", [128, NCH], F32, side="right"))
        u_bf = es5.enter_context(nc.sbuf_tensor("u_bf", [16, T_], BF16, side="right"))
        g1st = es5.enter_context(nc.sbuf_tensor("g1st", [128, 512], F32, side="right"))
        for c in range(NCH):
            p.dma("sync", kch[:, :], kern1[c * 128:(c + 1) * 128, :])
            p.add("vector", lambda e, c=c: e.reduce_max(
                am8[:, c:c + 1], kch[:, :], AX.X, apply_absolute_value=True))
        p.add("vector", lambda e: e.reduce_max(amc[:, :], am8[:, :], AX.X))
        p.dma("sync", am_d[:, :], amc[:, :])
        p.dma("sync", amr[:, :], _ap(am_d, 0, [[128, 1], [1, 128]]))
        p.add("vector", lambda e: e.reduce_max(am[:, :], amr[:, :], AX.X))
        p.add("vector", lambda e: e.tensor_scalar_mul(srow[:, :], am[:, :], 1.0 / 7.0))
        p.add("vector", lambda e: e.reciprocal(rsrow[:, :], srow[:, :]))
        p.add("tensor", lambda e: e.matmul(PSC[0:128, 0:1], ones_r[:, :], srow[:, :],
                                           start=True, stop=True))
        p.add("vector", lambda e: e.tensor_copy(scol[:, :], PSC[0:128, 0:1]))
        p.add("tensor", lambda e: e.matmul(PSC[0:128, 0:1], ones_r[:, :], rsrow[:, :],
                                           start=True, stop=True))
        p.add("vector", lambda e: e.tensor_copy(rscol[:, :], PSC[0:128, 0:1]))
        for c in range(NCH):
            p.dma("sync", kch[:, :], kern1[c * 128:(c + 1) * 128, :])
            p.add("vector", lambda e: e.tensor_scalar(
                tq[:, :], kch[:, :], rscol[:, 0:1], MAGIC, ALU.mult, ALU.add))
            p.add("vector", lambda e: e.tensor_scalar(
                kch[:, :], tq[:, :], MAGIC, -8.0, ALU.subtract, ALU.max))
            p.add("vector", lambda e, c=c: e.tensor_scalar(
                wq1[:, c * 1024:(c + 1) * 1024], kch[:, :], 7.0, scol[:, 0:1],
                ALU.min, ALU.mult))
        for c in range(NCH):
            p.dma("gpsimd", a_bf[:, c * 16:(c + 1) * 16], a1[c * 128:(c + 1) * 128, :])
            p.dma("sync", bias_s[:, c:c + 1], bias1[c * 128:(c + 1) * 128, :])
        p.dma("gpsimd", b_bf[:, :], b1[:, :])
        for ts in range(4):
            for c in range(NCH):
                p.add("tensor", lambda e, c=c, ts=ts: e.matmul(
                    PSC[0:16, :], a_bf[:, c * 16:(c + 1) * 16],
                    xn1[:, c * T_ + ts * 512: c * T_ + (ts + 1) * 512],
                    start=(c == 0), stop=(c == NCH - 1)))
            p.add("vector", lambda e, ts=ts: e.tensor_copy(
                u_bf[:, ts * 512:(ts + 1) * 512], PSC[0:16, :]))
        for fc in range(NCH):
            for ts in range(4):
                for c in range(NCH):
                    p.add("tensor", lambda e, fc=fc, ts=ts, c=c: e.matmul(
                        PSA[0:128, 0:512],
                        wq1[:, c * 1024 + fc * 128: c * 1024 + fc * 128 + 128],
                        xn1[:, c * T_ + ts * 512: c * T_ + (ts + 1) * 512],
                        start=(c == 0), stop=False))
                p.add("tensor", lambda e, fc=fc, ts=ts: e.matmul(
                    PSA[0:128, 0:512], b_bf[:, fc * 128:(fc + 1) * 128],
                    u_bf[:, ts * 512:(ts + 1) * 512], start=False, stop=True))
                p.add("scalar", lambda e, fc=fc: e.activation(
                    g1st[:, :], PSA[0:128, 0:512], AF.Silu, bias=bias_s[:, fc:fc + 1]))
                p.dma("sync", g1_o[fc * 128:(fc + 1) * 128, ts * 512:(ts + 1) * 512],
                      g1st[:, :])
        es5.__exit__(None, None, None)
        p.emit(block, [sem])
    return nc


def build_D():
    nc = bass.Bass("TRN2", target_bir_lowering=False)
    qtg = nc.dram_tensor("qtg", [256, S_], BF16, kind="ExternalInput")
    ktg = nc.dram_tensor("ktg", [256, S_], BF16, kind="ExternalInput")
    vg1 = nc.dram_tensor("vg1", [S_, 256], BF16, kind="ExternalInput")
    gamh = nc.dram_tensor("gamh", [4, 1], F32, kind="ExternalInput")
    mgrid = nc.dram_tensor("mgrid", [1, 4 * 67], F32, kind="ExternalInput")
    onesr = nc.dram_tensor("onesr", [1, 128], F32, kind="ExternalInput")
    y1T_o = nc.dram_tensor("y1T_o", [256, S_], BF16, kind="ExternalOutput")

    from contextlib import ExitStack
    es = ExitStack()
    with es:
        qtA = es.enter_context(nc.sbuf_tensor("qtA", [128, S_], BF16))
        qtB = es.enter_context(nc.sbuf_tensor("qtB", [128, S_], BF16))
        ktA = es.enter_context(nc.sbuf_tensor("ktA", [128, S_], BF16))
        ktB = es.enter_context(nc.sbuf_tensor("ktB", [128, S_], BF16))
        Vs = es.enter_context(nc.sbuf_tensor("Vs", [128, 64 * 256], BF16))
        y1s = es.enter_context(nc.sbuf_tensor("y1s", [128, 2 * S_], BF16))
        ones_r = es.enter_context(nc.sbuf_tensor("ones_r", [1, 128], F32))
        mg = es.enter_context(nc.sbuf_tensor("mg", [1, 4 * 67], F32))
        gr = es.enter_context(nc.sbuf_tensor("gr", [1, 4], F32))
        lnr = es.enter_context(nc.sbuf_tensor("lnr", [1, 4], F32))
        erow = es.enter_context(nc.sbuf_tensor("erow", [1, 4 * 67], F32))
        crow = es.enter_context(nc.sbuf_tensor("crow", [1, 4 * 67], F32))
        ctab = es.enter_context(nc.sbuf_tensor("ctab", [128, 4 * 67], F32))
        UD = es.enter_context(nc.sbuf_tensor("UD", [128, 512], F32))
        M01 = es.enter_context(nc.sbuf_tensor("M01", [128, 512], F32))
        Wbf = es.enter_context(nc.sbuf_tensor("Wbf", [128, 512], BF16))
        PSA = es.enter_context(nc.psum_tensor("PSA", [128, 512], F32))
        PSB = es.enter_context(nc.psum_tensor("PSB", [128, 512], F32))
        sem = es.enter_context(nc.semaphore("step"))
        block = es.enter_context(nc.Block())
        p = Prog(nc)

        p.dma("sync", ones_r[:, :], onesr[:, :])
        p.dma("sync", mg[:, :], mgrid[:, :])
        p.dma("sync", gr[:, :], _ap(gamh, 0, [[4, 1], [1, 4]]))
        p.dma("sync", qtA[:, :], qtg[0:128, :])
        p.dma("sync", qtB[:, :], qtg[128:256, :])
        p.dma("sync", ktA[:, :], ktg[0:128, :])
        p.dma("sync", ktB[:, :], ktg[128:256, :])
        p.dma("sync", Vs[:, :], _ap(vg1, 0, [[256, 128], [128 * 256, 64], [1, 256]]))
        p.add("scalar", lambda e: e.activation(lnr[:, :], gr[:, :], AF.Ln))
        lnb = _ap(lnr, 0, [[4, 1], [1, 4], [0, 67]])
        p.add("vector", lambda e, lnb=lnb: e.tensor_tensor(erow[:, :], mg[:, :], lnb, ALU.mult))
        p.add("scalar", lambda e: e.activation(crow[:, :], erow[:, :], AF.Exp))
        p.add("tensor", lambda e: e.matmul(PSB[0:128, 0:268], ones_r[:, :], crow[:, :],
                                           start=True, stop=True))
        p.add("vector", lambda e: e.tensor_copy(ctab[:, :], PSB[0:128, 0:268]))
        p.add("gpsimd", lambda e: e.iota(UD[:, :], [[1, 512]], base=0,
                                         channel_multiplier=-1,
                                         allow_small_or_imprecise_dtypes=True))
        p.add("vector", lambda e: e.tensor_single_scalar(M01[:, :], UD[:, :], 0.0, ALU.is_ge))

        for hl in range(4):
            kt = ktA if hl < 2 else ktB
            qt = qtA if hl < 2 else qtB
            r0 = 64 * (hl % 2)
            for ts in range(16):
                nsb = 4 * ts + 4
                for sb in range(nsb):
                    p.add("tensor", lambda e, kt=kt, qt=qt, r0=r0, sb=sb, ts=ts: e.matmul(
                        PSA[:, :], kt[r0:r0 + 64, sb * 128:(sb + 1) * 128],
                        qt[r0:r0 + 64, ts * 512:(ts + 1) * 512], start=True, stop=True))
                    if sb < 4 * ts:
                        idx = hl * 67 + (4 * ts - sb) + 3
                        p.add("vector", lambda e, idx=idx: e.tensor_scalar_mul(
                            Wbf[:, :], PSA[:, :], ctab[:, idx:idx + 1]))
                    else:
                        j = sb - 4 * ts
                        idx = hl * 67 + 3 - j
                        if j > 0:
                            p.add("vector", lambda e, j=j: e.memset(Wbf[:, 0:j * 128], 0.0))
                        p.add("vector", lambda e, j=j, idx=idx: e.scalar_tensor_tensor(
                            Wbf[:, j * 128:512], PSA[:, j * 128:512], ctab[:, idx:idx + 1],
                            M01[:, 0:512 - j * 128], ALU.mult, ALU.mult))
                    p.add("tensor", lambda e, sb=sb, hl=hl, nsb=nsb: e.matmul(
                        PSB[0:64, :], Vs[:, sb * 256 + hl * 64: sb * 256 + (hl + 1) * 64],
                        Wbf[:, :], start=(sb == 0), stop=(sb == nsb - 1)))
                p.add("vector", lambda e, r0=r0, hl=hl, ts=ts: e.tensor_copy(
                    y1s[r0:r0 + 64, (hl // 2) * S_ + ts * 512: (hl // 2) * S_ + (ts + 1) * 512],
                    PSB[0:64, :]))
        p.dma("sync", y1T_o[0:128, :], y1s[:, 0:S_])
        p.dma("sync", y1T_o[128:256, :], y1s[:, S_:2 * S_])
        p.emit(block, [sem])
    return nc


def build_E():
    nc = bass.Bass("TRN2", target_bir_lowering=False)
    y1T = nc.dram_tensor("y1T", [E_, T_], BF16, kind="ExternalInput")
    wo1T = nc.dram_tensor("wo1T", [E_, E_], F32, kind="ExternalInput")
    x1T = nc.dram_tensor("x1T", [E_, T_], F32, kind="ExternalInput")
    g1T = nc.dram_tensor("g1T", [E_, T_], F32, kind="ExternalInput")
    x2T_o = nc.dram_tensor("x2T_o", [E_, T_], F32, kind="ExternalOutput")

    from contextlib import ExitStack
    es = ExitStack()
    with es:
        y1_sb = es.enter_context(nc.sbuf_tensor("y1_sb", [128, NCH * T_], BF16))
        wo1_bf = es.enter_context(nc.sbuf_tensor("wo1_bf", [128, NCH * 1024], BF16))
        xsl = es.enter_context(nc.sbuf_tensor("xsl", [128, 512], F32))
        gsl = es.enter_context(nc.sbuf_tensor("gsl", [128, 512], F32))
        tsl = es.enter_context(nc.sbuf_tensor("tsl", [128, 512], F32))
        osl = es.enter_context(nc.sbuf_tensor("osl", [128, 512], F32))
        PSA = es.enter_context(nc.psum_tensor("PSA", [128, 512], F32))
        sem = es.enter_context(nc.semaphore("step"))
        block = es.enter_context(nc.Block())
        p = Prog(nc)
        for c in range(NCH):
            p.dma("sync", y1_sb[:, c * T_:(c + 1) * T_], y1T[c * 128:(c + 1) * 128, :])
            p.dma("gpsimd", wo1_bf[:, c * 1024:(c + 1) * 1024],
                  wo1T[c * 128:(c + 1) * 128, :])
        for oc in range(NCH):
            for ts in range(4):
                for c in range(NCH):
                    p.add("tensor", lambda e, oc=oc, ts=ts, c=c: e.matmul(
                        PSA[:, :],
                        wo1_bf[:, c * 1024 + oc * 128: c * 1024 + oc * 128 + 128],
                        y1_sb[:, c * T_ + ts * 512: c * T_ + (ts + 1) * 512],
                        start=(c == 0), stop=(c == NCH - 1)))
                p.dma("sync", xsl[:, :], x1T[oc * 128:(oc + 1) * 128, ts * 512:(ts + 1) * 512])
                p.dma("sync", gsl[:, :], g1T[oc * 128:(oc + 1) * 128, ts * 512:(ts + 1) * 512])
                p.add("vector", lambda e: e.tensor_tensor(
                    tsl[:, :], PSA[:, :], gsl[:, :], ALU.mult))
                p.add("vector", lambda e: e.tensor_tensor(
                    osl[:, :], tsl[:, :], xsl[:, :], ALU.add))
                p.dma("sync", x2T_o[oc * 128:(oc + 1) * 128, ts * 512:(ts + 1) * 512],
                      osl[:, :])
        p.emit(block, [sem])
    return nc


def build_full():
    from concourse.aot_env import allow_ici_all_to_all
    from contextlib import ExitStack
    nc = bass.Bass("TRN2", target_bir_lowering=False)
    GRP = [[0, 1, 2, 3], [4, 5, 6, 7]]

    xT = nc.dram_tensor("xT", [E_, T_], F32, kind="ExternalInput")
    wqT = nc.dram_tensor("wqT", [E_, E_], F32, kind="ExternalInput")
    wkT = nc.dram_tensor("wkT", [E_, E_], F32, kind="ExternalInput")
    wvT = nc.dram_tensor("wvT", [E_, E_], F32, kind="ExternalInput")
    selT = nc.dram_tensor("selT", [E_, 64], F32, kind="ExternalInput")
    keys = nc.dram_tensor("keys", [64, E_, 128], F32, kind="ExternalInput")
    values = nc.dram_tensor("values", [64, 128, E_], F32, kind="ExternalInput")
    cosL = nc.dram_tensor("cosL", [T_, 32], F32, kind="ExternalInput")
    sinL = nc.dram_tensor("sinL", [T_, 32], F32, kind="ExternalInput")
    idf = nc.dram_tensor("idf", [128, 128], F32, kind="ExternalInput")
    onesr = nc.dram_tensor("onesr", [1, 128], F32, kind="ExternalInput")
    onesc = nc.dram_tensor("onesc", [128, 1], F32, kind="ExternalInput")
    woT = nc.dram_tensor("woT", [E_, E_], F32, kind="ExternalInput")
    wq1T = nc.dram_tensor("wq1T", [E_, E_], F32, kind="ExternalInput")
    wk1T = nc.dram_tensor("wk1T", [E_, E_], F32, kind="ExternalInput")
    wv1T = nc.dram_tensor("wv1T", [E_, E_], F32, kind="ExternalInput")
    wo1T = nc.dram_tensor("wo1T", [E_, E_], F32, kind="ExternalInput")
    kern1 = nc.dram_tensor("kern1", [E_, E_], F32, kind="ExternalInput")
    bias1 = nc.dram_tensor("bias1", [E_, 1], F32, kind="ExternalInput")
    a1 = nc.dram_tensor("a1", [E_, 16], F32, kind="ExternalInput")
    b1 = nc.dram_tensor("b1", [16, E_], F32, kind="ExternalInput")
    gam = nc.dram_tensor("gam", [16, 1], F32, kind="ExternalInput")
    selq = nc.dram_tensor("selq", [16, NCH * 128], F32, kind="ExternalInput")
    gamh = nc.dram_tensor("gamh", [4, 1], F32, kind="ExternalInput")
    mgrid = nc.dram_tensor("mgrid", [1, 4 * 67], F32, kind="ExternalInput")
    hoff = nc.dram_tensor("hoff", [1, 1], mybir.dt.int32, kind="ExternalInput")
    toff = nc.dram_tensor("toff", [1, 1], mybir.dt.int32, kind="ExternalInput")

    x2T_o = nc.dram_tensor("x2T_o", [E_, T_], F32, kind="ExternalOutput")

    qT_d = nc.dram_tensor("qT_d", [E_, T_], BF16)
    kT_d = nc.dram_tensor("kT_d", [E_, T_], BF16)
    vg_d = nc.dram_tensor("vg_d", [T_, 16 * 65], BF16)
    g0_d = nc.dram_tensor("g0_d", [E_, T_], F32)
    hg_d = nc.dram_tensor("hg_d", [T_, 64 * 128], BF16)
    x1_d = nc.dram_tensor("x1_d", [E_, T_], F32)
    qt1_d = nc.dram_tensor("qt1_d", [E_, T_], BF16)
    kt1_d = nc.dram_tensor("kt1_d", [E_, T_], BF16)
    v1_d = nc.dram_tensor("v1_d", [T_, E_], BF16)
    g1_d = nc.dram_tensor("g1_d", [E_, T_], F32)
    y1_d = nc.dram_tensor("y1_d", [256, S_], BF16)
    am_d = nc.dram_tensor("am_d", [128, 1], F32)
    kT_g = nc.dram_tensor("kT_g", [4 * E_, T_], BF16)
    vg_g = nc.dram_tensor("vg_g", [4 * T_, 16 * 65], BF16)
    qt1_g = nc.dram_tensor("qt1_g", [4 * E_, T_], BF16)
    kt1_g = nc.dram_tensor("kt1_g", [4 * E_, T_], BF16)
    v1_g = nc.dram_tensor("v1_g", [4 * T_, E_], BF16)
    y1_g = nc.dram_tensor("y1_g", [E_, S_], BF16)

    es = ExitStack()
    with es:
        id_f = es.enter_context(nc.sbuf_tensor("id_f", [128, 128], F32))
        id_bf = es.enter_context(nc.sbuf_tensor("id_bf", [128, 128], BF16))
        ones_r = es.enter_context(nc.sbuf_tensor("ones_r", [1, 128], F32))
        ones_c = es.enter_context(nc.sbuf_tensor("ones_c", [128, 1], F32))
        ones16 = es.enter_context(nc.sbuf_tensor("ones16", [128, 16], BF16))
        offs = es.enter_context(nc.sbuf_tensor("offs", [1, 2], mybir.dt.int32))
        PSA = es.enter_context(nc.psum_tensor("PSA", [128, 2048], F32))
        PSB = es.enter_context(nc.psum_tensor("PSB", [128, 1024], F32))
        PSC = es.enter_context(nc.psum_tensor("PSC", [128, 512], F32))
        PST = es.enter_context(nc.psum_tensor("PST", [128, 1024], BF16))
        ccs = [es.enter_context(nc.semaphore(f"cc{i}")) for i in range(6)]
        block = es.enter_context(nc.Block())
        p = P2(nc)

        def BAR():
            return [p.tok(e) for e in P2.ENGS]

        MARKS["A"] = len(p.items)
        esA = ExitStack()
        esA.__enter__()
        xn_bf = esA.enter_context(nc.sbuf_tensor("xn_bf", [128, NCH * T_], BF16))
        gates = esA.enter_context(nc.sbuf_tensor("gates", [128, 16 * 64], F32))
        es1 = ExitStack()
        es1.__enter__()
        selT_s = es1.enter_context(nc.sbuf_tensor("selT_s", [128, NCH * 64], F32, side="right"))
        mu_rep = es1.enter_context(nc.sbuf_tensor("mu_rep", [128, T_], F32, side="right"))
        rs_rep = es1.enter_context(nc.sbuf_tensor("rs_rep", [128, T_], F32, side="right"))
        xch = es1.enter_context(nc.sbuf_tensor("xch", [128, 2 * T_], F32, side="right"))
        sq = es1.enter_context(nc.sbuf_tensor("sq", [128, T_], F32, side="right"))
        row_a = es1.enter_context(nc.sbuf_tensor("row_a", [1, T_], F32, side="right"))
        row_b = es1.enter_context(nc.sbuf_tensor("row_b", [1, T_], F32, side="right"))
        row_c = es1.enter_context(nc.sbuf_tensor("row_c", [1, 512], F32, side="right"))
        row_d = es1.enter_context(nc.sbuf_tensor("row_d", [1, 512], F32, side="right"))
        sels = es1.enter_context(nc.sbuf_tensor("sels", [64, T_], F32, side="right"))
        seltm = es1.enter_context(nc.sbuf_tensor("seltm", [128, 64], F32, side="right"))
        wk1_ = es1.enter_context(nc.sbuf_tensor("wk1_", [128, 64], F32, side="right"))
        wk2_ = es1.enter_context(nc.sbuf_tensor("wk2_", [128, 64], F32, side="right"))
        mcol = es1.enter_context(nc.sbuf_tensor("mcol", [128, 1], F32, side="right"))

        tid = p.dma("sync", id_f[:, :], idf[:, :])
        tidb = p.dma("gpsimd", id_bf[:, :], idf[:, :])
        tor = p.dma("sync", ones_r[:, :], onesr[:, :])
        toc = p.dma("sync", ones_c[:, :], onesc[:, :])
        tof = p.dma("sync", offs[0:1, 0:1], hoff[:, :])
        tof2 = p.dma("sync", offs[0:1, 1:2], toff[:, :])
        t16 = p.add("gpsimd", lambda e: e.memset(ones16[:, :], 1.0))
        tsel = None
        for c in range(NCH):
            tsel = p.dma("sync", selT_s[:, c * 64:(c + 1) * 64],
                         selT[c * 128:(c + 1) * 128, :])

        tdx = [None] * 32
        tsq = [None] * 32
        ts1 = [None] * 32
        ts2 = [None] * 32
        t_muev = t_rsev = None
        i = 0
        for ts in range(4):
            for c in range(NCH):
                r4 = i % 4
                deps = [tsq[i - 4], ts1[i - 4]] if i >= 4 else []
                tdx[i] = p.dma("sync", xch[:, r4 * 512:(r4 + 1) * 512],
                               xT[c * 128:(c + 1) * 128, ts * 512:(ts + 1) * 512], deps)
                tsq[i] = p.add("scalar", lambda e, r4=r4: e.activation(
                    sq[:, r4 * 512:(r4 + 1) * 512], xch[:, r4 * 512:(r4 + 1) * 512],
                    AF.Square), deps=[tdx[i]] + ([ts2[i - 4]] if i >= 4 else []))
                d1 = [tdx[i]] + ([toc] if i == 0 else []) + ([t_rsev] if (c == 0 and ts > 0) else [])
                ts1[i] = p.add("tensor", lambda e, c=c, r4=r4: e.matmul(
                    PSC[0:1, :], ones_c[:, :], xch[:, r4 * 512:(r4 + 1) * 512],
                    start=(c == 0), stop=(c == NCH - 1)), deps=d1)
                d2 = [tsq[i]] + ([t_muev] if (c == 0 and ts > 0) else [])
                ts2[i] = p.add("tensor", lambda e, c=c, r4=r4: e.matmul(
                    PSB[0:1, 0:512], ones_c[:, :], sq[:, r4 * 512:(r4 + 1) * 512],
                    start=(c == 0), stop=(c == NCH - 1)), deps=d2)
                i += 1
            sl = slice(ts * 512, (ts + 1) * 512)
            t_mu = p.add("vector", lambda e, sl=sl: e.tensor_scalar_mul(
                row_a[:, sl], PSC[0:1, :], 1.0 / 1024.0), deps=[ts1[i - 1]])
            t_ex2 = p.add("vector", lambda e: e.tensor_scalar_mul(
                row_c[:, :], PSB[0:1, 0:512], 1.0 / 1024.0), deps=[ts2[i - 1]])
            p.add("vector", lambda e, sl=sl: e.tensor_tensor(
                row_d[:, :], row_a[:, sl], row_a[:, sl], ALU.mult))
            p.add("vector", lambda e: e.tensor_tensor(
                row_c[:, :], row_c[:, :], row_d[:, :], ALU.subtract))
            t_vp = p.add("vector", lambda e: e.tensor_scalar_add(row_c[:, :], row_c[:, :], 1e-5))
            t_sd = p.add("scalar", lambda e: e.activation(row_d[:, :], row_c[:, :], AF.Sqrt),
                         deps=[t_vp])
            t_rstd = p.add("vector", lambda e, sl=sl: e.reciprocal(row_b[:, sl], row_d[:, :]),
                           deps=[t_sd])
            t_repmu = p.add("tensor", lambda e, sl=sl: e.matmul(
                PSB[0:128, 0:512], ones_r[:, :], row_a[:, sl], start=True, stop=True),
                deps=[t_mu, t_ex2, tor])
            t_muev = p.add("vector", lambda e, sl=sl: e.tensor_copy(
                mu_rep[:, sl], PSB[0:128, 0:512]), deps=[t_repmu])
            t_reprs = p.add("tensor", lambda e, sl=sl: e.matmul(
                PSC[0:128, :], ones_r[:, :], row_b[:, sl], start=True, stop=True),
                deps=[t_rstd])
            t_rsev = p.add("vector", lambda e, sl=sl: e.tensor_copy(
                rs_rep[:, sl], PSC[0:128, :]), deps=[t_reprs])

        t_mult = [None] * NCH
        t_cast2 = [None] * NCH
        t_lg = [None] * NCH
        t_lgl = None
        for c in range(NCH):
            hb = (c % 2) * T_
            deps = ([t_cast2[c - 2], t_lg[c - 2]] if c > 1
                    else [tsq[31], ts1[31], ts2[31], t_rsev, t_muev])
            tdx2 = p.dma("sync", xch[:, hb:hb + T_], xT[c * 128:(c + 1) * 128, :], deps)
            p.add("vector", lambda e, hb=hb: e.tensor_tensor(
                sq[:, :], xch[:, hb:hb + T_], mu_rep[:, :], ALU.subtract), deps=[tdx2])
            t_mult[c] = p.add("vector", lambda e, hb=hb: e.tensor_tensor(
                xch[:, hb:hb + T_], sq[:, :], rs_rep[:, :], ALU.mult))
            t_cast2[c] = p.add("vector", lambda e, c=c, hb=hb: e.tensor_copy(
                xn_bf[:, c * T_:(c + 1) * T_], xch[:, hb:hb + T_]))
            for ts in range(4):
                t_lgl = p.add("tensor", lambda e, c=c, ts=ts, hb=hb: e.matmul(
                    PSA[0:64, ts * 512:(ts + 1) * 512], selT_s[:, c * 64:(c + 1) * 64],
                    xch[:, hb + ts * 512: hb + (ts + 1) * 512], start=(c == 0), stop=(c == NCH - 1)),
                    deps=[t_mult[c]] + ([tsel] if c == 0 else []))
            t_lg[c] = t_lgl

        t_sig = p.add("scalar", lambda e: e.activation(sels[:, :], PSA[0:64, :], AF.Sigmoid),
                      deps=[t_lgl])
        t_cp = None
        t_gate_last = None
        for tt in range(16):
            t_tr = p.add("tensor", lambda e, tt=tt: e.transpose(
                PSB[0:128, 0:64], sels[0:64, tt * 128:(tt + 1) * 128], id_f[0:64, 0:64]),
                deps=[t_sig, tid] + ([t_cp] if t_cp else []))
            t_cp = p.add("vector", lambda e: e.tensor_copy(seltm[:, :], PSB[0:128, 0:64]),
                         deps=[t_tr])
            p.add("vector", lambda e: e.tensor_copy(wk1_[:, :], seltm[:, :]))
            for r in range(4):
                src = wk1_ if r % 2 == 0 else wk2_
                dst = wk2_ if r % 2 == 0 else wk1_
                p.add("vector", lambda e, src=src: e.reduce_max(mcol[:, :], src[:, :], AX.X))
                if r < 3:
                    mb = _ap(mcol, 0, [[1, 128], [0, 64]])
                    p.add("vector", lambda e, src=src, dst=dst, mb=mb: e.tensor_tensor(
                        dst[:, :], src[:, :], mb, ALU.is_equal))
                    p.add("vector", lambda e, src=src, dst=dst: e.scalar_tensor_tensor(
                        dst[:, :], dst[:, :], -1e30, src[:, :], ALU.mult, ALU.add))
            mb = _ap(mcol, 0, [[1, 128], [0, 64]])
            p.add("vector", lambda e, mb=mb: e.tensor_tensor(wk1_[:, :], seltm[:, :], mb, ALU.is_ge))
            t_gate_last = p.add("vector", lambda e, tt=tt: e.tensor_tensor(
                gates[:, tt * 64:(tt + 1) * 64], seltm[:, :], wk1_[:, :], ALU.mult))
        es1.__exit__(None, None, None)

        MARKS["A4"] = len(p.items)
        es2 = ExitStack()
        es2.__enter__()
        W_sb = es2.enter_context(nc.sbuf_tensor("W_sb", [128, 3 * NCH * 1024], BF16, side="right"))
        qT_sb = es2.enter_context(nc.sbuf_tensor("qT_sb", [128, NCH * T_], BF16, side="right"))
        kT_sb = es2.enter_context(nc.sbuf_tensor("kT_sb", [128, NCH * T_], BF16, side="right"))
        qf = es2.enter_context(nc.sbuf_tensor("qf", [128, 1024], F32, side="right"))
        qr = es2.enter_context(nc.sbuf_tensor("qr", [128, 1024], F32, side="right"))
        qr_bf = es2.enter_context(nc.sbuf_tensor("qr_bf", [128, 2 * 1024], BF16, side="right"))
        t1s = es2.enter_context(nc.sbuf_tensor("t1s", [128, 512], F32, side="right"))
        t2s = es2.enter_context(nc.sbuf_tensor("t2s", [128, 512], F32, side="right"))
        ct = es2.enter_context(nc.sbuf_tensor("ct", [128, 32], F32, side="right"))
        st = es2.enter_context(nc.sbuf_tensor("st", [128, 32], F32, side="right"))
        vstage = es2.enter_context(nc.sbuf_tensor("vstage", [128, 1040], BF16, side="right"))
        a3bar = BAR()
        tW = None
        for w_i, wsrc in enumerate((wqT, wkT, wvT)):
            for c in range(NCH):
                tW = p.dma("gpsimd", W_sb[:, (w_i * NCH + c) * 1024:(w_i * NCH + c + 1) * 1024],
                           wsrc[c * 128:(c + 1) * 128, :],
                           deps=a3bar if (w_i == 0 and c == 0) else [])
        t_vst0 = p.add("vector", lambda e: e.memset(vstage[:, :], 1.0))
        cosap = _ap(ct, 0, [[32, 128], [0, 16], [1, 32]])
        sinap = _ap(st, 0, [[32, 128], [0, 16], [1, 32]])

        pend = []
        t_evq = {0: None, 1: None}
        t_rope6 = None
        t_ctst = None
        t_qtcpy = None
        t_vcpy = None
        t_vdma = None
        gpar = 0
        t_pend_trs = {0: None, 1: None}

        def flush_pend():
            nonlocal t_qtcpy
            for (par, dstT, ptt, t_cast) in pend:
                t_trs = None
                for c8 in range(NCH):
                    t_trs = p.add("tensor", lambda e, c8=c8, par=par: e.transpose(
                        PST[:, c8 * 128:(c8 + 1) * 128],
                        qr_bf[:, par * 1024 + c8 * 128: par * 1024 + (c8 + 1) * 128],
                        id_bf[:, :]),
                        deps=[t_cast, tidb] + ([t_qtcpy] if (t_qtcpy and c8 == 0) else []))
                t_pend_trs[par] = t_trs
                for c8 in range(NCH):
                    t_qtcpy = p.add("vector", lambda e, c8=c8, ptt=ptt, dstT=dstT: e.tensor_copy(
                        dstT[:, c8 * T_ + ptt * 128: c8 * T_ + (ptt + 1) * 128],
                        PST[:, c8 * 128:(c8 + 1) * 128]),
                        deps=[t_trs] if c8 == 0 else [])
            pend.clear()

        for tt in range(16):
            tsl = slice(tt * 128, (tt + 1) * 128)
            t_ct = p.dma("sync", ct[:, :], cosL[tsl, :], deps=[t_ctst] if t_ctst else [])
            t_st = p.dma("sync", st[:, :], sinL[tsl, :])
            for w_i, dstT in ((0, qT_sb), (1, kT_sb)):
                t_mm = None
                for os in range(2):
                    for c in range(NCH):
                        d = []
                        if os == 0 and c == 0:
                            d = [tW, t_gate_last] + ([t_evq[w_i]] if t_evq[w_i] else [])
                        t_mm = p.add("tensor", lambda e, w_i=w_i, os=os, c=c, tt=tt: e.matmul(
                            PSA[0:128, w_i * 1024 + os * 512: w_i * 1024 + (os + 1) * 512],
                            xn_bf[:, c * T_ + tt * 128: c * T_ + (tt + 1) * 128],
                            W_sb[:, (w_i * NCH + c) * 1024 + os * 512:
                                    (w_i * NCH + c) * 1024 + (os + 1) * 512],
                            start=(c == 0), stop=(c == NCH - 1)), deps=d)
                flush_pend()
                t_evq[w_i] = p.add("scalar", lambda e, w_i=w_i: e.activation(
                    qf[:, :], PSA[0:128, w_i * 1024:(w_i + 1) * 1024], AF.Copy),
                    deps=[t_mm] + ([t_rope6] if t_rope6 else []))
                ev = lambda t, o=0: _ap(t, o, [[1024, 128], [64, 16], [2, 32]])
                p.add("vector", lambda e: e.tensor_tensor(t1s[:, :], ev(qf), cosap, ALU.mult),
                      deps=[t_evq[w_i], t_ct, t_st])
                p.add("vector", lambda e: e.tensor_tensor(t2s[:, :], ev(qf, 1), sinap, ALU.mult))
                p.add("vector", lambda e: e.tensor_tensor(ev(qr), t1s[:, :], t2s[:, :], ALU.subtract))
                p.add("vector", lambda e: e.tensor_tensor(t1s[:, :], ev(qf), sinap, ALU.mult))
                t_ctst = p.add("vector", lambda e: e.tensor_tensor(t2s[:, :], ev(qf, 1), cosap, ALU.mult))
                t_rope6 = p.add("vector", lambda e: e.tensor_tensor(ev(qr, 1), t1s[:, :], t2s[:, :], ALU.add))
                t_cast = p.add("vector", lambda e, gpar=gpar: e.tensor_copy(
                    qr_bf[:, gpar * 1024:(gpar + 1) * 1024], qr[:, :]),
                    deps=[t_pend_trs[gpar]] if t_pend_trs[gpar] else [])
                pend.append((gpar, dstT, tt, t_cast))
                gpar ^= 1
                if A4_NOLAG:
                    flush_pend()
            t_vmm = None
            for os in range(2):
                for c in range(NCH):
                    d = []
                    if os == 0 and c == 0:
                        d = [tW] + ([t_vcpy] if t_vcpy else [t_gate_last])
                    t_vmm = p.add("tensor", lambda e, os=os, c=c, tt=tt: e.matmul(
                        PSB[0:128, os * 512:(os + 1) * 512],
                        xn_bf[:, c * T_ + tt * 128: c * T_ + (tt + 1) * 128],
                        W_sb[:, (2 * NCH + c) * 1024 + os * 512:
                                (2 * NCH + c) * 1024 + (os + 1) * 512],
                        start=(c == 0), stop=(c == NCH - 1)), deps=d)
            t_vcpy = p.add("vector", lambda e: e.tensor_copy(
                _ap(vstage, 0, [[1040, 128], [65, 16], [1, 64]]), PSB[0:128, :]),
                deps=[t_vmm, t_vst0] + ([t_vdma] if t_vdma else []))
            t_vdma = p.dma("sync", vg_d[tt * 128:(tt + 1) * 128, :], vstage[:, :],
                           deps=[t_vcpy, t16])
        flush_pend()
        tqd = p.dma("sync", _ap(qT_d, 0, [[T_, 128], [128 * T_, NCH], [1, T_]]),
                    _ap(qT_sb, 0, [[NCH * T_, 128], [T_, NCH], [1, T_]]), deps=[t_qtcpy])
        tkd = p.dma("sync", _ap(kT_d, 0, [[T_, 128], [128 * T_, NCH], [1, T_]]),
                    _ap(kT_sb, 0, [[NCH * T_, 128], [T_, NCH], [1, T_]]), deps=[t_qtcpy])
        if DEBUG_FULL:
            xn_dbg = nc.dram_tensor("dbg_xn", [128, NCH * T_], BF16, kind="ExternalOutput")
            gates_dbg = nc.dram_tensor("dbg_gates", [128, 16 * 64], F32, kind="ExternalOutput")
            p.dma("sync", xn_dbg.ap(), xn_bf[:, :], deps=[p.tok(e) for e in P2.ENGS])
            p.dma("sync", gates_dbg.ap(), gates[:, :], deps=[p.tok(e) for e in P2.ENGS])
        es2.__exit__(None, None, None)

        MARKS["A5a"] = len(p.items)
        es3 = ExitStack()
        es3.__enter__()
        keys4 = es3.enter_context(nc.sbuf_tensor("keys4", [128, 2 * NCH * 512], BF16, side="right"))
        hg_bf = es3.enter_context(nc.sbuf_tensor("hg_bf", [128, 2 * 512], BF16, side="right"))
        tkey = {}
        tmm_last = {}
        tstt = {}
        thg = {}
        gi = 0
        for kg in range(16):
            kpar = kg % 2
            dk = [tmm_last[kg - 2]] if kg >= 2 else [tqd, tkd]
            tk = None
            for c in range(NCH):
                tk = p.dma("gpsimd", keys4[:, kpar * 4096 + c * 512: kpar * 4096 + (c + 1) * 512],
                           _ap(keys, (4 * kg) * E_ * 128 + c * 128 * 128,
                               [[128, 128], [E_ * 128, 4], [1, 128]]),
                           deps=dk if c == 0 else [])
            tkey[kg] = tk
            for tt in range(16):
                bank = gi % 2
                hpar = gi % 2
                t_mm = None
                for c in range(NCH):
                    d = []
                    if c == 0:
                        d = [tkey[kg]] + ([tstt[gi - 2]] if gi >= 2 else [])
                    t_mm = p.add("tensor", lambda e, c=c, tt=tt, kpar=kpar, bank=bank: e.matmul(
                        PSA[0:128, bank * 512:(bank + 1) * 512],
                        xn_bf[:, c * T_ + tt * 128: c * T_ + (tt + 1) * 128],
                        keys4[:, kpar * 4096 + c * 512: kpar * 4096 + (c + 1) * 512],
                        start=(c == 0), stop=(c == NCH - 1)), deps=d)
                tmm_last[kg] = t_mm
                gap = _ap(gates, tt * 64 + kg * 4, [[16 * 64, 128], [1, 4], [0, 128]])
                tstt[gi] = p.add("vector", lambda e, gap=gap, bank=bank, hpar=hpar: e.scalar_tensor_tensor(
                    hg_bf[:, hpar * 512:(hpar + 1) * 512], PSA[0:128, bank * 512:(bank + 1) * 512],
                    0.0, gap, ALU.max, ALU.mult),
                    deps=[t_mm] + ([thg[gi - 2]] if gi >= 2 else []))
                thg[gi] = p.dma("sync", hg_d[tt * 128:(tt + 1) * 128, kg * 512:(kg + 1) * 512],
                                hg_bf[:, hpar * 512:(hpar + 1) * 512], deps=[tstt[gi]])
                gi += 1
        es3.__exit__(None, None, None)

        MARKS["A5b"] = len(p.items)
        es4 = ExitStack()
        es4.__enter__()
        vals = es4.enter_context(nc.sbuf_tensor("vals", [128, 64 * 1024], BF16, side="right"))
        hgrow = es4.enter_context(nc.sbuf_tensor("hgrow", [128, 64 * 128], BF16, side="right"))
        gsT = es4.enter_context(nc.sbuf_tensor("gsT", [128, 2 * 512], BF16, side="right"))
        gstage = es4.enter_context(nc.sbuf_tensor("gstage", [128, 1024], F32, side="right"))
        tvals_k = {}
        a5bar = [tmm_last[15], tstt[gi - 1], thg[gi - 1]]
        for k in range(64):
            tvals_k[k] = p.dma("gpsimd", vals[:, k * 1024:(k + 1) * 1024],
                               _ap(values, k * 128 * E_, [[E_, 128], [1, E_]]),
                               deps=a5bar if k == 0 else [])
        t_trs_first = {}
        tev = {}
        tvmm = {}
        tsilu = None
        tgdma = None
        for tt in range(16):
            dh = [thg[15 * 16 + tt]]
            if tt >= 1:
                dh.append(t_trs_first[tt - 1])
            thgr = p.dma("sync", hgrow[:, :], hg_d[tt * 128:(tt + 1) * 128, :], deps=dh)
            for kg in range(16):
                gp = kg % 2
                t_trs = None
                for k4 in range(4):
                    d = []
                    if k4 == 0:
                        d = [thgr] + ([tev[(tt, kg - 1)]] if kg >= 1 else
                                      ([tev[(tt - 1, 15)]] if tt >= 1 else []))
                    t_trs = p.add("tensor", lambda e, kg=kg, k4=k4: e.transpose(
                        PST[:, k4 * 128:(k4 + 1) * 128],
                        hgrow[:, kg * 512 + k4 * 128: kg * 512 + (k4 + 1) * 128],
                        id_bf[:, :]), deps=d)
                    if k4 == 3:
                        t_trs_first[tt] = t_trs
                dv = [t_trs]
                if (tt, kg) >= (0, 2):
                    prev = (tt, kg - 2) if kg >= 2 else ((tt - 1, 14 + kg) if tt >= 1 else None)
                    if prev is not None and prev in tvmm:
                        dv.append(tvmm[prev])
                tev[(tt, kg)] = p.add("vector", lambda e, gp=gp: e.tensor_copy(
                    gsT[:, gp * 512:(gp + 1) * 512], PST[:, 0:512]), deps=dv)
                t_mm = None
                for k4 in range(4):
                    k = kg * 4 + k4
                    for vc in range(NCH):
                        first = (kg == 0 and k4 == 0 and vc in (0, 4))
                        last = (kg == 15 and k4 == 3 and vc in (3, 7))
                        d = []
                        if k4 == 0 and vc == 0:
                            d = [tev[(tt, kg)], tvals_k[4 * kg + 3]]
                            if kg == 0 and tsilu is not None:
                                d.append(tsilu)
                        t_mm = p.add("tensor", lambda e, k=k, k4=k4, vc=vc, first=first, last=last, gp=gp: e.matmul(
                            PSA[0:128, vc * 128:(vc + 1) * 128],
                            vals[:, k * 1024 + vc * 128: k * 1024 + (vc + 1) * 128],
                            gsT[:, gp * 512 + k4 * 128: gp * 512 + (k4 + 1) * 128],
                            start=first, stop=last, skip_group_check=True), deps=d)
                tvmm[(tt, kg)] = t_mm
            p.add("scalar", lambda e: e.activation(gstage[:, 0:512], PSA[0:128, 0:512], AF.Silu),
                  deps=[tvmm[(tt, 15)]] + ([tgdma] if tgdma else []))
            tsilu = p.add("scalar", lambda e: e.activation(
                gstage[:, 512:1024], PSA[0:128, 512:1024], AF.Silu))
            tgdma = p.dma("sync", _ap(g0_d, tt * 128, [[T_, 128], [128 * T_, NCH], [1, 128]]),
                          _ap(gstage, 0, [[1024, 128], [128, NCH], [1, 128]]), deps=[tsilu])
        es4.__exit__(None, None, None)
        esA.__exit__(None, None, None)

        MARKS["B"] = len(p.items)
        p.new_phase()
        p.add("gpsimd", lambda e: e.collective_compute(
            "AllGather", ALU.bypass, replica_groups=GRP,
            ins=[kT_d.ap()], outs=[kT_g.ap()]).then_inc(ccs[0], 1), inc=0)
        p.add("gpsimd", lambda e: e.collective_compute(
            "AllGather", ALU.bypass, replica_groups=GRP,
            ins=[vg_d.ap()], outs=[vg_g.ap()]).then_inc(ccs[1], 1), inc=0,
            deps=[(ccs[0], 1)])

        esB = ExitStack()
        esB.__enter__()
        esY = ExitStack()
        esY.__enter__()
        Yn = esY.enter_context(nc.sbuf_tensor("Yn", [128, NCH * T_], BF16))
        esb1 = ExitStack()
        esb1.__enter__()
        Qs = esb1.enter_context(nc.sbuf_tensor("Qs", [128, NCH * T_], BF16, side="right"))
        KA = esb1.enter_context(nc.sbuf_tensor("KA", [128, S_], BF16, side="right"))
        KB = esb1.enter_context(nc.sbuf_tensor("KB", [128, S_], BF16, side="right"))
        Vg = esb1.enter_context(nc.sbuf_tensor("Vg", [128, 64 * 260], BF16, side="right"))
        Ebf = esb1.enter_context(nc.sbuf_tensor("Ebf", [128, 4 * 1024], BF16, side="right"))
        rrow = esb1.enter_context(nc.sbuf_tensor("rrow", [1, 512], F32, side="right"))
        Rs = esb1.enter_context(nc.sbuf_tensor("Rs", [64, 512], F32, side="right"))
        tq0 = None
        for c in range(NCH):
            tq0 = p.dma("sync", Qs[:, c * T_:(c + 1) * T_], qT_d[c * 128:(c + 1) * 128, :])
        texp = {}
        tavt = {}
        t_ynorm = None
        t_recip = None
        t_repl = None
        t_Rs = None
        bp = 0
        t_klast = {}
        t_avlastg = {}
        for g in range(4):
            dK = [(ccs[0], 1)] + ([t_klast[g - 1]] if g >= 1 else [])
            tka = p.dma("sync", KA[:, :],
                        _ap(kT_g, g * 256 * T_, [[T_, 128], [E_ * T_, 4], [1, T_]]), deps=dK)
            tkb = p.dma("sync", KB[:, :],
                        _ap(kT_g, (g * 256 + 128) * T_, [[T_, 128], [E_ * T_, 4], [1, T_]]))
            dV = [(ccs[1], 1)] + ([t_avlastg[g - 1]] if g >= 1 else [])
            tvg = p.dma("sync", Vg[:, :],
                        _ap(vg_g, g * 260, [[1040, 128], [T_ * 1040, 4], [128 * 1040, 16], [1, 260]]),
                        deps=dV)
            for hlp in range(2):
                h0 = 4 * g + 2 * hlp
                K_t = KA if hlp == 0 else KB
                qc0 = (h0 // 2) * T_
                hl0 = 2 * hlp
                hl1 = 2 * hlp + 1
                for ts in range(4):
                    t_av = None

                    def em_av(u, hl0=hl0, hl1=hl1):
                        nonlocal t_av
                        sb = u - bp
                        ep = u % 4
                        d = [texp[u]]
                        if sb == 0:
                            d = d + [tvg] + ([t_ynorm] if t_ynorm else [])
                        p.add("tensor", lambda e, sb=sb, hl0=hl0, ep=ep: e.matmul(
                            PSB[0:65, 0:512],
                            Vg[:, sb * 260 + hl0 * 65: sb * 260 + (hl0 + 1) * 65],
                            Ebf[:, ep * 1024: ep * 1024 + 512],
                            start=(sb == 0), stop=(sb == 63)), deps=d)
                        t_av = p.add("tensor", lambda e, sb=sb, hl1=hl1, ep=ep: e.matmul(
                            PSB[0:65, 512:1024],
                            Vg[:, sb * 260 + hl1 * 65: sb * 260 + (hl1 + 1) * 65],
                            Ebf[:, ep * 1024 + 512: ep * 1024 + 1024],
                            start=(sb == 0), stop=(sb == 63)),
                            deps=[texp[u]] + ([tvg, t_ynorm] if (sb == 0 and t_ynorm) else
                                              ([tvg] if sb == 0 else [])))
                        tavt[u] = t_av

                    for sb in range(64):
                        u = bp + sb
                        rg = u % 2
                        d = [texp[u - 2]] if u >= 2 else []
                        if sb == 0:
                            d = d + [tka, tkb, tq0]
                        p.add("tensor",
                            lambda e, K_t=K_t, sb=sb, qc0=qc0, ts=ts, rg=rg: e.matmul(
                                PSA[0:128, rg * 1024: rg * 1024 + 512],
                                K_t[0:64, sb * 128:(sb + 1) * 128],
                                Qs[0:64, qc0 + ts * 512: qc0 + (ts + 1) * 512],
                                start=True, stop=True), deps=d)
                        tm = p.add("tensor",
                            lambda e, K_t=K_t, sb=sb, qc0=qc0, ts=ts, rg=rg: e.matmul(
                                PSA[0:128, rg * 1024 + 512: rg * 1024 + 1024],
                                K_t[64:128, sb * 128:(sb + 1) * 128],
                                Qs[64:128, qc0 + ts * 512: qc0 + (ts + 1) * 512],
                                start=True, stop=True))
                        ed = [tm] + ([tavt[u - 4]] if u >= 4 else [])
                        texp[u] = p.add("scalar",
                            lambda e, rg=rg, ep=u % 4: e.activation(
                                Ebf[:, ep * 1024:(ep + 1) * 1024],
                                PSA[0:128, rg * 1024:(rg + 1) * 1024],
                                AF.Exp, scale=0.125), deps=ed)
                        if sb >= 1:
                            em_av(u - 1)
                    em_av(bp + 63)
                    bp += 64
                    t_r0 = p.add("vector", lambda e: e.reciprocal(rrow[:, :], PSB[64:65, 0:512]),
                                 deps=[t_av] + ([t_repl] if t_repl else []))
                    t_p0 = p.add("tensor", lambda e: e.matmul(
                        PSC[0:64, :], ones_r[0:1, 0:64], rrow[:, :], start=True, stop=True),
                        deps=[t_r0] + ([t_Rs] if t_Rs else []))
                    t_R0 = p.add("scalar", lambda e: e.activation(Rs[:, :], PSC[0:64, :], AF.Copy),
                                 deps=[t_p0] + ([t_ynorm] if t_ynorm else []))
                    t_y0 = p.add("vector", lambda e, qc0=qc0, ts=ts: e.tensor_tensor(
                        Yn[0:64, qc0 + ts * 512: qc0 + (ts + 1) * 512],
                        PSB[0:64, 0:512], Rs[:, :], ALU.mult), deps=[t_R0])
                    t_r1 = p.add("vector", lambda e: e.reciprocal(rrow[:, :], PSB[64:65, 512:1024]),
                                 deps=[t_av, t_p0])
                    t_p1 = p.add("tensor", lambda e: e.matmul(
                        PSC[0:64, :], ones_r[0:1, 0:64], rrow[:, :], start=True, stop=True),
                        deps=[t_r1, t_R0])
                    t_R1 = p.add("scalar", lambda e: e.activation(Rs[:, :], PSC[0:64, :], AF.Copy),
                                 deps=[t_p1, t_y0])
                    t_ynorm = p.add("vector", lambda e, qc0=qc0, ts=ts: e.tensor_tensor(
                        Yn[64:128, qc0 + ts * 512: qc0 + (ts + 1) * 512],
                        PSB[0:64, 512:1024], Rs[:, :], ALU.mult), deps=[t_R1])
                    t_repl = t_p1
                    t_Rs = t_R1
                t_klast[g] = texp[bp - 1]
            t_avlastg[g] = tavt[bp - 1]
        esb1.__exit__(None, None, None)

        MARKS["B2"] = len(p.items)
        esb2 = ExitStack()
        esb2.__enter__()
        X1 = esb2.enter_context(nc.sbuf_tensor("X1", [128, NCH * T_], F32, side="right"))
        wo_bf = esb2.enter_context(nc.sbuf_tensor("wo_bf", [128, NCH * 1024], BF16, side="right"))
        xsl = esb2.enter_context(nc.sbuf_tensor("xsl", [128, 2 * 512], F32, side="right"))
        gsl = esb2.enter_context(nc.sbuf_tensor("gsl", [128, 2 * 512], F32, side="right"))
        tsl_ = esb2.enter_context(nc.sbuf_tensor("tsl_", [128, 512], F32, side="right"))
        b1bar = BAR()
        tWo = None
        for c in range(NCH):
            tWo = p.dma("gpsimd", wo_bf[:, c * 1024:(c + 1) * 1024], woT[c * 128:(c + 1) * 128, :],
                        deps=b1bar if c == 0 else [])
        t_addi = {}
        t_muli = {}
        txd = {}
        tgd = {}
        i = 0
        for oc in range(NCH):
            for ts in range(4):
                par = i % 2
                d = [tWo, t_ynorm] if i == 0 else ([t_muli[i - 1]] if i >= 1 else [])
                t_mm = None
                for c in range(NCH):
                    t_mm = p.add("tensor", lambda e, oc=oc, ts=ts, c=c: e.matmul(
                        PSA[0:128, 0:512],
                        wo_bf[:, c * 1024 + oc * 128: c * 1024 + oc * 128 + 128],
                        Yn[:, c * T_ + ts * 512: c * T_ + (ts + 1) * 512],
                        start=(c == 0), stop=(c == NCH - 1)), deps=d if c == 0 else [])
                txd[i] = p.dma("sync", xsl[:, par * 512:(par + 1) * 512],
                               xT[oc * 128:(oc + 1) * 128, ts * 512:(ts + 1) * 512],
                               deps=[t_addi[i - 2]] if i >= 2 else b1bar)
                tgd[i] = p.dma("sync", gsl[:, par * 512:(par + 1) * 512],
                               g0_d[oc * 128:(oc + 1) * 128, ts * 512:(ts + 1) * 512],
                               deps=[t_muli[i - 2]] if i >= 2 else b1bar)
                t_muli[i] = p.add("vector", lambda e, par=par: e.tensor_tensor(
                    tsl_[:, :], PSA[0:128, 0:512], gsl[:, par * 512:(par + 1) * 512], ALU.mult),
                    deps=[t_mm, tgd[i]])
                t_addi[i] = p.add("vector", lambda e, oc=oc, ts=ts, par=par: e.tensor_tensor(
                    X1[:, oc * T_ + ts * 512: oc * T_ + (ts + 1) * 512],
                    tsl_[:, :], xsl[:, par * 512:(par + 1) * 512], ALU.add), deps=[txd[i]])
                p.dma("sync", x1_d[oc * 128:(oc + 1) * 128, ts * 512:(ts + 1) * 512],
                      X1[:, oc * T_ + ts * 512: oc * T_ + (ts + 1) * 512], deps=[t_addi[i]])
                i += 1
        esY.__exit__(None, None, None)

        MARKS["B3"] = len(p.items)
        xn1 = esB.enter_context(nc.sbuf_tensor("xn1", [128, NCH * T_], BF16))
        esb3 = ExitStack()
        esb3.__enter__()
        mu1 = esb3.enter_context(nc.sbuf_tensor("mu1", [128, T_], F32, side="right"))
        rs1 = esb3.enter_context(nc.sbuf_tensor("rs1", [128, T_], F32, side="right"))
        sqs = esb3.enter_context(nc.sbuf_tensor("sqs", [128, 512], F32, side="right"))
        row_a2 = esb3.enter_context(nc.sbuf_tensor("row_a2", [1, T_], F32, side="right"))
        row_b2 = esb3.enter_context(nc.sbuf_tensor("row_b2", [1, T_], F32, side="right"))
        row_c2 = esb3.enter_context(nc.sbuf_tensor("row_c2", [1, 512], F32, side="right"))
        row_d2 = esb3.enter_context(nc.sbuf_tensor("row_d2", [1, 512], F32, side="right"))
        scr_a = esb3.enter_context(nc.sbuf_tensor("scr_a", [128, T_], F32, side="right"))
        scr_b = esb3.enter_context(nc.sbuf_tensor("scr_b", [128, T_], F32, side="right"))
        p.serial = True
        p._last = t_addi[31]
        for ts in range(4):
            for c in range(NCH):
                p.add("scalar", lambda e, c=c, ts=ts: e.activation(
                    sqs[:, :], X1[:, c * T_ + ts * 512: c * T_ + (ts + 1) * 512], AF.Square))
                p.add("tensor", lambda e, c=c, ts=ts: e.matmul(
                    PSC[0:1, :], ones_c[:, :],
                    X1[:, c * T_ + ts * 512: c * T_ + (ts + 1) * 512],
                    start=(c == 0), stop=(c == NCH - 1)))
                p.add("tensor", lambda e, c=c: e.matmul(
                    PSB[0:1, 0:512], ones_c[:, :], sqs[:, :],
                    start=(c == 0), stop=(c == NCH - 1)))
            sl = slice(ts * 512, (ts + 1) * 512)
            p.add("vector", lambda e, sl=sl: e.tensor_scalar_mul(row_a2[:, sl], PSC[0:1, :], 1.0 / 1024.0))
            p.add("vector", lambda e: e.tensor_scalar_mul(row_c2[:, :], PSB[0:1, 0:512], 1.0 / 1024.0))
            p.add("vector", lambda e, sl=sl: e.tensor_tensor(row_d2[:, :], row_a2[:, sl], row_a2[:, sl], ALU.mult))
            p.add("vector", lambda e: e.tensor_tensor(row_c2[:, :], row_c2[:, :], row_d2[:, :], ALU.subtract))
            p.add("vector", lambda e: e.tensor_scalar_add(row_c2[:, :], row_c2[:, :], 1e-5))
            p.add("scalar", lambda e: e.activation(row_d2[:, :], row_c2[:, :], AF.Sqrt))
            p.add("vector", lambda e, sl=sl: e.reciprocal(row_b2[:, sl], row_d2[:, :]))
            p.add("tensor", lambda e, sl=sl: e.matmul(
                PSB[0:128, 0:512], ones_r[:, :], row_a2[:, sl], start=True, stop=True))
            p.add("vector", lambda e, sl=sl: e.tensor_copy(mu1[:, sl], PSB[0:128, 0:512]))
            p.add("tensor", lambda e, sl=sl: e.matmul(
                PSC[0:128, :], ones_r[:, :], row_b2[:, sl], start=True, stop=True))
            p.add("vector", lambda e, sl=sl: e.tensor_copy(rs1[:, sl], PSC[0:128, :]))
        t_xn1 = None
        for c in range(NCH):
            p.add("vector", lambda e, c=c: e.tensor_tensor(
                scr_a[:, :], X1[:, c * T_:(c + 1) * T_], mu1[:, :], ALU.subtract))
            p.add("vector", lambda e: e.tensor_tensor(scr_b[:, :], scr_a[:, :], rs1[:, :], ALU.mult))
            t_xn1 = p.add("vector", lambda e, c=c: e.tensor_copy(xn1[:, c * T_:(c + 1) * T_], scr_b[:, :]))
        p.serial = False
        esb3.__exit__(None, None, None)
        esb2.__exit__(None, None, None)

        MARKS["B4"] = len(p.items)
        esb4 = ExitStack()
        esb4.__enter__()
        W1 = esb4.enter_context(nc.sbuf_tensor("W1", [128, 3 * NCH * 1024], BF16, side="right"))
        U512 = esb4.enter_context(nc.sbuf_tensor("U512", [128, T_], F32, side="right"))
        U128 = esb4.enter_context(nc.sbuf_tensor("U128", [128, T_], F32, side="right"))
        LG = esb4.enter_context(nc.sbuf_tensor("LG", [128, NCH], F32, side="right"))
        NLG = esb4.enter_context(nc.sbuf_tensor("NLG", [128, NCH], F32, side="right"))
        lnGr = esb4.enter_context(nc.sbuf_tensor("lnGr", [16, 1], F32, side="right"))
        lnG = esb4.enter_context(nc.sbuf_tensor("lnG", [16, 1], F32, side="right"))
        selq_s = esb4.enter_context(nc.sbuf_tensor("selq_s", [16, NCH * 128], F32, side="right"))
        Pq = esb4.enter_context(nc.sbuf_tensor("Pq", [128, T_], F32, side="right"))
        Pk = esb4.enter_context(nc.sbuf_tensor("Pk", [128, T_], F32, side="right"))
        scp = esb4.enter_context(nc.sbuf_tensor("scp", [128, T_], F32, side="right"))
        qst = esb4.enter_context(nc.sbuf_tensor("qst", [128, 2 * T_], BF16, side="right"))
        v1st = esb4.enter_context(nc.sbuf_tensor("v1st", [128, 2 * 1024], BF16, side="right"))
        b3bar = BAR()
        tW1 = None
        for w_i, wsrc in enumerate((wq1T, wk1T, wv1T)):
            for c in range(NCH):
                tW1 = p.dma("gpsimd", W1[:, (w_i * NCH + c) * 1024:(w_i * NCH + c + 1) * 1024],
                            wsrc[c * 128:(c + 1) * 128, :],
                            deps=b3bar if (w_i == 0 and c == 0) else [])
        p.serial = True
        p._last = t_xn1
        tselq = p.dma("sync", selq_s[:, :], selq[:, :])
        p.dma("sync", lnGr[:, :], gam[:, :])
        p.add("scalar", lambda e: e.activation(lnG[:, :], lnGr[:, :], AF.Ln))
        for c in range(NCH):
            p.add("tensor", lambda e, c=c: e.matmul(
                PSC[0:128, 0:1], selq_s[:, c * 128:(c + 1) * 128], lnG[:, :],
                start=True, stop=True))
            p.add("vector", lambda e, c=c: e.tensor_copy(LG[:, c:c + 1], PSC[0:128, 0:1]))
        p.add("vector", lambda e: e.tensor_scalar_mul(NLG[:, :], LG[:, :], -1.0))
        p.add("gpsimd", lambda e: e.iota(U512[:, :], [[0, 4], [1, 512]], base=0,
                                         channel_multiplier=0,
                                         allow_small_or_imprecise_dtypes=True))
        t_pre = p.add("gpsimd", lambda e: e.iota(U128[:, :], [[0, 16], [1, 128]], base=0,
                                                 channel_multiplier=0,
                                                 allow_small_or_imprecise_dtypes=True))
        p.serial = False
        LN8 = float(np.log(0.125))
        t_qev = {}
        t_qdma = {}
        gidx = 0
        t_Pq = t_Pk = None
        for oc in range(NCH):
            lgb = _ap(LG, oc, [[NCH, 128], [0, T_]])
            nlgb = _ap(NLG, oc, [[NCH, 128], [0, T_]])
            t_s1 = p.add("vector", lambda e, lgb=lgb: e.tensor_tensor(scp[:, :], U512[:, :], lgb, ALU.mult),
                         deps=[t_pre] + ([t_Pq] if t_Pq else []))
            t_Pq = p.add("scalar", lambda e: e.activation(Pq[:, :], scp[:, :], AF.Exp,
                                                          bias=np.float32(LN8)), deps=[t_s1])
            t_s2 = p.add("vector", lambda e, nlgb=nlgb: e.tensor_tensor(scp[:, :], U128[:, :], nlgb, ALU.mult),
                         deps=[t_Pq])
            t_Pk = p.add("scalar", lambda e: e.activation(Pk[:, :], scp[:, :], AF.Exp), deps=[t_s2])
            for w_i, dst, Pt, t_P in ((0, qt1_d, Pq, t_Pq), (1, kt1_d, Pk, t_Pk)):
                par = gidx % 2
                t_mm = None
                ev_in = {}
                for ts in range(4):
                    for c in range(NCH):
                        d = []
                        if ts == 0 and c == 0:
                            d = [tW1, t_xn1] + ([t_qev[gidx - 2]] if gidx >= 2 else [])
                        if c == 0 and ts >= 2:
                            d = d + [ev_in[ts - 2]]
                        t_mm = p.add("tensor", lambda e, w_i=w_i, oc=oc, c=c, ts=ts, par=par: e.matmul(
                            PSA[0:128, par * 1024 + (ts % 2) * 512: par * 1024 + (ts % 2 + 1) * 512],
                            W1[:, (w_i * NCH + c) * 1024 + oc * 128:
                                  (w_i * NCH + c) * 1024 + oc * 128 + 128],
                            xn1[:, c * T_ + ts * 512: c * T_ + (ts + 1) * 512],
                            start=(c == 0), stop=(c == NCH - 1)), deps=d)
                    t_qev[gidx] = p.add("vector", lambda e, ts=ts, Pt=Pt, par=par: e.tensor_tensor(
                        qst[:, par * T_ + ts * 512: par * T_ + (ts + 1) * 512],
                        PSA[0:128, par * 1024 + (ts % 2) * 512: par * 1024 + (ts % 2 + 1) * 512],
                        Pt[:, ts * 512:(ts + 1) * 512], ALU.mult),
                        deps=[t_mm, t_P] + ([t_qdma[gidx - 2]] if gidx >= 2 else []))
                    ev_in[ts] = t_qev[gidx]
                t_qdma[gidx] = p.dma("sync", dst[oc * 128:(oc + 1) * 128, :],
                                     qst[:, par * T_:(par + 1) * T_], deps=[t_qev[gidx]])
                gidx += 1
        t_v1c = {}
        t_v1d = {}
        for tt in range(16):
            par = tt % 2
            t_mm = None
            for os in range(2):
                for c in range(NCH):
                    d = []
                    if os == 0 and c == 0:
                        d = [tW1, t_xn1] + ([t_v1c[tt - 1]] if tt >= 1 else [])
                    t_mm = p.add("tensor", lambda e, os=os, c=c, tt=tt: e.matmul(
                        PSB[0:128, os * 512:(os + 1) * 512],
                        xn1[:, c * T_ + tt * 128: c * T_ + (tt + 1) * 128],
                        W1[:, (2 * NCH + c) * 1024 + os * 512:
                              (2 * NCH + c) * 1024 + (os + 1) * 512],
                        start=(c == 0), stop=(c == NCH - 1)), deps=d)
            t_v1c[tt] = p.add("vector", lambda e, par=par: e.tensor_copy(
                v1st[:, par * 1024:(par + 1) * 1024], PSB[0:128, 0:1024]),
                deps=[t_mm] + ([t_v1d[tt - 2]] if tt >= 2 else []))
            t_v1d[tt] = p.dma("sync", v1_d[tt * 128:(tt + 1) * 128, :],
                              v1st[:, par * 1024:(par + 1) * 1024], deps=[t_v1c[tt]])
        esb4.__exit__(None, None, None)

        MARKS["B5"] = len(p.items)
        esb5 = ExitStack()
        esb5.__enter__()
        wq1 = esb5.enter_context(nc.sbuf_tensor("wq1", [128, NCH * 1024], BF16, side="right"))
        kch = esb5.enter_context(nc.sbuf_tensor("kch", [128, 1024], F32, side="right"))
        tqx = esb5.enter_context(nc.sbuf_tensor("tqx", [128, 1024], F32, side="right"))
        am8 = esb5.enter_context(nc.sbuf_tensor("am8", [128, NCH], F32, side="right"))
        amc = esb5.enter_context(nc.sbuf_tensor("amc", [128, 1], F32, side="right"))
        amr = esb5.enter_context(nc.sbuf_tensor("amr", [1, 128], F32, side="right"))
        am = esb5.enter_context(nc.sbuf_tensor("am", [1, 1], F32, side="right"))
        srow = esb5.enter_context(nc.sbuf_tensor("srow", [1, 1], F32, side="right"))
        rsrow = esb5.enter_context(nc.sbuf_tensor("rsrow", [1, 1], F32, side="right"))
        scol = esb5.enter_context(nc.sbuf_tensor("scol", [128, 1], F32, side="right"))
        rscol = esb5.enter_context(nc.sbuf_tensor("rscol", [128, 1], F32, side="right"))
        a_bf = esb5.enter_context(nc.sbuf_tensor("a_bf", [128, NCH * 16], BF16, side="right"))
        b_bf = esb5.enter_context(nc.sbuf_tensor("b_bf", [16, 1024], BF16, side="right"))
        bias_s = esb5.enter_context(nc.sbuf_tensor("bias_s", [128, NCH], F32, side="right"))
        u_bf = esb5.enter_context(nc.sbuf_tensor("u_bf", [16, T_], BF16, side="right"))
        g1st = esb5.enter_context(nc.sbuf_tensor("g1st", [128, 2 * 512], F32, side="right"))
        p.serial = True
        for c in range(NCH):
            p.dma("sync", kch[:, :], kern1[c * 128:(c + 1) * 128, :])
            p.add("vector", lambda e, c=c: e.reduce_max(
                am8[:, c:c + 1], kch[:, :], AX.X, apply_absolute_value=True))
        p.add("vector", lambda e: e.reduce_max(amc[:, :], am8[:, :], AX.X))
        p.dma("sync", am_d[:, :], amc[:, :])
        p.dma("sync", amr[:, :], _ap(am_d, 0, [[128, 1], [1, 128]]))
        p.add("vector", lambda e: e.reduce_max(am[:, :], amr[:, :], AX.X))
        p.add("vector", lambda e: e.tensor_scalar_mul(srow[:, :], am[:, :], 1.0 / 7.0))
        p.add("vector", lambda e: e.reciprocal(rsrow[:, :], srow[:, :]))
        p.add("tensor", lambda e: e.matmul(PSC[0:128, 0:1], ones_r[:, :], srow[:, :],
                                           start=True, stop=True))
        p.add("vector", lambda e: e.tensor_copy(scol[:, :], PSC[0:128, 0:1]))
        p.add("tensor", lambda e: e.matmul(PSC[0:128, 0:1], ones_r[:, :], rsrow[:, :],
                                           start=True, stop=True))
        p.add("vector", lambda e: e.tensor_copy(rscol[:, :], PSC[0:128, 0:1]))
        for c in range(NCH):
            p.dma("sync", kch[:, :], kern1[c * 128:(c + 1) * 128, :])
            p.add("vector", lambda e: e.tensor_scalar(
                tqx[:, :], kch[:, :], rscol[:, 0:1], MAGIC, ALU.mult, ALU.add))
            p.add("vector", lambda e: e.tensor_scalar(
                kch[:, :], tqx[:, :], MAGIC, -8.0, ALU.subtract, ALU.max))
            p.add("vector", lambda e, c=c: e.tensor_scalar(
                wq1[:, c * 1024:(c + 1) * 1024], kch[:, :], 7.0, scol[:, 0:1],
                ALU.min, ALU.mult))
        for c in range(NCH):
            p.dma("gpsimd", a_bf[:, c * 16:(c + 1) * 16], a1[c * 128:(c + 1) * 128, :])
            p.dma("sync", bias_s[:, c:c + 1], bias1[c * 128:(c + 1) * 128, :])
        p.dma("gpsimd", b_bf[:, :], b1[:, :])
        for ts in range(4):
            for c in range(NCH):
                p.add("tensor", lambda e, c=c, ts=ts: e.matmul(
                    PSC[0:16, :], a_bf[:, c * 16:(c + 1) * 16],
                    xn1[:, c * T_ + ts * 512: c * T_ + (ts + 1) * 512],
                    start=(c == 0), stop=(c == NCH - 1)))
            p.add("vector", lambda e, ts=ts: e.tensor_copy(u_bf[:, ts * 512:(ts + 1) * 512], PSC[0:16, :]))
        t_quant = p._last
        p.serial = False
        t_sil = {}
        t_gd = {}
        i = 0
        for fc in range(NCH):
            for ts in range(4):
                par = i % 2
                t_mm = None
                for c in range(NCH):
                    d = [t_quant] + ([t_sil[i - 2]] if i >= 2 else []) if c == 0 else []
                    t_mm = p.add("tensor", lambda e, fc=fc, ts=ts, c=c, par=par: e.matmul(
                        PSA[0:128, par * 512:(par + 1) * 512],
                        wq1[:, c * 1024 + fc * 128: c * 1024 + fc * 128 + 128],
                        xn1[:, c * T_ + ts * 512: c * T_ + (ts + 1) * 512],
                        start=(c == 0), stop=False), deps=d)
                t_mm = p.add("tensor", lambda e, fc=fc, ts=ts, par=par: e.matmul(
                    PSA[0:128, par * 512:(par + 1) * 512], b_bf[:, fc * 128:(fc + 1) * 128],
                    u_bf[:, ts * 512:(ts + 1) * 512], start=False, stop=True))
                t_sil[i] = p.add("scalar", lambda e, fc=fc, par=par: e.activation(
                    g1st[:, par * 512:(par + 1) * 512], PSA[0:128, par * 512:(par + 1) * 512],
                    AF.Silu, bias=bias_s[:, fc:fc + 1]),
                    deps=[t_mm] + ([t_gd[i - 2]] if i >= 2 else []))
                t_gd[i] = p.dma("sync", g1_d[fc * 128:(fc + 1) * 128, ts * 512:(ts + 1) * 512],
                                g1st[:, par * 512:(par + 1) * 512], deps=[t_sil[i]])
                i += 1
        esb5.__exit__(None, None, None)
        esB.__exit__(None, None, None)

        MARKS["D"] = len(p.items)
        p.new_phase()
        p.add("gpsimd", lambda e: e.collective_compute(
            "AllGather", ALU.bypass, replica_groups=GRP,
            ins=[qt1_d.ap()], outs=[qt1_g.ap()]).then_inc(ccs[2], 1), inc=0)
        p.add("gpsimd", lambda e: e.collective_compute(
            "AllGather", ALU.bypass, replica_groups=GRP,
            ins=[kt1_d.ap()], outs=[kt1_g.ap()]).then_inc(ccs[3], 1), inc=0,
            deps=[(ccs[2], 1)])
        p.add("gpsimd", lambda e: e.collective_compute(
            "AllGather", ALU.bypass, replica_groups=GRP,
            ins=[v1_d.ap()], outs=[v1_g.ap()]).then_inc(ccs[4], 1), inc=0,
            deps=[(ccs[3], 1)])

        esD = ExitStack()
        esD.__enter__()
        qtA = esD.enter_context(nc.sbuf_tensor("qtA", [128, S_], BF16, side="right"))
        qtB = esD.enter_context(nc.sbuf_tensor("qtB", [128, S_], BF16, side="right"))
        ktA = esD.enter_context(nc.sbuf_tensor("ktA", [128, S_], BF16, side="right"))
        ktB = esD.enter_context(nc.sbuf_tensor("ktB", [128, S_], BF16, side="right"))
        Vs = esD.enter_context(nc.sbuf_tensor("Vs", [128, 64 * 256], BF16, side="right"))
        y1s = esD.enter_context(nc.sbuf_tensor("y1s", [128, 2 * S_], BF16, side="right"))
        mg = esD.enter_context(nc.sbuf_tensor("mg", [1, 4 * 67], F32, side="right"))
        gr2 = esD.enter_context(nc.sbuf_tensor("gr2", [1, 4], F32, side="right"))
        lnr2 = esD.enter_context(nc.sbuf_tensor("lnr2", [1, 4], F32, side="right"))
        erow = esD.enter_context(nc.sbuf_tensor("erow", [1, 4 * 67], F32, side="right"))
        crow = esD.enter_context(nc.sbuf_tensor("crow", [1, 4 * 67], F32, side="right"))
        ctab = esD.enter_context(nc.sbuf_tensor("ctab", [128, 4 * 67], F32, side="right"))
        UD = esD.enter_context(nc.sbuf_tensor("UD", [128, 512], F32, side="right"))
        M01 = esD.enter_context(nc.sbuf_tensor("M01", [128, 512], F32, side="right"))
        Wbf = esD.enter_context(nc.sbuf_tensor("Wbf", [128, 4 * 1024], BF16, side="right"))
        rh = nc.gpsimd.alloc_register("rh")
        rh2 = nc.gpsimd.alloc_register("rh2")
        p.serial = True
        p._last = None
        p.add("gpsimd", lambda e: e.reg_load(rh, offs[0:1, 0:1]),
              deps=[(ccs[2], 1), (ccs[3], 1), (ccs[4], 1), tof], inc=0)
        p.add("gpsimd", lambda e: e.reg_mul(rh2, rh, T_), inc=0)
        tqa = p.dma("gpsimd", qtA[:, :], _ap(qt1_g, rh2, [[T_, 128], [E_ * T_, 4], [1, T_]]))
        tkaD = p.dma("gpsimd", ktA[:, :], _ap(kt1_g, rh2, [[T_, 128], [E_ * T_, 4], [1, T_]]))
        p.add("gpsimd", lambda e: e.reg_add(rh2, rh2, 128 * T_), inc=0)
        tqb = p.dma("gpsimd", qtB[:, :], _ap(qt1_g, rh2, [[T_, 128], [E_ * T_, 4], [1, T_]]))
        tkbD = p.dma("gpsimd", ktB[:, :], _ap(kt1_g, rh2, [[T_, 128], [E_ * T_, 4], [1, T_]]))
        tvsD = p.dma("gpsimd", Vs[:, :], _ap(v1_g, rh, [[E_, 128], [T_ * E_, 4], [128 * E_, 16], [1, 256]]))
        p.dma("sync", mg[:, :], mgrid[:, :])
        p.dma("sync", gr2[:, :], _ap(gamh, 0, [[4, 1], [1, 4]]))
        p.add("scalar", lambda e: e.activation(lnr2[:, :], gr2[:, :], AF.Ln))
        lnb = _ap(lnr2, 0, [[4, 1], [1, 4], [0, 67]])
        p.add("vector", lambda e, lnb=lnb: e.tensor_tensor(erow[:, :], mg[:, :], lnb, ALU.mult))
        p.add("scalar", lambda e: e.activation(crow[:, :], erow[:, :], AF.Exp))
        p.add("tensor", lambda e: e.matmul(PSB[0:128, 0:268], ones_r[:, :], crow[:, :],
                                           start=True, stop=True))
        p.add("vector", lambda e: e.tensor_copy(ctab[:, :], PSB[0:128, 0:268]))
        p.add("gpsimd", lambda e: e.iota(UD[:, :], [[1, 512]], base=0,
                                         channel_multiplier=-1,
                                         allow_small_or_imprecise_dtypes=True))
        t_setupD = p.add("vector", lambda e: e.tensor_single_scalar(M01[:, :], UD[:, :], 0.0, ALU.is_ge))
        p.serial = False
        bpD = 0
        tevD = {}
        tav2 = {}
        t_yev = None
        for hlp in range(2):
            kt = ktA if hlp == 0 else ktB
            qt = qtA if hlp == 0 else qtB
            for ts in range(16):
                nsb = 4 * ts + 4
                t_av = None

                def em_avD(u, hlp=hlp, nsb=nsb):
                    nonlocal t_av
                    sb = u - bpD
                    wq = u % 4
                    for hh in (0, 1):
                        hl = 2 * hlp + hh
                        d = [tevD[(u, hh)]]
                        if sb == 0:
                            d = d + [tvsD] + ([t_yev] if t_yev else [])
                        t_av = p.add("tensor",
                            lambda e, sb=sb, hl=hl, nsb=nsb, wq=wq, hh=hh: e.matmul(
                                PSB[0:64, hh * 512:(hh + 1) * 512],
                                Vs[:, sb * 256 + hl * 64: sb * 256 + (hl + 1) * 64],
                                Wbf[:, wq * 1024 + hh * 512: wq * 1024 + (hh + 1) * 512],
                                start=(sb == 0), stop=(sb == nsb - 1)), deps=d)
                        tav2[(u, hh)] = t_av

                for sb in range(nsb):
                    u = bpD + sb
                    rg = u % 2
                    d = ([tevD[(u - 2, 0)], tevD[(u - 2, 1)]] if u >= 2
                         else [t_setupD, tqa, tkaD, tqb, tkbD])
                    p.add("tensor",
                        lambda e, kt=kt, qt=qt, sb=sb, ts=ts, rg=rg: e.matmul(
                            PSA[0:128, rg * 1024: rg * 1024 + 512],
                            kt[0:64, sb * 128:(sb + 1) * 128],
                            qt[0:64, ts * 512:(ts + 1) * 512],
                            start=True, stop=True), deps=d if sb == 0 or u >= 2 else [])
                    tm = p.add("tensor",
                        lambda e, kt=kt, qt=qt, sb=sb, ts=ts, rg=rg: e.matmul(
                            PSA[0:128, rg * 1024 + 512: rg * 1024 + 1024],
                            kt[64:128, sb * 128:(sb + 1) * 128],
                            qt[64:128, ts * 512:(ts + 1) * 512],
                            start=True, stop=True))
                    wq = u % 4
                    for hh in (0, 1):
                        hl = 2 * hlp + hh
                        ed = [tm] + ([tav2[(u - 4, hh)]] if u >= 4 else [])
                        if sb < 4 * ts:
                            idx = hl * 67 + (4 * ts - sb) + 3
                            if hh == 1:
                                tevD[(u, hh)] = p.add("scalar",
                                    lambda e, idx=idx, rg=rg, hh=hh, wq=wq: e.activation(
                                        Wbf[:, wq * 1024 + hh * 512: wq * 1024 + (hh + 1) * 512],
                                        PSA[0:128, rg * 1024 + hh * 512: rg * 1024 + (hh + 1) * 512],
                                        AF.Copy, scale=ctab[:, idx:idx + 1]), deps=ed)
                            else:
                                tevD[(u, hh)] = p.add("vector",
                                    lambda e, idx=idx, rg=rg, hh=hh, wq=wq: e.tensor_scalar_mul(
                                        Wbf[:, wq * 1024 + hh * 512: wq * 1024 + (hh + 1) * 512],
                                        PSA[0:128, rg * 1024 + hh * 512: rg * 1024 + (hh + 1) * 512],
                                        ctab[:, idx:idx + 1]), deps=ed)
                        else:
                            j = sb - 4 * ts
                            idx = hl * 67 + 3 - j
                            if j > 0:
                                p.add("vector", lambda e, j=j, wq=wq, hh=hh: e.memset(
                                    Wbf[:, wq * 1024 + hh * 512: wq * 1024 + hh * 512 + j * 128],
                                    0.0), deps=ed)
                            tevD[(u, hh)] = p.add("vector",
                                lambda e, j=j, idx=idx, rg=rg, hh=hh, wq=wq: e.scalar_tensor_tensor(
                                    Wbf[:, wq * 1024 + hh * 512 + j * 128: wq * 1024 + (hh + 1) * 512],
                                    PSA[0:128, rg * 1024 + hh * 512 + j * 128: rg * 1024 + (hh + 1) * 512],
                                    ctab[:, idx:idx + 1], M01[:, 0:512 - j * 128],
                                    ALU.mult, ALU.mult), deps=ed)
                    if sb >= 1:
                        em_avD(u - 1)
                em_avD(bpD + nsb - 1)
                bpD += nsb
                t_y0 = p.add("vector", lambda e, hlp=hlp, ts=ts: e.tensor_copy(
                    y1s[0:64, hlp * S_ + ts * 512: hlp * S_ + (ts + 1) * 512],
                    PSB[0:64, 0:512]), deps=[tav2[(bpD - 1, 0)], tav2[(bpD - 1, 1)]])
                t_yev = p.add("vector", lambda e, hlp=hlp, ts=ts: e.tensor_copy(
                    y1s[64:128, hlp * S_ + ts * 512: hlp * S_ + (ts + 1) * 512],
                    PSB[0:64, 512:1024]), deps=[tav2[(bpD - 1, 1)]])
        ty1a = p.dma("sync", y1_d[0:128, :], y1s[:, 0:S_], deps=[t_yev])
        ty1b = p.dma("sync", y1_d[128:256, :], y1s[:, S_:2 * S_], deps=[t_yev])
        esD.__exit__(None, None, None)

        MARKS["E"] = len(p.items)
        p.new_phase()
        p.add("gpsimd", lambda e: e.collective_compute(
            "AllGather", ALU.bypass, replica_groups=GRP,
            ins=[y1_d.ap()], outs=[y1_g.ap()]).then_inc(ccs[5], 1), inc=0)

        esE = ExitStack()
        esE.__enter__()
        y1_sb = esE.enter_context(nc.sbuf_tensor("y1_sb", [128, NCH * T_], BF16, side="right"))
        wo1_bf = esE.enter_context(nc.sbuf_tensor("wo1_bf", [128, NCH * 1024], BF16, side="right"))
        xsl2 = esE.enter_context(nc.sbuf_tensor("xsl2", [128, 2 * 512], F32, side="right"))
        gsl2 = esE.enter_context(nc.sbuf_tensor("gsl2", [128, 2 * 512], F32, side="right"))
        tsl2 = esE.enter_context(nc.sbuf_tensor("tsl2", [128, 512], F32, side="right"))
        osl2 = esE.enter_context(nc.sbuf_tensor("osl2", [128, 2 * 512], F32, side="right"))
        rt = nc.gpsimd.alloc_register("rt")
        rt2 = nc.gpsimd.alloc_register("rt2")
        p.add("gpsimd", lambda e: e.reg_load(rt, offs[0:1, 1:2]),
              deps=[(ccs[5], 1), tof2], inc=0)
        ty1l = None
        for c in range(NCH):
            p.add("gpsimd", lambda e, c=c: e.reg_add(rt2, rt, c * 128 * S_), inc=0)
            ty1l = p.dma("gpsimd", y1_sb[:, c * T_:(c + 1) * T_],
                         _ap(y1_g, rt2, [[S_, 128], [1, T_]]))
            p.dma("gpsimd", wo1_bf[:, c * 1024:(c + 1) * 1024], wo1T[c * 128:(c + 1) * 128, :])
        tWo1 = p.tok("gpsimd")
        t_addE = {}
        t_mulE = {}
        i = 0
        for oc in range(NCH):
            for ts in range(4):
                par = i % 2
                t_mm = None
                for c in range(NCH):
                    d = [tWo1] if (c == 0) else []
                    if c == 0 and i >= 1:
                        d.append(t_mulE[i - 1])
                    t_mm = p.add("tensor", lambda e, oc=oc, ts=ts, c=c: e.matmul(
                        PSA[0:128, 0:512],
                        wo1_bf[:, c * 1024 + oc * 128: c * 1024 + oc * 128 + 128],
                        y1_sb[:, c * T_ + ts * 512: c * T_ + (ts + 1) * 512],
                        start=(c == 0), stop=(c == NCH - 1)), deps=d)
                txd2 = p.dma("sync", xsl2[:, par * 512:(par + 1) * 512],
                             x1_d[oc * 128:(oc + 1) * 128, ts * 512:(ts + 1) * 512],
                             deps=[t_addE[i - 2]] if i >= 2 else [])
                tgd2 = p.dma("sync", gsl2[:, par * 512:(par + 1) * 512],
                             g1_d[oc * 128:(oc + 1) * 128, ts * 512:(ts + 1) * 512],
                             deps=[t_mulE[i - 2]] if i >= 2 else [])
                t_mulE[i] = p.add("vector", lambda e, par=par: e.tensor_tensor(
                    tsl2[:, :], PSA[0:128, 0:512], gsl2[:, par * 512:(par + 1) * 512], ALU.mult),
                    deps=[t_mm, tgd2])
                t_addE[i] = p.add("vector", lambda e, par=par: e.tensor_tensor(
                    osl2[:, par * 512:(par + 1) * 512], tsl2[:, :],
                    xsl2[:, par * 512:(par + 1) * 512], ALU.add), deps=[txd2])
                p.dma("sync", x2T_o[oc * 128:(oc + 1) * 128, ts * 512:(ts + 1) * 512],
                      osl2[:, par * 512:(par + 1) * 512], deps=[t_addE[i]])
                i += 1
        esE.__exit__(None, None, None)

        if DEBUG_FULL:
            dbg_srcs = dict(qT_d=qT_d, kT_d=kT_d, g0_d=g0_d, x1_d=x1_d,
                            qt1_d=qt1_d, kt1_d=kt1_d, v1_d=v1_d, g1_d=g1_d,
                            y1_d=y1_d)
            b0 = [p.tok(e) for e in P2.ENGS]
            for nm, t in dbg_srcs.items():
                dt = {"g0_d": F32, "x1_d": F32, "g1_d": F32}.get(nm, BF16)
                sh = [t.shape[0], t.shape[1]] if hasattr(t, "shape") else None
                od = nc.dram_tensor("dbg_" + nm, list(t.shape), dt, kind="ExternalOutput")
                p.dma("sync", od.ap(), t.ap(), deps=b0)

        MARKS["END"] = len(p.items)
        with allow_ici_all_to_all():
            p.emit(block)
    return nc


def _rope_tables_host():
    import jax.numpy as jnp
    with jax.default_device(jax.devices("cpu")[0]):
        idx = jnp.arange(HD_ // 2, dtype=jnp.float32)
        ang = jnp.power(jnp.asarray(10000.0, jnp.float32), 2.0 * idx / HD_)
        th = jnp.arange(S_, dtype=jnp.float32)[:, None] * ang[None, :]
        return np.asarray(jnp.cos(th)), np.asarray(jnp.sin(th))


def kernel(**inputs):
    asc = np.ascontiguousarray
    cores = list(range(NCORE))

    cos, sin = _rope_tables_host()
    idf = np.eye(128, dtype=np.float32)
    onesr = np.ones((1, 128), np.float32)
    onesc = np.ones((128, 1), np.float32)
    selq = np.zeros((16, 1024), np.float32)
    for c in range(NCH):
        selq[2 * c, c * 128:c * 128 + 64] = 1.0
        selq[2 * c + 1, c * 128 + 64:(c + 1) * 128] = 1.0
    mgrid = np.concatenate(
        [128.0 * np.arange(-3, 64, dtype=np.float32)[None, :]] * 4, axis=1)

    x = np.asarray(inputs["L0_attn_x"], np.float32)
    wqT = asc(inputs["L0_attn_w_q"].T)
    wkT = asc(inputs["L0_attn_w_k"].T)
    wvT = asc(inputs["L0_attn_w_v"].T)
    woT = asc(inputs["L0_attn_w_o"].T)
    selT = asc(inputs["L0_ffn_expert_sel"].T)
    keys = np.asarray(inputs["L0_ffn_keys"], np.float32)
    values = np.asarray(inputs["L0_ffn_values"], np.float32)
    wq1T = asc(inputs["L1_attn_wq"].T)
    wk1T = asc(inputs["L1_attn_wk"].T)
    wv1T = asc(inputs["L1_attn_wv"].T)
    wo1T = asc(inputs["L1_attn_wo"].T)
    gammas = np.asarray(inputs["L1_attn_gammas"], np.float32)
    kern1 = np.asarray(inputs["L1_ffn_kernel"], np.float32)
    bias1 = asc(inputs["L1_ffn_bias"].reshape(E_, 1))
    a1 = np.asarray(inputs["L1_ffn_a"], np.float32)
    b1 = np.asarray(inputs["L1_ffn_b"], np.float32)

    def core_bt(c):
        return c // 4, 2048 * (c % 4)

    in_maps = []
    for c in cores:
        b, t0 = core_bt(c)
        sl = slice(t0, t0 + T_)
        hg0 = 4 * (c % 4)
        in_maps.append(dict(
            xT=asc(x[b, sl].T), wqT=wqT, wkT=wkT, wvT=wvT,
            selT=selT, keys=keys, values=values,
            cosL=asc(cos[sl]), sinL=asc(sin[sl]),
            idf=idf, onesr=onesr, onesc=onesc,
            woT=woT, wq1T=wq1T, wk1T=wk1T, wv1T=wv1T, wo1T=wo1T,
            kern1=kern1, bias1=bias1, a1=a1, b1=b1,
            gam=asc(gammas.reshape(16, 1)), selq=selq,
            gamh=asc(gammas[hg0:hg0 + 4].reshape(4, 1)), mgrid=mgrid,
            hoff=np.array([[hg0 * 64]], np.int32),
            toff=np.array([[t0]], np.int32)))
    res = run_bass_kernel_spmd(build_full(), in_maps, core_ids=cores).results

    out = np.empty((B_, S_, E_), np.float32)
    for c in cores:
        b, t0 = core_bt(c)
        out[b, t0:t0 + T_, :] = np.asarray(res[c]["x2T_o"]).T
    return out


# version 21
# speedup vs baseline: 1.0260x; 1.0126x over previous
import numpy as np
import ml_dtypes
import jax
import concourse.bass as bass
import concourse.mybir as mybir
from concourse.bass_utils import run_bass_kernel_spmd

F32 = mybir.dt.float32
BF16 = mybir.dt.bfloat16
AF = mybir.ActivationFunctionType
ALU = mybir.AluOpType
AX = mybir.AxisListType

B_, S_, E_, H_ = 2, 8192, 1024, 16
HD_ = 64
NCORE = 8
T_ = 2048
NCH = 8
MAGIC = 12582912.0


class Prog:

    def __init__(self, nc):
        self.nc = nc
        self.items = []
        self.phase = 0

    def new_phase(self):
        self.phase += 1

    def add(self, eng, fn, inc=1, pre=()):
        self.items.append((eng, fn, inc, self.phase, tuple(pre)))

    def dma(self, eng, out, in_, pre=()):
        self.add(eng, (lambda e, o=out, i=in_: e.dma_start(out=o, in_=i)), 16, pre)

    def emit(self, block, sems):
        nph = self.phase + 1
        assert len(sems) >= nph
        cum = [0] * nph
        plan = []
        for eng, fn, inc, ph, pre in self.items:
            w = [(sems[ph], cum[ph])] if cum[ph] > 0 else []
            if cum[ph] == 0 and ph > 0:
                w.append((sems[ph - 1], cum[ph - 1]))
            w.extend(pre)
            plan.append((eng, fn, inc, sems[ph], w))
            cum[ph] += inc
        used = sorted({en for en, *_ in plan})
        final = (sems[self.phase], cum[self.phase])

        def mk(engname):
            def emitter(eng):
                for en, fn, inc, sem, waits in plan:
                    if en != engname:
                        continue
                    for ws, wv in waits:
                        eng.wait_ge(ws, wv)
                    r = fn(eng)
                    if inc:
                        r.then_inc(sem, inc)
                if engname == used[0]:
                    eng.wait_ge(final[0], final[1])
            return emitter
        for name in used:
            getattr(block, name)(mk(name))


def _ap(t, off, dims):
    return bass.AP(t, off, dims)


class P2:
    ENGS = ("sync", "scalar", "vector", "tensor", "gpsimd")

    def __init__(self, nc):
        self.nc = nc
        self.items = []
        self.phase = 0
        self.cnt = {}
        self.serial = False
        self._last = None

    def tok(self, eng):
        return (self.phase, eng, self.cnt.get((self.phase, eng), 0))

    def new_phase(self):
        self.phase += 1

    def add(self, eng, fn, inc=1, deps=()):
        deps = list(deps)
        if self.serial and self._last is not None:
            deps.append(self._last)
        ph = self.phase
        c = self.cnt.get((ph, eng), 0) + inc
        self.cnt[(ph, eng)] = c
        self.items.append((eng, fn, inc, ph, tuple(deps)))
        t = (ph, eng, c, inc)
        self._last = t
        return t

    def dma(self, eng, out, in_, deps=()):
        return self.add(eng, (lambda e, o=out, i=in_: e.dma_start(out=o, in_=i)),
                        16, deps)

    SERIALIZE_ALL = False
    SERIALIZE_PHASES = set()
    SERIALIZE_RANGES = []

    def emit(self, block):
        nc = self.nc
        nph = self.phase + 1
        sems = {}
        guards = []
        for ph in range(nph):
            for eng in self.ENGS:
                if self.cnt.get((ph, eng), 0) > 0:
                    g = nc.semaphore(f"p{ph}_{eng}")
                    sems[(ph, eng)] = g.__enter__()
                    guards.append(g)
        finals = {}
        for ph in range(nph):
            finals[ph] = [(ph, eng, self.cnt[(ph, eng)], 1)
                          for eng in self.ENGS if self.cnt.get((ph, eng), 0) > 0]
        seen = set()
        plan = []
        cum = {}
        prev_cmp = {}
        for eng, fn, inc, ph, deps in self.items:
            waits = []
            if (ph, eng) not in seen:
                seen.add((ph, eng))
                if ph > 0:
                    waits.extend((p_, e_, c_) for p_, e_, c_, *_ in finals[ph - 1])
            if inc != 16 and eng in ("vector", "scalar", "gpsimd"):
                pc = prev_cmp.get(eng)
                if pc is not None:
                    waits.append(pc)
                if inc == 1:
                    prev_cmp[eng] = (ph, eng, cum.get((ph, eng), 0) + inc)
            for d in deps:
                if isinstance(d[0], int):
                    waits.append((d[0], d[1], d[2]))
                else:
                    waits.append(d)
            idx = len(plan)
            in_rng = any(lo <= idx < hi for lo, hi in self.SERIALIZE_RANGES)
            if (self.SERIALIZE_ALL or ph in self.SERIALIZE_PHASES or in_rng) and plan:
                pe, _, pinc, pph, _ = plan[-1]
                waits.append((pph, pe, cum.get((pph, pe), 0)))
            plan.append((eng, fn, inc, ph, waits))
            cum[(ph, eng)] = cum.get((ph, eng), 0) + inc
        used = sorted({e for e, *_ in plan})
        lastph = self.phase

        def mk(engname):
            def emitter(eng):
                for en, fn, inc, ph, waits in plan:
                    if en != engname:
                        continue
                    dedup = {}
                    for w in waits:
                        if isinstance(w[0], int):
                            key = ("s", w[0], w[1])
                            dedup[key] = max(dedup.get(key, 0), w[2])
                        else:
                            key = ("r", id(w[0]))
                            dedup[key] = (w[0], max(
                                dedup[key][1], w[1]) if key in dedup else w[1])
                    for key, v in dedup.items():
                        if key[0] == "s":
                            if v > 0 and (key[1], key[2]) in sems:
                                eng.wait_ge(sems[(key[1], key[2])], v)
                        else:
                            eng.wait_ge(v[0], v[1])
                    r = fn(eng)
                    if inc:
                        r.then_inc(sems[(ph, en)], inc)
                if engname == used[0]:
                    for (p_, e_, c_, *_x) in finals[lastph]:
                        eng.wait_ge(sems[(p_, e_)], c_)
            return emitter
        for name in used:
            getattr(block, name)(mk(name))
        return guards


DEBUG_A = False
DEBUG_FULL = False
MARKS = {}
A4_NOLAG = False


def build_A():
    nc = bass.Bass("TRN2", target_bir_lowering=False)
    xT = nc.dram_tensor("xT", [E_, T_], F32, kind="ExternalInput")
    wqT = nc.dram_tensor("wqT", [E_, E_], F32, kind="ExternalInput")
    wkT = nc.dram_tensor("wkT", [E_, E_], F32, kind="ExternalInput")
    wvT = nc.dram_tensor("wvT", [E_, E_], F32, kind="ExternalInput")
    selT = nc.dram_tensor("selT", [E_, 64], F32, kind="ExternalInput")
    keys = nc.dram_tensor("keys", [64, E_, 128], F32, kind="ExternalInput")
    values = nc.dram_tensor("values", [64, 128, E_], F32, kind="ExternalInput")
    cosL = nc.dram_tensor("cosL", [T_, 32], F32, kind="ExternalInput")
    sinL = nc.dram_tensor("sinL", [T_, 32], F32, kind="ExternalInput")
    idf = nc.dram_tensor("idf", [128, 128], F32, kind="ExternalInput")
    onesr = nc.dram_tensor("onesr", [1, 128], F32, kind="ExternalInput")
    onesc = nc.dram_tensor("onesc", [128, 1], F32, kind="ExternalInput")

    qT_o = nc.dram_tensor("qT_o", [E_, T_], BF16, kind="ExternalOutput")
    kT_o = nc.dram_tensor("kT_o", [E_, T_], BF16, kind="ExternalOutput")
    v_o = nc.dram_tensor("v_o", [T_, E_], BF16, kind="ExternalOutput")
    g_o = nc.dram_tensor("g_o", [E_, T_], F32, kind="ExternalOutput")
    hg_d = nc.dram_tensor("hg_d", [T_, 64 * 128], BF16,
                          kind="ExternalOutput" if DEBUG_A else "Internal")

    from contextlib import ExitStack
    es = ExitStack()
    with es:
        xn_bf = es.enter_context(nc.sbuf_tensor("xn_bf", [128, NCH * T_], BF16))
        gates = es.enter_context(nc.sbuf_tensor("gates", [128, 16 * 64], F32))
        id_f = es.enter_context(nc.sbuf_tensor("id_f", [128, 128], F32))
        id_bf = es.enter_context(nc.sbuf_tensor("id_bf", [128, 128], BF16))
        ones_r = es.enter_context(nc.sbuf_tensor("ones_r", [1, 128], F32))
        ones_c = es.enter_context(nc.sbuf_tensor("ones_c", [128, 1], F32))
        PSA = es.enter_context(nc.psum_tensor("PSA", [128, 2048], F32))
        PSB = es.enter_context(nc.psum_tensor("PSB", [128, 1024], F32))
        PSC = es.enter_context(nc.psum_tensor("PSC", [128, 512], F32))
        PST = es.enter_context(nc.psum_tensor("PST", [128, 1024], BF16))

        sem = es.enter_context(nc.semaphore("step"))
        block = es.enter_context(nc.Block())
        p = Prog(nc)

        es1 = ExitStack()
        es1.__enter__()
        selT_s = es1.enter_context(nc.sbuf_tensor("selT_s", [128, NCH * 64], F32))
        mu_rep = es1.enter_context(nc.sbuf_tensor("mu_rep", [128, T_], F32))
        rs_rep = es1.enter_context(nc.sbuf_tensor("rs_rep", [128, T_], F32))
        xch = es1.enter_context(nc.sbuf_tensor("xch", [128, T_], F32))
        sq = es1.enter_context(nc.sbuf_tensor("sq", [128, T_], F32))
        row_a = es1.enter_context(nc.sbuf_tensor("row_a", [1, T_], F32))
        row_b = es1.enter_context(nc.sbuf_tensor("row_b", [1, T_], F32))
        row_c = es1.enter_context(nc.sbuf_tensor("row_c", [1, 512], F32))
        row_d = es1.enter_context(nc.sbuf_tensor("row_d", [1, 512], F32))
        sels = es1.enter_context(nc.sbuf_tensor("sels", [64, T_], F32))
        seltm = es1.enter_context(nc.sbuf_tensor("seltm", [128, 64], F32))
        wk1 = es1.enter_context(nc.sbuf_tensor("wk1", [128, 64], F32))
        wk2 = es1.enter_context(nc.sbuf_tensor("wk2", [128, 64], F32))
        mcol = es1.enter_context(nc.sbuf_tensor("mcol", [128, 1], F32))

        p.dma("sync", id_f[:, :], idf[:, :])
        p.dma("gpsimd", id_bf[:, :], idf[:, :])
        p.dma("sync", ones_r[:, :], onesr[:, :])
        p.dma("sync", ones_c[:, :], onesc[:, :])
        for c in range(NCH):
            p.dma("sync", selT_s[:, c * 64:(c + 1) * 64],
                  selT[c * 128:(c + 1) * 128, :])

        for ts in range(4):
            for c in range(NCH):
                p.dma("sync", xch[:, 0:512],
                      xT[c * 128:(c + 1) * 128, ts * 512:(ts + 1) * 512])
                p.add("scalar", lambda e, c=c: e.activation(
                    sq[:, 0:512], xch[:, 0:512], AF.Square))
                p.add("tensor", lambda e, c=c: e.matmul(
                    PSC[0:1, :], ones_c[:, :], xch[:, 0:512],
                    start=(c == 0), stop=(c == NCH - 1)))
                p.add("tensor", lambda e, c=c: e.matmul(
                    PSB[0:1, 0:512], ones_c[:, :], sq[:, 0:512],
                    start=(c == 0), stop=(c == NCH - 1)))
            sl = slice(ts * 512, (ts + 1) * 512)
            p.add("vector", lambda e, sl=sl: e.tensor_scalar_mul(
                row_a[:, sl], PSC[0:1, :], 1.0 / 1024.0))
            p.add("vector", lambda e: e.tensor_scalar_mul(
                row_c[:, :], PSB[0:1, 0:512], 1.0 / 1024.0))
            p.add("vector", lambda e, sl=sl: e.tensor_tensor(
                row_d[:, :], row_a[:, sl], row_a[:, sl], ALU.mult))
            p.add("vector", lambda e: e.tensor_tensor(
                row_c[:, :], row_c[:, :], row_d[:, :], ALU.subtract))
            p.add("vector", lambda e: e.tensor_scalar_add(
                row_c[:, :], row_c[:, :], 1e-5))
            p.add("scalar", lambda e: e.activation(
                row_d[:, :], row_c[:, :], AF.Sqrt))
            p.add("vector", lambda e, sl=sl: e.reciprocal(row_b[:, sl], row_d[:, :]))
            p.add("tensor", lambda e, sl=sl: e.matmul(
                PSB[0:128, 0:512], ones_r[:, :], row_a[:, sl], start=True, stop=True))
            p.add("vector", lambda e, sl=sl: e.tensor_copy(
                mu_rep[:, sl], PSB[0:128, 0:512]))
            p.add("tensor", lambda e, sl=sl: e.matmul(
                PSC[0:128, :], ones_r[:, :], row_b[:, sl], start=True, stop=True))
            p.add("vector", lambda e, sl=sl: e.tensor_copy(
                rs_rep[:, sl], PSC[0:128, :]))

        for c in range(NCH):
            p.dma("sync", xch[:, :], xT[c * 128:(c + 1) * 128, :])
            p.add("vector", lambda e: e.tensor_tensor(
                sq[:, :], xch[:, :], mu_rep[:, :], ALU.subtract))
            p.add("vector", lambda e: e.tensor_tensor(
                xch[:, :], sq[:, :], rs_rep[:, :], ALU.mult))
            p.add("vector", lambda e, c=c: e.tensor_copy(
                xn_bf[:, c * T_:(c + 1) * T_], xch[:, :]))
            for ts in range(4):
                p.add("tensor", lambda e, c=c, ts=ts: e.matmul(
                    PSA[0:64, ts * 512:(ts + 1) * 512],
                    selT_s[:, c * 64:(c + 1) * 64],
                    xch[:, ts * 512:(ts + 1) * 512],
                    start=(c == 0), stop=(c == NCH - 1)))

        p.add("scalar", lambda e: e.activation(sels[:, :], PSA[0:64, :], AF.Sigmoid))
        for tt in range(16):
            p.add("tensor", lambda e, tt=tt: e.transpose(
                PSB[0:128, 0:64], sels[0:64, tt * 128:(tt + 1) * 128], id_f[0:64, 0:64]))
            p.add("vector", lambda e: e.tensor_copy(seltm[:, :], PSB[0:128, 0:64]))
            p.add("vector", lambda e: e.tensor_copy(wk1[:, :], seltm[:, :]))
            for r in range(4):
                src = wk1 if r % 2 == 0 else wk2
                dst = wk2 if r % 2 == 0 else wk1
                p.add("vector", lambda e, src=src: e.reduce_max(
                    mcol[:, :], src[:, :], AX.X))
                if r < 3:
                    mb = _ap(mcol, 0, [[1, 128], [0, 64]])
                    p.add("vector", lambda e, src=src, dst=dst, mb=mb: e.tensor_tensor(
                        dst[:, :], src[:, :], mb, ALU.is_equal))
                    p.add("vector", lambda e, src=src, dst=dst: e.scalar_tensor_tensor(
                        dst[:, :], dst[:, :], -1e30, src[:, :], ALU.mult, ALU.add))
            mb = _ap(mcol, 0, [[1, 128], [0, 64]])
            p.add("vector", lambda e, mb=mb: e.tensor_tensor(
                wk1[:, :], seltm[:, :], mb, ALU.is_ge))
            p.add("vector", lambda e, tt=tt: e.tensor_tensor(
                gates[:, tt * 64:(tt + 1) * 64], seltm[:, :], wk1[:, :], ALU.mult))
        if DEBUG_A:
            sels_o = nc.dram_tensor("sels_o", [64, T_], F32, kind="ExternalOutput")
            gates_o = nc.dram_tensor("gates_o", [128, 1024], F32, kind="ExternalOutput")
            p.dma("sync", sels_o[:, :], sels[:, :])
            p.dma("sync", gates_o[:, :], gates[:, :])
        es1.__exit__(None, None, None)

        es2 = ExitStack()
        with es2:
            W_sb = es2.enter_context(nc.sbuf_tensor("W_sb", [128, 3 * NCH * 1024], BF16))
            qT_sb = es2.enter_context(nc.sbuf_tensor("qT_sb", [128, NCH * T_], BF16))
            kT_sb = es2.enter_context(nc.sbuf_tensor("kT_sb", [128, NCH * T_], BF16))
            qf = es2.enter_context(nc.sbuf_tensor("qf", [128, 1024], F32))
            qr = es2.enter_context(nc.sbuf_tensor("qr", [128, 1024], F32))
            qr_bf = es2.enter_context(nc.sbuf_tensor("qr_bf", [128, 1024], BF16))
            t1s = es2.enter_context(nc.sbuf_tensor("t1s", [128, 512], F32))
            t2s = es2.enter_context(nc.sbuf_tensor("t2s", [128, 512], F32))
            ct = es2.enter_context(nc.sbuf_tensor("ct", [128, 32], F32))
            st = es2.enter_context(nc.sbuf_tensor("st", [128, 32], F32))
            vstage = es2.enter_context(nc.sbuf_tensor("vstage", [128, 1024], BF16))

            for w_i, wsrc in enumerate((wqT, wkT, wvT)):
                for c in range(NCH):
                    p.dma("gpsimd", W_sb[:, (w_i * NCH + c) * 1024:(w_i * NCH + c + 1) * 1024],
                          wsrc[c * 128:(c + 1) * 128, :])

            cosap = _ap(ct, 0, [[32, 128], [0, 16], [1, 32]])
            sinap = _ap(st, 0, [[32, 128], [0, 16], [1, 32]])

            def rope(p, src, dst):
                ev = lambda t, o=0: _ap(t, o, [[1024, 128], [64, 16], [2, 32]])
                p.add("vector", lambda e: e.tensor_tensor(t1s[:, :], ev(src), cosap, ALU.mult))
                p.add("vector", lambda e: e.tensor_tensor(t2s[:, :], ev(src, 1), sinap, ALU.mult))
                p.add("vector", lambda e: e.tensor_tensor(ev(dst), t1s[:, :], t2s[:, :], ALU.subtract))
                p.add("vector", lambda e: e.tensor_tensor(t1s[:, :], ev(src), sinap, ALU.mult))
                p.add("vector", lambda e: e.tensor_tensor(t2s[:, :], ev(src, 1), cosap, ALU.mult))
                p.add("vector", lambda e: e.tensor_tensor(ev(dst, 1), t1s[:, :], t2s[:, :], ALU.add))

            for tt in range(16):
                tsl = slice(tt * 128, (tt + 1) * 128)
                p.dma("sync", ct[:, :], cosL[tsl, :])
                p.dma("sync", st[:, :], sinL[tsl, :])
                for w_i, dstT in ((0, qT_sb), (1, kT_sb)):
                    for os in range(2):
                        for c in range(NCH):
                            p.add("tensor", lambda e, w_i=w_i, os=os, c=c, tt=tt: e.matmul(
                                PSA[0:128, w_i * 1024 + os * 512: w_i * 1024 + (os + 1) * 512],
                                xn_bf[:, c * T_ + tt * 128: c * T_ + (tt + 1) * 128],
                                W_sb[:, (w_i * NCH + c) * 1024 + os * 512:
                                        (w_i * NCH + c) * 1024 + (os + 1) * 512],
                                start=(c == 0), stop=(c == NCH - 1)))
                    p.add("scalar", lambda e, w_i=w_i: e.activation(
                        qf[:, :], PSA[0:128, w_i * 1024:(w_i + 1) * 1024], AF.Copy))
                    rope(p, qf, qr)
                    p.add("vector", lambda e: e.tensor_copy(qr_bf[:, :], qr[:, :]))
                    for c8 in range(NCH):
                        p.add("tensor", lambda e, c8=c8: e.transpose(
                            PST[:, c8 * 128:(c8 + 1) * 128],
                            qr_bf[:, c8 * 128:(c8 + 1) * 128], id_bf[:, :]))
                    for c8 in range(NCH):
                        p.add("vector", lambda e, c8=c8, tt=tt, dstT=dstT: e.tensor_copy(
                            dstT[:, c8 * T_ + tt * 128: c8 * T_ + (tt + 1) * 128],
                            PST[:, c8 * 128:(c8 + 1) * 128]))
                for os in range(2):
                    for c in range(NCH):
                        p.add("tensor", lambda e, os=os, c=c, tt=tt: e.matmul(
                            PSB[0:128, os * 512:(os + 1) * 512],
                            xn_bf[:, c * T_ + tt * 128: c * T_ + (tt + 1) * 128],
                            W_sb[:, (2 * NCH + c) * 1024 + os * 512:
                                    (2 * NCH + c) * 1024 + (os + 1) * 512],
                            start=(c == 0), stop=(c == NCH - 1)))
                p.add("vector", lambda e: e.tensor_copy(vstage[:, :], PSB[0:128, :]))
                p.dma("sync", v_o[tsl, :], vstage[:, :])
            p.dma("sync",
                  _ap(qT_o, 0, [[T_, 128], [128 * T_, NCH], [1, T_]]),
                  _ap(qT_sb, 0, [[NCH * T_, 128], [T_, NCH], [1, T_]]))
            p.dma("sync",
                  _ap(kT_o, 0, [[T_, 128], [128 * T_, NCH], [1, T_]]),
                  _ap(kT_sb, 0, [[NCH * T_, 128], [T_, NCH], [1, T_]]))

        es3 = ExitStack()
        with es3:
            keys4 = es3.enter_context(nc.sbuf_tensor("keys4", [128, NCH * 512], BF16))
            hg_bf = es3.enter_context(nc.sbuf_tensor("hg_bf", [128, 512], BF16))
            for kg in range(16):
                for c in range(NCH):
                    p.dma("gpsimd", keys4[:, c * 512:(c + 1) * 512],
                          _ap(keys, (4 * kg) * E_ * 128 + c * 128 * 128,
                              [[128, 128], [E_ * 128, 4], [1, 128]]))
                for tt in range(16):
                    for c in range(NCH):
                        p.add("tensor", lambda e, c=c, tt=tt: e.matmul(
                            PSC[:, :],
                            xn_bf[:, c * T_ + tt * 128: c * T_ + (tt + 1) * 128],
                            keys4[:, c * 512:(c + 1) * 512],
                            start=(c == 0), stop=(c == NCH - 1)))
                    gap = _ap(gates, tt * 64 + kg * 4, [[16 * 64, 128], [1, 4], [0, 128]])
                    p.add("vector", lambda e, gap=gap: e.scalar_tensor_tensor(
                        hg_bf[:, :], PSC[:, :], 0.0, gap, ALU.max, ALU.mult))
                    p.dma("sync", hg_d[tt * 128:(tt + 1) * 128, kg * 512:(kg + 1) * 512],
                          hg_bf[:, :])

        es4 = ExitStack()
        with es4:
            vals = es4.enter_context(nc.sbuf_tensor("vals", [128, 64 * 1024], BF16))
            hgrow = es4.enter_context(nc.sbuf_tensor("hgrow", [128, 64 * 128], BF16))
            gsT = es4.enter_context(nc.sbuf_tensor("gsT", [128, 512], BF16))
            gstage = es4.enter_context(nc.sbuf_tensor("gstage", [128, 1024], F32))
            for k in range(64):
                p.dma("gpsimd", vals[:, k * 1024:(k + 1) * 1024],
                      _ap(values, k * 128 * E_, [[E_, 128], [1, E_]]))
            for tt in range(16):
                p.dma("sync", hgrow[:, :], hg_d[tt * 128:(tt + 1) * 128, :])
                for kg in range(16):
                    for k4 in range(4):
                        p.add("tensor", lambda e, kg=kg, k4=k4: e.transpose(
                            PST[:, k4 * 128:(k4 + 1) * 128],
                            hgrow[:, kg * 512 + k4 * 128: kg * 512 + (k4 + 1) * 128],
                            id_bf[:, :]))
                    p.add("vector", lambda e: e.tensor_copy(gsT[:, :], PST[:, 0:512]))
                    for k4 in range(4):
                        k = kg * 4 + k4
                        for vc in range(NCH):
                            first = (kg == 0 and k4 == 0 and vc in (0, 4))
                            last = (kg == 15 and k4 == 3 and vc in (3, 7))
                            p.add("tensor", lambda e, k=k, k4=k4, vc=vc, first=first, last=last: e.matmul(
                                PSA[0:128, vc * 128:(vc + 1) * 128],
                                vals[:, k * 1024 + vc * 128: k * 1024 + (vc + 1) * 128],
                                gsT[:, k4 * 128:(k4 + 1) * 128],
                                start=first, stop=last, skip_group_check=True))
                p.add("scalar", lambda e: e.activation(
                    gstage[:, 0:512], PSA[0:128, 0:512], AF.Silu))
                p.add("scalar", lambda e: e.activation(
                    gstage[:, 512:1024], PSA[0:128, 512:1024], AF.Silu))
                p.dma("sync",
                      _ap(g_o, tt * 128, [[T_, 128], [128 * T_, NCH], [1, 128]]),
                      _ap(gstage, 0, [[1024, 128], [128, NCH], [1, 128]]))

        p.emit(block, [sem])
    return nc


def build_B():
    nc = bass.Bass("TRN2", target_bir_lowering=False)
    qT = nc.dram_tensor("qT", [E_, T_], BF16, kind="ExternalInput")
    kTf = nc.dram_tensor("kTf", [E_, S_], BF16, kind="ExternalInput")
    vgf = nc.dram_tensor("vgf", [4 * S_, 260], BF16, kind="ExternalInput")
    xT = nc.dram_tensor("xT", [E_, T_], F32, kind="ExternalInput")
    g0T = nc.dram_tensor("g0T", [E_, T_], F32, kind="ExternalInput")
    woT = nc.dram_tensor("woT", [E_, E_], F32, kind="ExternalInput")
    wq1T = nc.dram_tensor("wq1T", [E_, E_], F32, kind="ExternalInput")
    wk1T = nc.dram_tensor("wk1T", [E_, E_], F32, kind="ExternalInput")
    wv1T = nc.dram_tensor("wv1T", [E_, E_], F32, kind="ExternalInput")
    kern1 = nc.dram_tensor("kern1", [E_, E_], F32, kind="ExternalInput")
    bias1 = nc.dram_tensor("bias1", [E_, 1], F32, kind="ExternalInput")
    a1 = nc.dram_tensor("a1", [E_, 16], F32, kind="ExternalInput")
    b1 = nc.dram_tensor("b1", [16, E_], F32, kind="ExternalInput")
    gam = nc.dram_tensor("gam", [16, 1], F32, kind="ExternalInput")
    selq = nc.dram_tensor("selq", [16, NCH * 128], F32, kind="ExternalInput")
    onesr = nc.dram_tensor("onesr", [1, 128], F32, kind="ExternalInput")
    onesc = nc.dram_tensor("onesc", [128, 1], F32, kind="ExternalInput")

    x1T_o = nc.dram_tensor("x1T_o", [E_, T_], F32, kind="ExternalOutput")
    qt1_o = nc.dram_tensor("qt1_o", [E_, T_], BF16, kind="ExternalOutput")
    kt1_o = nc.dram_tensor("kt1_o", [E_, T_], BF16, kind="ExternalOutput")
    v1_o = nc.dram_tensor("v1_o", [T_, E_], BF16, kind="ExternalOutput")
    g1_o = nc.dram_tensor("g1_o", [E_, T_], F32, kind="ExternalOutput")
    am_d = nc.dram_tensor("am_d", [128, 1], F32)

    from contextlib import ExitStack
    es = ExitStack()
    with es:
        ones_r = es.enter_context(nc.sbuf_tensor("ones_r", [1, 128], F32))
        ones_c = es.enter_context(nc.sbuf_tensor("ones_c", [128, 1], F32))
        esY = ExitStack()
        esY.__enter__()
        Yn = esY.enter_context(nc.sbuf_tensor("Yn", [128, NCH * T_], BF16))
        PSA = es.enter_context(nc.psum_tensor("PSA", [128, 2048], F32))
        PSB = es.enter_context(nc.psum_tensor("PSB", [128, 1024], F32))
        PSC = es.enter_context(nc.psum_tensor("PSC", [128, 512], F32))
        sem = es.enter_context(nc.semaphore("step"))
        block = es.enter_context(nc.Block())
        p = Prog(nc)
        p.dma("sync", ones_r[:, :], onesr[:, :])
        p.dma("sync", ones_c[:, :], onesc[:, :])

        es1 = ExitStack()
        es1.__enter__()
        Qs = es1.enter_context(nc.sbuf_tensor("Qs", [128, NCH * T_], BF16, side="right"))
        KA = es1.enter_context(nc.sbuf_tensor("KA", [128, S_], BF16, side="right"))
        KB = es1.enter_context(nc.sbuf_tensor("KB", [128, S_], BF16, side="right"))
        Vg = es1.enter_context(nc.sbuf_tensor("Vg", [128, 64 * 260], BF16, side="right"))
        Ebf = es1.enter_context(nc.sbuf_tensor("Ebf", [128, 512], BF16, side="right"))
        rrow = es1.enter_context(nc.sbuf_tensor("rrow", [1, 512], F32, side="right"))
        Rs = es1.enter_context(nc.sbuf_tensor("Rs", [64, 512], F32, side="right"))
        for c in range(NCH):
            p.dma("sync", Qs[:, c * T_:(c + 1) * T_], qT[c * 128:(c + 1) * 128, :])
        for g in range(4):
            p.dma("sync", KA[:, :], kTf[g * 256:g * 256 + 128, :])
            p.dma("sync", KB[:, :], kTf[g * 256 + 128:g * 256 + 256, :])
            p.dma("sync", Vg[:, :],
                  _ap(vgf, g * S_ * 260, [[260, 128], [128 * 260, 64], [1, 260]]))
            for hl in range(4):
                h = 4 * g + hl
                K_t = KA if hl < 2 else KB
                kr = 64 * (hl % 2)
                qr0 = 64 * (h % 2)
                qc0 = (h // 2) * T_
                for ts in range(4):
                    for sb in range(64):
                        p.add("tensor", lambda e, K_t=K_t, kr=kr, sb=sb, qr0=qr0, qc0=qc0, ts=ts: e.matmul(
                            PSA[0:128, 0:512],
                            K_t[kr:kr + 64, sb * 128:(sb + 1) * 128],
                            Qs[qr0:qr0 + 64, qc0 + ts * 512: qc0 + (ts + 1) * 512],
                            start=True, stop=True))
                        p.add("scalar", lambda e: e.activation(
                            Ebf[:, :], PSA[0:128, 0:512], AF.Exp, scale=0.125))
                        p.add("tensor", lambda e, sb=sb, hl=hl: e.matmul(
                            PSB[0:65, 0:512],
                            Vg[:, sb * 260 + hl * 65: sb * 260 + (hl + 1) * 65],
                            Ebf[:, :], start=(sb == 0), stop=(sb == 63)))
                    p.add("vector", lambda e: e.reciprocal(rrow[:, :], PSB[64:65, 0:512]))
                    p.add("tensor", lambda e: e.matmul(
                        PSC[0:64, :], ones_r[0:1, 0:64], rrow[:, :], start=True, stop=True))
                    p.add("scalar", lambda e: e.activation(Rs[:, :], PSC[0:64, :], AF.Copy))
                    p.add("vector", lambda e, qr0=qr0, qc0=qc0, ts=ts: e.tensor_tensor(
                        Yn[qr0:qr0 + 64, qc0 + ts * 512: qc0 + (ts + 1) * 512],
                        PSB[0:64, 0:512], Rs[:, :], ALU.mult))
        es1.__exit__(None, None, None)

        es2 = ExitStack()
        es2.__enter__()
        X1 = es2.enter_context(nc.sbuf_tensor("X1", [128, NCH * T_], F32, side="right"))
        wo_bf = es2.enter_context(nc.sbuf_tensor("wo_bf", [128, NCH * 1024], BF16, side="right"))
        xsl = es2.enter_context(nc.sbuf_tensor("xsl", [128, 512], F32, side="right"))
        gsl = es2.enter_context(nc.sbuf_tensor("gsl", [128, 512], F32, side="right"))
        tsl = es2.enter_context(nc.sbuf_tensor("tsl", [128, 512], F32, side="right"))
        for c in range(NCH):
            p.dma("gpsimd", wo_bf[:, c * 1024:(c + 1) * 1024],
                  woT[c * 128:(c + 1) * 128, :])
        for oc in range(NCH):
            for ts in range(4):
                for c in range(NCH):
                    p.add("tensor", lambda e, oc=oc, ts=ts, c=c: e.matmul(
                        PSA[0:128, 0:512],
                        wo_bf[:, c * 1024 + oc * 128: c * 1024 + oc * 128 + 128],
                        Yn[:, c * T_ + ts * 512: c * T_ + (ts + 1) * 512],
                        start=(c == 0), stop=(c == NCH - 1)))
                p.dma("sync", xsl[:, :], xT[oc * 128:(oc + 1) * 128, ts * 512:(ts + 1) * 512])
                p.dma("sync", gsl[:, :], g0T[oc * 128:(oc + 1) * 128, ts * 512:(ts + 1) * 512])
                p.add("vector", lambda e: e.tensor_tensor(
                    tsl[:, :], PSA[0:128, 0:512], gsl[:, :], ALU.mult))
                sl = lambda t, oc=oc, ts=ts: t[:, oc * T_ + ts * 512: oc * T_ + (ts + 1) * 512]
                p.add("vector", lambda e, oc=oc, ts=ts: e.tensor_tensor(
                    X1[:, oc * T_ + ts * 512: oc * T_ + (ts + 1) * 512],
                    tsl[:, :], xsl[:, :], ALU.add))
                p.dma("sync", x1T_o[oc * 128:(oc + 1) * 128, ts * 512:(ts + 1) * 512],
                      X1[:, oc * T_ + ts * 512: oc * T_ + (ts + 1) * 512])
        esY.__exit__(None, None, None)

        xn1 = es.enter_context(nc.sbuf_tensor("xn1", [128, NCH * T_], BF16))
        es3 = ExitStack()
        es3.__enter__()
        mu1 = es3.enter_context(nc.sbuf_tensor("mu1", [128, T_], F32, side="right"))
        rs1 = es3.enter_context(nc.sbuf_tensor("rs1", [128, T_], F32, side="right"))
        sqs = es3.enter_context(nc.sbuf_tensor("sqs", [128, 512], F32, side="right"))
        row_a = es3.enter_context(nc.sbuf_tensor("row_a", [1, T_], F32, side="right"))
        row_b = es3.enter_context(nc.sbuf_tensor("row_b", [1, T_], F32, side="right"))
        row_c = es3.enter_context(nc.sbuf_tensor("row_c", [1, 512], F32, side="right"))
        row_d = es3.enter_context(nc.sbuf_tensor("row_d", [1, 512], F32, side="right"))
        scr_a = es3.enter_context(nc.sbuf_tensor("scr_a", [128, T_], F32, side="right"))
        scr_b = es3.enter_context(nc.sbuf_tensor("scr_b", [128, T_], F32, side="right"))
        for ts in range(4):
            for c in range(NCH):
                csl = lambda t, c=c, ts=ts: t[:, c * T_ + ts * 512: c * T_ + (ts + 1) * 512]
                p.add("scalar", lambda e, c=c, ts=ts: e.activation(
                    sqs[:, :], X1[:, c * T_ + ts * 512: c * T_ + (ts + 1) * 512], AF.Square))
                p.add("tensor", lambda e, c=c, ts=ts: e.matmul(
                    PSC[0:1, :], ones_c[:, :],
                    X1[:, c * T_ + ts * 512: c * T_ + (ts + 1) * 512],
                    start=(c == 0), stop=(c == NCH - 1)))
                p.add("tensor", lambda e, c=c: e.matmul(
                    PSB[0:1, 0:512], ones_c[:, :], sqs[:, :],
                    start=(c == 0), stop=(c == NCH - 1)))
            sl = slice(ts * 512, (ts + 1) * 512)
            p.add("vector", lambda e, sl=sl: e.tensor_scalar_mul(
                row_a[:, sl], PSC[0:1, :], 1.0 / 1024.0))
            p.add("vector", lambda e: e.tensor_scalar_mul(
                row_c[:, :], PSB[0:1, 0:512], 1.0 / 1024.0))
            p.add("vector", lambda e, sl=sl: e.tensor_tensor(
                row_d[:, :], row_a[:, sl], row_a[:, sl], ALU.mult))
            p.add("vector", lambda e: e.tensor_tensor(
                row_c[:, :], row_c[:, :], row_d[:, :], ALU.subtract))
            p.add("vector", lambda e: e.tensor_scalar_add(row_c[:, :], row_c[:, :], 1e-5))
            p.add("scalar", lambda e: e.activation(row_d[:, :], row_c[:, :], AF.Sqrt))
            p.add("vector", lambda e, sl=sl: e.reciprocal(row_b[:, sl], row_d[:, :]))
            p.add("tensor", lambda e, sl=sl: e.matmul(
                PSB[0:128, 0:512], ones_r[:, :], row_a[:, sl], start=True, stop=True))
            p.add("vector", lambda e, sl=sl: e.tensor_copy(mu1[:, sl], PSB[0:128, 0:512]))
            p.add("tensor", lambda e, sl=sl: e.matmul(
                PSC[0:128, :], ones_r[:, :], row_b[:, sl], start=True, stop=True))
            p.add("vector", lambda e, sl=sl: e.tensor_copy(rs1[:, sl], PSC[0:128, :]))
        for c in range(NCH):
            p.add("vector", lambda e, c=c: e.tensor_tensor(
                scr_a[:, :], X1[:, c * T_:(c + 1) * T_], mu1[:, :], ALU.subtract))
            p.add("vector", lambda e: e.tensor_tensor(
                scr_b[:, :], scr_a[:, :], rs1[:, :], ALU.mult))
            p.add("vector", lambda e, c=c: e.tensor_copy(
                xn1[:, c * T_:(c + 1) * T_], scr_b[:, :]))
        es3.__exit__(None, None, None)
        es2.__exit__(None, None, None)

        es4 = ExitStack()
        es4.__enter__()
        W1 = es4.enter_context(nc.sbuf_tensor("W1", [128, 3 * NCH * 1024], BF16, side="right"))
        U512 = es4.enter_context(nc.sbuf_tensor("U512", [128, T_], F32, side="right"))
        U128 = es4.enter_context(nc.sbuf_tensor("U128", [128, T_], F32, side="right"))
        LG = es4.enter_context(nc.sbuf_tensor("LG", [128, NCH], F32, side="right"))
        NLG = es4.enter_context(nc.sbuf_tensor("NLG", [128, NCH], F32, side="right"))
        lnGr = es4.enter_context(nc.sbuf_tensor("lnGr", [16, 1], F32, side="right"))
        lnG = es4.enter_context(nc.sbuf_tensor("lnG", [16, 1], F32, side="right"))
        selq_s = es4.enter_context(nc.sbuf_tensor("selq_s", [16, NCH * 128], F32, side="right"))
        Pq = es4.enter_context(nc.sbuf_tensor("Pq", [128, T_], F32, side="right"))
        Pk = es4.enter_context(nc.sbuf_tensor("Pk", [128, T_], F32, side="right"))
        scp = es4.enter_context(nc.sbuf_tensor("scp", [128, T_], F32, side="right"))
        qst = es4.enter_context(nc.sbuf_tensor("qst", [128, T_], BF16, side="right"))
        v1st = es4.enter_context(nc.sbuf_tensor("v1st", [128, 1024], BF16, side="right"))
        for w_i, wsrc in enumerate((wq1T, wk1T, wv1T)):
            for c in range(NCH):
                p.dma("gpsimd", W1[:, (w_i * NCH + c) * 1024:(w_i * NCH + c + 1) * 1024],
                      wsrc[c * 128:(c + 1) * 128, :])
        p.dma("sync", selq_s[:, :], selq[:, :])
        p.dma("sync", lnGr[:, :], gam[:, :])
        p.add("scalar", lambda e: e.activation(lnG[:, :], lnGr[:, :], AF.Ln))
        for c in range(NCH):
            p.add("tensor", lambda e, c=c: e.matmul(
                PSC[0:128, 0:1], selq_s[:, c * 128:(c + 1) * 128], lnG[:, :],
                start=True, stop=True))
            p.add("vector", lambda e, c=c: e.tensor_copy(LG[:, c:c + 1], PSC[0:128, 0:1]))
        p.add("vector", lambda e: e.tensor_scalar_mul(NLG[:, :], LG[:, :], -1.0))
        p.add("gpsimd", lambda e: e.iota(U512[:, :], [[0, 4], [1, 512]], base=0,
                                         channel_multiplier=0,
                                         allow_small_or_imprecise_dtypes=True))
        p.add("gpsimd", lambda e: e.iota(U128[:, :], [[0, 16], [1, 128]], base=0,
                                         channel_multiplier=0,
                                         allow_small_or_imprecise_dtypes=True))
        LN8 = float(np.log(0.125))
        for oc in range(NCH):
            lgb = _ap(LG, oc, [[NCH, 128], [0, T_]])
            nlgb = _ap(NLG, oc, [[NCH, 128], [0, T_]])
            p.add("vector", lambda e, lgb=lgb: e.tensor_tensor(scp[:, :], U512[:, :], lgb, ALU.mult))
            p.add("scalar", lambda e: e.activation(Pq[:, :], scp[:, :], AF.Exp, bias=np.float32(LN8)))
            p.add("vector", lambda e, nlgb=nlgb: e.tensor_tensor(scp[:, :], U128[:, :], nlgb, ALU.mult))
            p.add("scalar", lambda e: e.activation(Pk[:, :], scp[:, :], AF.Exp))
            for w_i, dst, P in ((0, qt1_o, Pq), (1, kt1_o, Pk)):
                for ts in range(4):
                    for c in range(NCH):
                        p.add("tensor", lambda e, w_i=w_i, oc=oc, c=c, ts=ts: e.matmul(
                            PSA[0:128, 0:512],
                            W1[:, (w_i * NCH + c) * 1024 + oc * 128:
                                  (w_i * NCH + c) * 1024 + oc * 128 + 128],
                            xn1[:, c * T_ + ts * 512: c * T_ + (ts + 1) * 512],
                            start=(c == 0), stop=(c == NCH - 1)))
                    p.add("vector", lambda e, ts=ts, P=P: e.tensor_tensor(
                        qst[:, ts * 512:(ts + 1) * 512], PSA[0:128, 0:512],
                        P[:, ts * 512:(ts + 1) * 512], ALU.mult))
                p.dma("sync", dst[oc * 128:(oc + 1) * 128, :], qst[:, :])
        for tt in range(16):
            for os in range(2):
                for c in range(NCH):
                    p.add("tensor", lambda e, os=os, c=c, tt=tt: e.matmul(
                        PSB[0:128, os * 512:(os + 1) * 512],
                        xn1[:, c * T_ + tt * 128: c * T_ + (tt + 1) * 128],
                        W1[:, (2 * NCH + c) * 1024 + os * 512:
                              (2 * NCH + c) * 1024 + (os + 1) * 512],
                        start=(c == 0), stop=(c == NCH - 1)))
            p.add("vector", lambda e: e.tensor_copy(v1st[:, :], PSB[0:128, :]))
            p.dma("sync", v1_o[tt * 128:(tt + 1) * 128, :], v1st[:, :])
        es4.__exit__(None, None, None)

        es5 = ExitStack()
        es5.__enter__()
        wq1 = es5.enter_context(nc.sbuf_tensor("wq1", [128, NCH * 1024], BF16, side="right"))
        kch = es5.enter_context(nc.sbuf_tensor("kch", [128, 1024], F32, side="right"))
        tq = es5.enter_context(nc.sbuf_tensor("tq", [128, 1024], F32, side="right"))
        am8 = es5.enter_context(nc.sbuf_tensor("am8", [128, NCH], F32, side="right"))
        amc = es5.enter_context(nc.sbuf_tensor("amc", [128, 1], F32, side="right"))
        amr = es5.enter_context(nc.sbuf_tensor("amr", [1, 128], F32, side="right"))
        am = es5.enter_context(nc.sbuf_tensor("am", [1, 1], F32, side="right"))
        srow = es5.enter_context(nc.sbuf_tensor("srow", [1, 1], F32, side="right"))
        rsrow = es5.enter_context(nc.sbuf_tensor("rsrow", [1, 1], F32, side="right"))
        scol = es5.enter_context(nc.sbuf_tensor("scol", [128, 1], F32, side="right"))
        rscol = es5.enter_context(nc.sbuf_tensor("rscol", [128, 1], F32, side="right"))
        a_bf = es5.enter_context(nc.sbuf_tensor("a_bf", [128, NCH * 16], BF16, side="right"))
        b_bf = es5.enter_context(nc.sbuf_tensor("b_bf", [16, 1024], BF16, side="right"))
        bias_s = es5.enter_context(nc.sbuf_tensor("bias_s", [128, NCH], F32, side="right"))
        u_bf = es5.enter_context(nc.sbuf_tensor("u_bf", [16, T_], BF16, side="right"))
        g1st = es5.enter_context(nc.sbuf_tensor("g1st", [128, 512], F32, side="right"))
        for c in range(NCH):
            p.dma("sync", kch[:, :], kern1[c * 128:(c + 1) * 128, :])
            p.add("vector", lambda e, c=c: e.reduce_max(
                am8[:, c:c + 1], kch[:, :], AX.X, apply_absolute_value=True))
        p.add("vector", lambda e: e.reduce_max(amc[:, :], am8[:, :], AX.X))
        p.dma("sync", am_d[:, :], amc[:, :])
        p.dma("sync", amr[:, :], _ap(am_d, 0, [[128, 1], [1, 128]]))
        p.add("vector", lambda e: e.reduce_max(am[:, :], amr[:, :], AX.X))
        p.add("vector", lambda e: e.tensor_scalar_mul(srow[:, :], am[:, :], 1.0 / 7.0))
        p.add("vector", lambda e: e.reciprocal(rsrow[:, :], srow[:, :]))
        p.add("tensor", lambda e: e.matmul(PSC[0:128, 0:1], ones_r[:, :], srow[:, :],
                                           start=True, stop=True))
        p.add("vector", lambda e: e.tensor_copy(scol[:, :], PSC[0:128, 0:1]))
        p.add("tensor", lambda e: e.matmul(PSC[0:128, 0:1], ones_r[:, :], rsrow[:, :],
                                           start=True, stop=True))
        p.add("vector", lambda e: e.tensor_copy(rscol[:, :], PSC[0:128, 0:1]))
        for c in range(NCH):
            p.dma("sync", kch[:, :], kern1[c * 128:(c + 1) * 128, :])
            p.add("vector", lambda e: e.tensor_scalar(
                tq[:, :], kch[:, :], rscol[:, 0:1], MAGIC, ALU.mult, ALU.add))
            p.add("vector", lambda e: e.tensor_scalar(
                kch[:, :], tq[:, :], MAGIC, -8.0, ALU.subtract, ALU.max))
            p.add("vector", lambda e, c=c: e.tensor_scalar(
                wq1[:, c * 1024:(c + 1) * 1024], kch[:, :], 7.0, scol[:, 0:1],
                ALU.min, ALU.mult))
        for c in range(NCH):
            p.dma("gpsimd", a_bf[:, c * 16:(c + 1) * 16], a1[c * 128:(c + 1) * 128, :])
            p.dma("sync", bias_s[:, c:c + 1], bias1[c * 128:(c + 1) * 128, :])
        p.dma("gpsimd", b_bf[:, :], b1[:, :])
        for ts in range(4):
            for c in range(NCH):
                p.add("tensor", lambda e, c=c, ts=ts: e.matmul(
                    PSC[0:16, :], a_bf[:, c * 16:(c + 1) * 16],
                    xn1[:, c * T_ + ts * 512: c * T_ + (ts + 1) * 512],
                    start=(c == 0), stop=(c == NCH - 1)))
            p.add("vector", lambda e, ts=ts: e.tensor_copy(
                u_bf[:, ts * 512:(ts + 1) * 512], PSC[0:16, :]))
        for fc in range(NCH):
            for ts in range(4):
                for c in range(NCH):
                    p.add("tensor", lambda e, fc=fc, ts=ts, c=c: e.matmul(
                        PSA[0:128, 0:512],
                        wq1[:, c * 1024 + fc * 128: c * 1024 + fc * 128 + 128],
                        xn1[:, c * T_ + ts * 512: c * T_ + (ts + 1) * 512],
                        start=(c == 0), stop=False))
                p.add("tensor", lambda e, fc=fc, ts=ts: e.matmul(
                    PSA[0:128, 0:512], b_bf[:, fc * 128:(fc + 1) * 128],
                    u_bf[:, ts * 512:(ts + 1) * 512], start=False, stop=True))
                p.add("scalar", lambda e, fc=fc: e.activation(
                    g1st[:, :], PSA[0:128, 0:512], AF.Silu, bias=bias_s[:, fc:fc + 1]))
                p.dma("sync", g1_o[fc * 128:(fc + 1) * 128, ts * 512:(ts + 1) * 512],
                      g1st[:, :])
        es5.__exit__(None, None, None)
        p.emit(block, [sem])
    return nc


def build_D():
    nc = bass.Bass("TRN2", target_bir_lowering=False)
    qtg = nc.dram_tensor("qtg", [256, S_], BF16, kind="ExternalInput")
    ktg = nc.dram_tensor("ktg", [256, S_], BF16, kind="ExternalInput")
    vg1 = nc.dram_tensor("vg1", [S_, 256], BF16, kind="ExternalInput")
    gamh = nc.dram_tensor("gamh", [4, 1], F32, kind="ExternalInput")
    mgrid = nc.dram_tensor("mgrid", [1, 4 * 67], F32, kind="ExternalInput")
    onesr = nc.dram_tensor("onesr", [1, 128], F32, kind="ExternalInput")
    y1T_o = nc.dram_tensor("y1T_o", [256, S_], BF16, kind="ExternalOutput")

    from contextlib import ExitStack
    es = ExitStack()
    with es:
        qtA = es.enter_context(nc.sbuf_tensor("qtA", [128, S_], BF16))
        qtB = es.enter_context(nc.sbuf_tensor("qtB", [128, S_], BF16))
        ktA = es.enter_context(nc.sbuf_tensor("ktA", [128, S_], BF16))
        ktB = es.enter_context(nc.sbuf_tensor("ktB", [128, S_], BF16))
        Vs = es.enter_context(nc.sbuf_tensor("Vs", [128, 64 * 256], BF16))
        y1s = es.enter_context(nc.sbuf_tensor("y1s", [128, 2 * S_], BF16))
        ones_r = es.enter_context(nc.sbuf_tensor("ones_r", [1, 128], F32))
        mg = es.enter_context(nc.sbuf_tensor("mg", [1, 4 * 67], F32))
        gr = es.enter_context(nc.sbuf_tensor("gr", [1, 4], F32))
        lnr = es.enter_context(nc.sbuf_tensor("lnr", [1, 4], F32))
        erow = es.enter_context(nc.sbuf_tensor("erow", [1, 4 * 67], F32))
        crow = es.enter_context(nc.sbuf_tensor("crow", [1, 4 * 67], F32))
        ctab = es.enter_context(nc.sbuf_tensor("ctab", [128, 4 * 67], F32))
        UD = es.enter_context(nc.sbuf_tensor("UD", [128, 512], F32))
        M01 = es.enter_context(nc.sbuf_tensor("M01", [128, 512], F32))
        Wbf = es.enter_context(nc.sbuf_tensor("Wbf", [128, 512], BF16))
        PSA = es.enter_context(nc.psum_tensor("PSA", [128, 512], F32))
        PSB = es.enter_context(nc.psum_tensor("PSB", [128, 512], F32))
        sem = es.enter_context(nc.semaphore("step"))
        block = es.enter_context(nc.Block())
        p = Prog(nc)

        p.dma("sync", ones_r[:, :], onesr[:, :])
        p.dma("sync", mg[:, :], mgrid[:, :])
        p.dma("sync", gr[:, :], _ap(gamh, 0, [[4, 1], [1, 4]]))
        p.dma("sync", qtA[:, :], qtg[0:128, :])
        p.dma("sync", qtB[:, :], qtg[128:256, :])
        p.dma("sync", ktA[:, :], ktg[0:128, :])
        p.dma("sync", ktB[:, :], ktg[128:256, :])
        p.dma("sync", Vs[:, :], _ap(vg1, 0, [[256, 128], [128 * 256, 64], [1, 256]]))
        p.add("scalar", lambda e: e.activation(lnr[:, :], gr[:, :], AF.Ln))
        lnb = _ap(lnr, 0, [[4, 1], [1, 4], [0, 67]])
        p.add("vector", lambda e, lnb=lnb: e.tensor_tensor(erow[:, :], mg[:, :], lnb, ALU.mult))
        p.add("scalar", lambda e: e.activation(crow[:, :], erow[:, :], AF.Exp))
        p.add("tensor", lambda e: e.matmul(PSB[0:128, 0:268], ones_r[:, :], crow[:, :],
                                           start=True, stop=True))
        p.add("vector", lambda e: e.tensor_copy(ctab[:, :], PSB[0:128, 0:268]))
        p.add("gpsimd", lambda e: e.iota(UD[:, :], [[1, 512]], base=0,
                                         channel_multiplier=-1,
                                         allow_small_or_imprecise_dtypes=True))
        p.add("vector", lambda e: e.tensor_single_scalar(M01[:, :], UD[:, :], 0.0, ALU.is_ge))

        for hl in range(4):
            kt = ktA if hl < 2 else ktB
            qt = qtA if hl < 2 else qtB
            r0 = 64 * (hl % 2)
            for ts in range(16):
                nsb = 4 * ts + 4
                for sb in range(nsb):
                    p.add("tensor", lambda e, kt=kt, qt=qt, r0=r0, sb=sb, ts=ts: e.matmul(
                        PSA[:, :], kt[r0:r0 + 64, sb * 128:(sb + 1) * 128],
                        qt[r0:r0 + 64, ts * 512:(ts + 1) * 512], start=True, stop=True))
                    if sb < 4 * ts:
                        idx = hl * 67 + (4 * ts - sb) + 3
                        p.add("vector", lambda e, idx=idx: e.tensor_scalar_mul(
                            Wbf[:, :], PSA[:, :], ctab[:, idx:idx + 1]))
                    else:
                        j = sb - 4 * ts
                        idx = hl * 67 + 3 - j
                        if j > 0:
                            p.add("vector", lambda e, j=j: e.memset(Wbf[:, 0:j * 128], 0.0))
                        p.add("vector", lambda e, j=j, idx=idx: e.scalar_tensor_tensor(
                            Wbf[:, j * 128:512], PSA[:, j * 128:512], ctab[:, idx:idx + 1],
                            M01[:, 0:512 - j * 128], ALU.mult, ALU.mult))
                    p.add("tensor", lambda e, sb=sb, hl=hl, nsb=nsb: e.matmul(
                        PSB[0:64, :], Vs[:, sb * 256 + hl * 64: sb * 256 + (hl + 1) * 64],
                        Wbf[:, :], start=(sb == 0), stop=(sb == nsb - 1)))
                p.add("vector", lambda e, r0=r0, hl=hl, ts=ts: e.tensor_copy(
                    y1s[r0:r0 + 64, (hl // 2) * S_ + ts * 512: (hl // 2) * S_ + (ts + 1) * 512],
                    PSB[0:64, :]))
        p.dma("sync", y1T_o[0:128, :], y1s[:, 0:S_])
        p.dma("sync", y1T_o[128:256, :], y1s[:, S_:2 * S_])
        p.emit(block, [sem])
    return nc


def build_E():
    nc = bass.Bass("TRN2", target_bir_lowering=False)
    y1T = nc.dram_tensor("y1T", [E_, T_], BF16, kind="ExternalInput")
    wo1T = nc.dram_tensor("wo1T", [E_, E_], F32, kind="ExternalInput")
    x1T = nc.dram_tensor("x1T", [E_, T_], F32, kind="ExternalInput")
    g1T = nc.dram_tensor("g1T", [E_, T_], F32, kind="ExternalInput")
    x2T_o = nc.dram_tensor("x2T_o", [E_, T_], F32, kind="ExternalOutput")

    from contextlib import ExitStack
    es = ExitStack()
    with es:
        y1_sb = es.enter_context(nc.sbuf_tensor("y1_sb", [128, NCH * T_], BF16))
        wo1_bf = es.enter_context(nc.sbuf_tensor("wo1_bf", [128, NCH * 1024], BF16))
        xsl = es.enter_context(nc.sbuf_tensor("xsl", [128, 512], F32))
        gsl = es.enter_context(nc.sbuf_tensor("gsl", [128, 512], F32))
        tsl = es.enter_context(nc.sbuf_tensor("tsl", [128, 512], F32))
        osl = es.enter_context(nc.sbuf_tensor("osl", [128, 512], F32))
        PSA = es.enter_context(nc.psum_tensor("PSA", [128, 512], F32))
        sem = es.enter_context(nc.semaphore("step"))
        block = es.enter_context(nc.Block())
        p = Prog(nc)
        for c in range(NCH):
            p.dma("sync", y1_sb[:, c * T_:(c + 1) * T_], y1T[c * 128:(c + 1) * 128, :])
            p.dma("gpsimd", wo1_bf[:, c * 1024:(c + 1) * 1024],
                  wo1T[c * 128:(c + 1) * 128, :])
        for oc in range(NCH):
            for ts in range(4):
                for c in range(NCH):
                    p.add("tensor", lambda e, oc=oc, ts=ts, c=c: e.matmul(
                        PSA[:, :],
                        wo1_bf[:, c * 1024 + oc * 128: c * 1024 + oc * 128 + 128],
                        y1_sb[:, c * T_ + ts * 512: c * T_ + (ts + 1) * 512],
                        start=(c == 0), stop=(c == NCH - 1)))
                p.dma("sync", xsl[:, :], x1T[oc * 128:(oc + 1) * 128, ts * 512:(ts + 1) * 512])
                p.dma("sync", gsl[:, :], g1T[oc * 128:(oc + 1) * 128, ts * 512:(ts + 1) * 512])
                p.add("vector", lambda e: e.tensor_tensor(
                    tsl[:, :], PSA[:, :], gsl[:, :], ALU.mult))
                p.add("vector", lambda e: e.tensor_tensor(
                    osl[:, :], tsl[:, :], xsl[:, :], ALU.add))
                p.dma("sync", x2T_o[oc * 128:(oc + 1) * 128, ts * 512:(ts + 1) * 512],
                      osl[:, :])
        p.emit(block, [sem])
    return nc


def build_full():
    from concourse.aot_env import allow_ici_all_to_all
    from contextlib import ExitStack
    nc = bass.Bass("TRN2", target_bir_lowering=False)
    GRP = [[0, 1, 2, 3], [4, 5, 6, 7]]

    xT = nc.dram_tensor("xT", [E_, T_], F32, kind="ExternalInput")
    wqT = nc.dram_tensor("wqT", [E_, E_], F32, kind="ExternalInput")
    wkT = nc.dram_tensor("wkT", [E_, E_], F32, kind="ExternalInput")
    wvT = nc.dram_tensor("wvT", [E_, E_], F32, kind="ExternalInput")
    selT = nc.dram_tensor("selT", [E_, 64], F32, kind="ExternalInput")
    keys = nc.dram_tensor("keys", [64, E_, 128], F32, kind="ExternalInput")
    values = nc.dram_tensor("values", [64, 128, E_], F32, kind="ExternalInput")
    cosL = nc.dram_tensor("cosL", [T_, 32], F32, kind="ExternalInput")
    sinL = nc.dram_tensor("sinL", [T_, 32], F32, kind="ExternalInput")
    idf = nc.dram_tensor("idf", [128, 128], F32, kind="ExternalInput")
    onesr = nc.dram_tensor("onesr", [1, 128], F32, kind="ExternalInput")
    onesc = nc.dram_tensor("onesc", [128, 1], F32, kind="ExternalInput")
    woT = nc.dram_tensor("woT", [E_, E_], F32, kind="ExternalInput")
    wq1T = nc.dram_tensor("wq1T", [E_, E_], F32, kind="ExternalInput")
    wk1T = nc.dram_tensor("wk1T", [E_, E_], F32, kind="ExternalInput")
    wv1T = nc.dram_tensor("wv1T", [E_, E_], F32, kind="ExternalInput")
    wo1T = nc.dram_tensor("wo1T", [E_, E_], F32, kind="ExternalInput")
    kern1 = nc.dram_tensor("kern1", [E_, E_], F32, kind="ExternalInput")
    bias1 = nc.dram_tensor("bias1", [E_, 1], F32, kind="ExternalInput")
    a1 = nc.dram_tensor("a1", [E_, 16], F32, kind="ExternalInput")
    b1 = nc.dram_tensor("b1", [16, E_], F32, kind="ExternalInput")
    gam = nc.dram_tensor("gam", [16, 1], F32, kind="ExternalInput")
    selq = nc.dram_tensor("selq", [16, NCH * 128], F32, kind="ExternalInput")
    gamh = nc.dram_tensor("gamh", [4, 1], F32, kind="ExternalInput")
    mgrid = nc.dram_tensor("mgrid", [1, 4 * 67], F32, kind="ExternalInput")
    hoff = nc.dram_tensor("hoff", [1, 1], mybir.dt.int32, kind="ExternalInput")
    toff = nc.dram_tensor("toff", [1, 1], mybir.dt.int32, kind="ExternalInput")

    x2T_o = nc.dram_tensor("x2T_o", [E_, T_], F32, kind="ExternalOutput")

    qT_d = nc.dram_tensor("qT_d", [E_, T_], BF16)
    kT_d = nc.dram_tensor("kT_d", [E_, T_], BF16)
    vg_d = nc.dram_tensor("vg_d", [T_, 16 * 65], BF16)
    g0_d = nc.dram_tensor("g0_d", [E_, T_], F32)
    hg_d = nc.dram_tensor("hg_d", [T_, 64 * 128], BF16)
    x1_d = nc.dram_tensor("x1_d", [E_, T_], F32)
    qt1_d = nc.dram_tensor("qt1_d", [E_, T_], BF16)
    kt1_d = nc.dram_tensor("kt1_d", [E_, T_], BF16)
    v1_d = nc.dram_tensor("v1_d", [T_, E_], BF16)
    g1_d = nc.dram_tensor("g1_d", [E_, T_], F32)
    y1_d = nc.dram_tensor("y1_d", [256, S_], BF16)
    am_d = nc.dram_tensor("am_d", [128, 1], F32)
    kT_g = nc.dram_tensor("kT_g", [4 * E_, T_], BF16)
    vg_g = nc.dram_tensor("vg_g", [4 * T_, 16 * 65], BF16)
    qt1_g = nc.dram_tensor("qt1_g", [4 * E_, T_], BF16)
    kt1_g = nc.dram_tensor("kt1_g", [4 * E_, T_], BF16)
    v1_g = nc.dram_tensor("v1_g", [4 * T_, E_], BF16)
    y1_g = nc.dram_tensor("y1_g", [E_, S_], BF16)

    es = ExitStack()
    with es:
        id_f = es.enter_context(nc.sbuf_tensor("id_f", [128, 128], F32))
        id_bf = es.enter_context(nc.sbuf_tensor("id_bf", [128, 128], BF16))
        ones_r = es.enter_context(nc.sbuf_tensor("ones_r", [1, 128], F32))
        ones_c = es.enter_context(nc.sbuf_tensor("ones_c", [128, 1], F32))
        ones16 = es.enter_context(nc.sbuf_tensor("ones16", [128, 16], BF16))
        offs = es.enter_context(nc.sbuf_tensor("offs", [1, 2], mybir.dt.int32))
        PSA = es.enter_context(nc.psum_tensor("PSA", [128, 2048], F32))
        PSB = es.enter_context(nc.psum_tensor("PSB", [128, 1024], F32))
        PSC = es.enter_context(nc.psum_tensor("PSC", [128, 512], F32))
        PST = es.enter_context(nc.psum_tensor("PST", [128, 1024], BF16))
        ccs = [es.enter_context(nc.semaphore(f"cc{i}")) for i in range(6)]
        block = es.enter_context(nc.Block())
        p = P2(nc)

        def BAR():
            return [p.tok(e) for e in P2.ENGS]

        MARKS["A"] = len(p.items)
        esA = ExitStack()
        esA.__enter__()
        xn_bf = esA.enter_context(nc.sbuf_tensor("xn_bf", [128, NCH * T_], BF16))
        gates = esA.enter_context(nc.sbuf_tensor("gates", [128, 16 * 64], F32))
        es1 = ExitStack()
        es1.__enter__()
        selT_s = es1.enter_context(nc.sbuf_tensor("selT_s", [128, NCH * 64], F32, side="right"))
        mu_rep = es1.enter_context(nc.sbuf_tensor("mu_rep", [128, T_], F32, side="right"))
        rs_rep = es1.enter_context(nc.sbuf_tensor("rs_rep", [128, T_], F32, side="right"))
        xch = es1.enter_context(nc.sbuf_tensor("xch", [128, 2 * T_], F32, side="right"))
        sq = es1.enter_context(nc.sbuf_tensor("sq", [128, T_], F32, side="right"))
        row_a = es1.enter_context(nc.sbuf_tensor("row_a", [1, T_], F32, side="right"))
        row_b = es1.enter_context(nc.sbuf_tensor("row_b", [1, T_], F32, side="right"))
        row_c = es1.enter_context(nc.sbuf_tensor("row_c", [1, 512], F32, side="right"))
        row_d = es1.enter_context(nc.sbuf_tensor("row_d", [1, 512], F32, side="right"))
        sels = es1.enter_context(nc.sbuf_tensor("sels", [64, T_], F32, side="right"))
        seltm = es1.enter_context(nc.sbuf_tensor("seltm", [128, 64], F32, side="right"))
        wk1_ = es1.enter_context(nc.sbuf_tensor("wk1_", [128, 64], F32, side="right"))
        wk2_ = es1.enter_context(nc.sbuf_tensor("wk2_", [128, 64], F32, side="right"))
        mcol = es1.enter_context(nc.sbuf_tensor("mcol", [128, 1], F32, side="right"))

        tid = p.dma("sync", id_f[:, :], idf[:, :])
        tidb = p.dma("gpsimd", id_bf[:, :], idf[:, :])
        tor = p.dma("sync", ones_r[:, :], onesr[:, :])
        toc = p.dma("sync", ones_c[:, :], onesc[:, :])
        tof = p.dma("sync", offs[0:1, 0:1], hoff[:, :])
        tof2 = p.dma("sync", offs[0:1, 1:2], toff[:, :])
        t16 = p.add("gpsimd", lambda e: e.memset(ones16[:, :], 1.0))
        tsel = None
        for c in range(NCH):
            tsel = p.dma("sync", selT_s[:, c * 64:(c + 1) * 64],
                         selT[c * 128:(c + 1) * 128, :])

        tdx = [None] * 32
        tsq = [None] * 32
        ts1 = [None] * 32
        ts2 = [None] * 32
        t_muev = t_rsev = None
        i = 0
        for ts in range(4):
            for c in range(NCH):
                r4 = i % 4
                deps = [tsq[i - 4], ts1[i - 4]] if i >= 4 else []
                tdx[i] = p.dma("sync", xch[:, r4 * 512:(r4 + 1) * 512],
                               xT[c * 128:(c + 1) * 128, ts * 512:(ts + 1) * 512], deps)
                tsq[i] = p.add("scalar", lambda e, r4=r4: e.activation(
                    sq[:, r4 * 512:(r4 + 1) * 512], xch[:, r4 * 512:(r4 + 1) * 512],
                    AF.Square), deps=[tdx[i]] + ([ts2[i - 4]] if i >= 4 else []))
                d1 = [tdx[i]] + ([toc] if i == 0 else []) + ([t_rsev] if (c == 0 and ts > 0) else [])
                ts1[i] = p.add("tensor", lambda e, c=c, r4=r4: e.matmul(
                    PSC[0:1, :], ones_c[:, :], xch[:, r4 * 512:(r4 + 1) * 512],
                    start=(c == 0), stop=(c == NCH - 1)), deps=d1)
                d2 = [tsq[i]] + ([t_muev] if (c == 0 and ts > 0) else [])
                ts2[i] = p.add("tensor", lambda e, c=c, r4=r4: e.matmul(
                    PSB[0:1, 0:512], ones_c[:, :], sq[:, r4 * 512:(r4 + 1) * 512],
                    start=(c == 0), stop=(c == NCH - 1)), deps=d2)
                i += 1
            sl = slice(ts * 512, (ts + 1) * 512)
            t_mu = p.add("vector", lambda e, sl=sl: e.tensor_scalar_mul(
                row_a[:, sl], PSC[0:1, :], 1.0 / 1024.0), deps=[ts1[i - 1]])
            t_ex2 = p.add("vector", lambda e: e.tensor_scalar_mul(
                row_c[:, :], PSB[0:1, 0:512], 1.0 / 1024.0), deps=[ts2[i - 1]])
            p.add("vector", lambda e, sl=sl: e.tensor_tensor(
                row_d[:, :], row_a[:, sl], row_a[:, sl], ALU.mult))
            p.add("vector", lambda e: e.tensor_tensor(
                row_c[:, :], row_c[:, :], row_d[:, :], ALU.subtract))
            t_vp = p.add("vector", lambda e: e.tensor_scalar_add(row_c[:, :], row_c[:, :], 1e-5))
            t_sd = p.add("scalar", lambda e: e.activation(row_d[:, :], row_c[:, :], AF.Sqrt),
                         deps=[t_vp])
            t_rstd = p.add("vector", lambda e, sl=sl: e.reciprocal(row_b[:, sl], row_d[:, :]),
                           deps=[t_sd])
            t_repmu = p.add("tensor", lambda e, sl=sl: e.matmul(
                PSB[0:128, 0:512], ones_r[:, :], row_a[:, sl], start=True, stop=True),
                deps=[t_mu, t_ex2, tor])
            t_muev = p.add("vector", lambda e, sl=sl: e.tensor_copy(
                mu_rep[:, sl], PSB[0:128, 0:512]), deps=[t_repmu])
            t_reprs = p.add("tensor", lambda e, sl=sl: e.matmul(
                PSC[0:128, :], ones_r[:, :], row_b[:, sl], start=True, stop=True),
                deps=[t_rstd])
            t_rsev = p.add("vector", lambda e, sl=sl: e.tensor_copy(
                rs_rep[:, sl], PSC[0:128, :]), deps=[t_reprs])

        t_mult = [None] * NCH
        t_cast2 = [None] * NCH
        t_lg = [None] * NCH
        t_lgl = None
        for c in range(NCH):
            hb = (c % 2) * T_
            deps = ([t_cast2[c - 2], t_lg[c - 2]] if c > 1
                    else [tsq[31], ts1[31], ts2[31], t_rsev, t_muev])
            tdx2 = p.dma("sync", xch[:, hb:hb + T_], xT[c * 128:(c + 1) * 128, :], deps)
            p.add("vector", lambda e, hb=hb: e.tensor_tensor(
                sq[:, :], xch[:, hb:hb + T_], mu_rep[:, :], ALU.subtract), deps=[tdx2])
            t_mult[c] = p.add("vector", lambda e, hb=hb: e.tensor_tensor(
                xch[:, hb:hb + T_], sq[:, :], rs_rep[:, :], ALU.mult))
            t_cast2[c] = p.add("vector", lambda e, c=c, hb=hb: e.tensor_copy(
                xn_bf[:, c * T_:(c + 1) * T_], xch[:, hb:hb + T_]))
            for ts in range(4):
                t_lgl = p.add("tensor", lambda e, c=c, ts=ts, hb=hb: e.matmul(
                    PSA[0:64, ts * 512:(ts + 1) * 512], selT_s[:, c * 64:(c + 1) * 64],
                    xch[:, hb + ts * 512: hb + (ts + 1) * 512], start=(c == 0), stop=(c == NCH - 1)),
                    deps=[t_mult[c]] + ([tsel] if c == 0 else []))
            t_lg[c] = t_lgl

        t_sig = p.add("scalar", lambda e: e.activation(sels[:, :], PSA[0:64, :], AF.Sigmoid),
                      deps=[t_lgl])
        t_cp = None
        t_gate_last = None
        for tt in range(16):
            t_tr = p.add("tensor", lambda e, tt=tt: e.transpose(
                PSB[0:128, 0:64], sels[0:64, tt * 128:(tt + 1) * 128], id_f[0:64, 0:64]),
                deps=[t_sig, tid] + ([t_cp] if t_cp else []))
            t_cp = p.add("vector", lambda e: e.tensor_copy(seltm[:, :], PSB[0:128, 0:64]),
                         deps=[t_tr])
            p.add("vector", lambda e: e.tensor_copy(wk1_[:, :], seltm[:, :]))
            for r in range(4):
                src = wk1_ if r % 2 == 0 else wk2_
                dst = wk2_ if r % 2 == 0 else wk1_
                p.add("vector", lambda e, src=src: e.reduce_max(mcol[:, :], src[:, :], AX.X))
                if r < 3:
                    mb = _ap(mcol, 0, [[1, 128], [0, 64]])
                    p.add("vector", lambda e, src=src, dst=dst, mb=mb: e.tensor_tensor(
                        dst[:, :], src[:, :], mb, ALU.is_equal))
                    p.add("vector", lambda e, src=src, dst=dst: e.scalar_tensor_tensor(
                        dst[:, :], dst[:, :], -1e30, src[:, :], ALU.mult, ALU.add))
            mb = _ap(mcol, 0, [[1, 128], [0, 64]])
            p.add("vector", lambda e, mb=mb: e.tensor_tensor(wk1_[:, :], seltm[:, :], mb, ALU.is_ge))
            t_gate_last = p.add("vector", lambda e, tt=tt: e.tensor_tensor(
                gates[:, tt * 64:(tt + 1) * 64], seltm[:, :], wk1_[:, :], ALU.mult))
        es1.__exit__(None, None, None)

        MARKS["A4"] = len(p.items)
        es2 = ExitStack()
        es2.__enter__()
        W_sb = es2.enter_context(nc.sbuf_tensor("W_sb", [128, 3 * NCH * 1024], BF16, side="right"))
        qT_sb = es2.enter_context(nc.sbuf_tensor("qT_sb", [128, NCH * T_], BF16, side="right"))
        kT_sb = es2.enter_context(nc.sbuf_tensor("kT_sb", [128, NCH * T_], BF16, side="right"))
        qf = es2.enter_context(nc.sbuf_tensor("qf", [128, 1024], F32, side="right"))
        qr = es2.enter_context(nc.sbuf_tensor("qr", [128, 1024], F32, side="right"))
        qr_bf = es2.enter_context(nc.sbuf_tensor("qr_bf", [128, 2 * 1024], BF16, side="right"))
        t1s = es2.enter_context(nc.sbuf_tensor("t1s", [128, 512], F32, side="right"))
        t2s = es2.enter_context(nc.sbuf_tensor("t2s", [128, 512], F32, side="right"))
        ct = es2.enter_context(nc.sbuf_tensor("ct", [128, 32], F32, side="right"))
        st = es2.enter_context(nc.sbuf_tensor("st", [128, 32], F32, side="right"))
        vstage = es2.enter_context(nc.sbuf_tensor("vstage", [128, 1040], BF16, side="right"))
        a3bar = BAR()
        tW = None
        for w_i, wsrc in enumerate((wqT, wkT, wvT)):
            for c in range(NCH):
                tW = p.dma("gpsimd", W_sb[:, (w_i * NCH + c) * 1024:(w_i * NCH + c + 1) * 1024],
                           wsrc[c * 128:(c + 1) * 128, :],
                           deps=a3bar if (w_i == 0 and c == 0) else [])
        t_vst0 = p.add("vector", lambda e: e.memset(vstage[:, :], 1.0))
        cosap = _ap(ct, 0, [[32, 128], [0, 16], [1, 32]])
        sinap = _ap(st, 0, [[32, 128], [0, 16], [1, 32]])

        pend = []
        t_evq = {0: None, 1: None}
        t_rope6 = None
        t_ctst = None
        t_qtcpy = None
        t_vcpy = None
        t_vdma = None
        gpar = 0
        t_pend_trs = {0: None, 1: None}

        def flush_pend():
            nonlocal t_qtcpy
            for (par, dstT, ptt, t_cast) in pend:
                t_trs = None
                for c8 in range(NCH):
                    t_trs = p.add("tensor", lambda e, c8=c8, par=par: e.transpose(
                        PST[:, c8 * 128:(c8 + 1) * 128],
                        qr_bf[:, par * 1024 + c8 * 128: par * 1024 + (c8 + 1) * 128],
                        id_bf[:, :]),
                        deps=[t_cast, tidb] + ([t_qtcpy] if (t_qtcpy and c8 == 0) else []))
                t_pend_trs[par] = t_trs
                for c8 in range(NCH):
                    t_qtcpy = p.add("vector", lambda e, c8=c8, ptt=ptt, dstT=dstT: e.tensor_copy(
                        dstT[:, c8 * T_ + ptt * 128: c8 * T_ + (ptt + 1) * 128],
                        PST[:, c8 * 128:(c8 + 1) * 128]),
                        deps=[t_trs] if c8 == 0 else [])
            pend.clear()

        for tt in range(16):
            tsl = slice(tt * 128, (tt + 1) * 128)
            t_ct = p.dma("sync", ct[:, :], cosL[tsl, :], deps=[t_ctst] if t_ctst else [])
            t_st = p.dma("sync", st[:, :], sinL[tsl, :])
            for w_i, dstT in ((0, qT_sb), (1, kT_sb)):
                t_mm = None
                for os in range(2):
                    for c in range(NCH):
                        d = []
                        if os == 0 and c == 0:
                            d = [tW, t_gate_last] + ([t_evq[w_i]] if t_evq[w_i] else [])
                        t_mm = p.add("tensor", lambda e, w_i=w_i, os=os, c=c, tt=tt: e.matmul(
                            PSA[0:128, w_i * 1024 + os * 512: w_i * 1024 + (os + 1) * 512],
                            xn_bf[:, c * T_ + tt * 128: c * T_ + (tt + 1) * 128],
                            W_sb[:, (w_i * NCH + c) * 1024 + os * 512:
                                    (w_i * NCH + c) * 1024 + (os + 1) * 512],
                            start=(c == 0), stop=(c == NCH - 1)), deps=d)
                flush_pend()
                t_evq[w_i] = p.add("scalar", lambda e, w_i=w_i: e.activation(
                    qf[:, :], PSA[0:128, w_i * 1024:(w_i + 1) * 1024], AF.Copy),
                    deps=[t_mm] + ([t_rope6] if t_rope6 else []))
                ev = lambda t, o=0: _ap(t, o, [[1024, 128], [64, 16], [2, 32]])
                p.add("vector", lambda e: e.tensor_tensor(t1s[:, :], ev(qf), cosap, ALU.mult),
                      deps=[t_evq[w_i], t_ct, t_st])
                p.add("vector", lambda e: e.tensor_tensor(t2s[:, :], ev(qf, 1), sinap, ALU.mult))
                p.add("vector", lambda e: e.tensor_tensor(ev(qr), t1s[:, :], t2s[:, :], ALU.subtract))
                p.add("vector", lambda e: e.tensor_tensor(t1s[:, :], ev(qf), sinap, ALU.mult))
                t_ctst = p.add("vector", lambda e: e.tensor_tensor(t2s[:, :], ev(qf, 1), cosap, ALU.mult))
                t_rope6 = p.add("vector", lambda e: e.tensor_tensor(ev(qr, 1), t1s[:, :], t2s[:, :], ALU.add))
                t_cast = p.add("vector", lambda e, gpar=gpar: e.tensor_copy(
                    qr_bf[:, gpar * 1024:(gpar + 1) * 1024], qr[:, :]),
                    deps=[t_pend_trs[gpar]] if t_pend_trs[gpar] else [])
                pend.append((gpar, dstT, tt, t_cast))
                gpar ^= 1
                if A4_NOLAG:
                    flush_pend()
            t_vmm = None
            for os in range(2):
                for c in range(NCH):
                    d = []
                    if os == 0 and c == 0:
                        d = [tW] + ([t_vcpy] if t_vcpy else [t_gate_last])
                    t_vmm = p.add("tensor", lambda e, os=os, c=c, tt=tt: e.matmul(
                        PSB[0:128, os * 512:(os + 1) * 512],
                        xn_bf[:, c * T_ + tt * 128: c * T_ + (tt + 1) * 128],
                        W_sb[:, (2 * NCH + c) * 1024 + os * 512:
                                (2 * NCH + c) * 1024 + (os + 1) * 512],
                        start=(c == 0), stop=(c == NCH - 1)), deps=d)
            t_vcpy = p.add("vector", lambda e: e.tensor_copy(
                _ap(vstage, 0, [[1040, 128], [65, 16], [1, 64]]), PSB[0:128, :]),
                deps=[t_vmm, t_vst0] + ([t_vdma] if t_vdma else []))
            t_vdma = p.dma("sync", vg_d[tt * 128:(tt + 1) * 128, :], vstage[:, :],
                           deps=[t_vcpy, t16])
        flush_pend()
        tqd = p.dma("sync", _ap(qT_d, 0, [[T_, 128], [128 * T_, NCH], [1, T_]]),
                    _ap(qT_sb, 0, [[NCH * T_, 128], [T_, NCH], [1, T_]]), deps=[t_qtcpy])
        tkd = p.dma("sync", _ap(kT_d, 0, [[T_, 128], [128 * T_, NCH], [1, T_]]),
                    _ap(kT_sb, 0, [[NCH * T_, 128], [T_, NCH], [1, T_]]), deps=[t_qtcpy])
        if DEBUG_FULL:
            xn_dbg = nc.dram_tensor("dbg_xn", [128, NCH * T_], BF16, kind="ExternalOutput")
            gates_dbg = nc.dram_tensor("dbg_gates", [128, 16 * 64], F32, kind="ExternalOutput")
            p.dma("sync", xn_dbg.ap(), xn_bf[:, :], deps=[p.tok(e) for e in P2.ENGS])
            p.dma("sync", gates_dbg.ap(), gates[:, :], deps=[p.tok(e) for e in P2.ENGS])
        es2.__exit__(None, None, None)

        MARKS["A5a"] = len(p.items)
        es3 = ExitStack()
        es3.__enter__()
        keys4 = es3.enter_context(nc.sbuf_tensor("keys4", [128, 2 * NCH * 512], BF16, side="right"))
        hg_bf = es3.enter_context(nc.sbuf_tensor("hg_bf", [128, 2 * 512], BF16, side="right"))
        tkey = {}
        tmm_last = {}
        tstt = {}
        thg = {}
        gi = 0
        for kg in range(16):
            kpar = kg % 2
            dk = [tmm_last[kg - 2]] if kg >= 2 else [tqd, tkd]
            tk = None
            for c in range(NCH):
                tk = p.dma("gpsimd", keys4[:, kpar * 4096 + c * 512: kpar * 4096 + (c + 1) * 512],
                           _ap(keys, (4 * kg) * E_ * 128 + c * 128 * 128,
                               [[128, 128], [E_ * 128, 4], [1, 128]]),
                           deps=dk if c == 0 else [])
            tkey[kg] = tk
            for tt in range(16):
                bank = gi % 2
                hpar = gi % 2
                t_mm = None
                for c in range(NCH):
                    d = []
                    if c == 0:
                        d = [tkey[kg]] + ([tstt[gi - 2]] if gi >= 2 else [])
                    t_mm = p.add("tensor", lambda e, c=c, tt=tt, kpar=kpar, bank=bank: e.matmul(
                        PSA[0:128, bank * 512:(bank + 1) * 512],
                        xn_bf[:, c * T_ + tt * 128: c * T_ + (tt + 1) * 128],
                        keys4[:, kpar * 4096 + c * 512: kpar * 4096 + (c + 1) * 512],
                        start=(c == 0), stop=(c == NCH - 1)), deps=d)
                tmm_last[kg] = t_mm
                gap = _ap(gates, tt * 64 + kg * 4, [[16 * 64, 128], [1, 4], [0, 128]])
                tstt[gi] = p.add("vector", lambda e, gap=gap, bank=bank, hpar=hpar: e.scalar_tensor_tensor(
                    hg_bf[:, hpar * 512:(hpar + 1) * 512], PSA[0:128, bank * 512:(bank + 1) * 512],
                    0.0, gap, ALU.max, ALU.mult),
                    deps=[t_mm] + ([thg[gi - 2]] if gi >= 2 else []))
                thg[gi] = p.dma("sync", hg_d[tt * 128:(tt + 1) * 128, kg * 512:(kg + 1) * 512],
                                hg_bf[:, hpar * 512:(hpar + 1) * 512], deps=[tstt[gi]])
                gi += 1
        es3.__exit__(None, None, None)

        MARKS["A5b"] = len(p.items)
        es4 = ExitStack()
        es4.__enter__()
        vals = es4.enter_context(nc.sbuf_tensor("vals", [128, 64 * 1024], BF16, side="right"))
        hgrow = es4.enter_context(nc.sbuf_tensor("hgrow", [128, 64 * 128], BF16, side="right"))
        gsT = es4.enter_context(nc.sbuf_tensor("gsT", [128, 2 * 512], BF16, side="right"))
        gstage = es4.enter_context(nc.sbuf_tensor("gstage", [128, 1024], F32, side="right"))
        tvals_k = {}
        a5bar = [tmm_last[15], tstt[gi - 1], thg[gi - 1]]
        for k in range(64):
            tvals_k[k] = p.dma("gpsimd", vals[:, k * 1024:(k + 1) * 1024],
                               _ap(values, k * 128 * E_, [[E_, 128], [1, E_]]),
                               deps=a5bar if k == 0 else [])
        t_trs_first = {}
        tev = {}
        tvmm = {}
        tsilu = None
        tgdma = None
        for tt in range(16):
            dh = [thg[15 * 16 + tt]]
            if tt >= 1:
                dh.append(t_trs_first[tt - 1])
            thgr = p.dma("sync" if tt % 2 == 0 else "scalar",
                         hgrow[:, :], hg_d[tt * 128:(tt + 1) * 128, :], deps=dh)
            for kg in range(16):
                gp = kg % 2
                t_trs = None
                for k4 in range(4):
                    d = []
                    if k4 == 0:
                        d = [thgr] + ([tev[(tt, kg - 1)]] if kg >= 1 else
                                      ([tev[(tt - 1, 15)]] if tt >= 1 else []))
                    t_trs = p.add("tensor", lambda e, kg=kg, k4=k4: e.transpose(
                        PST[:, k4 * 128:(k4 + 1) * 128],
                        hgrow[:, kg * 512 + k4 * 128: kg * 512 + (k4 + 1) * 128],
                        id_bf[:, :]), deps=d)
                    if k4 == 3:
                        t_trs_first[tt] = t_trs
                dv = [t_trs]
                if (tt, kg) >= (0, 2):
                    prev = (tt, kg - 2) if kg >= 2 else ((tt - 1, 14 + kg) if tt >= 1 else None)
                    if prev is not None and prev in tvmm:
                        dv.append(tvmm[prev])
                tev[(tt, kg)] = p.add("vector", lambda e, gp=gp: e.tensor_copy(
                    gsT[:, gp * 512:(gp + 1) * 512], PST[:, 0:512]), deps=dv)
                t_mm = None
                for k4 in range(4):
                    k = kg * 4 + k4
                    for vc in range(NCH):
                        first = (kg == 0 and k4 == 0 and vc in (0, 4))
                        last = (kg == 15 and k4 == 3 and vc in (3, 7))
                        d = []
                        if k4 == 0 and vc == 0:
                            d = [tev[(tt, kg)], tvals_k[4 * kg + 3]]
                            if kg == 0 and tsilu is not None:
                                d.append(tsilu)
                        t_mm = p.add("tensor", lambda e, k=k, k4=k4, vc=vc, first=first, last=last, gp=gp: e.matmul(
                            PSA[0:128, vc * 128:(vc + 1) * 128],
                            vals[:, k * 1024 + vc * 128: k * 1024 + (vc + 1) * 128],
                            gsT[:, gp * 512 + k4 * 128: gp * 512 + (k4 + 1) * 128],
                            start=first, stop=last, skip_group_check=True), deps=d)
                tvmm[(tt, kg)] = t_mm
            p.add("scalar", lambda e: e.activation(gstage[:, 0:512], PSA[0:128, 0:512], AF.Silu),
                  deps=[tvmm[(tt, 15)]] + ([tgdma] if tgdma else []))
            tsilu = p.add("scalar", lambda e: e.activation(
                gstage[:, 512:1024], PSA[0:128, 512:1024], AF.Silu))
            tgdma = p.dma("sync", _ap(g0_d, tt * 128, [[T_, 128], [128 * T_, NCH], [1, 128]]),
                          _ap(gstage, 0, [[1024, 128], [128, NCH], [1, 128]]), deps=[tsilu])
        es4.__exit__(None, None, None)
        esA.__exit__(None, None, None)

        MARKS["B"] = len(p.items)
        p.new_phase()
        p.add("gpsimd", lambda e: e.collective_compute(
            "AllGather", ALU.bypass, replica_groups=GRP,
            ins=[kT_d.ap()], outs=[kT_g.ap()]).then_inc(ccs[0], 1), inc=0)
        p.add("gpsimd", lambda e: e.collective_compute(
            "AllGather", ALU.bypass, replica_groups=GRP,
            ins=[vg_d.ap()], outs=[vg_g.ap()]).then_inc(ccs[1], 1), inc=0,
            deps=[(ccs[0], 1)])

        esB = ExitStack()
        esB.__enter__()
        esY = ExitStack()
        esY.__enter__()
        Yn = esY.enter_context(nc.sbuf_tensor("Yn", [128, NCH * T_], BF16))
        esb1 = ExitStack()
        esb1.__enter__()
        Qs = esb1.enter_context(nc.sbuf_tensor("Qs", [128, NCH * T_], BF16, side="right"))
        KA = esb1.enter_context(nc.sbuf_tensor("KA", [128, S_], BF16, side="right"))
        KB = esb1.enter_context(nc.sbuf_tensor("KB", [128, S_], BF16, side="right"))
        Vg = esb1.enter_context(nc.sbuf_tensor("Vg", [128, 64 * 260], BF16, side="right"))
        Ebf = esb1.enter_context(nc.sbuf_tensor("Ebf", [128, 4 * 1024], BF16, side="right"))
        rrow = esb1.enter_context(nc.sbuf_tensor("rrow", [1, 512], F32, side="right"))
        Rs = esb1.enter_context(nc.sbuf_tensor("Rs", [64, 512], F32, side="right"))
        tq0 = None
        for c in range(NCH):
            tq0 = p.dma("sync", Qs[:, c * T_:(c + 1) * T_], qT_d[c * 128:(c + 1) * 128, :])
        texp = {}
        tavt = {}
        t_ynorm = None
        t_recip = None
        t_repl = None
        t_Rs = None
        bp = 0
        t_klast = {}
        t_avlastg = {}
        for g in range(4):
            dK = [(ccs[0], 1)] + ([t_klast[g - 1]] if g >= 1 else [])
            tka = p.dma("sync", KA[:, :],
                        _ap(kT_g, g * 256 * T_, [[T_, 128], [E_ * T_, 4], [1, T_]]), deps=dK)
            tkb = p.dma("sync", KB[:, :],
                        _ap(kT_g, (g * 256 + 128) * T_, [[T_, 128], [E_ * T_, 4], [1, T_]]))
            dV = [(ccs[1], 1)] + ([t_avlastg[g - 1]] if g >= 1 else [])
            tvg = p.dma("sync", Vg[:, :],
                        _ap(vg_g, g * 260, [[1040, 128], [T_ * 1040, 4], [128 * 1040, 16], [1, 260]]),
                        deps=dV)
            for hlp in range(2):
                h0 = 4 * g + 2 * hlp
                K_t = KA if hlp == 0 else KB
                qc0 = (h0 // 2) * T_
                hl0 = 2 * hlp
                hl1 = 2 * hlp + 1
                for ts in range(4):
                    t_av = None

                    def em_av(u, hl0=hl0, hl1=hl1):
                        nonlocal t_av
                        sb = u - bp
                        ep = u % 4
                        d = [texp[u]]
                        if sb == 0:
                            d = d + [tvg] + ([t_ynorm] if t_ynorm else [])
                        p.add("tensor", lambda e, sb=sb, hl0=hl0, ep=ep: e.matmul(
                            PSB[0:65, 0:512],
                            Vg[:, sb * 260 + hl0 * 65: sb * 260 + (hl0 + 1) * 65],
                            Ebf[:, ep * 1024: ep * 1024 + 512],
                            start=(sb == 0), stop=(sb == 63)), deps=d)
                        t_av = p.add("tensor", lambda e, sb=sb, hl1=hl1, ep=ep: e.matmul(
                            PSB[0:65, 512:1024],
                            Vg[:, sb * 260 + hl1 * 65: sb * 260 + (hl1 + 1) * 65],
                            Ebf[:, ep * 1024 + 512: ep * 1024 + 1024],
                            start=(sb == 0), stop=(sb == 63)),
                            deps=[texp[u]] + ([tvg, t_ynorm] if (sb == 0 and t_ynorm) else
                                              ([tvg] if sb == 0 else [])))
                        tavt[u] = t_av

                    for sb in range(64):
                        u = bp + sb
                        rg = u % 2
                        d = [texp[u - 2]] if u >= 2 else []
                        if sb == 0:
                            d = d + [tka, tkb, tq0]
                        p.add("tensor",
                            lambda e, K_t=K_t, sb=sb, qc0=qc0, ts=ts, rg=rg: e.matmul(
                                PSA[0:128, rg * 1024: rg * 1024 + 512],
                                K_t[0:64, sb * 128:(sb + 1) * 128],
                                Qs[0:64, qc0 + ts * 512: qc0 + (ts + 1) * 512],
                                start=True, stop=True), deps=d)
                        tm = p.add("tensor",
                            lambda e, K_t=K_t, sb=sb, qc0=qc0, ts=ts, rg=rg: e.matmul(
                                PSA[0:128, rg * 1024 + 512: rg * 1024 + 1024],
                                K_t[64:128, sb * 128:(sb + 1) * 128],
                                Qs[64:128, qc0 + ts * 512: qc0 + (ts + 1) * 512],
                                start=True, stop=True))
                        ed = [tm] + ([tavt[u - 4]] if u >= 4 else [])
                        texp[u] = p.add("scalar",
                            lambda e, rg=rg, ep=u % 4: e.activation(
                                Ebf[:, ep * 1024:(ep + 1) * 1024],
                                PSA[0:128, rg * 1024:(rg + 1) * 1024],
                                AF.Exp, scale=0.125), deps=ed)
                        if sb >= 1:
                            em_av(u - 1)
                    em_av(bp + 63)
                    bp += 64
                    t_r0 = p.add("vector", lambda e: e.reciprocal(rrow[:, :], PSB[64:65, 0:512]),
                                 deps=[t_av] + ([t_repl] if t_repl else []))
                    t_p0 = p.add("tensor", lambda e: e.matmul(
                        PSC[0:64, :], ones_r[0:1, 0:64], rrow[:, :], start=True, stop=True),
                        deps=[t_r0] + ([t_Rs] if t_Rs else []))
                    t_R0 = p.add("scalar", lambda e: e.activation(Rs[:, :], PSC[0:64, :], AF.Copy),
                                 deps=[t_p0] + ([t_ynorm] if t_ynorm else []))
                    t_y0 = p.add("vector", lambda e, qc0=qc0, ts=ts: e.tensor_tensor(
                        Yn[0:64, qc0 + ts * 512: qc0 + (ts + 1) * 512],
                        PSB[0:64, 0:512], Rs[:, :], ALU.mult), deps=[t_R0])
                    t_r1 = p.add("vector", lambda e: e.reciprocal(rrow[:, :], PSB[64:65, 512:1024]),
                                 deps=[t_av, t_p0])
                    t_p1 = p.add("tensor", lambda e: e.matmul(
                        PSC[0:64, :], ones_r[0:1, 0:64], rrow[:, :], start=True, stop=True),
                        deps=[t_r1, t_R0])
                    t_R1 = p.add("scalar", lambda e: e.activation(Rs[:, :], PSC[0:64, :], AF.Copy),
                                 deps=[t_p1, t_y0])
                    t_ynorm = p.add("vector", lambda e, qc0=qc0, ts=ts: e.tensor_tensor(
                        Yn[64:128, qc0 + ts * 512: qc0 + (ts + 1) * 512],
                        PSB[0:64, 512:1024], Rs[:, :], ALU.mult), deps=[t_R1])
                    t_repl = t_p1
                    t_Rs = t_R1
                t_klast[g] = texp[bp - 1]
            t_avlastg[g] = tavt[bp - 1]
        esb1.__exit__(None, None, None)

        MARKS["B2"] = len(p.items)
        esb2 = ExitStack()
        esb2.__enter__()
        X1 = esb2.enter_context(nc.sbuf_tensor("X1", [128, NCH * T_], F32, side="right"))
        wo_bf = esb2.enter_context(nc.sbuf_tensor("wo_bf", [128, NCH * 1024], BF16, side="right"))
        xsl = esb2.enter_context(nc.sbuf_tensor("xsl", [128, 2 * 512], F32, side="right"))
        gsl = esb2.enter_context(nc.sbuf_tensor("gsl", [128, 2 * 512], F32, side="right"))
        tsl_ = esb2.enter_context(nc.sbuf_tensor("tsl_", [128, 512], F32, side="right"))
        b1bar = BAR()
        tWo = None
        for c in range(NCH):
            tWo = p.dma("gpsimd", wo_bf[:, c * 1024:(c + 1) * 1024], woT[c * 128:(c + 1) * 128, :],
                        deps=b1bar if c == 0 else [])
        t_addi = {}
        t_muli = {}
        txd = {}
        tgd = {}
        i = 0
        for oc in range(NCH):
            for ts in range(4):
                par = i % 2
                d = [tWo, t_ynorm] if i == 0 else ([t_muli[i - 1]] if i >= 1 else [])
                t_mm = None
                for c in range(NCH):
                    t_mm = p.add("tensor", lambda e, oc=oc, ts=ts, c=c: e.matmul(
                        PSA[0:128, 0:512],
                        wo_bf[:, c * 1024 + oc * 128: c * 1024 + oc * 128 + 128],
                        Yn[:, c * T_ + ts * 512: c * T_ + (ts + 1) * 512],
                        start=(c == 0), stop=(c == NCH - 1)), deps=d if c == 0 else [])
                txd[i] = p.dma("sync", xsl[:, par * 512:(par + 1) * 512],
                               xT[oc * 128:(oc + 1) * 128, ts * 512:(ts + 1) * 512],
                               deps=[t_addi[i - 2]] if i >= 2 else b1bar)
                tgd[i] = p.dma("scalar", gsl[:, par * 512:(par + 1) * 512],
                               g0_d[oc * 128:(oc + 1) * 128, ts * 512:(ts + 1) * 512],
                               deps=[t_muli[i - 2]] if i >= 2 else b1bar)
                t_muli[i] = p.add("vector", lambda e, par=par: e.tensor_tensor(
                    tsl_[:, :], PSA[0:128, 0:512], gsl[:, par * 512:(par + 1) * 512], ALU.mult),
                    deps=[t_mm, tgd[i]])
                t_addi[i] = p.add("vector", lambda e, oc=oc, ts=ts, par=par: e.tensor_tensor(
                    X1[:, oc * T_ + ts * 512: oc * T_ + (ts + 1) * 512],
                    tsl_[:, :], xsl[:, par * 512:(par + 1) * 512], ALU.add), deps=[txd[i]])
                p.dma("sync", x1_d[oc * 128:(oc + 1) * 128, ts * 512:(ts + 1) * 512],
                      X1[:, oc * T_ + ts * 512: oc * T_ + (ts + 1) * 512], deps=[t_addi[i]])
                i += 1
        esY.__exit__(None, None, None)

        MARKS["B3"] = len(p.items)
        xn1 = esB.enter_context(nc.sbuf_tensor("xn1", [128, NCH * T_], BF16))
        esb3 = ExitStack()
        esb3.__enter__()
        mu1 = esb3.enter_context(nc.sbuf_tensor("mu1", [128, T_], F32, side="right"))
        rs1 = esb3.enter_context(nc.sbuf_tensor("rs1", [128, T_], F32, side="right"))
        sqs = esb3.enter_context(nc.sbuf_tensor("sqs", [128, 512], F32, side="right"))
        row_a2 = esb3.enter_context(nc.sbuf_tensor("row_a2", [1, T_], F32, side="right"))
        row_b2 = esb3.enter_context(nc.sbuf_tensor("row_b2", [1, T_], F32, side="right"))
        row_c2 = esb3.enter_context(nc.sbuf_tensor("row_c2", [1, 512], F32, side="right"))
        row_d2 = esb3.enter_context(nc.sbuf_tensor("row_d2", [1, 512], F32, side="right"))
        scr_a = esb3.enter_context(nc.sbuf_tensor("scr_a", [128, T_], F32, side="right"))
        scr_b = esb3.enter_context(nc.sbuf_tensor("scr_b", [128, T_], F32, side="right"))
        p.serial = True
        p._last = t_addi[31]
        for ts in range(4):
            for c in range(NCH):
                p.add("scalar", lambda e, c=c, ts=ts: e.activation(
                    sqs[:, :], X1[:, c * T_ + ts * 512: c * T_ + (ts + 1) * 512], AF.Square))
                p.add("tensor", lambda e, c=c, ts=ts: e.matmul(
                    PSC[0:1, :], ones_c[:, :],
                    X1[:, c * T_ + ts * 512: c * T_ + (ts + 1) * 512],
                    start=(c == 0), stop=(c == NCH - 1)))
                p.add("tensor", lambda e, c=c: e.matmul(
                    PSB[0:1, 0:512], ones_c[:, :], sqs[:, :],
                    start=(c == 0), stop=(c == NCH - 1)))
            sl = slice(ts * 512, (ts + 1) * 512)
            p.add("vector", lambda e, sl=sl: e.tensor_scalar_mul(row_a2[:, sl], PSC[0:1, :], 1.0 / 1024.0))
            p.add("vector", lambda e: e.tensor_scalar_mul(row_c2[:, :], PSB[0:1, 0:512], 1.0 / 1024.0))
            p.add("vector", lambda e, sl=sl: e.tensor_tensor(row_d2[:, :], row_a2[:, sl], row_a2[:, sl], ALU.mult))
            p.add("vector", lambda e: e.tensor_tensor(row_c2[:, :], row_c2[:, :], row_d2[:, :], ALU.subtract))
            p.add("vector", lambda e: e.tensor_scalar_add(row_c2[:, :], row_c2[:, :], 1e-5))
            p.add("scalar", lambda e: e.activation(row_d2[:, :], row_c2[:, :], AF.Sqrt))
            p.add("vector", lambda e, sl=sl: e.reciprocal(row_b2[:, sl], row_d2[:, :]))
            p.add("tensor", lambda e, sl=sl: e.matmul(
                PSB[0:128, 0:512], ones_r[:, :], row_a2[:, sl], start=True, stop=True))
            p.add("vector", lambda e, sl=sl: e.tensor_copy(mu1[:, sl], PSB[0:128, 0:512]))
            p.add("tensor", lambda e, sl=sl: e.matmul(
                PSC[0:128, :], ones_r[:, :], row_b2[:, sl], start=True, stop=True))
            p.add("vector", lambda e, sl=sl: e.tensor_copy(rs1[:, sl], PSC[0:128, :]))
        t_xn1 = None
        for c in range(NCH):
            p.add("vector", lambda e, c=c: e.tensor_tensor(
                scr_a[:, :], X1[:, c * T_:(c + 1) * T_], mu1[:, :], ALU.subtract))
            p.add("vector", lambda e: e.tensor_tensor(scr_b[:, :], scr_a[:, :], rs1[:, :], ALU.mult))
            t_xn1 = p.add("vector", lambda e, c=c: e.tensor_copy(xn1[:, c * T_:(c + 1) * T_], scr_b[:, :]))
        p.serial = False
        esb3.__exit__(None, None, None)
        esb2.__exit__(None, None, None)

        MARKS["B4"] = len(p.items)
        esb4 = ExitStack()
        esb4.__enter__()
        W1 = esb4.enter_context(nc.sbuf_tensor("W1", [128, 3 * NCH * 1024], BF16, side="right"))
        U512 = esb4.enter_context(nc.sbuf_tensor("U512", [128, T_], F32, side="right"))
        U128 = esb4.enter_context(nc.sbuf_tensor("U128", [128, T_], F32, side="right"))
        LG = esb4.enter_context(nc.sbuf_tensor("LG", [128, NCH], F32, side="right"))
        NLG = esb4.enter_context(nc.sbuf_tensor("NLG", [128, NCH], F32, side="right"))
        lnGr = esb4.enter_context(nc.sbuf_tensor("lnGr", [16, 1], F32, side="right"))
        lnG = esb4.enter_context(nc.sbuf_tensor("lnG", [16, 1], F32, side="right"))
        selq_s = esb4.enter_context(nc.sbuf_tensor("selq_s", [16, NCH * 128], F32, side="right"))
        Pq = esb4.enter_context(nc.sbuf_tensor("Pq", [128, T_], F32, side="right"))
        Pk = esb4.enter_context(nc.sbuf_tensor("Pk", [128, T_], F32, side="right"))
        scp = esb4.enter_context(nc.sbuf_tensor("scp", [128, T_], F32, side="right"))
        qst = esb4.enter_context(nc.sbuf_tensor("qst", [128, 2 * T_], BF16, side="right"))
        v1st = esb4.enter_context(nc.sbuf_tensor("v1st", [128, 2 * 1024], BF16, side="right"))
        b3bar = BAR()
        tW1 = None
        for w_i, wsrc in enumerate((wq1T, wk1T, wv1T)):
            for c in range(NCH):
                tW1 = p.dma("gpsimd", W1[:, (w_i * NCH + c) * 1024:(w_i * NCH + c + 1) * 1024],
                            wsrc[c * 128:(c + 1) * 128, :],
                            deps=b3bar if (w_i == 0 and c == 0) else [])
        p.serial = True
        p._last = t_xn1
        tselq = p.dma("sync", selq_s[:, :], selq[:, :])
        p.dma("sync", lnGr[:, :], gam[:, :])
        p.add("scalar", lambda e: e.activation(lnG[:, :], lnGr[:, :], AF.Ln))
        for c in range(NCH):
            p.add("tensor", lambda e, c=c: e.matmul(
                PSC[0:128, 0:1], selq_s[:, c * 128:(c + 1) * 128], lnG[:, :],
                start=True, stop=True))
            p.add("vector", lambda e, c=c: e.tensor_copy(LG[:, c:c + 1], PSC[0:128, 0:1]))
        p.add("vector", lambda e: e.tensor_scalar_mul(NLG[:, :], LG[:, :], -1.0))
        p.add("gpsimd", lambda e: e.iota(U512[:, :], [[0, 4], [1, 512]], base=0,
                                         channel_multiplier=0,
                                         allow_small_or_imprecise_dtypes=True))
        t_pre = p.add("gpsimd", lambda e: e.iota(U128[:, :], [[0, 16], [1, 128]], base=0,
                                                 channel_multiplier=0,
                                                 allow_small_or_imprecise_dtypes=True))
        p.serial = False
        LN8 = float(np.log(0.125))
        t_qev = {}
        t_qdma = {}
        gidx = 0
        t_Pq = t_Pk = None
        for oc in range(NCH):
            lgb = _ap(LG, oc, [[NCH, 128], [0, T_]])
            nlgb = _ap(NLG, oc, [[NCH, 128], [0, T_]])
            t_s1 = p.add("vector", lambda e, lgb=lgb: e.tensor_tensor(scp[:, :], U512[:, :], lgb, ALU.mult),
                         deps=[t_pre] + ([t_Pq] if t_Pq else []))
            t_Pq = p.add("scalar", lambda e: e.activation(Pq[:, :], scp[:, :], AF.Exp,
                                                          bias=np.float32(LN8)), deps=[t_s1])
            t_s2 = p.add("vector", lambda e, nlgb=nlgb: e.tensor_tensor(scp[:, :], U128[:, :], nlgb, ALU.mult),
                         deps=[t_Pq])
            t_Pk = p.add("scalar", lambda e: e.activation(Pk[:, :], scp[:, :], AF.Exp), deps=[t_s2])
            for w_i, dst, Pt, t_P in ((0, qt1_d, Pq, t_Pq), (1, kt1_d, Pk, t_Pk)):
                par = gidx % 2
                t_mm = None
                ev_in = {}
                for ts in range(4):
                    for c in range(NCH):
                        d = []
                        if ts == 0 and c == 0:
                            d = [tW1, t_xn1] + ([t_qev[gidx - 2]] if gidx >= 2 else [])
                        if c == 0 and ts >= 2:
                            d = d + [ev_in[ts - 2]]
                        t_mm = p.add("tensor", lambda e, w_i=w_i, oc=oc, c=c, ts=ts, par=par: e.matmul(
                            PSA[0:128, par * 1024 + (ts % 2) * 512: par * 1024 + (ts % 2 + 1) * 512],
                            W1[:, (w_i * NCH + c) * 1024 + oc * 128:
                                  (w_i * NCH + c) * 1024 + oc * 128 + 128],
                            xn1[:, c * T_ + ts * 512: c * T_ + (ts + 1) * 512],
                            start=(c == 0), stop=(c == NCH - 1)), deps=d)
                    t_qev[gidx] = p.add("vector", lambda e, ts=ts, Pt=Pt, par=par: e.tensor_tensor(
                        qst[:, par * T_ + ts * 512: par * T_ + (ts + 1) * 512],
                        PSA[0:128, par * 1024 + (ts % 2) * 512: par * 1024 + (ts % 2 + 1) * 512],
                        Pt[:, ts * 512:(ts + 1) * 512], ALU.mult),
                        deps=[t_mm, t_P] + ([t_qdma[gidx - 2]] if gidx >= 2 else []))
                    ev_in[ts] = t_qev[gidx]
                t_qdma[gidx] = p.dma("sync", dst[oc * 128:(oc + 1) * 128, :],
                                     qst[:, par * T_:(par + 1) * T_], deps=[t_qev[gidx]])
                gidx += 1
        t_v1c = {}
        t_v1d = {}
        for tt in range(16):
            par = tt % 2
            t_mm = None
            for os in range(2):
                for c in range(NCH):
                    d = []
                    if os == 0 and c == 0:
                        d = [tW1, t_xn1] + ([t_v1c[tt - 1]] if tt >= 1 else [])
                    t_mm = p.add("tensor", lambda e, os=os, c=c, tt=tt: e.matmul(
                        PSB[0:128, os * 512:(os + 1) * 512],
                        xn1[:, c * T_ + tt * 128: c * T_ + (tt + 1) * 128],
                        W1[:, (2 * NCH + c) * 1024 + os * 512:
                              (2 * NCH + c) * 1024 + (os + 1) * 512],
                        start=(c == 0), stop=(c == NCH - 1)), deps=d)
            t_v1c[tt] = p.add("vector", lambda e, par=par: e.tensor_copy(
                v1st[:, par * 1024:(par + 1) * 1024], PSB[0:128, 0:1024]),
                deps=[t_mm] + ([t_v1d[tt - 2]] if tt >= 2 else []))
            t_v1d[tt] = p.dma("sync", v1_d[tt * 128:(tt + 1) * 128, :],
                              v1st[:, par * 1024:(par + 1) * 1024], deps=[t_v1c[tt]])
        esb4.__exit__(None, None, None)

        MARKS["B5"] = len(p.items)
        esb5 = ExitStack()
        esb5.__enter__()
        wq1 = esb5.enter_context(nc.sbuf_tensor("wq1", [128, NCH * 1024], BF16, side="right"))
        kch = esb5.enter_context(nc.sbuf_tensor("kch", [128, 1024], F32, side="right"))
        tqx = esb5.enter_context(nc.sbuf_tensor("tqx", [128, 1024], F32, side="right"))
        am8 = esb5.enter_context(nc.sbuf_tensor("am8", [128, NCH], F32, side="right"))
        amc = esb5.enter_context(nc.sbuf_tensor("amc", [128, 1], F32, side="right"))
        amr = esb5.enter_context(nc.sbuf_tensor("amr", [1, 128], F32, side="right"))
        am = esb5.enter_context(nc.sbuf_tensor("am", [1, 1], F32, side="right"))
        srow = esb5.enter_context(nc.sbuf_tensor("srow", [1, 1], F32, side="right"))
        rsrow = esb5.enter_context(nc.sbuf_tensor("rsrow", [1, 1], F32, side="right"))
        scol = esb5.enter_context(nc.sbuf_tensor("scol", [128, 1], F32, side="right"))
        rscol = esb5.enter_context(nc.sbuf_tensor("rscol", [128, 1], F32, side="right"))
        a_bf = esb5.enter_context(nc.sbuf_tensor("a_bf", [128, NCH * 16], BF16, side="right"))
        b_bf = esb5.enter_context(nc.sbuf_tensor("b_bf", [16, 1024], BF16, side="right"))
        bias_s = esb5.enter_context(nc.sbuf_tensor("bias_s", [128, NCH], F32, side="right"))
        u_bf = esb5.enter_context(nc.sbuf_tensor("u_bf", [16, T_], BF16, side="right"))
        g1st = esb5.enter_context(nc.sbuf_tensor("g1st", [128, 2 * 512], F32, side="right"))
        p.serial = True
        for c in range(NCH):
            p.dma("sync", kch[:, :], kern1[c * 128:(c + 1) * 128, :])
            p.add("vector", lambda e, c=c: e.reduce_max(
                am8[:, c:c + 1], kch[:, :], AX.X, apply_absolute_value=True))
        p.add("vector", lambda e: e.reduce_max(amc[:, :], am8[:, :], AX.X))
        p.dma("sync", am_d[:, :], amc[:, :])
        p.dma("sync", amr[:, :], _ap(am_d, 0, [[128, 1], [1, 128]]))
        p.add("vector", lambda e: e.reduce_max(am[:, :], amr[:, :], AX.X))
        p.add("vector", lambda e: e.tensor_scalar_mul(srow[:, :], am[:, :], 1.0 / 7.0))
        p.add("vector", lambda e: e.reciprocal(rsrow[:, :], srow[:, :]))
        p.add("tensor", lambda e: e.matmul(PSC[0:128, 0:1], ones_r[:, :], srow[:, :],
                                           start=True, stop=True))
        p.add("vector", lambda e: e.tensor_copy(scol[:, :], PSC[0:128, 0:1]))
        p.add("tensor", lambda e: e.matmul(PSC[0:128, 0:1], ones_r[:, :], rsrow[:, :],
                                           start=True, stop=True))
        p.add("vector", lambda e: e.tensor_copy(rscol[:, :], PSC[0:128, 0:1]))
        for c in range(NCH):
            p.dma("sync", kch[:, :], kern1[c * 128:(c + 1) * 128, :])
            p.add("vector", lambda e: e.tensor_scalar(
                tqx[:, :], kch[:, :], rscol[:, 0:1], MAGIC, ALU.mult, ALU.add))
            p.add("vector", lambda e: e.tensor_scalar(
                kch[:, :], tqx[:, :], MAGIC, -8.0, ALU.subtract, ALU.max))
            p.add("vector", lambda e, c=c: e.tensor_scalar(
                wq1[:, c * 1024:(c + 1) * 1024], kch[:, :], 7.0, scol[:, 0:1],
                ALU.min, ALU.mult))
        for c in range(NCH):
            p.dma("gpsimd", a_bf[:, c * 16:(c + 1) * 16], a1[c * 128:(c + 1) * 128, :])
            p.dma("sync", bias_s[:, c:c + 1], bias1[c * 128:(c + 1) * 128, :])
        p.dma("gpsimd", b_bf[:, :], b1[:, :])
        for ts in range(4):
            for c in range(NCH):
                p.add("tensor", lambda e, c=c, ts=ts: e.matmul(
                    PSC[0:16, :], a_bf[:, c * 16:(c + 1) * 16],
                    xn1[:, c * T_ + ts * 512: c * T_ + (ts + 1) * 512],
                    start=(c == 0), stop=(c == NCH - 1)))
            p.add("vector", lambda e, ts=ts: e.tensor_copy(u_bf[:, ts * 512:(ts + 1) * 512], PSC[0:16, :]))
        t_quant = p._last
        p.serial = False
        t_sil = {}
        t_gd = {}
        i = 0
        for fc in range(NCH):
            for ts in range(4):
                par = i % 2
                t_mm = None
                for c in range(NCH):
                    d = [t_quant] + ([t_sil[i - 2]] if i >= 2 else []) if c == 0 else []
                    t_mm = p.add("tensor", lambda e, fc=fc, ts=ts, c=c, par=par: e.matmul(
                        PSA[0:128, par * 512:(par + 1) * 512],
                        wq1[:, c * 1024 + fc * 128: c * 1024 + fc * 128 + 128],
                        xn1[:, c * T_ + ts * 512: c * T_ + (ts + 1) * 512],
                        start=(c == 0), stop=False), deps=d)
                t_mm = p.add("tensor", lambda e, fc=fc, ts=ts, par=par: e.matmul(
                    PSA[0:128, par * 512:(par + 1) * 512], b_bf[:, fc * 128:(fc + 1) * 128],
                    u_bf[:, ts * 512:(ts + 1) * 512], start=False, stop=True))
                t_sil[i] = p.add("scalar", lambda e, fc=fc, par=par: e.activation(
                    g1st[:, par * 512:(par + 1) * 512], PSA[0:128, par * 512:(par + 1) * 512],
                    AF.Silu, bias=bias_s[:, fc:fc + 1]),
                    deps=[t_mm] + ([t_gd[i - 2]] if i >= 2 else []))
                t_gd[i] = p.dma("sync", g1_d[fc * 128:(fc + 1) * 128, ts * 512:(ts + 1) * 512],
                                g1st[:, par * 512:(par + 1) * 512], deps=[t_sil[i]])
                i += 1
        esb5.__exit__(None, None, None)
        esB.__exit__(None, None, None)

        MARKS["D"] = len(p.items)
        p.new_phase()
        p.add("gpsimd", lambda e: e.collective_compute(
            "AllGather", ALU.bypass, replica_groups=GRP,
            ins=[qt1_d.ap()], outs=[qt1_g.ap()]).then_inc(ccs[2], 1), inc=0)
        p.add("gpsimd", lambda e: e.collective_compute(
            "AllGather", ALU.bypass, replica_groups=GRP,
            ins=[kt1_d.ap()], outs=[kt1_g.ap()]).then_inc(ccs[3], 1), inc=0,
            deps=[(ccs[2], 1)])
        p.add("gpsimd", lambda e: e.collective_compute(
            "AllGather", ALU.bypass, replica_groups=GRP,
            ins=[v1_d.ap()], outs=[v1_g.ap()]).then_inc(ccs[4], 1), inc=0,
            deps=[(ccs[3], 1)])

        esD = ExitStack()
        esD.__enter__()
        qtA = esD.enter_context(nc.sbuf_tensor("qtA", [128, S_], BF16, side="right"))
        qtB = esD.enter_context(nc.sbuf_tensor("qtB", [128, S_], BF16, side="right"))
        ktA = esD.enter_context(nc.sbuf_tensor("ktA", [128, S_], BF16, side="right"))
        ktB = esD.enter_context(nc.sbuf_tensor("ktB", [128, S_], BF16, side="right"))
        Vs = esD.enter_context(nc.sbuf_tensor("Vs", [128, 64 * 256], BF16, side="right"))
        y1s = esD.enter_context(nc.sbuf_tensor("y1s", [128, 2 * S_], BF16, side="right"))
        mg = esD.enter_context(nc.sbuf_tensor("mg", [1, 4 * 67], F32, side="right"))
        gr2 = esD.enter_context(nc.sbuf_tensor("gr2", [1, 4], F32, side="right"))
        lnr2 = esD.enter_context(nc.sbuf_tensor("lnr2", [1, 4], F32, side="right"))
        erow = esD.enter_context(nc.sbuf_tensor("erow", [1, 4 * 67], F32, side="right"))
        crow = esD.enter_context(nc.sbuf_tensor("crow", [1, 4 * 67], F32, side="right"))
        ctab = esD.enter_context(nc.sbuf_tensor("ctab", [128, 4 * 67], F32, side="right"))
        UD = esD.enter_context(nc.sbuf_tensor("UD", [128, 512], F32, side="right"))
        M01 = esD.enter_context(nc.sbuf_tensor("M01", [128, 512], F32, side="right"))
        Wbf = esD.enter_context(nc.sbuf_tensor("Wbf", [128, 4 * 1024], BF16, side="right"))
        rh = nc.gpsimd.alloc_register("rh")
        rh2 = nc.gpsimd.alloc_register("rh2")
        p.serial = True
        p._last = None
        p.add("gpsimd", lambda e: e.reg_load(rh, offs[0:1, 0:1]),
              deps=[(ccs[2], 1), (ccs[3], 1), (ccs[4], 1), tof], inc=0)
        p.add("gpsimd", lambda e: e.reg_mul(rh2, rh, T_), inc=0)
        tqa = p.dma("gpsimd", qtA[:, :], _ap(qt1_g, rh2, [[T_, 128], [E_ * T_, 4], [1, T_]]))
        tkaD = p.dma("gpsimd", ktA[:, :], _ap(kt1_g, rh2, [[T_, 128], [E_ * T_, 4], [1, T_]]))
        p.add("gpsimd", lambda e: e.reg_add(rh2, rh2, 128 * T_), inc=0)
        tqb = p.dma("gpsimd", qtB[:, :], _ap(qt1_g, rh2, [[T_, 128], [E_ * T_, 4], [1, T_]]))
        tkbD = p.dma("gpsimd", ktB[:, :], _ap(kt1_g, rh2, [[T_, 128], [E_ * T_, 4], [1, T_]]))
        tvsD = p.dma("gpsimd", Vs[:, :], _ap(v1_g, rh, [[E_, 128], [T_ * E_, 4], [128 * E_, 16], [1, 256]]))
        p.dma("sync", mg[:, :], mgrid[:, :])
        p.dma("sync", gr2[:, :], _ap(gamh, 0, [[4, 1], [1, 4]]))
        p.add("scalar", lambda e: e.activation(lnr2[:, :], gr2[:, :], AF.Ln))
        lnb = _ap(lnr2, 0, [[4, 1], [1, 4], [0, 67]])
        p.add("vector", lambda e, lnb=lnb: e.tensor_tensor(erow[:, :], mg[:, :], lnb, ALU.mult))
        p.add("scalar", lambda e: e.activation(crow[:, :], erow[:, :], AF.Exp))
        p.add("tensor", lambda e: e.matmul(PSB[0:128, 0:268], ones_r[:, :], crow[:, :],
                                           start=True, stop=True))
        p.add("vector", lambda e: e.tensor_copy(ctab[:, :], PSB[0:128, 0:268]))
        p.add("gpsimd", lambda e: e.iota(UD[:, :], [[1, 512]], base=0,
                                         channel_multiplier=-1,
                                         allow_small_or_imprecise_dtypes=True))
        t_setupD = p.add("vector", lambda e: e.tensor_single_scalar(M01[:, :], UD[:, :], 0.0, ALU.is_ge))
        p.serial = False
        bpD = 0
        tevD = {}
        tav2 = {}
        t_yev = None
        for hlp in range(2):
            kt = ktA if hlp == 0 else ktB
            qt = qtA if hlp == 0 else qtB
            for ts in range(16):
                nsb = 4 * ts + 4
                t_av = None

                def em_avD(u, hlp=hlp, nsb=nsb):
                    nonlocal t_av
                    sb = u - bpD
                    wq = u % 4
                    for hh in (0, 1):
                        hl = 2 * hlp + hh
                        d = [tevD[(u, hh)]]
                        if sb == 0:
                            d = d + [tvsD] + ([t_yev] if t_yev else [])
                        t_av = p.add("tensor",
                            lambda e, sb=sb, hl=hl, nsb=nsb, wq=wq, hh=hh: e.matmul(
                                PSB[0:64, hh * 512:(hh + 1) * 512],
                                Vs[:, sb * 256 + hl * 64: sb * 256 + (hl + 1) * 64],
                                Wbf[:, wq * 1024 + hh * 512: wq * 1024 + (hh + 1) * 512],
                                start=(sb == 0), stop=(sb == nsb - 1)), deps=d)
                        tav2[(u, hh)] = t_av

                for sb in range(nsb):
                    u = bpD + sb
                    rg = u % 2
                    d = ([tevD[(u - 2, 0)], tevD[(u - 2, 1)]] if u >= 2
                         else [t_setupD, tqa, tkaD, tqb, tkbD])
                    p.add("tensor",
                        lambda e, kt=kt, qt=qt, sb=sb, ts=ts, rg=rg: e.matmul(
                            PSA[0:128, rg * 1024: rg * 1024 + 512],
                            kt[0:64, sb * 128:(sb + 1) * 128],
                            qt[0:64, ts * 512:(ts + 1) * 512],
                            start=True, stop=True), deps=d if sb == 0 or u >= 2 else [])
                    tm = p.add("tensor",
                        lambda e, kt=kt, qt=qt, sb=sb, ts=ts, rg=rg: e.matmul(
                            PSA[0:128, rg * 1024 + 512: rg * 1024 + 1024],
                            kt[64:128, sb * 128:(sb + 1) * 128],
                            qt[64:128, ts * 512:(ts + 1) * 512],
                            start=True, stop=True))
                    wq = u % 4
                    for hh in (0, 1):
                        hl = 2 * hlp + hh
                        ed = [tm] + ([tav2[(u - 4, hh)]] if u >= 4 else [])
                        if sb < 4 * ts:
                            idx = hl * 67 + (4 * ts - sb) + 3
                            if hh == 1:
                                tevD[(u, hh)] = p.add("scalar",
                                    lambda e, idx=idx, rg=rg, hh=hh, wq=wq: e.activation(
                                        Wbf[:, wq * 1024 + hh * 512: wq * 1024 + (hh + 1) * 512],
                                        PSA[0:128, rg * 1024 + hh * 512: rg * 1024 + (hh + 1) * 512],
                                        AF.Copy, scale=ctab[:, idx:idx + 1]), deps=ed)
                            else:
                                tevD[(u, hh)] = p.add("vector",
                                    lambda e, idx=idx, rg=rg, hh=hh, wq=wq: e.tensor_scalar_mul(
                                        Wbf[:, wq * 1024 + hh * 512: wq * 1024 + (hh + 1) * 512],
                                        PSA[0:128, rg * 1024 + hh * 512: rg * 1024 + (hh + 1) * 512],
                                        ctab[:, idx:idx + 1]), deps=ed)
                        else:
                            j = sb - 4 * ts
                            idx = hl * 67 + 3 - j
                            if j > 0:
                                p.add("vector", lambda e, j=j, wq=wq, hh=hh: e.memset(
                                    Wbf[:, wq * 1024 + hh * 512: wq * 1024 + hh * 512 + j * 128],
                                    0.0), deps=ed)
                            tevD[(u, hh)] = p.add("vector",
                                lambda e, j=j, idx=idx, rg=rg, hh=hh, wq=wq: e.scalar_tensor_tensor(
                                    Wbf[:, wq * 1024 + hh * 512 + j * 128: wq * 1024 + (hh + 1) * 512],
                                    PSA[0:128, rg * 1024 + hh * 512 + j * 128: rg * 1024 + (hh + 1) * 512],
                                    ctab[:, idx:idx + 1], M01[:, 0:512 - j * 128],
                                    ALU.mult, ALU.mult), deps=ed)
                    if sb >= 1:
                        em_avD(u - 1)
                em_avD(bpD + nsb - 1)
                bpD += nsb
                t_y0 = p.add("vector", lambda e, hlp=hlp, ts=ts: e.tensor_copy(
                    y1s[0:64, hlp * S_ + ts * 512: hlp * S_ + (ts + 1) * 512],
                    PSB[0:64, 0:512]), deps=[tav2[(bpD - 1, 0)], tav2[(bpD - 1, 1)]])
                t_yev = p.add("vector", lambda e, hlp=hlp, ts=ts: e.tensor_copy(
                    y1s[64:128, hlp * S_ + ts * 512: hlp * S_ + (ts + 1) * 512],
                    PSB[0:64, 512:1024]), deps=[tav2[(bpD - 1, 1)]])
        ty1a = p.dma("sync", y1_d[0:128, :], y1s[:, 0:S_], deps=[t_yev])
        ty1b = p.dma("sync", y1_d[128:256, :], y1s[:, S_:2 * S_], deps=[t_yev])
        esD.__exit__(None, None, None)

        MARKS["E"] = len(p.items)
        p.new_phase()
        p.add("gpsimd", lambda e: e.collective_compute(
            "AllGather", ALU.bypass, replica_groups=GRP,
            ins=[y1_d.ap()], outs=[y1_g.ap()]).then_inc(ccs[5], 1), inc=0)

        esE = ExitStack()
        esE.__enter__()
        y1_sb = esE.enter_context(nc.sbuf_tensor("y1_sb", [128, NCH * T_], BF16, side="right"))
        wo1_bf = esE.enter_context(nc.sbuf_tensor("wo1_bf", [128, NCH * 1024], BF16, side="right"))
        xsl2 = esE.enter_context(nc.sbuf_tensor("xsl2", [128, 2 * 512], F32, side="right"))
        gsl2 = esE.enter_context(nc.sbuf_tensor("gsl2", [128, 2 * 512], F32, side="right"))
        tsl2 = esE.enter_context(nc.sbuf_tensor("tsl2", [128, 512], F32, side="right"))
        osl2 = esE.enter_context(nc.sbuf_tensor("osl2", [128, 2 * 512], F32, side="right"))
        rt = nc.gpsimd.alloc_register("rt")
        rt2 = nc.gpsimd.alloc_register("rt2")
        p.add("gpsimd", lambda e: e.reg_load(rt, offs[0:1, 1:2]),
              deps=[(ccs[5], 1), tof2], inc=0)
        ty1l = None
        for c in range(NCH):
            p.add("gpsimd", lambda e, c=c: e.reg_add(rt2, rt, c * 128 * S_), inc=0)
            ty1l = p.dma("gpsimd", y1_sb[:, c * T_:(c + 1) * T_],
                         _ap(y1_g, rt2, [[S_, 128], [1, T_]]))
            p.dma("gpsimd", wo1_bf[:, c * 1024:(c + 1) * 1024], wo1T[c * 128:(c + 1) * 128, :])
        tWo1 = p.tok("gpsimd")
        t_addE = {}
        t_mulE = {}
        i = 0
        for oc in range(NCH):
            for ts in range(4):
                par = i % 2
                t_mm = None
                for c in range(NCH):
                    d = [tWo1] if (c == 0) else []
                    if c == 0 and i >= 1:
                        d.append(t_mulE[i - 1])
                    t_mm = p.add("tensor", lambda e, oc=oc, ts=ts, c=c: e.matmul(
                        PSA[0:128, 0:512],
                        wo1_bf[:, c * 1024 + oc * 128: c * 1024 + oc * 128 + 128],
                        y1_sb[:, c * T_ + ts * 512: c * T_ + (ts + 1) * 512],
                        start=(c == 0), stop=(c == NCH - 1)), deps=d)
                txd2 = p.dma("sync", xsl2[:, par * 512:(par + 1) * 512],
                             x1_d[oc * 128:(oc + 1) * 128, ts * 512:(ts + 1) * 512],
                             deps=[t_addE[i - 2]] if i >= 2 else [])
                tgd2 = p.dma("scalar", gsl2[:, par * 512:(par + 1) * 512],
                             g1_d[oc * 128:(oc + 1) * 128, ts * 512:(ts + 1) * 512],
                             deps=[t_mulE[i - 2]] if i >= 2 else [])
                t_mulE[i] = p.add("vector", lambda e, par=par: e.tensor_tensor(
                    tsl2[:, :], PSA[0:128, 0:512], gsl2[:, par * 512:(par + 1) * 512], ALU.mult),
                    deps=[t_mm, tgd2])
                t_addE[i] = p.add("vector", lambda e, par=par: e.tensor_tensor(
                    osl2[:, par * 512:(par + 1) * 512], tsl2[:, :],
                    xsl2[:, par * 512:(par + 1) * 512], ALU.add), deps=[txd2])
                p.dma("sync", x2T_o[oc * 128:(oc + 1) * 128, ts * 512:(ts + 1) * 512],
                      osl2[:, par * 512:(par + 1) * 512], deps=[t_addE[i]])
                i += 1
        esE.__exit__(None, None, None)

        if DEBUG_FULL:
            dbg_srcs = dict(qT_d=qT_d, kT_d=kT_d, g0_d=g0_d, x1_d=x1_d,
                            qt1_d=qt1_d, kt1_d=kt1_d, v1_d=v1_d, g1_d=g1_d,
                            y1_d=y1_d)
            b0 = [p.tok(e) for e in P2.ENGS]
            for nm, t in dbg_srcs.items():
                dt = {"g0_d": F32, "x1_d": F32, "g1_d": F32}.get(nm, BF16)
                sh = [t.shape[0], t.shape[1]] if hasattr(t, "shape") else None
                od = nc.dram_tensor("dbg_" + nm, list(t.shape), dt, kind="ExternalOutput")
                p.dma("sync", od.ap(), t.ap(), deps=b0)

        MARKS["END"] = len(p.items)
        with allow_ici_all_to_all():
            p.emit(block)
    return nc


def _rope_tables_host():
    import jax.numpy as jnp
    with jax.default_device(jax.devices("cpu")[0]):
        idx = jnp.arange(HD_ // 2, dtype=jnp.float32)
        ang = jnp.power(jnp.asarray(10000.0, jnp.float32), 2.0 * idx / HD_)
        th = jnp.arange(S_, dtype=jnp.float32)[:, None] * ang[None, :]
        return np.asarray(jnp.cos(th)), np.asarray(jnp.sin(th))


def kernel(**inputs):
    asc = np.ascontiguousarray
    cores = list(range(NCORE))

    cos, sin = _rope_tables_host()
    idf = np.eye(128, dtype=np.float32)
    onesr = np.ones((1, 128), np.float32)
    onesc = np.ones((128, 1), np.float32)
    selq = np.zeros((16, 1024), np.float32)
    for c in range(NCH):
        selq[2 * c, c * 128:c * 128 + 64] = 1.0
        selq[2 * c + 1, c * 128 + 64:(c + 1) * 128] = 1.0
    mgrid = np.concatenate(
        [128.0 * np.arange(-3, 64, dtype=np.float32)[None, :]] * 4, axis=1)

    x = np.asarray(inputs["L0_attn_x"], np.float32)
    wqT = asc(inputs["L0_attn_w_q"].T)
    wkT = asc(inputs["L0_attn_w_k"].T)
    wvT = asc(inputs["L0_attn_w_v"].T)
    woT = asc(inputs["L0_attn_w_o"].T)
    selT = asc(inputs["L0_ffn_expert_sel"].T)
    keys = np.asarray(inputs["L0_ffn_keys"], np.float32)
    values = np.asarray(inputs["L0_ffn_values"], np.float32)
    wq1T = asc(inputs["L1_attn_wq"].T)
    wk1T = asc(inputs["L1_attn_wk"].T)
    wv1T = asc(inputs["L1_attn_wv"].T)
    wo1T = asc(inputs["L1_attn_wo"].T)
    gammas = np.asarray(inputs["L1_attn_gammas"], np.float32)
    kern1 = np.asarray(inputs["L1_ffn_kernel"], np.float32)
    bias1 = asc(inputs["L1_ffn_bias"].reshape(E_, 1))
    a1 = np.asarray(inputs["L1_ffn_a"], np.float32)
    b1 = np.asarray(inputs["L1_ffn_b"], np.float32)

    def core_bt(c):
        return c // 4, 2048 * (c % 4)

    in_maps = []
    for c in cores:
        b, t0 = core_bt(c)
        sl = slice(t0, t0 + T_)
        hg0 = 4 * (c % 4)
        in_maps.append(dict(
            xT=asc(x[b, sl].T), wqT=wqT, wkT=wkT, wvT=wvT,
            selT=selT, keys=keys, values=values,
            cosL=asc(cos[sl]), sinL=asc(sin[sl]),
            idf=idf, onesr=onesr, onesc=onesc,
            woT=woT, wq1T=wq1T, wk1T=wk1T, wv1T=wv1T, wo1T=wo1T,
            kern1=kern1, bias1=bias1, a1=a1, b1=b1,
            gam=asc(gammas.reshape(16, 1)), selq=selq,
            gamh=asc(gammas[hg0:hg0 + 4].reshape(4, 1)), mgrid=mgrid,
            hoff=np.array([[hg0 * 64]], np.int32),
            toff=np.array([[t0]], np.int32)))
    res = run_bass_kernel_spmd(build_full(), in_maps, core_ids=cores).results

    out = np.empty((B_, S_, E_), np.float32)
    for c in cores:
        b, t0 = core_bt(c)
        out[b, t0:t0 + T_, :] = np.asarray(res[c]["x2T_o"]).T
    return out
